# version 04
# baseline (speedup 1.0000x reference)
import sys
sys.path.insert(0, '/opt/trn_rl_repo')
import math
from contextlib import ExitStack

import numpy as np
import ml_dtypes

import concourse.bass as bass
import concourse.mybir as mybir
from concourse.bass_utils import run_bass_kernel_spmd
from concourse.alu_op_type import AluOpType as ALU

F32 = mybir.dt.float32
BF16 = mybir.dt.bfloat16
AF = mybir.ActivationFunctionType
AX = mybir.AxisListType
BF = ml_dtypes.bfloat16

B, S, D = 4, 4096, 2048
H, KVH, DH = 16, 8, 128
TLOC = S // 2
FC = D // 128
TOPK0 = 1638
K1 = 512
DFF = 4 * D
ROPE_THETA = 1e6
ISQ = 1.0 / math.sqrt(DH)
NEG = -1.0e9
MSHIFT = -8.0

ENGINES = ("sync", "scalar", "vector", "tensor", "gpsimd")


class Sch:

    def __init__(self, nc, ctx):
        self.nc = nc
        self.ctx = ctx
        self.ops = {e: [] for e in ENGINES}
        self.cnt = {}
        self.sems = {}
        self.chain_last = {}
        self.bar = None
        for name in ("tC", "vC", "sC", "gC"):
            self.sems[name] = ctx.enter_context(nc.semaphore(name))
            self.cnt[name] = 0

    PARANOID = False

    def barrier(self):
        self.bar = [v for v in self.chain_last.values()]

    def op(self, eng, fn, waits=(), inc=None, amt=1):
        waits = list(waits)
        if eng in ("vector", "scalar", "gpsimd") and inc is not None:
            waits.append(self.chain_last.get("eng_" + eng))
        if self.bar:
            waits += self.bar
        if getattr(self, "serialize", False):
            waits += [v for k, v in self.chain_last.items()]
        if Sch.PARANOID is True or (isinstance(Sch.PARANOID, str) and
                                    ((Sch.PARANOID == "tensor" and eng == "tensor") or
                                     (Sch.PARANOID == "nontensor" and eng != "tensor"))):
            waits += [v for k, v in self.chain_last.items()]
        sem = self.sems[inc] if inc else None
        tok = None
        if sem is not None:
            self.cnt[inc] += amt
            tok = (sem, self.cnt[inc])
        self.ops[eng].append((fn, tuple(w for w in waits if w is not None),
                              sem, amt if sem else 0))
        if tok is not None and inc in ("vC", "sC", "gC", "tC"):
            self.chain_last["eng_" + eng] = tok
        return tok

    def dma(self, eng, out, in_, waits=(), chain=None, **kw):
        if chain is None:
            chain = "q_" + eng
        if chain not in self.sems:
            self.sems[chain] = self.ctx.enter_context(self.nc.semaphore(chain))
            self.cnt[chain] = 0
        prev = self.chain_last.get(chain)
        tok = self.op(eng, lambda e: e.dma_start(out=out, in_=in_, **kw),
                      waits=list(waits) + [prev], inc=chain, amt=16)
        self.chain_last[chain] = tok
        return tok

    def emit(self, ctx):
        blk = ctx.enter_context(self.nc.Block())
        for eng in ENGINES:
            ops = self.ops[eng]
            if not ops:
                continue

            def body(e, ops=ops):
                hw = {}
                for fn, waits, sem, amt in ops:
                    for (wsem, val) in waits:
                        if hw.get(id(wsem), -1) < val:
                            e.wait_ge(wsem, val)
                            hw[id(wsem)] = val
                    inst = fn(e)
                    if sem is not None:
                        inst.then_inc(sem, amt)

            getattr(blk, eng)(body)


def ceil_div(a, b):
    return -(-a // b)



_HC = None


def host_consts():
    global _HC
    if _HC is not None:
        return _HC
    c = {}
    pos = np.arange(S, dtype=np.float64)
    inv_freq = 1.0 / (ROPE_THETA ** (np.arange(0, DH, 2, dtype=np.float64) / DH))
    freqs = pos[None, :] * inv_freq[:, None]
    emb = np.concatenate([freqs, freqs], axis=0)
    c["cos_t"] = np.cos(emb).astype(np.float32)
    c["sin_t"] = np.sin(emb).astype(np.float32)
    M = np.zeros((DH, DH), dtype=np.float32)
    for j in range(64):
        M[j + 64, j] = -1.0
    for j in range(64, 128):
        M[j - 64, j] = 1.0
    c["prot"] = M.astype(BF)
    c["ident"] = np.eye(128, dtype=np.float32).astype(BF)
    c["ones_col"] = np.ones((128, 1), dtype=np.float32).astype(BF)
    c["ones_row"] = np.ones((1, 128), dtype=np.float32).astype(BF)
    bd = np.zeros((128, 16), dtype=np.float32)
    for h in range(16):
        bd[h * 8:(h + 1) * 8, h] = 1.0
    c["bdiag16"] = bd
    c["bcast16"] = np.ascontiguousarray(bd.T)
    _HC = c
    return c


def tile_vec(v):
    v = np.asarray(v, np.float32).reshape(-1)
    n = ceil_div(v.shape[0], 128)
    pad = np.zeros(128 * n, np.float32)
    pad[:v.shape[0]] = v
    return np.ascontiguousarray(pad.reshape(n, 128).T)


def perm_for(core):
    hh = core % 2
    loc = np.arange(hh * TLOC, (hh + 1) * TLOC)
    oth = np.arange((1 - hh) * TLOC, (2 - hh) * TLOC)
    return np.concatenate([loc, oth])



def emit_ln(sch, t, pA, pB, xf, hf, deps, st=None):
    o = sch.op
    if st is None:
        st = {"pa": None, "pb": None, "sq": [None, None]}
    hdone = []
    for half in range(2):
        sl = slice(half * 512, (half + 1) * 512)
        m2 = None
        for c in range(FC):
            sq = o("scalar", lambda e, c=c, sl=sl: e.activation(
                out=t["sq"][:, c % 2, :], in_=xf(c, sl), func=AF.Square),
                waits=list(deps) + [st["sq"][c % 2]], inc="sC")
            o("tensor", lambda e, c=c, sl=sl: e.matmul(
                pA[0:1, 0:512], t["onec"][:, :], xf(c, sl),
                start=(c == 0), stop=(c == FC - 1)),
                waits=list(deps) + ([st["pa"]] if c == 0 else []), inc="tC")
            m2 = o("tensor", lambda e, c=c: e.matmul(
                pA[0:1, 512:1024], t["onec"][:, :], t["sq"][:, c % 2, :],
                start=(c == 0), stop=(c == FC - 1)),
                waits=[sq], inc="tC")
            st["sq"][c % 2] = m2
        s0 = o("vector", lambda e: e.tensor_scalar(
            t["stat"][:, 0, :], pA[0:1, 0:512], 1.0 / D, None, ALU.mult, ALU.bypass),
            waits=[m2], inc="vC")
        s1 = o("vector", lambda e: e.tensor_scalar(
            t["stat"][:, 1, :], pA[0:1, 512:1024], 1.0 / D, None, ALU.mult, ALU.bypass),
            waits=[m2], inc="vC")
        s2 = o("vector", lambda e: e.scalar_tensor_tensor(
            t["stat"][:, 2, :], t["stat"][:, 0, :], -1.0, t["stat"][:, 0, :],
            ALU.mult, ALU.mult), waits=[s0], inc="vC")
        s3 = o("vector", lambda e: e.tensor_tensor(
            t["stat"][:, 1, :], t["stat"][:, 1, :], t["stat"][:, 2, :], ALU.add),
            waits=[s1, s2], inc="vC")
        s4 = o("scalar", lambda e: e.activation(
            out=t["stat"][:, 2, :], in_=t["stat"][:, 1, :], func=AF.Ln,
            bias=t["eps1"][:, :]), waits=[s3], inc="sC")
        s5 = o("scalar", lambda e: e.activation(
            out=t["stat"][:, 2, :], in_=t["stat"][:, 2, :], func=AF.Exp,
            scale=-0.5), waits=[s4], inc="sC")
        s6 = o("vector", lambda e: e.scalar_tensor_tensor(
            t["stat"][:, 3, :], t["stat"][:, 2, :], -1.0, t["stat"][:, 0, :],
            ALU.mult, ALU.mult), waits=[s5], inc="vC")
        s7 = o("vector", lambda e: e.tensor_copy(
            t["statb"][:, 0, :], t["stat"][:, 2, :]), waits=[s5], inc="vC")
        s8 = o("vector", lambda e: e.tensor_copy(
            t["statb"][:, 1, :], t["stat"][:, 3, :]), waits=[s6], inc="vC")
        b0 = o("tensor", lambda e: e.matmul(
            pB[:, 0:512], t["oner"][:, :], t["statb"][:, 0, :],
            start=True, stop=True), waits=[s7, st["pb"]], inc="tC")
        b1 = o("tensor", lambda e: e.matmul(
            pB[:, 512:1024], t["oner"][:, :], t["statb"][:, 1, :],
            start=True, stop=True), waits=[s8], inc="tC")
        st["pa"] = s1
        for c in range(FC):
            a0 = o("vector", lambda e, c=c, sl=sl: e.tensor_tensor(
                t["tmp"][:], xf(c, sl), pB[:, 0:512], ALU.mult),
                waits=[b0, hdone[-1] if hdone else None], inc="vC")
            a1 = o("vector", lambda e, c=c, sl=sl: e.tensor_tensor(
                hf(c, sl), t["tmp"][:], pB[:, 512:1024], ALU.add),
                waits=[b1, a0], inc="vC")
            hdone.append(a1)
        st["pb"] = hdone[-1]
    return hdone, st



def build_l0():
    nc = bass.Bass("TRN2", target_bir_lowering=False)
    ap = lambda *a, **k: nc.dram_tensor(*a, **k).ap()
    g = {}
    g["xT"] = ap("xT", [D, S], F32, kind="ExternalInput")
    g["wq"] = ap("wq", [D, D], F32, kind="ExternalInput")
    g["wk"] = ap("wk", [D, KVH * DH], F32, kind="ExternalInput")
    g["wv"] = ap("wv", [D, KVH * DH], F32, kind="ExternalInput")
    g["wo"] = ap("wo", [D, D], F32, kind="ExternalInput")
    g["wr"] = ap("wr", [D, H], F32, kind="ExternalInput")
    g["fwT"] = ap("fwT", [D, D], F32, kind="ExternalInput")
    g["bq"] = ap("bq", [128, FC], F32, kind="ExternalInput")
    g["bk"] = ap("bk", [128, KVH], F32, kind="ExternalInput")
    g["bv"] = ap("bv", [128, KVH], F32, kind="ExternalInput")
    g["bo"] = ap("bo", [128, FC], F32, kind="ExternalInput")
    g["br"] = ap("br", [128, 1], F32, kind="ExternalInput")
    g["fb"] = ap("fb", [128, FC], F32, kind="ExternalInput")
    g["cos_t"] = ap("cos_t", [DH, S], BF16, kind="ExternalInput")
    g["sin_t"] = ap("sin_t", [DH, S], BF16, kind="ExternalInput")
    g["prot"] = ap("prot", [DH, DH], BF16, kind="ExternalInput")
    g["ones_col"] = ap("ones_col", [128, 1], BF16, kind="ExternalInput")
    g["ones_row"] = ap("ones_row", [1, 128], BF16, kind="ExternalInput")
    g["bdiag16"] = ap("bdiag16", [128, 16], F32, kind="ExternalInput")
    g["bcast16"] = ap("bcast16", [16, 128], F32, kind="ExternalInput")
    g["x2T"] = ap("x2T", [D, TLOC], BF16, kind="ExternalOutput")
    g["kdr"] = nc.dram_tensor("kdr", [KVH, DH, S], BF16).ap()
    g["vdr"] = nc.dram_tensor("vdr", [KVH, DH, S], BF16).ap()
    g["qdr"] = nc.dram_tensor("qdr", [H, DH, TLOC], BF16).ap()
    g["wqdr"] = nc.dram_tensor("wqdr", [D, D], BF16).ap()
    g["rdr"] = nc.dram_tensor("rdr", [16, S], F32).ap()
    g["mdr"] = nc.dram_tensor("mdr", [512, 128], BF16).ap()

    ctx = ExitStack()
    with ctx:
        E = ctx.enter_context
        t = {}
        t["xres"] = E(nc.sbuf_tensor("xres", [128, FC, TLOC], BF16))
        t["hbuf"] = E(nc.sbuf_tensor("hbuf", [128, 2, FC, 1024], BF16))
        t["u4"] = E(nc.sbuf_tensor("u4", [128, 16384], BF16))
        t["wbuf"] = E(nc.sbuf_tensor("wbuf", [128, 2, FC, 128], BF16))
        t["qb"] = E(nc.sbuf_tensor("qb", [128, TLOC], BF16))
        t["expb"] = E(nc.sbuf_tensor("expb", [128, 2, 1024], BF16))
        t["prot"] = E(nc.sbuf_tensor("protb", [128, 128], BF16))
        t["onec"] = E(nc.sbuf_tensor("onec", [128, 1], BF16))
        t["oner"] = E(nc.sbuf_tensor("oner", [1, 128], BF16))
        t["bd16"] = E(nc.sbuf_tensor("bd16", [128, 16], F32))
        t["bc16"] = E(nc.sbuf_tensor("bc16", [16, 128], F32))
        t["bias"] = E(nc.sbuf_tensor("biast", [128, 64], F32))
        t["br"] = E(nc.sbuf_tensor("brt", [128, 1], F32))
        t["eps1"] = E(nc.sbuf_tensor("eps1", [1, 1], F32))
        t["sq"] = E(nc.sbuf_tensor("sq", [128, 2, 512], BF16))
        t["stat"] = E(nc.sbuf_tensor("stat", [1, 4, 512], F32))
        t["statb"] = E(nc.sbuf_tensor("statb", [1, 2, 512], BF16))
        t["tmp"] = E(nc.sbuf_tensor("tmp", [128, 512], BF16))
        t["rr"] = E(nc.sbuf_tensor("rr", [16, 512], F32))
        t["rb"] = E(nc.sbuf_tensor("rb", [128, 512], F32))
        t["lo"] = E(nc.sbuf_tensor("lo", [16, 1], F32))
        t["hi"] = E(nc.sbuf_tensor("hi", [16, 1], F32))
        t["tt16"] = E(nc.sbuf_tensor("tt16", [16, 1], F32))
        t["ge"] = E(nc.sbuf_tensor("ge", [16, 1], F32))
        t["gi"] = E(nc.sbuf_tensor("gi", [16, 1], F32))
        t["d16"] = E(nc.sbuf_tensor("d16", [16, 1], F32))
        t["cmp"] = E(nc.sbuf_tensor("cmp", [128, 512], F32))
        t["cnt1"] = E(nc.sbuf_tensor("cnt1", [128, 1], F32))
        t["thr128"] = E(nc.sbuf_tensor("thr128", [128, 1], F32))
        t["maskf"] = E(nc.sbuf_tensor("maskf", [128, 512], F32))
        t["rzrep"] = E(nc.sbuf_tensor("rzrep", [128, 1024], F32))
        t["wqc"] = E(nc.sbuf_tensor("wqc", [128, 512], F32))
        t["wqo"] = E(nc.sbuf_tensor("wqo", [128, 2, 512], BF16))
        t["acc"] = E(nc.sbuf_tensor("acc", [128, 1], F32))
        t["red"] = E(nc.sbuf_tensor("red", [128, 1], F32))
        t["onef"] = E(nc.sbuf_tensor("onef", [128, 1], F32))
        t["onerf"] = E(nc.sbuf_tensor("onerf", [1, 128], F32))
        t["srow"] = E(nc.sbuf_tensor("srow", [1, 2], F32))
        t["sinv"] = E(nc.sbuf_tensor("sinv", [128, 2], F32))

        t["mtmp"] = t["sq"][:, 0, :]
        t["rz"] = t["stat"].ap().rearrange("a b c -> a (b c)")[:, 0:1024]
        t["rzb"] = t["statb"].ap().rearrange("a b c -> a (b c)")[:, 0:1024]
        t["wqa"] = t["rb"][:, :]
        t["wqx"] = t["cmp"][:, :]
        t["qa"] = t["expb"][:, 0, 0:512]
        t["qc"] = t["expb"][:, 1, 0:512]
        t["qs"] = t["expb"][:, 1, 512:1024]
        pA = E(nc.psum_tensor("pA", [128, 1024], F32))
        pB = E(nc.psum_tensor("pB", [128, 1024], F32))
        pC = E(nc.psum_tensor("pC", [128, 1024], F32))
        pD = E(nc.psum_tensor("pD", [128, 1024], F32))

        sch = Sch(nc, ctx)
        emit_l0(sch, g, t, pA, pB, pC, pD)
        sch.emit(ctx)
    return nc


def emit_l0(sch, g, t, pA, pB, pC, pD):
    o = sch.op
    dma = sch.dma
    cosv = t["u4"][:, 0:S]
    sinv = t["u4"][:, S:2 * S]

    consts = None
    for dst, src in [(cosv, g["cos_t"][:, :]), (sinv, g["sin_t"][:, :]),
                     (t["prot"][:], g["prot"][:, :]), (t["onec"][:], g["ones_col"][:, :]),
                     (t["oner"][:], g["ones_row"][:, :]), (t["bd16"][:], g["bdiag16"][:, :]),
                     (t["bc16"][:], g["bcast16"][:, :]),
                     (t["bias"][:, 0:16], g["bq"][:, :]), (t["bias"][:, 16:32], g["bo"][:, :]),
                     (t["bias"][:, 32:40], g["bk"][:, :]), (t["bias"][:, 40:48], g["bv"][:, :]),
                     (t["bias"][:, 48:64], g["fb"][:, :]), (t["br"][:], g["br"][:, :])]:
        consts = dma("sync", dst, src, chain="const")
    epsr = o("vector", lambda e: e.memset(t["eps1"][:], 1e-5), inc="vC")
    onefm = o("vector", lambda e: e.memset(t["onef"][:], 1.0), inc="vC")
    onerfm = o("vector", lambda e: e.memset(t["onerf"][:], 1.0), inc="vC")

    xTr = g["xT"].rearrange("(c p) s -> p c s", p=128)

    rr_done = []
    kv_done = []
    q_done = []
    hbuf_free = [None, None]
    xstage_free = [None, None]
    ropeS = {"buf": None, "pd": None}
    lnst = [None]

    def rope_evac(psh, bcol, base, out_ap, mmtok):
        e1 = o("vector", lambda e: e.tensor_scalar(
            t["qa"], psh, bcol, None, ALU.add, ALU.bypass),
            waits=[mmtok, ropeS["buf"]], inc="vC")
        rmm = o("tensor", lambda e: e.matmul(
            pD[:, 0:512], t["prot"][:, :], t["qa"], start=True, stop=True),
            waits=[e1, ropeS["pd"]], inc="tC")
        e2 = o("vector", lambda e: e.tensor_tensor(
            t["qc"], t["qa"], cosv[:, base:base + 512], ALU.mult),
            waits=[e1], inc="vC")
        e3 = o("vector", lambda e: e.tensor_tensor(
            t["qs"], pD[:, 0:512], sinv[:, base:base + 512], ALU.mult),
            waits=[rmm], inc="vC")
        ropeS["pd"] = e3
        e4 = o("vector", lambda e: e.tensor_tensor(
            t["qa"], t["qc"], t["qs"], ALU.add),
            waits=[e2, e3, rmm], inc="vC")
        dt_ = dma("sync", out_ap, t["qa"], waits=[e4], chain="rope")
        ropeS["buf"] = dt_
        return dt_

    proj_ps = {0: None, 1: None}
    wbuf_free = [None, None]

    def proj(wdram, n_ec, hf, hdeps, evac_fn, wstart, ecw=128):
        wre = wdram.rearrange("(c p) eo -> p c eo", p=128)
        for ec in range(n_ec):
            par = ec % 2
            lw = dma("gpsimd", t["wbuf"][:, par, :, 0:ecw],
                     wre[:, :, ec * ecw:(ec + 1) * ecw],
                     waits=[wbuf_free[par], wstart], chain="wld")
            for half in range(2):
                sl = slice(half * 512, (half + 1) * 512)
                mm = None
                for c in range(FC):
                    mm = o("tensor", lambda e, c=c, sl=sl, half=half, par=par: e.matmul(
                        pC[0:ecw, half * 512:half * 512 + 512],
                        t["wbuf"][:, par, c, 0:ecw], hf(c, sl),
                        start=(c == 0), stop=(c == FC - 1)),
                        waits=list(hdeps) + [lw, proj_ps[half]], inc="tC")
                ft = evac_fn(ec, half, pC[0:ecw, half * 512:half * 512 + 512], mm)
                proj_ps[half] = ft
            wbuf_free[par] = mm

    for tt4 in range(4):
        cols = slice(tt4 * 1024, (tt4 + 1) * 1024)
        hhalf = tt4 % 2
        if tt4 < 2:
            xf = lambda c, sl, tt4=tt4: t["xres"][:, c, tt4 * 1024 + sl.start:tt4 * 1024 + sl.stop]
            lx = dma("gpsimd", t["xres"][:, :, cols], xTr[:, :, cols],
                     waits=[xstage_free[hhalf]], chain="xld")
        else:
            xf = lambda c, sl, hhalf=hhalf: t["hbuf"][:, 1 - hhalf, c, sl]
            lx = dma("gpsimd", t["hbuf"][:, 1 - hhalf], xTr[:, :, cols],
                     waits=[xstage_free[1 - hhalf], hbuf_free[1 - hhalf]],
                     chain="xld")
        hf = lambda c, sl, hhalf=hhalf: t["hbuf"][:, hhalf, c, sl]
        hdone, lnst[0] = emit_ln(
            sch, t, pA, pB, xf, hf,
            [lx, epsr, consts, hbuf_free[hhalf]], lnst[0])
        hdeps = [hdone[-1]]
        if tt4 >= 2:
            xstage_free[1 - hhalf] = hdone[-1]

        if tt4 < 2:
            def evq(ec, half, psh, mm, tt4=tt4):
                base = tt4 * 1024 + half * 512
                dt_ = rope_evac(psh, t["bias"][:, ec:ec + 1], base,
                                g["qdr"][ec, :, base:base + 512], mm)
                q_done.append(dt_)
                return dt_
            proj(g["wq"], FC, hf, hdeps, evq, hbuf_free[hhalf])

        def evk(ec, half, psh, mm, tt4=tt4):
            base = tt4 * 1024 + half * 512
            dt_ = rope_evac(psh, t["bias"][:, 32 + ec:33 + ec], base,
                            g["kdr"][ec, :, base:base + 512], mm)
            kv_done.append(dt_)
            return dt_
        proj(g["wk"], KVH, hf, hdeps, evk, hbuf_free[hhalf])

        def evv(ec, half, psh, mm, tt4=tt4):
            base = tt4 * 1024 + half * 512
            e1 = o("vector", lambda e: e.tensor_scalar(
                t["qa"], psh, t["bias"][:, 40 + ec:41 + ec], None,
                ALU.add, ALU.bypass), waits=[mm, ropeS["buf"]], inc="vC")
            dt_ = dma("sync", g["vdr"][ec, :, base:base + 512], t["qa"], waits=[e1], chain="rope")
            ropeS["buf"] = dt_
            kv_done.append(dt_)
            return e1
        proj(g["wv"], KVH, hf, hdeps, evv, hbuf_free[hhalf])

        def evr(ec, half, psh, mm, tt4=tt4):
            base = tt4 * 1024 + half * 512
            e1 = o("scalar", lambda e: e.activation(
                out=t["rr"][:, :], in_=psh,
                func=AF.Sigmoid, bias=t["br"][0:16, :]),
                waits=[mm, rr_done[-1] if rr_done else None], inc="sC")
            dr = dma("sync", g["rdr"][:, base:base + 512], t["rr"][:, :], waits=[e1], chain="rr")
            rr_done.append(dr)
            return e1
        proj(g["wr"], 1, hf, hdeps, evr, hbuf_free[hhalf], ecw=16)
        hbuf_free[hhalf] = rr_done[-1]

    d2 = dma("sync", t["rb"][:], g["rdr"].rearrange("h (bk f) -> (h bk) f", bk=8),
             waits=[rr_done[-1]], chain="rr")
    mzl = o("vector", lambda e: e.memset(t["lo"][:], 0.0), inc="vC")
    mzh = o("vector", lambda e: e.memset(t["hi"][:], 1.0), inc="vC")
    lastsel = [mzl, mzh, d2]
    pd_b_free = [ropeS["pd"]]
    for it in range(30):
        a1 = o("vector", lambda e: e.tensor_tensor(
            t["tt16"][:], t["lo"][:], t["hi"][:], ALU.add), waits=lastsel, inc="vC")
        a2 = o("vector", lambda e: e.tensor_scalar(
            t["tt16"][:], t["tt16"][:], 0.5, None, ALU.mult, ALU.bypass),
            waits=[a1], inc="vC")
        bmm = o("tensor", lambda e: e.matmul(
            pD[:, 512:513], t["bc16"][:, :], t["tt16"][:], start=True, stop=True),
            waits=[a2, pd_b_free[0], consts], inc="tC")
        cp1 = o("vector", lambda e: e.tensor_copy(t["thr128"][:], pD[:, 512:513]),
                waits=[bmm], inc="vC")
        c1 = o("vector", lambda e: e.tensor_scalar(
            t["cmp"][:], t["rb"][:], t["thr128"][:, :], None, ALU.is_ge, ALU.bypass),
            waits=[cp1, d2], inc="vC")
        c2 = o("vector", lambda e: e.tensor_reduce(
            t["cnt1"][:], t["cmp"][:], AX.X, ALU.add), waits=[c1], inc="vC")
        cmm = o("tensor", lambda e: e.matmul(
            pD[0:16, 640:641], t["bd16"][:, :], t["cnt1"][:], start=True, stop=True),
            waits=[c2], inc="tC")
        g1 = o("vector", lambda e: e.tensor_scalar(
            t["ge"][:], pD[0:16, 640:641], float(TOPK0), None, ALU.is_ge, ALU.bypass),
            waits=[cmm], inc="vC")
        g2 = o("vector", lambda e: e.tensor_scalar(
            t["gi"][:], t["ge"][:], -1.0, 1.0, ALU.mult, ALU.add),
            waits=[g1], inc="vC")
        dl = o("vector", lambda e: e.tensor_tensor(
            t["d16"][:], t["tt16"][:], t["lo"][:], ALU.subtract), waits=[a2], inc="vC")
        l1 = o("vector", lambda e: e.scalar_tensor_tensor(
            t["lo"][:], t["d16"][:], t["ge"][:, :], t["lo"][:], ALU.mult, ALU.add),
            waits=[g1, dl], inc="vC")
        dh = o("vector", lambda e: e.tensor_tensor(
            t["d16"][:], t["tt16"][:], t["hi"][:], ALU.subtract),
            waits=[l1], inc="vC")
        h1 = o("vector", lambda e: e.scalar_tensor_tensor(
            t["hi"][:], t["d16"][:], t["gi"][:, :], t["hi"][:], ALU.mult, ALU.add),
            waits=[g2, dh], inc="vC")
        lastsel = [l1, h1]
        pd_b_free[0] = g1
    bmmf = o("tensor", lambda e: e.matmul(
        pD[:, 512:513], t["bc16"][:, :], t["lo"][:], start=True, stop=True),
        waits=lastsel + [pd_b_free[0]], inc="tC")
    cpf = o("vector", lambda e: e.tensor_copy(t["thr128"][:], pD[:, 512:513]),
            waits=[bmmf], inc="vC")
    mb1 = o("vector", lambda e: e.tensor_scalar(
        t["cmp"][:], t["rb"][:], t["thr128"][:, :], None, ALU.is_ge, ALU.bypass),
        waits=[cpf], inc="vC")
    mb2 = o("vector", lambda e: e.tensor_scalar(
        t["cmp"][:], t["cmp"][:], -NEG + MSHIFT, NEG, ALU.mult, ALU.add),
        waits=[mb1], inc="vC")
    mb4 = o("vector", lambda e: e.tensor_copy(t["mtmp"], t["cmp"][:]),
            waits=[mb2], inc="vC")
    d3 = dma("sync", g["mdr"].rearrange("(p ff) c -> p (ff c)", p=128),
             t["mtmp"], waits=[mb4], chain="rr")
    fwr = g["fwT"].rearrange("(c p) o -> p c o", p=128)
    mz = o("vector", lambda e: e.memset(t["acc"][:], 0.0), inc="vC")
    acc_tok = mz
    lq_free = None
    for cc in range(FC * 4):
        c, q4 = cc // 4, cc % 4
        lq = dma("gpsimd", t["wqc"][:], fwr[:, c, q4 * 512:(q4 + 1) * 512],
                 waits=[lq_free], chain="wqld")
        r1 = o("vector", lambda e: e.tensor_reduce(
            t["red"][:], t["wqc"][:], AX.X, ALU.add, apply_absolute_value=True),
            waits=[lq, acc_tok], inc="vC")
        lq_free = r1
        acc_tok = o("vector", lambda e: e.tensor_tensor(
            t["acc"][:], t["acc"][:], t["red"][:], ALU.add), waits=[r1, acc_tok],
            inc="vC")
    tmm = o("tensor", lambda e: e.matmul(
        pD[0:1, 768:769], t["onef"][:, :], t["acc"][:], start=True, stop=True),
        waits=[acc_tok, onefm, pd_b_free[0], cpf], inc="tC")
    sv = o("vector", lambda e: e.tensor_scalar(
        t["srow"][:, 0:1], pD[0:1, 768:769], 1.0 / (D * D), None, ALU.mult, ALU.bypass),
        waits=[tmm], inc="vC")
    iv = o("vector", lambda e: e.reciprocal(t["srow"][:, 1:2], t["srow"][:, 0:1]),
           waits=[sv], inc="vC")
    bmm2 = o("tensor", lambda e: e.matmul(
        pD[:, 768:770], t["onerf"][:, :], t["srow"][:, :], start=True, stop=True),
        waits=[iv, onerfm, sv], inc="tC")
    cps = o("vector", lambda e: e.tensor_copy(t["sinv"][:], pD[:, 768:770]),
            waits=[bmm2], inc="vC")
    qdone = []
    wqo_free = [cps, cps]
    pass
    for cc in range(FC * 4):
        c, q4 = cc // 4, cc % 4
        par = cc % 2
        lq = dma("gpsimd", t["wqc"][:], fwr[:, c, q4 * 512:(q4 + 1) * 512],
                 waits=[lq_free], chain="wqld")
        q1 = o("vector", lambda e: e.tensor_scalar(
            t["wqa"], t["wqc"][:], t["sinv"][:, 1:2], 0.5, ALU.mult, ALU.is_ge),
            waits=[lq, cps], inc="vC")
        q2 = o("vector", lambda e: e.tensor_scalar(
            t["wqx"], t["wqc"][:], t["sinv"][:, 1:2], -0.5, ALU.mult, ALU.is_le),
            waits=[lq], inc="vC")
        lq_free = q2
        q3 = o("vector", lambda e: e.tensor_tensor(
            t["wqa"], t["wqa"], t["wqx"], ALU.subtract), waits=[q1, q2],
            inc="vC")
        q4t = o("vector", lambda e, par=par: e.tensor_scalar(
            t["wqo"][:, par, :], t["wqa"], t["sinv"][:, 0:1], None,
            ALU.mult, ALU.bypass), waits=[q3, wqo_free[par]], inc="vC")
        dq = dma("sync", g["wqdr"].rearrange("(c p) o -> p c o", p=128)[:, c, q4 * 512:(q4 + 1) * 512],
                 t["wqo"][:, par, :], waits=[q4t], chain="wqout")
        wqo_free[par] = dq
        qdone.append(dq)

    tde = []
    for sl4 in range(4):
        td = dma("scalar", t["expb"][:, 0, sl4 * 128:(sl4 + 1) * 128],
                 g["mdr"][sl4 * 128:(sl4 + 1) * 128, :], transpose=True,
                 waits=[d3, ropeS["buf"]], chain="vb")
        tde.append(td)
    mkf = o("vector", lambda e: e.tensor_copy(t["maskf"][:], t["expb"][:, 0, 0:512]),
            waits=tde, inc="vC")

    hbflat = t["hbuf"].ap().rearrange("p a c n -> p (a c n)")
    attn_done = []
    expb_free = [mkf, mkf]
    sc_free = [lnst[0]["pa"], lnst[0]["pb"]]
    o_free = [proj_ps[0], proj_ps[1]]
    z_free = [cps]
    kb_free = [None, None]
    vb_free = [None, None]
    qb_free = [None]
    Kb = t["u4"]
    Vb = t["u4"]
    h_last = hbuf_free
    for kvh in range(KVH):
        par = kvh % 2
        kl = dma("sync", Kb[:, par * S:(par + 1) * S], g["kdr"][kvh, :, :],
                 waits=[kv_done[-1], kb_free[par]], chain="kb")
        vts = []
        for kc in range(32):
            vts.append(dma("scalar", Vb[:, 2 * S + par * S + kc * 128:2 * S + par * S + (kc + 1) * 128],
                           g["vdr"][kvh, :, kc * 128:(kc + 1) * 128],
                           transpose=True, waits=[kv_done[-1], vb_free[par]],
                           chain="vb"))
        for hh in (2 * kvh, 2 * kvh + 1):
            lqd = dma("gpsimd", t["qb"][:], g["qdr"][hh, :, :],
                      waits=[q_done[-1], qb_free[0]], chain="qbld")
            for qt in range(2):
                zs = []
                pvs = []
                for kc in range(32):
                    sel = kc % 2
                    psc = (pA, pB)[sel]
                    m1 = o("tensor", lambda e, kc=kc, psc=psc, qt=qt, par=par: e.matmul(
                        psc[:, 0:512], Kb[:, par * S + kc * 128:par * S + (kc + 1) * 128],
                        t["qb"][:, qt * 1024:qt * 1024 + 512],
                        start=True, stop=True),
                        waits=[kl, lqd, sc_free[sel]], inc="tC")
                    m2 = o("tensor", lambda e, kc=kc, psc=psc, qt=qt, par=par: e.matmul(
                        psc[:, 512:1024], Kb[:, par * S + kc * 128:par * S + (kc + 1) * 128],
                        t["qb"][:, qt * 1024 + 512:qt * 1024 + 1024],
                        start=True, stop=True), waits=[], inc="tC")
                    ex = o("scalar", lambda e, kc=kc, psc=psc, hh=hh, sel=sel: e.activation(
                        out=t["expb"][:, sel, :], in_=psc[:, :], func=AF.Exp,
                        scale=ISQ, bias=t["maskf"][:, hh * 32 + kc:hh * 32 + kc + 1]),
                        waits=[m1, m2, expb_free[sel], mkf], inc="sC")
                    sc_free[sel] = ex
                    z1 = o("tensor", lambda e, kc=kc, sel=sel: e.matmul(
                        pD[0:1, 0:512], t["onec"][:, :], t["expb"][:, sel, 0:512],
                        start=(kc == 0), stop=(kc == 31)),
                        waits=[ex] + ([z_free[0], ropeS["pd"], pd_b_free[0]]
                                      if kc == 0 else []), inc="tC")
                    z2 = o("tensor", lambda e, kc=kc, sel=sel: e.matmul(
                        pD[0:1, 512:1024], t["onec"][:, :], t["expb"][:, sel, 512:1024],
                        start=(kc == 0), stop=(kc == 31)), waits=[], inc="tC")
                    v1 = o("tensor", lambda e, kc=kc, sel=sel, par=par: e.matmul(
                        pC[:, 0:512], Vb[:, 2 * S + par * S + kc * 128:2 * S + par * S + (kc + 1) * 128],
                        t["expb"][:, sel, 0:512],
                        start=(kc == 0), stop=(kc == 31)),
                        waits=[vts[kc]] + (o_free if kc == 0 else []), inc="tC")
                    v2 = o("tensor", lambda e, kc=kc, sel=sel, par=par: e.matmul(
                        pC[:, 512:1024], Vb[:, 2 * S + par * S + kc * 128:2 * S + par * S + (kc + 1) * 128],
                        t["expb"][:, sel, 512:1024],
                        start=(kc == 0), stop=(kc == 31)), waits=[], inc="tC")
                    expb_free[sel] = v2
                    zs.append(z2)
                    pvs.append(v2)
                rc = o("vector", lambda e: e.reciprocal(t["rz"], pD[0:1, 0:1024]),
                       waits=[zs[-1]], inc="vC")
                rcb = o("vector", lambda e: e.tensor_copy(t["rzb"], t["rz"]),
                        waits=[rc], inc="vC")
                bz1 = o("tensor", lambda e: e.matmul(
                    pD[:, 0:512], t["oner"][:, :], t["rzb"][:, 0:512],
                    start=True, stop=True), waits=[rcb, rc], inc="tC")
                bz2 = o("tensor", lambda e: e.matmul(
                    pD[:, 512:1024], t["oner"][:, :], t["rzb"][:, 512:1024],
                    start=True, stop=True), waits=[], inc="tC")
                rce = o("vector", lambda e: e.tensor_copy(t["rzrep"][:], pD[:, 0:1024]),
                        waits=[bz1, bz2], inc="vC")
                z_free[0] = rce
                off = hh * 2048 + qt * 1024
                ao = o("vector", lambda e, off=off: e.tensor_tensor(
                    hbflat[:, off:off + 1024],
                    pC[:, 0:1024], t["rzrep"][:], ALU.mult),
                    waits=[rce, pvs[-1]] + ([h_last[0], h_last[1]]
                                            if hh == 0 and qt == 0 else []), inc="vC")
                o_free = [ao]
                attn_done.append(ao)
            qb_free[0] = attn_done[-1]
        kb_free[par] = attn_done[-1]
        vb_free[par] = attn_done[-1]

    wore = g["wo"].rearrange("(hc p) eo -> p hc eo", p=128)
    wo_free = [attn_done[-1], attn_done[-1]]
    d_free = [sc_free[0], sc_free[1]]
    wo_done = []
    for ec in range(FC):
        par = ec % 2
        lw = dma("gpsimd", t["wbuf"][:, par], wore[:, :, ec * 128:(ec + 1) * 128],
                 waits=[wo_free[par], wbuf_free[par]], chain="wld")
        for th in range(4):
            sel = th % 2
            psd = (pA, pB)[sel]
            mm = None
            for hc2 in range(FC):
                mm = o("tensor", lambda e, hc2=hc2, th=th, par=par, psd=psd: e.matmul(
                    psd[:, 0:512], t["wbuf"][:, par, hc2, :],
                    hbflat[:, hc2 * 2048 + th * 512:hc2 * 2048 + (th + 1) * 512],
                    start=(hc2 == 0), stop=(hc2 == FC - 1)),
                    waits=[lw, attn_done[-1], d_free[sel]], inc="tC")
            ev = o("vector", lambda e, ec=ec, th=th, psd=psd: e.scalar_tensor_tensor(
                t["xres"][:, ec, th * 512:(th + 1) * 512], psd[:, 0:512],
                t["bias"][:, 16 + ec:17 + ec],
                t["xres"][:, ec, th * 512:(th + 1) * 512], ALU.add, ALU.add),
                waits=[mm], inc="vC")
            d_free[sel] = ev
            wo_done.append(ev)
        wo_free[par] = wo_done[-1]
        wbuf_free[par] = mm

    lnst[0]["pa"] = d_free[0]
    lnst[0]["pb"] = d_free[1]
    h2_done = []
    for tt4 in range(2):
        xf2 = lambda c, sl, tt4=tt4: t["xres"][:, c, tt4 * 1024 + sl.start:tt4 * 1024 + sl.stop]
        hf2 = lambda c, sl, tt4=tt4: t["hbuf"][:, tt4, c, sl]
        hd, lnst[0] = emit_ln(sch, t, pA, pB, xf2, hf2, [wo_done[-1]], lnst[0])
        h2_done.append(hd[-1])
    wqre = g["wqdr"].rearrange("(c p) o -> p c o", p=128)
    f_wfree = [h2_done[-1], h2_done[-1]]
    f_free = [lnst[0]["pa"], lnst[0]["pb"]]
    out_done = []
    for ec in range(FC):
        par = ec % 2
        lw = dma("sync", t["wbuf"][:, par], wqre[:, :, ec * 128:(ec + 1) * 128],
                 waits=[f_wfree[par], qdone[-1], wbuf_free[par]], chain="ewt")
        for th in range(4):
            sel = th % 2
            psd = (pA, pB)[sel]
            tt4, half = th // 2, th % 2
            mm = None
            for c in range(FC):
                mm = o("tensor", lambda e, c=c, tt4=tt4, half=half, par=par, psd=psd: e.matmul(
                    psd[:, 0:512], t["wbuf"][:, par, c, :],
                    t["hbuf"][:, tt4, c, half * 512:half * 512 + 512],
                    start=(c == 0), stop=(c == FC - 1)),
                    waits=[lw, h2_done[-1], f_free[sel]], inc="tC")
            ev = o("vector", lambda e, ec=ec, th=th, psd=psd: e.scalar_tensor_tensor(
                t["xres"][:, ec, th * 512:(th + 1) * 512], psd[:, 0:512],
                t["bias"][:, 48 + ec:49 + ec],
                t["xres"][:, ec, th * 512:(th + 1) * 512], ALU.add, ALU.add),
                waits=[mm], inc="vC")
            f_free[sel] = ev
            out_done.append(ev)
        f_wfree[par] = out_done[-1]
        wbuf_free[par] = mm

    so = dma("sync", g["x2T"].rearrange("(c p) s -> p c s", p=128), t["xres"][:],
             waits=[out_done[-1]], chain="out")
    o("sync", lambda e: e.nop(), waits=[so])



_L0_NC = None


def run_l0(inputs):
    global _L0_NC
    hc = host_consts()
    if _L0_NC is None:
        _L0_NC = build_l0()
    x0 = np.asarray(inputs["L0_attn_hidden_states"], np.float32)
    in_maps = []
    for core in range(8):
        b = core // 2
        pm = perm_for(core)
        m = {
            "xT": np.ascontiguousarray(x0[b][pm].T),
            "wq": np.asarray(inputs["L0_attn_Wq"], np.float32),
            "wk": np.asarray(inputs["L0_attn_Wk"], np.float32),
            "wv": np.asarray(inputs["L0_attn_Wv"], np.float32),
            "wo": np.asarray(inputs["L0_attn_Wo"], np.float32),
            "wr": np.asarray(inputs["L0_attn_Wr"], np.float32),
            "fwT": np.ascontiguousarray(np.asarray(inputs["L0_ffn_weight"], np.float32).T),
            "bq": tile_vec(inputs["L0_attn_bq"]),
            "bk": tile_vec(inputs["L0_attn_bk"]),
            "bv": tile_vec(inputs["L0_attn_bv"]),
            "bo": tile_vec(inputs["L0_attn_bo"]),
            "br": tile_vec(inputs["L0_attn_br"]),
            "fb": tile_vec(inputs["L0_ffn_bias"]),
            "cos_t": hc["cos_t"][:, pm].astype(BF),
            "sin_t": hc["sin_t"][:, pm].astype(BF),
            "prot": hc["prot"],
            "ones_col": hc["ones_col"],
            "ones_row": hc["ones_row"],
            "bdiag16": hc["bdiag16"],
            "bcast16": hc["bcast16"],
        }
        in_maps.append(m)
    res = run_bass_kernel_spmd(_L0_NC, in_maps, core_ids=list(range(8)))
    x2 = np.zeros((B, S, D), np.float32)
    for core in range(8):
        b, hh = core // 2, core % 2
        xt = np.asarray(res.results[core]["x2T"]).astype(np.float32)
        x2[b, hh * TLOC:(hh + 1) * TLOC] = xt.T
    return x2


def kernel(**inputs):
    x2 = run_l0(inputs)
    raise NotImplementedError("L1 phases pending")



def build_l1a():
    nc = bass.Bass("TRN2", target_bir_lowering=False)
    ap = lambda *a, **k: nc.dram_tensor(*a, **k).ap()
    g = {}
    g["xT"] = ap("xT", [D, TLOC], BF16, kind="ExternalInput")
    for w in ("wq", "wk", "wv", "wg"):
        g[w] = ap(w, [D, D], F32, kind="ExternalInput")
    g["wa"] = ap("wa", [D, H], F32, kind="ExternalInput")
    g["wb"] = ap("wb", [D, H], F32, kind="ExternalInput")
    g["ba"] = ap("ba", [128, 1], F32, kind="ExternalInput")
    g["bb"] = ap("bb", [128, 1], F32, kind="ExternalInput")
    g["ones_col"] = ap("ones_col", [128, 1], BF16, kind="ExternalInput")
    g["ones_row"] = ap("ones_row", [1, 128], BF16, kind="ExternalInput")
    for w in ("oq", "ok", "ov", "og"):
        g[w] = ap(w, [D, TLOC], BF16, kind="ExternalOutput")
    g["oab"] = ap("oab", [16, 2, TLOC], F32, kind="ExternalOutput")

    ctx = ExitStack()
    with ctx:
        E = ctx.enter_context
        t = {}
        t["xt"] = E(nc.sbuf_tensor("xt", [128, FC, 1024], BF16))
        t["hbuf"] = E(nc.sbuf_tensor("hbuf", [128, 2, FC, 1024], BF16))
        t["wbuf"] = E(nc.sbuf_tensor("wbuf", [128, 2, FC, 128], BF16))
        t["onec"] = E(nc.sbuf_tensor("onec", [128, 1], BF16))
        t["oner"] = E(nc.sbuf_tensor("oner", [1, 128], BF16))
        t["bias"] = E(nc.sbuf_tensor("biast", [128, 2], F32))
        t["eps1"] = E(nc.sbuf_tensor("eps1", [1, 1], F32))
        t["sq"] = E(nc.sbuf_tensor("sq", [128, 2, 512], BF16))
        t["stat"] = E(nc.sbuf_tensor("stat", [1, 4, 512], F32))
        t["statb"] = E(nc.sbuf_tensor("statb", [1, 2, 512], BF16))
        t["tmp"] = E(nc.sbuf_tensor("tmp", [128, 512], BF16))
        t["ev"] = E(nc.sbuf_tensor("ev", [128, 2, 512], BF16))
        t["ab"] = E(nc.sbuf_tensor("ab", [16, 2, 512], F32))
        pA = E(nc.psum_tensor("pA", [128, 1024], F32))
        pB = E(nc.psum_tensor("pB", [128, 1024], F32))
        pC = E(nc.psum_tensor("pC", [128, 1024], F32))

        sch = Sch(nc, ctx)
        emit_l1a(sch, g, t, pA, pB, pC)
        sch.emit(ctx)
    return nc


def emit_l1a(sch, g, t, pA, pB, pC):
    o = sch.op
    dma = sch.dma
    c1 = dma("sync", t["onec"][:], g["ones_col"][:, :], chain="const")
    c2 = dma("sync", t["oner"][:], g["ones_row"][:, :], chain="const")
    c3 = dma("sync", t["bias"][:, 0:1], g["ba"][:, :], chain="const")
    c4 = dma("sync", t["bias"][:, 1:2], g["bb"][:, :], chain="const")
    consts = c4
    epsr = o("vector", lambda e: e.memset(t["eps1"][:], 1e-5), inc="vC")
    xTr = g["xT"].rearrange("(c p) s -> p c s", p=128)

    lnst = [None]
    hbuf_free = [None, None]
    proj_ps = {0: None, 1: None}
    wbuf_free = [None, None]
    ev_free = [None, None]
    ab_free = [None, None]

    def proj(wdram, n_ec, hf, hdeps, evac_fn, wstart, ecw=128):
        wre = wdram.rearrange("(c p) eo -> p c eo", p=128)
        for ec in range(n_ec):
            par = ec % 2
            lw = dma("gpsimd", t["wbuf"][:, par, :, 0:ecw],
                     wre[:, :, ec * ecw:(ec + 1) * ecw],
                     waits=[wbuf_free[par], wstart], chain="wld")
            for half in range(2):
                sl = slice(half * 512, (half + 1) * 512)
                mm = None
                for c in range(FC):
                    mm = o("tensor", lambda e, c=c, sl=sl, half=half, par=par: e.matmul(
                        pC[0:ecw, half * 512:half * 512 + 512],
                        t["wbuf"][:, par, c, 0:ecw], hf(c, sl),
                        start=(c == 0), stop=(c == FC - 1)),
                        waits=list(hdeps) + [lw, proj_ps[half]], inc="tC")
                ft = evac_fn(ec, half, pC[0:ecw, half * 512:half * 512 + 512], mm)
                proj_ps[half] = ft
            wbuf_free[par] = mm

    for tt4 in range(2):
        cols = slice(tt4 * 1024, (tt4 + 1) * 1024)
        lx = dma("gpsimd", t["xt"][:], xTr[:, :, cols], chain="xld")
        hhalf = tt4
        xf = lambda c, sl: t["xt"][:, c, sl]
        hf = lambda c, sl, hhalf=hhalf: t["hbuf"][:, hhalf, c, sl]
        hdone, lnst[0] = emit_ln(sch, t, pA, pB, xf, hf,
                                 [lx, epsr, consts, hbuf_free[hhalf]], lnst[0])
        hdeps = [hdone[-1]]

        for wname, oname in (("wq", "oq"), ("wk", "ok"), ("wv", "ov"), ("wg", "og")):
            def evp(ec, half, psh, mm, tt4=tt4, oname=oname):
                base = tt4 * 1024 + half * 512
                par2 = half
                e1 = o("vector", lambda e, par2=par2: e.tensor_copy(
                    t["ev"][:, par2, :], psh), waits=[mm, ev_free[par2]], inc="vC")
                dt_ = dma("sync", g[oname].rearrange("(c p) s -> p c s", p=128)[:, 0, 0:512]
                          if False else g[oname][0:128, 0:512],
                          t["ev"][:, par2, :], waits=[e1], chain="po")
                ev_free[par2] = dt_
                return e1
            def evp2(ec, half, psh, mm, tt4=tt4, oname=oname):
                base = tt4 * 1024 + half * 512
                par2 = half
                e1 = o("vector", lambda e, par2=par2: e.tensor_copy(
                    t["ev"][:, par2, :], psh), waits=[mm, ev_free[par2]], inc="vC")
                dt_ = dma("sync", g[oname][ec * 128:(ec + 1) * 128, base:base + 512],
                          t["ev"][:, par2, :], waits=[e1], chain="po")
                ev_free[par2] = dt_
                return e1
            proj(g[wname], FC, hf, hdeps, evp2, hbuf_free[hhalf])

        for wname, ai in (("wa", 0), ("wb", 1)):
            def eva(ec, half, psh, mm, tt4=tt4, ai=ai):
                base = tt4 * 1024 + half * 512
                par2 = half
                e1 = o("vector", lambda e, par2=par2, ai=ai: e.tensor_scalar(
                    t["ab"][:, par2, :], psh, t["bias"][0:16, ai:ai + 1], None,
                    ALU.add, ALU.bypass), waits=[mm, ab_free[par2]], inc="vC")
                dt_ = dma("sync", g["oab"][:, ai, base:base + 512],
                          t["ab"][:, par2, :], waits=[e1], chain="po")
                ab_free[par2] = dt_
                return e1
            proj(g[wname], 1, hf, hdeps, eva, hbuf_free[hhalf], ecw=16)
        hbuf_free[hhalf] = proj_ps[1]

    fin = sch.chain_last.get("po")
    o("sync", lambda e: e.nop(), waits=[fin])


_L1A_NC = None


def run_l1a(inputs, x2):
    global _L1A_NC
    hc = host_consts()
    if _L1A_NC is None:
        _L1A_NC = build_l1a()
    in_maps = []
    for core in range(8):
        b, hh = core // 2, core % 2
        m = {
            "xT": np.ascontiguousarray(x2[b, hh * TLOC:(hh + 1) * TLOC].T).astype(BF),
            "wq": np.asarray(inputs["L1_attn_Wq"], np.float32),
            "wk": np.asarray(inputs["L1_attn_Wk"], np.float32),
            "wv": np.asarray(inputs["L1_attn_Wv"], np.float32),
            "wg": np.asarray(inputs["L1_attn_Wg"], np.float32),
            "wa": np.asarray(inputs["L1_attn_Wa"], np.float32),
            "wb": np.asarray(inputs["L1_attn_Wb"], np.float32),
            "ba": tile_vec(inputs["L1_attn_ba"]),
            "bb": tile_vec(inputs["L1_attn_bb"]),
            "ones_col": hc["ones_col"],
            "ones_row": hc["ones_row"],
        }
        in_maps.append(m)
    res = run_bass_kernel_spmd(_L1A_NC, in_maps, core_ids=list(range(8)))
    out = {}
    for nm in ("oq", "ok", "ov", "og"):
        out[nm] = np.zeros((B, S, D), np.float32)
    out["ab"] = np.zeros((B, S, 2 * H), np.float32)
    for core in range(8):
        b, hh = core // 2, core % 2
        r = res.results[core]
        sl = slice(hh * TLOC, (hh + 1) * TLOC)
        for nm in ("oq", "ok", "ov", "og"):
            out[nm][b, sl] = np.asarray(r[nm]).astype(np.float32).T
        ab = np.asarray(r["oab"])
        out["ab"][b, sl, 0:H] = ab[:, 0, :].T
        out["ab"][b, sl, H:] = ab[:, 1, :].T
    return out



NCH = 32


def build_l1b(dbg=False):
    Sch.PARANOID = False
    nc = bass.Bass("TRN2", target_bir_lowering=False)
    ap = lambda *a, **k: nc.dram_tensor(*a, **k).ap()
    g = {}
    g["qraw"] = ap("qraw", [8, DH, S], BF16, kind="ExternalInput")
    g["kraw"] = ap("kraw", [8, DH, S], BF16, kind="ExternalInput")
    g["vraw"] = ap("vraw", [8, DH, S], BF16, kind="ExternalInput")
    g["convw"] = ap("convw", [3, 8, 128, 4], F32, kind="ExternalInput")
    g["convb"] = ap("convb", [3, 8, 128, 1], F32, kind="ExternalInput")
    g["ab_t"] = ap("ab_t", [128, 2, 256], F32, kind="ExternalInput")
    g["abr"] = ap("abr", [2, 8, S], F32, kind="ExternalInput")
    g["rmask"] = ap("rmask", [8, S], BF16, kind="ExternalInput")
    g["lnwrep"] = ap("lnwrep", [128, 2, S], BF16, kind="ExternalInput")
    g["smaskS8"] = ap("smaskS8", [128, 8, 128], BF16, kind="ExternalInput")
    g["smaskL8"] = ap("smaskL8", [128, 8, 128], BF16, kind="ExternalInput")
    g["smaskI8"] = ap("smaskI8", [128, 8, 128], BF16, kind="ExternalInput")
    g["triuI"] = ap("triuI", [128, 128], BF16, kind="ExternalInput")
    g["ident"] = ap("ident", [128, 128], BF16, kind="ExternalInput")
    g["ones_row"] = ap("ones_row", [1, 128], BF16, kind="ExternalInput")
    g["odr"] = ap("odr", [8, 128, NCH, 128], BF16, kind="ExternalOutput")
    for nm in ("qc", "kc", "vc"):
        if dbg:
            g[nm] = ap(nm + "dr", [8, DH, S], BF16, kind="ExternalOutput")
        else:
            g[nm] = nc.dram_tensor(nm + "dr", [8, DH, S], BF16).ap()
    g["crdr"] = nc.dram_tensor("crdr", [1, 2, 8, S], F32).ap()
    g["gcdr"] = nc.dram_tensor("gcdr", [1, 8, NCH], F32).ap()
    g["odrw"] = nc.dram_tensor("odrw", [8, 128, NCH, 128], BF16).ap()

    ctx = ExitStack()
    with ctx:
        E = ctx.enter_context
        t = {}
        t["ub"] = E(nc.sbuf_tensor("ub", [128, 4100], BF16))
        t["cvo"] = E(nc.sbuf_tensor("cvo", [128, 2, 512], BF16))
        t["diag"] = E(nc.sbuf_tensor("diag", [128, 2, 4, 128], BF16))
        t["cw"] = E(nc.sbuf_tensor("cw", [128, 2, 5], F32))
        t["ar"] = E(nc.sbuf_tensor("ar", [8, S], F32))
        t["rt1"] = E(nc.sbuf_tensor("rt1", [8, S], F32))
        t["rt2"] = E(nc.sbuf_tensor("rt2", [8, S], F32))
        t["rm"] = E(nc.sbuf_tensor("rm", [8, S], BF16))
        t["crg"] = E(nc.sbuf_tensor("crg", [8, S], F32))
        t["gC8"] = E(nc.sbuf_tensor("gC8", [8, NCH], F32))
        for nm in ("abt", "cums", "igc", "bet", "ccl", "dcl", "lat", "sc1", "lcc", "gtc"):
            t[nm] = E(nc.sbuf_tensor(nm, [128, 2, 256] if nm == "abt" else [128, 256], F32))
        t["lnab"] = E(nc.sbuf_tensor("lnab", [128, 256], BF16))
        t["gCrep"] = E(nc.sbuf_tensor("gCrep", [128, 256], F32))
        t["crowC"] = E(nc.sbuf_tensor("crowC", [1, 2, 2, 1024], F32))
        t["onerf"] = E(nc.sbuf_tensor("onerf", [1, 128], F32))
        t["ccrow"] = E(nc.sbuf_tensor("ccrow", [1, 256], F32))
        t["lam"] = E(nc.sbuf_tensor("lam", [128, 8, 128], F32))
        t["qkf"] = E(nc.sbuf_tensor("qkf", [128, 2, 2, 8, 128], BF16))
        t["ktv"] = E(nc.sbuf_tensor("ktv", [128, 2, 2, 8, 128], BF16))
        t["g1"] = E(nc.sbuf_tensor("g1", [128, 8, 128], BF16))
        t["h1"] = E(nc.sbuf_tensor("h1", [128, 8, 128], BF16))
        t["crS"] = E(nc.sbuf_tensor("crS", [128, 8, 128], BF16))
        t["igS"] = E(nc.sbuf_tensor("igS", [128, 8, 128], BF16))
        t["sqm"] = E(nc.sbuf_tensor("sqm", [128, 8, 128], BF16))
        t["sq1"] = E(nc.sbuf_tensor("sq1", [128, 8, 128], BF16))
        t["wt"] = E(nc.sbuf_tensor("wt", [128, 2, 8, 128], BF16))
        t["wu"] = E(nc.sbuf_tensor("wu", [128, 2, 8, 128], BF16))
        t["xch"] = E(nc.sbuf_tensor("xch", [128, 8, 256], BF16))
        t["dk"] = E(nc.sbuf_tensor("dk", [128, 8, 128], BF16))
        t["m2n"] = E(nc.sbuf_tensor("m2n", [128, 8, 128], BF16))
        t["b1n"] = E(nc.sbuf_tensor("b1n", [128, 8, 128], BF16))
        t["zst"] = E(nc.sbuf_tensor("zst", [128, 8, 128], F32))
        t["o8"] = E(nc.sbuf_tensor("o8", [128, 8, 128], BF16))
        t["zb"] = E(nc.sbuf_tensor("zb", [128, 8, 128], BF16))
        t["lsq"] = t["ub"][:, 0:4096].rearrange("p (nn d) -> p nn d", d=128)
        t["lnr2"] = E(nc.sbuf_tensor("lnr2", [128, 2, NCH], F32))
        t["eps128"] = E(nc.sbuf_tensor("eps128", [128, 1], F32))
        t["olane"] = E(nc.sbuf_tensor("olane", [128, NCH, 128], BF16))
        t["lnr"] = E(nc.sbuf_tensor("lnr", [128, 2, NCH], F32))
        
        t["lnw"] = E(nc.sbuf_tensor("lnw", [128, 2, S], BF16))
        t["sm8S"] = E(nc.sbuf_tensor("sm8S", [128, 8, 128], BF16))
        t["sm8L"] = E(nc.sbuf_tensor("sm8L", [128, 8, 128], BF16))
        t["sm8I"] = E(nc.sbuf_tensor("sm8I", [128, 8, 128], BF16))
        t["triuI"] = E(nc.sbuf_tensor("triuIt", [128, 128], BF16))
        t["ident"] = E(nc.sbuf_tensor("identt", [128, 128], BF16))
        t["oner"] = E(nc.sbuf_tensor("oner", [1, 128], BF16))
        t["one1"] = E(nc.sbuf_tensor("one1", [128, 1], F32))

        pA = E(nc.psum_tensor("pA", [128, 1024], F32))
        pB = E(nc.psum_tensor("pB", [128, 1024], F32))
        pC = E(nc.psum_tensor("pC", [128, 1024], F32))
        pD = E(nc.psum_tensor("pD", [128, 1024], F32))

        sch = Sch(nc, ctx)
        emit_l1b(sch, g, t, pA, pB, pC, pD)
        sch.emit(ctx)
    Sch.PARANOID = False
    return nc


def emit_l1b(sch, g, t, pA, pB, pC, pD):
    o = sch.op
    dma = sch.dma
    consts = None
    for dst, src in [(t["sm8S"][:], g["smaskS8"][:, :, :]), (t["sm8L"][:], g["smaskL8"][:, :, :]),
                     (t["sm8I"][:], g["smaskI8"][:, :, :]), (t["triuI"][:], g["triuI"][:, :]),
                     (t["ident"][:], g["ident"][:, :]), (t["oner"][:], g["ones_row"][:, :]),
                     (t["abt"][:], g["ab_t"][:, :, :]), (t["rm"][:], g["rmask"][:, :]),
                     (t["lnw"][:], g["lnwrep"][:, :, :])]:
        consts = dma("sync", dst, src, chain="const")
    one1 = o("vector", lambda e: e.memset(t["one1"][:], 1.0), inc="vC")
    eps128 = o("vector", lambda e: e.memset(t["eps128"][:], 1e-5), inc="vC")

    conv_done = []
    diag_free = [None, None]
    cwb_free = [None, None]
    cvo_free = [None, None]
    ub_free = [None]
    mz = o("vector", lambda e: e.memset(t["ub"][:, 0:4], 0.0), inc="vC")
    for tau, (rawnm, outnm) in enumerate((("qraw", "qc"), ("kraw", "kc"), ("vraw", "vc"))):
        for l in range(8):
            par = (tau * 8 + l) % 2
            lw = dma("gpsimd", t["cw"][:, par, 0:4], g["convw"][tau, l, :, :],
                     waits=[diag_free[par], cwb_free[par]], chain="cwld")
            lb = dma("gpsimd", t["cw"][:, par, 4:5], g["convb"][tau, l, :, :],
                     chain="cwld")
            dgs = []
            for j in range(4):
                dg = o("vector", lambda e, j=j, par=par: e.tensor_scalar(
                    t["diag"][:, par, j, :], t["ident"][:], t["cw"][:, par, j:j + 1],
                    None, ALU.mult, ALU.bypass), waits=[lw, consts, diag_free[par]],
                    inc="vC")
                dgs.append(dg)
            lu = dma("gpsimd", t["ub"][:, 4:4100], g[rawnm][l, :, :],
                     waits=[ub_free[0], mz], chain="uld")
            pv = None
            for sl8 in range(8):
                par2 = sl8 % 2
                mm = None
                for j in range(4):
                    mm = o("tensor", lambda e, j=j, sl8=sl8, par=par, par2=par2: e.matmul(
                        pD[:, par2 * 512:par2 * 512 + 512],
                        t["diag"][:, par, j, :],
                        t["ub"][:, sl8 * 512 + j + 1:sl8 * 512 + j + 513],
                        start=(j == 0), stop=(j == 3)),
                        waits=dgs + [lu, cvo_free[par2]], inc="tC")
                ev = o("scalar", lambda e, par=par, par2=par2: e.activation(
                    out=t["cvo"][:, par2, :], in_=pD[:, par2 * 512:par2 * 512 + 512],
                    func=AF.Silu, bias=t["cw"][:, par, 4:5]), waits=[mm, lb], inc="sC")
                dt_ = dma("sync", g[outnm][l, :, sl8 * 512:(sl8 + 1) * 512],
                          t["cvo"][:, par2, :], waits=[ev], chain="cvout")
                cvo_free[par2] = dt_
                pv = dt_
            diag_free[par] = mm
            cwb_free[par] = ev
            ub_free[0] = mm
            conv_done.append(pv)
    cvdone = conv_done[-1]

    e1 = o("scalar", lambda e: e.activation(out=t["sc1"][:], in_=t["abt"][:, 0, :],
                                            func=AF.Exp, scale=-1.0),
           waits=[consts], inc="sC")
    e2 = o("scalar", lambda e: e.activation(out=t["lat"][:], in_=t["sc1"][:],
                                            func=AF.Ln, bias=t["one1"][:, :]),
           waits=[e1, one1], inc="sC")
    e3 = o("vector", lambda e: e.tensor_scalar(t["lnab"][:], t["lat"][:], -1.0, None,
                                               ALU.mult, ALU.bypass), waits=[e2], inc="vC")
    cm = o("tensor", lambda e: e.matmul(pA[:, 0:256], t["triuI"][:], t["lnab"][:],
                                        start=True, stop=True),
           waits=[e3, consts], inc="tC")
    e4 = o("vector", lambda e: e.tensor_copy(t["cums"][:], pA[:, 0:256]),
           waits=[cm], inc="vC")
    e6 = o("scalar", lambda e: e.activation(out=t["bet"][:], in_=t["abt"][:, 1, :],
                                            func=AF.Sigmoid), waits=[consts], inc="sC")
    e7 = o("vector", lambda e: e.tensor_tensor(t["sc1"][:], t["cums"][:], t["lat"][:],
                                               ALU.add), waits=[e4, e2], inc="vC")
    e8 = o("scalar", lambda e: e.activation(out=t["sc1"][:], in_=t["sc1"][:],
                                            func=AF.Exp), waits=[e7], inc="sC")
    e9 = o("vector", lambda e: e.tensor_tensor(t["ccl"][:], t["bet"][:], t["sc1"][:],
                                               ALU.mult), waits=[e6, e8], inc="vC")
    x1 = o("scalar", lambda e: e.activation(out=t["igc"][:], in_=t["abt"][:, 1, :],
                                            func=AF.Exp, scale=-1.0), waits=[consts], inc="sC")
    x2_ = o("scalar", lambda e: e.activation(out=t["igc"][:], in_=t["igc"][:],
                                             func=AF.Ln, bias=t["one1"][:, :]),
            waits=[x1, one1], inc="sC")
    x3 = o("vector", lambda e: e.tensor_tensor(t["sc1"][:], t["cums"][:], t["lat"][:],
                                               ALU.add), waits=[e4, e2, e8], inc="vC")
    lcc = o("vector", lambda e: e.tensor_tensor(t["lcc"][:], t["sc1"][:], t["igc"][:],
                                                ALU.subtract), waits=[x3, x2_], inc="vC")
    gtc = o("scalar", lambda e: e.activation(out=t["gtc"][:], in_=t["cums"][:],
                                             func=AF.Exp), waits=[e4], inc="sC")

    la = dma("sync", t["ar"][:], g["abr"][0, :, :], chain="rr")
    r1 = o("scalar", lambda e: e.activation(out=t["rt1"][:], in_=t["ar"][:],
                                            func=AF.Exp, scale=-1.0), waits=[la], inc="sC")
    r2 = o("scalar", lambda e: e.activation(out=t["rt1"][:], in_=t["rt1"][:],
                                            func=AF.Ln, bias=t["one1"][0:8, :]),
           waits=[r1, one1], inc="sC")
    r3 = o("vector", lambda e: e.tensor_tensor_scan(t["rt2"][:], t["rm"][:], t["rt1"][:],
                                                    0.0, ALU.mult, ALU.add),
           waits=[r2, consts], inc="vC")
    r5 = o("vector", lambda e: e.tensor_scalar(
        t["gC8"][:], t["rt2"].ap().rearrange("l (n tp) -> l n tp", tp=128)[:, :, 127],
        -1.0, None, ALU.mult, ALU.bypass), waits=[r3], inc="vC")
    gout = dma("sync", g["gcdr"][0, :, :], t["gC8"][:], waits=[r5], chain="rr")
    igo = dma("sync", g["crdr"][0, 1, :, :], t["rt2"][:], waits=[r3], chain="rr")
    lb_ = dma("sync", t["ar"][:], g["abr"][1, :, :], waits=[r2], chain="rr")
    r6 = o("scalar", lambda e: e.activation(out=t["crg"][:], in_=t["ar"][:],
                                            func=AF.Exp, scale=-1.0),
           waits=[lb_], inc="sC")
    r7 = o("scalar", lambda e: e.activation(out=t["crg"][:], in_=t["crg"][:],
                                            func=AF.Ln, bias=t["one1"][0:8, :]),
           waits=[r6], inc="sC")
    ra = o("vector", lambda e: e.tensor_tensor(t["rt1"][:], t["rt1"][:], t["crg"][:],
                                               ALU.subtract), waits=[r7, r3], inc="vC")
    rb2 = o("vector", lambda e: e.tensor_tensor(t["crg"][:], t["rt1"][:], t["rt2"][:],
                                                ALU.subtract), waits=[ra], inc="vC")
    cro = dma("sync", g["crdr"][0, 0, :, :], t["crg"][:], waits=[rb2], chain="rr")
    onerf_m = o("vector", lambda e: e.memset(t["onerf"][:], 1.0), inc="vC")
    gcl = dma("sync", t["ccrow"][:, :], g["gcdr"].rearrange("x l n -> x (l n)"),
              waits=[gout], chain="rr")
    gmm = o("tensor", lambda e: e.matmul(pA[:, 256:512], t["onerf"][:],
                                         t["ccrow"][:, :], start=True, stop=True),
            waits=[gcl, cm, e4, onerf_m], inc="tC")
    dc1 = o("vector", lambda e: e.tensor_scalar(
        t["dcl"][:], pA[:, 256:512], 1.0, None, ALU.mult, ALU.bypass),
        waits=[gmm], inc="vC")
    gre = o("scalar", lambda e: e.activation(out=t["gCrep"][:], in_=t["dcl"][:],
                                             func=AF.Exp), waits=[dc1], inc="sC")
    dc2 = o("vector", lambda e: e.tensor_tensor(t["dcl"][:], t["dcl"][:], t["cums"][:],
                                                ALU.subtract), waits=[dc1, e4, gre], inc="vC")
    dstt = o("scalar", lambda e: e.activation(out=t["dcl"][:], in_=t["dcl"][:],
                                              func=AF.Exp), waits=[dc2], inc="sC")

    zinit = o("vector", lambda e: e.memset(t["zst"][:], 0.0), inc="vC")
    qk_free = [None, None]
    ktv_free = [None, None]
    crw_free = [gcl, None]
    pa_free = [e4, gre]
    pb_free = []
    pc_free = []
    pd_free = [cvo_free[0], cvo_free[1]]
    wtb_free = [None, None]
    wub_free = [None, None]
    x_free = [None]
    zb_tok = [zinit]
    z_tok = [zinit]
    o8_free = [None]
    odma = [None]

    sset = getattr(Sch, "SER_SET", None) or frozenset()
    SER = lambda name: name in sset
    for n in range(NCH):
        sch.barrier()
        sch.serialize = SER("s1a")
        par = n % 2
        csl = slice(n * 128, (n + 1) * 128)
        lqk = []
        for ti, nm in ((0, "qc"), (1, "kc")):
            lqk.append(dma("gpsimd", t["qkf"][:, par, ti],
                           g[nm].rearrange("l d s -> d l s")[:, :, csl],
                           waits=[qk_free[par], cvdone], chain="qkld"))
        lvf = dma("gpsimd", t["ktv"][:, par, 1], g["vc"].rearrange("l d s -> d l s")[:, :, csl],
                  waits=[ktv_free[par], cvdone], chain="vtld")
        sch.serialize = SER("s1c")
        lcr = dma("gpsimd", t["crowC"].ap().rearrange("x pp a n -> x pp a n")[0:1, par]
                  .rearrange("x a (l s) -> x a l s", s=128),
                  g["crdr"].rearrange("x a l (nn s) -> x a l nn s", s=128)[0:1, :, :, n, :],
                  waits=[crw_free[par], cro, igo], chain="crld")
        sch.serialize = SER("s1a")
        ks1 = o("gpsimd", lambda e, par=par: e.tensor_scalar(
            t["qkf"][:, par, 1], t["qkf"][:, par, 1], ISQ, None, ALU.mult, ALU.bypass),
            waits=lqk, inc="gC")
        sch.serialize = SER("s1b")
        kT = []
        vT = []
        for l in range(8):
            kT.append(o("tensor", lambda e, l=l, par=par: e.matmul(
                pA[:, l * 128:(l + 1) * 128], t["qkf"][:, par, 1, l, :],
                t["ident"][:], start=True, stop=True),
                waits=[ks1, consts] + pa_free, inc="tC"))
            vT.append(o("tensor", lambda e, l=l, par=par: e.matmul(
                pB[:, l * 128:(l + 1) * 128], t["ktv"][:, par, 1, l, :],
                t["ident"][:], start=True, stop=True),
                waits=[lvf, consts] + pb_free, inc="tC"))
        evacK = o("scalar", lambda e, par=par: e.activation(
            out=t["ktv"][:, par, 0], in_=pA[:, 0:1024], func=AF.Copy),
            waits=[kT[-1], ktv_free[par]], inc="sC")
        evacV = o("vector", lambda e, par=par: e.tensor_copy(
            t["ktv"][:, par, 1], pB[:, 0:1024]), waits=[vT[-1]], inc="vC")
        pa_free = [evacK]
        pb_free = [evacV]

        sch.serialize = SER("s2")
        kk = []
        sq = []
        for l in range(8):
            kk.append(o("tensor", lambda e, l=l, par=par: e.matmul(
                pA[:, l * 128:(l + 1) * 128], t["qkf"][:, par, 1, l, :],
                t["qkf"][:, par, 1, l, :], start=True, stop=True),
                waits=[ks1] + pa_free, inc="tC"))
            sq.append(o("tensor", lambda e, l=l, par=par: e.matmul(
                pB[:, l * 128:(l + 1) * 128], t["qkf"][:, par, 1, l, :],
                t["qkf"][:, par, 0, l, :], start=True, stop=True),
                waits=[ks1] + pb_free, inc="tC"))
        crep = []
        for l in range(8):
            crep.append(o("tensor", lambda e, l=l, par=par: e.matmul(
                pC[:, l * 128:(l + 1) * 128], t["onerf"][:],
                t["crowC"][0:1, par, 0, l * 128:(l + 1) * 128],
                start=True, stop=True), waits=[lcr, onerf_m] + pc_free, inc="tC"))
            crep.append(o("tensor", lambda e, l=l, par=par: e.matmul(
                pD[:, l * 128:(l + 1) * 128], t["onerf"][:],
                t["crowC"][0:1, par, 1, l * 128:(l + 1) * 128],
                start=True, stop=True), waits=list(pd_free), inc="tC"))
        pa_free, pb_free, pc_free, pd_free = [], [], [], []
        sch.serialize = SER("s3")
        kkn = o("vector", lambda e: e.tensor_scalar(
            t["g1"][:], pA[:, 0:1024], -1.0, None, ALU.mult, ALU.bypass),
            waits=[kk[-1]], inc="vC")
        pa_free = [kkn]
        lg1 = o("vector", lambda e, n=n: e.tensor_scalar(
            t["lam"][:], pC[:, 0:1024], 1.0, None, ALU.mult, ALU.bypass),
            waits=crep, inc="vC")
        lg2 = []
        for l in range(8):
            lg2.append(o("vector", lambda e, l=l, n=n: e.tensor_scalar(
                t["lam"][:, l, :], t["lam"][:, l, :],
                t["cums"][:, l * 32 + n:l * 32 + n + 1], None, ALU.subtract, ALU.bypass),
                waits=[lg1], inc="vC"))
        lg3 = o("vector", lambda e: e.tensor_tensor(
            t["lam"][:], t["lam"][:], t["sm8S"][:], ALU.add),
            waits=lg2 + [consts], inc="vC")
        lg4 = o("scalar", lambda e: e.activation(
            out=t["crS"][:], in_=t["lam"][:], func=AF.Exp), waits=[lg3], inc="sC")
        wt0 = o("vector", lambda e: e.tensor_tensor(
            t["wt"][:, 0], t["crS"][:], t["g1"][:], ALU.mult),
            waits=[lg4, kkn, wtb_free[0]], inc="vC")
        pc_free = [lg1]
        lu1 = []
        for l in range(8):
            lu1.append(o("vector", lambda e, l=l, n=n: e.tensor_scalar(
                t["lam"][:, l, :], pD[:, l * 128:(l + 1) * 128],
                t["lcc"][:, l * 32 + n:l * 32 + n + 1], None, ALU.add, ALU.bypass),
                waits=[crep[-1], lcc, wt0], inc="vC"))
        lu2 = o("vector", lambda e: e.tensor_tensor(
            t["lam"][:], t["lam"][:], t["sm8L"][:], ALU.add),
            waits=lu1 + [consts], inc="vC")
        lu3 = o("scalar", lambda e: e.activation(
            out=t["igS"][:], in_=t["lam"][:], func=AF.Exp), waits=[lu2], inc="sC")
        wu0 = o("vector", lambda e: e.tensor_tensor(
            t["wu"][:, 0], t["igS"][:], t["g1"][:], ALU.mult),
            waits=[lu3, kkn, wub_free[0]], inc="vC")
        pd_free = [lu1[-1]]
        ls1 = o("vector", lambda e: e.tensor_scalar(
            t["lam"][:], pD[:, 0:1024], -1.0, None, ALU.mult, ALU.bypass),
            waits=crep + [lu1[-1], wu0], inc="vC")
        ls2 = []
        for l in range(8):
            ls2.append(o("vector", lambda e, l=l, n=n: e.tensor_scalar(
                t["lam"][:, l, :], t["lam"][:, l, :],
                t["cums"][:, l * 32 + n:l * 32 + n + 1], None, ALU.subtract, ALU.bypass),
                waits=[ls1], inc="vC"))
        ls3 = o("vector", lambda e: e.tensor_tensor(
            t["lam"][:], t["lam"][:], t["sm8I"][:], ALU.add),
            waits=ls2 + [consts], inc="vC")
        ls4 = o("scalar", lambda e: e.activation(
            out=t["sq1"][:], in_=t["lam"][:], func=AF.Exp), waits=[ls3], inc="sC")
        sqmt = o("vector", lambda e: e.tensor_tensor(
            t["sqm"][:], t["sq1"][:], pB[:, 0:1024], ALU.mult),
            waits=[ls4, sq[-1]], inc="vC")
        pb_free = [sqmt]
        pd_free = pd_free + [ls2[-1]]
        sch.serialize = SER("s4")
        xinits = []
        for l in range(8):
            xinits.append(o("gpsimd", lambda e, l=l, par=par, n=n: e.tensor_scalar(
                t["xch"][:, l, 0:128], t["ktv"][:, par, 1, l, :],
                t["bet"][:, l * 32 + n:l * 32 + n + 1], None, ALU.mult, ALU.bypass),
                waits=[evacV, e6, x_free[0]], inc="gC"))
            xinits.append(o("gpsimd", lambda e, l=l, par=par, n=n: e.tensor_scalar(
                t["xch"][:, l, 128:256], t["ktv"][:, par, 0, l, :],
                t["ccl"][:, l * 32 + n:l * 32 + n + 1], None, ALU.mult, ALU.bypass),
                waits=[evacK, e9, x_free[0]], inc="gC"))
        x_cur = xinits[-1]
        sch.serialize = SER("s5")
        wtcur, wucur = wt0, wu0
        src = 0
        x_deps = [x_cur]
        for k in range(7):
            amm = []
            for l in range(8):
                psX = pC if l < 4 else pD
                off = (l % 4) * 256
                amm.append(o("tensor", lambda e, l=l, src=src, psX=psX, off=off: e.matmul(
                    psX[:, off:off + 256], t["wt"][:, src, l, :], t["xch"][:, l, :],
                    start=True, stop=False),
                    waits=[wtcur] + x_deps + pc_free + pd_free, inc="tC"))
                amm.append(o("tensor", lambda e, l=l, psX=psX, off=off: e.matmul(
                    psX[:, off:off + 256], t["ident"][:], t["xch"][:, l, :],
                    start=False, stop=True), waits=[consts], inc="tC"))
            pc_free, pd_free = [], []
            if k % 2 == 0:
                evC = o("scalar", lambda e: e.activation(
                    out=t["xch"][:, 0:4, :], in_=pC[:, 0:1024], func=AF.Copy),
                    waits=amm, inc="sC")
                evD = o("vector", lambda e: e.tensor_copy(
                    t["xch"][:, 4:8, :], pD[:, 0:1024]), waits=amm, inc="vC")
            else:
                evC = o("vector", lambda e: e.tensor_copy(
                    t["xch"][:, 0:4, :], pC[:, 0:1024]), waits=amm, inc="vC")
                evD = o("scalar", lambda e: e.activation(
                    out=t["xch"][:, 4:8, :], in_=pD[:, 0:1024], func=AF.Copy),
                    waits=amm, inc="sC")
            pc_free = [evC]
            pd_free = [evD]
            xall = [evC, evD]
            if k < 6:
                smm = []
                for l in range(8):
                    smm.append(o("tensor", lambda e, l=l, src=src: e.matmul(
                        pA[:, l * 128:(l + 1) * 128], t["wu"][:, src, l, :],
                        t["wt"][:, src, l, :], start=True, stop=True),
                        waits=[wtcur, wucur] + pa_free, inc="tC"))
                    smm.append(o("tensor", lambda e, l=l, src=src: e.matmul(
                        pB[:, l * 128:(l + 1) * 128], t["wt"][:, src, l, :],
                        t["wu"][:, src, l, :], start=True, stop=True),
                        waits=pb_free, inc="tC"))
                pa_free, pb_free = [], []
                dst = 1 - src
                nwt = o("scalar", lambda e, dst=dst: e.activation(
                    out=t["wt"][:, dst], in_=pA[:, 0:1024], func=AF.Copy),
                    waits=smm + [wtb_free[dst]], inc="sC")
                nwu = o("vector", lambda e, dst=dst: e.tensor_copy(
                    t["wu"][:, dst], pB[:, 0:1024]), waits=smm + [wub_free[dst]],
                    inc="vC")
                pa_free = [nwt]
                pb_free = [nwu]
                wtb_free[src] = amm[-1]
                wub_free[src] = smm[-1]
                wtcur, wucur = nwt, nwu
                src = dst
            else:
                wtb_free[src] = amm[-1]
            x_cur = xall[-1]
            x_deps = xall
        uv_tok = x_deps
        sch.serialize = SER("s6")
        zb = o("vector", lambda e: e.tensor_copy(t["zb"][:], t["zst"][:]),
               waits=[z_tok[0], zb_tok[0]], inc="vC")
        dks = []
        for l in range(8):
            dks.append(o("gpsimd", lambda e, l=l, par=par, n=n: e.tensor_scalar(
                t["dk"][:, l, :], t["ktv"][:, par, 0, l, :],
                t["dcl"][:, l * 32 + n:l * 32 + n + 1], None, ALU.mult, ALU.bypass),
                waits=[evacK, dstt, xinits[-1]], inc="gC"))
        sch.serialize = SER("s7")
        mmm = []
        for l in range(8):
            mmm.append(o("tensor", lambda e, l=l: e.matmul(
                pA[:, l * 128:(l + 1) * 128], t["xch"][:, l, 128:256],
                t["dk"][:, l, :], start=True, stop=True),
                waits=[dks[l]] + uv_tok + pa_free, inc="tC"))
            mmm.append(o("tensor", lambda e, l=l: e.matmul(
                pB[:, l * 128:(l + 1) * 128], t["xch"][:, l, 128:256],
                t["sqm"][:, l, :], start=True, stop=True),
                waits=[sqmt] + pb_free, inc="tC"))
        pa_free, pb_free = [], []
        m2e = o("scalar", lambda e: e.activation(
            out=t["m2n"][:], in_=pA[:, 0:1024], func=AF.Copy, scale=-1.0),
            waits=mmm, inc="sC")
        b1e = o("vector", lambda e: e.tensor_scalar(
            t["b1n"][:], pB[:, 0:1024], -1.0, None, ALU.mult, ALU.bypass),
            waits=mmm, inc="vC")
        pa_free = [m2e]
        pb_free = [b1e]
        sch.serialize = SER("s8")
        zmm = []
        for l in range(8):
            zmm.append(o("tensor", lambda e, l=l: e.matmul(
                pC[:, l * 128:(l + 1) * 128], t["dk"][:, l, :],
                t["xch"][:, l, 0:128], start=True, stop=False),
                waits=[dks[l]] + uv_tok + pc_free, inc="tC"))
            zmm.append(o("tensor", lambda e, l=l: e.matmul(
                pC[:, l * 128:(l + 1) * 128], t["m2n"][:, l, :],
                t["zb"][:, l, :], start=False, stop=True),
                waits=[m2e, zb], inc="tC"))
        pc_free = []
        znew = []
        for l in range(8):
            znew.append(o("vector", lambda e, l=l, n=n: e.scalar_tensor_tensor(
                t["zst"][:, l, :], t["zst"][:, l, :],
                t["gCrep"][:, l * 32 + n:l * 32 + n + 1],
                pC[:, l * 128:(l + 1) * 128], ALU.mult, ALU.add),
                waits=[zmm[-1], gre], inc="vC"))
        z_tok = [znew[-1]]
        pc_free = [znew[-1]]
        sch.serialize = SER("s9")
        omm = []
        for l in range(8):
            omm.append(o("tensor", lambda e, l=l, par=par: e.matmul(
                pD[:, l * 128:(l + 1) * 128], t["qkf"][:, par, 0, l, :],
                t["zb"][:, l, :], start=True, stop=True),
                waits=[zb] + pd_free, inc="tC"))
            omm.append(o("tensor", lambda e, l=l: e.matmul(
                pA[:, l * 128:(l + 1) * 128], t["b1n"][:, l, :],
                t["zb"][:, l, :], start=True, stop=False),
                waits=[b1e] + pa_free, inc="tC"))
            omm.append(o("tensor", lambda e, l=l: e.matmul(
                pA[:, l * 128:(l + 1) * 128], t["sqm"][:, l, :],
                t["xch"][:, l, 0:128], start=False, stop=True),
                waits=[sqmt] + uv_tok, inc="tC"))
        pd_free = []
        pa_free = []
        ov2 = o("vector", lambda e: e.tensor_copy(t["sq1"][:], pA[:, 0:1024]),
                waits=omm + [ls4], inc="vC")
        pa_free = [ov2]
        oevs = []
        for l in range(8):
            oevs.append(o("vector", lambda e, l=l, n=n: e.scalar_tensor_tensor(
                t["o8"][:, l, :], pD[:, l * 128:(l + 1) * 128],
                t["gtc"][:, l * 32 + n:l * 32 + n + 1], t["sq1"][:, l, :],
                ALU.mult, ALU.add), waits=[ov2, gtc, o8_free[0]], inc="vC"))
        oev = oevs[-1]
        pd_free = [oev]
        od = dma("sync", g["odrw"].rearrange("l tp nn d -> tp l nn d")[:, :, n, :],
                 t["o8"][:], waits=[oev], chain="odma")
        o8_free = [od]
        odma[0] = od
        qk_free[par] = omm[-1]
        ktv_free[par] = dks[-1]
        crw_free[par] = crep[-1]
        x_free = [omm[-1]]
        zb_tok = [omm[-1]]

    sch.barrier()
    sch.serialize = bool(getattr(Sch, "SER_LN", False))
    ol_free = [None]
    lnfin = None
    for l in range(8):
        lo = dma("gpsimd", t["olane"][:], g["odrw"][l, :, :, :],
                 waits=[odma[0], ol_free[0]], chain="olld")
        s1 = o("vector", lambda e: e.tensor_reduce(
            t["lnr"][:, 0, :], t["olane"][:], AX.X, ALU.add), waits=[lo], inc="vC")
        s2 = o("scalar", lambda e: e.activation(
            out=t["lsq"][:], in_=t["olane"][:], func=AF.Square), waits=[lo], inc="sC")
        s3 = o("vector", lambda e: e.tensor_reduce(
            t["lnr"][:, 1, :], t["lsq"][:], AX.X, ALU.add), waits=[s2], inc="vC")
        s4 = o("vector", lambda e: e.tensor_scalar(
            t["lnr"][:, 0, :], t["lnr"][:, 0, :], 1.0 / 128, None, ALU.mult, ALU.bypass),
            waits=[s1], inc="vC")
        s5 = o("vector", lambda e: e.tensor_scalar(
            t["lnr"][:, 1, :], t["lnr"][:, 1, :], 1.0 / 128, None, ALU.mult, ALU.bypass),
            waits=[s3], inc="vC")
        s6 = o("vector", lambda e: e.scalar_tensor_tensor(
            t["lnr2"][:, 0, :], t["lnr"][:, 0, :], -1.0, t["lnr"][:, 0, :],
            ALU.mult, ALU.mult), waits=[s4], inc="vC")
        s7 = o("vector", lambda e: e.tensor_tensor(
            t["lnr"][:, 1, :], t["lnr"][:, 1, :], t["lnr2"][:, 0, :], ALU.add),
            waits=[s5, s6], inc="vC")
        s8 = o("scalar", lambda e: e.activation(
            out=t["lnr"][:, 1, :], in_=t["lnr"][:, 1, :], func=AF.Ln,
            bias=t["eps128"][:, :]), waits=[s7], inc="sC")
        s9 = o("scalar", lambda e: e.activation(
            out=t["lnr"][:, 1, :], in_=t["lnr"][:, 1, :], func=AF.Exp, scale=-0.5),
            waits=[s8], inc="sC")
        sa = o("vector", lambda e: e.tensor_tensor(
            t["lnr2"][:, 1, :], t["lnr"][:, 0, :], t["lnr"][:, 1, :], ALU.mult),
            waits=[s9], inc="vC")
        aps = []
        for nn in range(NCH):
            aps.append(o("vector", lambda e, nn=nn: e.tensor_scalar(
                t["olane"][:, nn, :], t["olane"][:, nn, :],
                t["lnr"][:, 1, nn:nn + 1], t["lnr2"][:, 1, nn:nn + 1],
                ALU.mult, ALU.subtract), waits=[s9, sa], inc="vC"))
        w1 = o("vector", lambda e: e.tensor_tensor(
            t["olane"].ap().rearrange("p a b -> p (a b)"),
            t["olane"].ap().rearrange("p a b -> p (a b)"), t["lnw"][:, 0, :],
            ALU.mult), waits=aps + [consts], inc="vC")
        w2 = o("vector", lambda e: e.tensor_tensor(
            t["olane"].ap().rearrange("p a b -> p (a b)"),
            t["olane"].ap().rearrange("p a b -> p (a b)"), t["lnw"][:, 1, :],
            ALU.add), waits=[w1], inc="vC")
        fo = dma("sync", g["odr"][l, :, :, :], t["olane"][:], waits=[w2], chain="odma")
        ol_free = [fo]
        lnfin = fo
    o("sync", lambda e: e.nop(), waits=[lnfin])


_L1B_NC = None


def l1b_consts():
    c = {}
    NEGM = -1.0e5
    su = np.full((128, 128), NEGM, np.float32)
    sl = np.full((128, 128), NEGM, np.float32)
    si = np.full((128, 128), NEGM, np.float32)
    for a_ in range(128):
        for b_ in range(128):
            if a_ < b_:
                su[a_, b_] = 0.0
                si[a_, b_] = 0.0
            if b_ < a_:
                sl[a_, b_] = 0.0
            if a_ == b_:
                si[a_, b_] = 0.0
    c["smaskS8"] = np.ascontiguousarray(np.repeat(su[:, None, :], 8, axis=1)).astype(BF)
    c["smaskL8"] = np.ascontiguousarray(np.repeat(sl[:, None, :], 8, axis=1)).astype(BF)
    c["smaskI8"] = np.ascontiguousarray(np.repeat(si[:, None, :], 8, axis=1)).astype(BF)
    ti = np.zeros((128, 128), np.float32)
    for a_ in range(128):
        ti[a_, a_:] = 1.0
    c["triuI"] = ti.astype(BF)
    rm = np.ones((8, S), np.float32)
    rm[:, 0::128] = 0.0
    c["rmask"] = rm
    return c


_L1B_CONSTS = None


def l1b_inmaps(inputs, l1a_out):
    global _L1B_CONSTS
    hc = host_consts()
    if _L1B_CONSTS is None:
        _L1B_CONSTS = l1b_consts()
    bc = _L1B_CONSTS
    cw = {k: np.asarray(inputs[f"L1_attn_conv_{k}_w"], np.float32) for k in "qkv"}
    cb = {k: np.asarray(inputs[f"L1_attn_conv_{k}_b"], np.float32) for k in "qkv"}
    lnw = np.asarray(inputs["L1_attn_ln_w"], np.float32)
    lnb = np.asarray(inputs["L1_attn_ln_b"], np.float32)
    lnwrep = np.zeros((128, 2, S), np.float32)
    lnwrep[:, 0, :] = np.tile(lnw, NCH)[None, :]
    lnwrep[:, 1, :] = np.tile(lnb, NCH)[None, :]
    lnwrep = lnwrep.astype(BF)
    ab = l1a_out["ab"]
    in_maps = []
    for core in range(8):
        b, hh = core // 2, core % 2
        heads = [hh * 8 + l for l in range(8)]
        qraw = np.stack([l1a_out["oq"][b, :, h * 128:(h + 1) * 128].T for h in heads])
        kraw = np.stack([l1a_out["ok"][b, :, h * 128:(h + 1) * 128].T for h in heads])
        vraw = np.stack([l1a_out["ov"][b, :, h * 128:(h + 1) * 128].T for h in heads])
        convw = np.stack([np.stack([cw[k][h * 128:(h + 1) * 128, 0, :] for h in heads])
                          for k in "qkv"])
        convb = np.stack([np.stack([cb[k][h * 128:(h + 1) * 128] for h in heads])
                          for k in "qkv"])[..., None]
        abt = np.zeros((128, 2, 256), np.float32)
        abr = np.zeros((2, 8, S), np.float32)
        for li, h in enumerate(heads):
            a_ = ab[b, :, h]
            b_ = ab[b, :, 16 + h]
            abr[0, li] = a_
            abr[1, li] = b_
            abt[:, 0, li * 32:(li + 1) * 32] = a_.reshape(NCH, 128).T
            abt[:, 1, li * 32:(li + 1) * 32] = b_.reshape(NCH, 128).T
        m = {
            "qraw": qraw.astype(BF), "kraw": kraw.astype(BF), "vraw": vraw.astype(BF),
            "convw": convw, "convb": convb,
            "ab_t": abt, "abr": abr, "rmask": bc["rmask"].astype(BF),
            "lnwrep": lnwrep,
            "smaskS8": bc["smaskS8"], "smaskL8": bc["smaskL8"],
            "smaskI8": bc["smaskI8"], "triuI": bc["triuI"],
            "ident": hc["ident"], "ones_row": hc["ones_row"],
        }
        in_maps.append(m)
    return in_maps


def run_l1b(inputs, l1a_out, dbg=False):
    global _L1B_NC
    if dbg:
        _L1B_NC = build_l1b(dbg=True)
    elif _L1B_NC is None:
        _L1B_NC = build_l1b()
    in_maps = l1b_inmaps(inputs, l1a_out)
    res = run_bass_kernel_spmd(_L1B_NC, in_maps, core_ids=list(range(8)))
    o = np.zeros((B, S, D), np.float32)
    dbgout = {}
    for core in range(8):
        b, hh = core // 2, core % 2
        od = np.asarray(res.results[core]["odr"]).astype(np.float32).reshape(8, 128, NCH, 128)
        for li in range(8):
            h = hh * 8 + li
            o[b, :, h * 128:(h + 1) * 128] = od[li].transpose(1, 0, 2).reshape(S, 128)
        if dbg:
            dbgout[core] = {nm: np.asarray(res.results[core][nm + "dr"]).astype(np.float32)
                            for nm in ("qc", "kc", "vc")}
    if dbg:
        return o, dbgout
    return o



def build_l1c():
    nc = bass.Bass("TRN2", target_bir_lowering=False)
    ap = lambda *a, **k: nc.dram_tensor(*a, **k).ap()
    g = {}
    g["x2T"] = ap("x2T", [D, TLOC], BF16, kind="ExternalInput")
    g["oT"] = ap("oT", [D, TLOC], BF16, kind="ExternalInput")
    g["gT"] = ap("gT", [D, TLOC], BF16, kind="ExternalInput")
    g["wo"] = ap("wo", [D, D], F32, kind="ExternalInput")
    g["wr"] = ap("wr", [128, FC], F32, kind="ExternalInput")
    g["br"] = ap("br", [1, 1], F32, kind="ExternalInput")
    g["ones_col"] = ap("ones_col", [128, 1], BF16, kind="ExternalInput")
    g["ones_row"] = ap("ones_row", [1, 128], BF16, kind="ExternalInput")
    g["x3T"] = ap("x3T", [D, TLOC], BF16, kind="ExternalOutput")
    g["lg"] = ap("lg", [1, TLOC], F32, kind="ExternalOutput")

    ctx = ExitStack()
    with ctx:
        E = ctx.enter_context
        t = {}
        t["xres"] = E(nc.sbuf_tensor("xres", [128, FC, TLOC], BF16))
        t["og"] = E(nc.sbuf_tensor("og", [128, FC, TLOC], BF16))
        t["wbuf"] = E(nc.sbuf_tensor("wbuf", [128, 2, FC, 128], BF16))
        t["gs"] = E(nc.sbuf_tensor("gs", [128, 2048], BF16))
        t["onec"] = E(nc.sbuf_tensor("onec", [128, 1], BF16))
        t["oner"] = E(nc.sbuf_tensor("oner", [1, 128], BF16))
        t["eps1"] = E(nc.sbuf_tensor("eps1", [1, 1], F32))
        t["brs"] = E(nc.sbuf_tensor("brs", [1, 1], F32))
        t["sq"] = E(nc.sbuf_tensor("sq", [128, 2, 512], BF16))
        t["stat"] = E(nc.sbuf_tensor("stat", [1, 4, 512], F32))
        t["statb"] = E(nc.sbuf_tensor("statb", [1, 2, 512], BF16))
        t["tmp"] = E(nc.sbuf_tensor("tmp", [128, 512], BF16))
        t["lgr"] = E(nc.sbuf_tensor("lgr", [1, 2, 512], F32))
        t["wrt"] = E(nc.sbuf_tensor("wrt", [128, FC], BF16))
        pA = E(nc.psum_tensor("pA", [128, 1024], F32))
        pB = E(nc.psum_tensor("pB", [128, 1024], F32))
        pC = E(nc.psum_tensor("pC", [128, 1024], F32))

        sch = Sch(nc, ctx)
        emit_l1c(sch, g, t, pA, pB, pC)
        sch.emit(ctx)
    return nc


def emit_l1c(sch, g, t, pA, pB, pC):
    o = sch.op
    dma = sch.dma
    c1 = dma("sync", t["onec"][:], g["ones_col"][:, :], chain="const")
    c2 = dma("sync", t["oner"][:], g["ones_row"][:, :], chain="const")
    c3 = dma("sync", t["brs"][:], g["br"][:, :], chain="const")
    consts = c3
    epsr = o("vector", lambda e: e.memset(t["eps1"][:], 1e-5), inc="vC")
    x2r = g["x2T"].rearrange("(c p) s -> p c s", p=128)
    oTr = g["oT"].rearrange("(c p) s -> p c s", p=128)
    gTr = g["gT"].rearrange("(c p) s -> p c s", p=128)

    lx = dma("gpsimd", t["xres"][:], x2r[:, :, :], chain="xld")
    lo_ = dma("gpsimd", t["og"][:], oTr[:, :, :], chain="xld")
    gdone = []
    gfree = None
    for c in range(FC):
        lg_ = dma("scalar", t["gs"][:], gTr[:, c, :], waits=[gfree], chain="gld")
        sg = o("scalar", lambda e: e.activation(
            out=t["gs"][:], in_=t["gs"][:], func=AF.Sigmoid), waits=[lg_], inc="sC")
        mu = o("vector", lambda e, c=c: e.tensor_tensor(
            t["og"][:, c, :], t["og"][:, c, :], t["gs"][:], ALU.mult),
            waits=[sg, lo_], inc="vC")
        gfree = mu
        gdone.append(mu)

    wore = g["wo"].rearrange("(hc p) eo -> p hc eo", p=128)
    wfree = [None, None]
    d_free = [None, None]
    wo_done = []
    for ec in range(FC):
        par = ec % 2
        lw = dma("gpsimd", t["wbuf"][:, par], wore[:, :, ec * 128:(ec + 1) * 128],
                 waits=[wfree[par]], chain="wld")
        for th in range(4):
            sel = th % 2
            psd = (pA, pB)[sel]
            mm = None
            for hc in range(FC):
                mm = o("tensor", lambda e, hc=hc, th=th, par=par, psd=psd: e.matmul(
                    psd[:, 0:512], t["wbuf"][:, par, hc, :],
                    t["og"][:, hc, th * 512:(th + 1) * 512],
                    start=(hc == 0), stop=(hc == FC - 1)),
                    waits=[lw, gdone[-1], d_free[sel]], inc="tC")
            ev = o("vector", lambda e, ec=ec, th=th, psd=psd: e.tensor_tensor(
                t["xres"][:, ec, th * 512:(th + 1) * 512], psd[:, 0:512],
                t["xres"][:, ec, th * 512:(th + 1) * 512], ALU.add),
                waits=[mm, lx], inc="vC")
            d_free[sel] = ev
            wo_done.append(ev)
        wfree[par] = mm

    lnst = [None]
    h3_done = []
    for tt4 in range(2):
        xf = lambda c, sl, tt4=tt4: t["xres"][:, c, tt4 * 1024 + sl.start:tt4 * 1024 + sl.stop]
        hf = lambda c, sl, tt4=tt4: t["og"][:, c, tt4 * 1024 + sl.start:tt4 * 1024 + sl.stop]
        hd, lnst[0] = emit_ln(sch, t, pA, pB, xf, hf,
                              [wo_done[-1], epsr, consts], lnst[0])
        h3_done.append(hd[-1])
    add_done = None
    for c in range(FC):
        add_done = o("vector", lambda e, c=c: e.tensor_tensor(
            t["xres"][:, c, :], t["xres"][:, c, :], t["og"][:, c, :], ALU.add),
            waits=[h3_done[-1]], inc="vC")
    so = dma("sync", g["x3T"].rearrange("(c p) s -> p c s", p=128), t["xres"][:],
             waits=[add_done], chain="out")
    lwr = dma("gpsimd", t["wrt"][:], g["wr"][:, :], chain="wld")
    lg_done = []
    ps_free = [lnst[0]["pa"], None]
    for th in range(4):
        sel = th % 2
        tt4, half = th // 2, th % 2
        mm = None
        for c in range(FC):
            mm = o("tensor", lambda e, c=c, tt4=tt4, half=half, sel=sel: e.matmul(
                pC[0:1, sel * 512:sel * 512 + 512], t["wrt"][:, c:c + 1],
                t["og"][:, c, tt4 * 1024 + half * 512:tt4 * 1024 + half * 512 + 512],
                start=(c == 0), stop=(c == FC - 1)),
                waits=[lwr, h3_done[-1], ps_free[sel]], inc="tC")
        ev = o("vector", lambda e, th=th, sel=sel: e.tensor_scalar(
            t["lgr"][:, sel, :], pC[0:1, sel * 512:sel * 512 + 512],
            t["brs"][:, :], None, ALU.add, ALU.bypass), waits=[mm, consts], inc="vC")
        dt_ = dma("sync", g["lg"][:, th * 512:(th + 1) * 512], t["lgr"][:, sel, :],
                  waits=[ev], chain="out")
        ps_free[sel] = dt_
        lg_done.append(dt_)
    o("sync", lambda e: e.nop(), waits=[lg_done[-1], so])


_L1C_NC = None


def run_l1c(inputs, x2, o_gate, l1a_out):
    global _L1C_NC
    hc = host_consts()
    if _L1C_NC is None:
        _L1C_NC = build_l1c()
    in_maps = []
    for core in range(8):
        b, hh = core // 2, core % 2
        sl = slice(hh * TLOC, (hh + 1) * TLOC)
        m = {
            "x2T": np.ascontiguousarray(x2[b, sl].T).astype(BF),
            "oT": np.ascontiguousarray(o_gate[b, sl].T).astype(BF),
            "gT": np.ascontiguousarray(l1a_out["og"][b, sl].T).astype(BF),
            "wo": np.asarray(inputs["L1_attn_Wo"], np.float32),
            "wr": tile_vec(inputs["L1_ffn_Wr"]),
            "br": np.asarray(inputs["L1_ffn_br"], np.float32).reshape(1, 1),
            "ones_col": hc["ones_col"],
            "ones_row": hc["ones_row"],
        }
        in_maps.append(m)
    res = run_bass_kernel_spmd(_L1C_NC, in_maps, core_ids=list(range(8)))
    x3 = np.zeros((B, S, D), np.float32)
    lg = np.zeros((B, S), np.float32)
    for core in range(8):
        b, hh = core // 2, core % 2
        sl = slice(hh * TLOC, (hh + 1) * TLOC)
        x3[b, sl] = np.asarray(res.results[core]["x3T"]).astype(np.float32).T
        lg[b, sl] = np.asarray(res.results[core]["lg"]).astype(np.float32)[0]
    return x3, lg



def build_l1d():
    Sch.PARANOID = False
    nc = bass.Bass("TRN2", target_bir_lowering=False)
    ap = lambda *a, **k: nc.dram_tensor(*a, **k).ap()
    g = {}
    g["x3t"] = ap("x3t", [TLOC, D], BF16, kind="ExternalInput")
    g["x3T"] = ap("x3T", [D, TLOC], BF16, kind="ExternalInput")
    g["lgb"] = ap("lgb", [128, 32], F32, kind="ExternalInput")
    g["w1"] = ap("w1", [D, DFF], F32, kind="ExternalInput")
    g["w2"] = ap("w2", [DFF, D], F32, kind="ExternalInput")
    g["b1t"] = ap("b1t", [128, 64], F32, kind="ExternalInput")
    g["b2r"] = ap("b2r", [1, D], BF16, kind="ExternalInput")
    g["jiota"] = ap("jiota", [128, 4], F32, kind="ExternalInput")
    g["iota512"] = ap("iota512", [128, 512], F32, kind="ExternalInput")
    g["sue"] = ap("sue", [128, 128], F32, kind="ExternalInput")
    g["ident"] = ap("ident", [128, 128], BF16, kind="ExternalInput")
    g["identf"] = ap("identf", [128, 128], F32, kind="ExternalInput")
    g["ones_row"] = ap("ones_row", [1, 128], BF16, kind="ExternalInput")
    g["x4T"] = ap("x4T", [D, TLOC], F32, kind="ExternalOutput")
    g["sdr"] = nc.dram_tensor("sdr", [1, 2, 128], F32).ap()
    g["cmdr"] = nc.dram_tensor("cmdr", [1, 32, 128], F32).ap()

    ctx = ExitStack()
    with ctx:
        E = ctx.enter_context
        t = {}
        t["big8"] = E(nc.sbuf_tensor("big8", [128, FC, 2048], BF16))
        t["h1F"] = E(nc.sbuf_tensor("h1F", [128, 64, 512], BF16))
        t["hgF"] = E(nc.sbuf_tensor("hgF", [128, FC, 512], BF16))
        t["hg"] = E(nc.sbuf_tensor("hg", [128, 4, 2048], BF16))
        t["P"] = E(nc.sbuf_tensor("P", [128, 4, 2048], BF16))
        t["o2w"] = t["hg"]
        t["wbuf"] = E(nc.sbuf_tensor("wbuf", [128, 2, FC, 128], BF16))
        t["w2b"] = E(nc.sbuf_tensor("w2b", [128, 2, 512], BF16))
        t["lgb"] = E(nc.sbuf_tensor("lgbs", [128, 32], F32))
        t["m01"] = E(nc.sbuf_tensor("m01", [128, 32], F32))
        t["exw"] = E(nc.sbuf_tensor("exw", [128, 32], F32))
        t["exwb"] = E(nc.sbuf_tensor("exwb", [128, 16], BF16))
        t["cum"] = E(nc.sbuf_tensor("cum", [128, 16], F32))
        t["cm2"] = E(nc.sbuf_tensor("cm2", [128, 32], F32))
        t["rows"] = E(nc.sbuf_tensor("rows", [1, 2, 512], F32))
        t["red"] = E(nc.sbuf_tensor("red", [128, 1], F32))
        t["r1x"] = E(nc.sbuf_tensor("r1x", [1, 128], F32))
        t["sc"] = E(nc.sbuf_tensor("sc", [1, 8], F32))
        t["thr"] = E(nc.sbuf_tensor("thr", [128, 2], F32))
        t["wcol"] = E(nc.sbuf_tensor("wcol", [128, 4], F32))
        t["onef"] = E(nc.sbuf_tensor("onef", [128, 1], F32))
        t["onerf"] = E(nc.sbuf_tensor("onerf", [1, 128], F32))
        t["oner"] = E(nc.sbuf_tensor("onerb", [1, 128], BF16))
        t["ident"] = E(nc.sbuf_tensor("identt", [128, 128], BF16))
        t["jio"] = E(nc.sbuf_tensor("jio", [128, 4], F32))
        t["io512"] = E(nc.sbuf_tensor("io512", [128, 512], F32))
        t["sue"] = E(nc.sbuf_tensor("suet", [128, 128], F32))
        t["eps1"] = E(nc.sbuf_tensor("eps1", [128, 1], F32))
        t["lnc"] = E(nc.sbuf_tensor("lnc", [128, 4], F32))
        t["lnc2"] = E(nc.sbuf_tensor("lnc2", [128, 4], F32))
        t["sq"] = t["hgF"].ap().rearrange("p c n -> p (c n)")[:, 0:2048]
        t["fo"] = E(nc.sbuf_tensor("fo", [128, 2, 512], F32))
        t["tpe"] = E(nc.sbuf_tensor("tpe", [32, 128], F32))
        t["b1t"] = E(nc.sbuf_tensor("b1ts", [128, 64], F32))
        t["b2s"] = E(nc.sbuf_tensor("b2s", [1, D], BF16))
        t["identf"] = E(nc.sbuf_tensor("identfs", [128, 128], F32))
        pA = E(nc.psum_tensor("pA", [128, 1024], F32))
        pB = E(nc.psum_tensor("pB", [128, 1024], F32))
        pC = E(nc.psum_tensor("pC", [128, 1024], F32))
        pT = E(nc.psum_tensor("pT", [128, 512], BF16))

        sch = Sch(nc, ctx)
        emit_l1d(sch, g, t, pA, pB, pC, pT)
        sch.emit(ctx)
    Sch.PARANOID = False
    return nc


def emit_l1d(sch, g, t, pA, pB, pC, pT):
    o = sch.op
    dma = sch.dma
    consts = None
    for dst, src in [(t["lgb"][:], g["lgb"][:, :]), (t["jio"][:], g["jiota"][:, :]),
                     (t["io512"][:], g["iota512"][:, :]), (t["sue"][:], g["sue"][:, :]),
                     (t["ident"][:], g["ident"][:, :]), (t["oner"][:], g["ones_row"][:, :]),
                     (t["identf"][:], g["identf"][:, :]), (t["b1t"][:], g["b1t"][:, :]),
                     (t["b2s"][:], g["b2r"][:, :])]:
        consts = dma("sync", dst, src, chain="const")
    onefm = o("vector", lambda e: e.memset(t["onef"][:], 1.0), inc="vC")
    onerfm = o("vector", lambda e: e.memset(t["onerf"][:], 1.0), inc="vC")
    epsm = o("vector", lambda e: e.memset(t["eps1"][:], 1e-5), inc="vC")
    lx3 = dma("gpsimd", t["big8"][:],
              g["x3t"].rearrange("(tc p) dd -> p tc dd", p=128), chain="xld")

    sch.barrier()
    lo1 = o("vector", lambda e: e.memset(t["sc"][:, 0:1], -1000.0), inc="vC")
    hi1 = o("vector", lambda e: e.memset(t["sc"][:, 1:2], 1000.0), inc="vC")
    lastb = [lo1, hi1]
    pd_free = [None]
    for it in range(34):
        a1 = o("vector", lambda e: e.tensor_tensor(t["sc"][:, 2:3], t["sc"][:, 0:1],
                                                   t["sc"][:, 1:2], ALU.add),
               waits=lastb, inc="vC")
        a2 = o("vector", lambda e: e.tensor_scalar(t["sc"][:, 2:3], t["sc"][:, 2:3], 0.5,
                                                   None, ALU.mult, ALU.bypass),
               waits=[a1], inc="vC")
        bmm = o("tensor", lambda e: e.matmul(pC[:, 0:1], t["onerf"][:], t["sc"][:, 2:3],
                                             start=True, stop=True),
                waits=[a2, onerfm] + pd_free, inc="tC")
        c1 = o("vector", lambda e: e.tensor_scalar(t["m01"][:], t["lgb"][:],
                                                   pC[:, 0:1], None, ALU.is_ge, ALU.bypass),
               waits=[bmm, consts], inc="vC")
        c2 = o("vector", lambda e: e.tensor_reduce(t["red"][:], t["m01"][:], AX.X, ALU.add),
               waits=[c1], inc="vC")
        cmm = o("tensor", lambda e: e.matmul(pC[0:1, 4:5], t["onef"][:], t["red"][:],
                                             start=True, stop=True),
                waits=[c2, onefm], inc="tC")
        g1 = o("vector", lambda e: e.tensor_scalar(t["sc"][:, 3:4], pC[0:1, 4:5],
                                                   float(K1), None, ALU.is_ge, ALU.bypass),
               waits=[cmm], inc="vC")
        g2 = o("vector", lambda e: e.tensor_scalar(t["sc"][:, 4:5], t["sc"][:, 3:4], -1.0,
                                                   1.0, ALU.mult, ALU.add),
               waits=[g1], inc="vC")
        dl = o("vector", lambda e: e.tensor_tensor(t["sc"][:, 5:6], t["sc"][:, 2:3],
                                                   t["sc"][:, 0:1], ALU.subtract),
               waits=[a2], inc="vC")
        l1 = o("vector", lambda e: e.scalar_tensor_tensor(
            t["sc"][:, 0:1], t["sc"][:, 5:6], t["sc"][:, 3:4], t["sc"][:, 0:1],
            ALU.mult, ALU.add), waits=[g1, dl], inc="vC")
        dh = o("vector", lambda e: e.tensor_tensor(t["sc"][:, 5:6], t["sc"][:, 2:3],
                                                   t["sc"][:, 1:2], ALU.subtract),
               waits=[l1], inc="vC")
        h1 = o("vector", lambda e: e.scalar_tensor_tensor(
            t["sc"][:, 1:2], t["sc"][:, 5:6], t["sc"][:, 4:5], t["sc"][:, 1:2],
            ALU.mult, ALU.add), waits=[g2, dh], inc="vC")
        lastb = [l1, h1]
        pd_free = [g1, c1]

    sch.barrier()
    bmf = o("tensor", lambda e: e.matmul(pC[:, 0:1], t["onerf"][:], t["sc"][:, 0:1],
                                         start=True, stop=True),
            waits=lastb + pd_free, inc="tC")
    th1 = o("vector", lambda e: e.tensor_copy(t["thr"][:, 0:1], pC[:, 0:1]),
            waits=[bmf], inc="vC")
    th2 = o("vector", lambda e: e.tensor_scalar(t["thr"][:, 1:2], t["thr"][:, 0:1], -1.0,
                                                None, ALU.mult, ALU.bypass),
            waits=[th1], inc="vC")
    m1 = o("vector", lambda e: e.tensor_scalar(t["m01"][:], t["lgb"][:], t["thr"][:, 0:1],
                                               None, ALU.is_ge, ALU.bypass),
          waits=[th1], inc="vC")
    e1 = o("scalar", lambda e: e.activation(out=t["exw"][:], in_=t["lgb"][:], func=AF.Exp,
                                            bias=t["thr"][:, 1:2]),
          waits=[th2, consts], inc="sC")
    e2 = o("vector", lambda e: e.tensor_tensor(t["exw"][:], t["exw"][:], t["m01"][:],
                                               ALU.mult), waits=[e1, m1], inc="vC")
    e3 = o("vector", lambda e: e.tensor_reduce(t["red"][:], t["exw"][:], AX.X, ALU.add),
          waits=[e2], inc="vC")
    smm = o("tensor", lambda e: e.matmul(pC[0:1, 4:5], t["onef"][:], t["red"][:],
                                         start=True, stop=True), waits=[e3], inc="tC")
    rs1 = o("vector", lambda e: e.reciprocal(t["sc"][:, 6:7], pC[0:1, 4:5]),
            waits=[smm], inc="vC")
    eb = o("vector", lambda e: e.tensor_copy(t["exwb"][:], t["exw"][:, 0:16]),
          waits=[e2], inc="vC")

    sch.barrier()
    im1 = o("tensor", lambda e: e.matmul(pA[:, 0:16], t["sue"][:], t["m01"][:, 0:16],
                                         start=True, stop=True),
            waits=[m1, consts], inc="tC")
    cs1 = o("tensor", lambda e: e.matmul(pA[0:1, 16:32], t["onef"][:], t["m01"][:, 0:16],
                                         start=True, stop=True),
            waits=[m1, onefm], inc="tC")
    sc0 = o("vector", lambda e: e.tensor_copy(t["rows"][0:1, 0, 0:16], pA[0:1, 16:32]),
            waits=[cs1], inc="vC")
    sc1 = o("vector", lambda e: e.tensor_tensor_scan(
        t["cm2"][0:1, 0:16], t["rows"][0:1, 0, 0:16], t["rows"][0:1, 0, 0:16],
        0.0, ALU.add, ALU.bypass), waits=[sc0], inc="vC")
    sc2 = o("vector", lambda e: e.tensor_tensor(t["cm2"][0:1, 16:32], t["cm2"][0:1, 0:16],
                                                pA[0:1, 16:32], ALU.subtract),
            waits=[sc1], inc="vC")
    com = o("tensor", lambda e: e.matmul(pB[:, 0:16], t["onerf"][:], t["cm2"][0:1, 16:32],
                                         start=True, stop=True),
            waits=[sc2, onerfm], inc="tC")
    cu1 = o("vector", lambda e: e.tensor_copy(t["cum"][:], pA[:, 0:16]),
            waits=[im1, sc2], inc="vC")
    cu2 = o("vector", lambda e: e.tensor_tensor(t["cum"][:], t["cum"][:], pB[:, 0:16],
                                                ALU.add), waits=[cu1, com], inc="vC")

    sch.barrier()
    PT = t["h1F"]
    pt_done = []
    for tc in range(FC):
        p1 = o("vector", lambda e, tc=tc: e.tensor_scalar(
            PT[:, tc, :], t["io512"][:], t["cum"][:, tc:tc + 1], None,
            ALU.is_equal, ALU.bypass), waits=[cu2, consts], inc="vC")
        p2 = o("vector", lambda e, tc=tc: e.tensor_scalar(
            PT[:, tc, :], PT[:, tc, :], t["m01"][:, tc:tc + 1], None,
            ALU.mult, ALU.bypass), waits=[p1], inc="vC")
        pt_done.append(p2)

    sch.barrier()
    cm1 = o("vector", lambda e: e.tensor_copy(t["cm2"][:, 0:16], t["cum"][:]),
            waits=[cu2], inc="vC")
    cm2b = o("vector", lambda e: e.tensor_copy(t["cm2"][:, 16:32], t["m01"][:, 0:16]),
             waits=[m1, sc2], inc="vC")
    tp = o("tensor", lambda e: e.transpose(pA[0:32, 0:128], t["cm2"][:, :], t["identf"][:]),
           waits=[cm1, cm2b, cu1, im1, cs1, consts], inc="tC")
    te = o("vector", lambda e: e.tensor_copy(t["tpe"][:], pA[0:32, 0:128]),
           waits=[tp], inc="vC")
    td = dma("sync", g["cmdr"][0, :, :], t["tpe"][:], waits=[te], chain="rr")
    sch.barrier()
    p_done = []
    rep_free = [None]
    cmr = g["cmdr"].rearrange("x (a tc) pp -> x a (tc pp)", a=2)
    for tt in range(4):
        tl = dma("sync", t["rows"][0:1, :, :],
                 cmr[0:1, :, tt * 512:(tt + 1) * 512],
                 waits=[td, rep_free[0]], chain="rr")
        rm1 = o("tensor", lambda e, tt=tt: e.matmul(
            pB[:, 0:512], t["onerf"][:], t["rows"][0:1, 0, :],
            start=True, stop=True), waits=[tl, rep_free[0], onerfm], inc="tC")
        rm2 = o("tensor", lambda e, tt=tt: e.matmul(
            pB[:, 512:1024], t["onerf"][:], t["rows"][0:1, 1, :],
            start=True, stop=True), waits=[tl], inc="tC")
        mre = o("vector", lambda e: e.tensor_copy(t["fo"][:, 0, :], pB[:, 512:1024]),
                waits=[rm2], inc="vC")
        for jc in range(4):
            q1 = o("vector", lambda e, jc=jc, tt=tt: e.tensor_scalar(
                t["P"][:, jc, tt * 512:(tt + 1) * 512], pB[:, 0:512],
                t["jio"][:, jc:jc + 1], None, ALU.is_equal, ALU.bypass),
                waits=[rm1, consts], inc="vC")
            q2 = o("vector", lambda e, jc=jc, tt=tt: e.tensor_tensor(
                t["P"][:, jc, tt * 512:(tt + 1) * 512],
                t["P"][:, jc, tt * 512:(tt + 1) * 512], t["fo"][:, 0, :], ALU.mult),
                waits=[q1, mre], inc="vC")
            p_done.append(q2)
        rep_free[0] = p_done[-1]

    sch.barrier()
    hg_done = []
    gp_free = [None, None]
    for jb in range(4):
        for ft in range(4):
            sel = ft % 2
            psd = (pA, pB)[sel]
            mm = None
            for tc in range(FC):
                mm = o("tensor", lambda e, jb=jb, ft=ft, tc=tc, psd=psd: e.matmul(
                    psd[:, 0:512], PT[:, tc, jb * 128:(jb + 1) * 128],
                    t["big8"][:, tc, ft * 512:(ft + 1) * 512],
                    start=(tc == 0), stop=(tc == FC - 1)),
                    waits=[pt_done[-1], lx3, gp_free[sel], tp], inc="tC")
            ev = o("vector", lambda e, jb=jb, ft=ft, psd=psd: e.tensor_copy(
                t["hg"][:, jb, ft * 512:(ft + 1) * 512], psd[:, 0:512]),
                waits=[mm], inc="vC")
            gp_free[sel] = ev
            hg_done.append(ev)
    wg = []
    for jb in range(4):
        for tc in range(FC):
            wg.append(o("tensor", lambda e, jb=jb, tc=tc: e.matmul(
                pC[:, 5 + jb:6 + jb], PT[:, tc, jb * 128:(jb + 1) * 128],
                t["exwb"][:, tc:tc + 1], start=(tc == 0), stop=(tc == FC - 1)),
                waits=[pt_done[-1], eb, rs1], inc="tC"))
    rsb = o("tensor", lambda e: e.matmul(pC[:, 9:10], t["onerf"][:], t["sc"][:, 6:7],
                                         start=True, stop=True),
            waits=[rs1, onerfm], inc="tC")
    rse = o("vector", lambda e: e.tensor_copy(t["thr"][:, 1:2], pC[:, 9:10]),
            waits=[rsb], inc="vC")
    wc1 = o("vector", lambda e: e.tensor_scalar(
        t["wcol"][:], pC[:, 5:9], t["thr"][:, 1:2], None, ALU.mult, ALU.bypass),
        waits=[wg[-1], rse], inc="vC")

    sch.barrier()
    ln_done = []
    for jb in range(4):
        s1 = o("vector", lambda e, jb=jb: e.tensor_reduce(
            t["lnc"][:, jb:jb + 1], t["hg"][:, jb, :], AX.X, ALU.add),
            waits=[hg_done[4 * jb + 3]], inc="vC")
        s2 = o("scalar", lambda e, jb=jb: e.activation(
            out=t["sq"], in_=t["hg"][:, jb, :], func=AF.Square),
            waits=[hg_done[4 * jb + 3], ln_done[-1] if ln_done else None], inc="sC")
        s3 = o("vector", lambda e, jb=jb: e.tensor_reduce(
            t["lnc2"][:, jb:jb + 1], t["sq"], AX.X, ALU.add), waits=[s2], inc="vC")
        s4 = o("vector", lambda e, jb=jb: e.tensor_scalar(
            t["lnc"][:, jb:jb + 1], t["lnc"][:, jb:jb + 1], 1.0 / D, None,
            ALU.mult, ALU.bypass), waits=[s1], inc="vC")
        s5 = o("vector", lambda e, jb=jb: e.scalar_tensor_tensor(
            t["red"][:], t["lnc"][:, jb:jb + 1], -1.0, t["lnc"][:, jb:jb + 1],
            ALU.mult, ALU.mult), waits=[s4], inc="vC")
        s6 = o("vector", lambda e, jb=jb: e.tensor_scalar(
            t["lnc2"][:, jb:jb + 1], t["lnc2"][:, jb:jb + 1], 1.0 / D, None,
            ALU.mult, ALU.bypass), waits=[s3], inc="vC")
        s7 = o("vector", lambda e, jb=jb: e.tensor_tensor(
            t["lnc2"][:, jb:jb + 1], t["lnc2"][:, jb:jb + 1], t["red"][:], ALU.add),
            waits=[s5, s6], inc="vC")
        s8 = o("scalar", lambda e, jb=jb: e.activation(
            out=t["lnc2"][:, jb:jb + 1], in_=t["lnc2"][:, jb:jb + 1], func=AF.Ln,
            bias=t["eps1"][:, :]), waits=[s7, epsm], inc="sC")
        s9 = o("scalar", lambda e, jb=jb: e.activation(
            out=t["lnc2"][:, jb:jb + 1], in_=t["lnc2"][:, jb:jb + 1], func=AF.Exp,
            scale=-0.5), waits=[s8], inc="sC")
        sa = o("vector", lambda e, jb=jb: e.tensor_tensor(
            t["red"][:], t["lnc"][:, jb:jb + 1], t["lnc2"][:, jb:jb + 1], ALU.mult),
            waits=[s9], inc="vC")
        sb = o("vector", lambda e, jb=jb: e.tensor_scalar(
            t["hg"][:, jb, :], t["hg"][:, jb, :], t["lnc2"][:, jb:jb + 1],
            t["red"][:, :], ALU.mult, ALU.subtract), waits=[sa], inc="vC")
        ln_done.append(sb)

    sch.barrier()
    tr_done = []
    pt_free = [None]
    for fc in range(FC):
        tms = []
        for jb in range(4):
            tms.append(o("tensor", lambda e, fc=fc, jb=jb: e.transpose(
                pT[:, jb * 128:(jb + 1) * 128],
                t["hg"][:, jb, fc * 128:(fc + 1) * 128], t["ident"][:]),
                waits=[ln_done[jb], pt_free[0], consts], inc="tC"))
        ev = o("scalar", lambda e, fc=fc: e.activation(
            out=t["hgF"][:, fc, :], in_=pT[:, 0:512], func=AF.Copy),
            waits=tms, inc="sC")
        pt_free[0] = ev
        tr_done.append(ev)

    sch.barrier()
    w1re = g["w1"].rearrange("(c p) ff -> p c ff", p=128)
    h1_done = []
    w1_free = [None, None]
    k_free = [None, None]
    for dffc in range(64):
        par = dffc % 2
        lw = dma("gpsimd", t["wbuf"][:, par], w1re[:, :, dffc * 128:(dffc + 1) * 128],
                 waits=[w1_free[par]], chain="wld")
        mm = None
        psd = (pA, pB)[par]
        for c in range(FC):
            mm = o("tensor", lambda e, c=c, par=par, psd=psd: e.matmul(
                psd[:, 0:512], t["wbuf"][:, par, c, :], t["hgF"][:, c, :],
                start=(c == 0), stop=(c == FC - 1)),
                waits=[lw, tr_done[-1], k_free[par], wc1, wg[-1]], inc="tC")
        ev = o("scalar", lambda e, dffc=dffc, psd=psd: e.activation(
            out=t["h1F"][:, dffc, :], in_=psd[:, 0:512], func=AF.Gelu,
            bias=t["b1t"][:, dffc:dffc + 1]), waits=[mm, pt_done[-1]], inc="sC")
        k_free[par] = ev
        w1_free[par] = mm
        h1_done.append(ev)
    sch.barrier()
    w2re = g["w2"].rearrange("(dc p) ee -> p dc ee", p=128)
    l_done = []
    w2_free = [None, None]
    for et in range(4):
        mm2 = {}
        for dffc in range(64):
            par = dffc % 2
            lw2 = dma("gpsimd", t["w2b"][:, par, :],
                      w2re[:, dffc, et * 512:(et + 1) * 512],
                      waits=[w2_free[par]], chain="w2ld")
            for jb in range(4):
                psd = (pA, pB)[jb // 2]
                off = (jb % 2) * 512
                mm2[jb] = o("tensor", lambda e, dffc=dffc, jb=jb, par=par, psd=psd, off=off: e.matmul(
                    psd[:, off:off + 512], t["h1F"][:, dffc, jb * 128:(jb + 1) * 128],
                    t["w2b"][:, par, :], start=(dffc == 0), stop=(dffc == 63)),
                    waits=[lw2, h1_done[-1]] + ([l_done[-1]] if dffc == 0 and l_done else []),
                    inc="tC")
            w2_free[par] = mm2[3]
        badd = {}
        for jb in range(4):
            psd = (pA, pB)[jb // 2]
            off = (jb % 2) * 512
            badd[jb] = o("tensor", lambda e, jb=jb, et=et, psd=psd, off=off: e.matmul(
                psd[:, off:off + 512], t["oner"][:],
                t["b2s"][0:1, et * 512:(et + 1) * 512],
                start=False, stop=True), waits=[mm2[jb], consts], inc="tC")
        for jb in range(4):
            psd = (pA, pB)[jb // 2]
            off = (jb % 2) * 512
            l_done.append(o("vector", lambda e, jb=jb, et=et, psd=psd, off=off: e.tensor_scalar(
                t["o2w"][:, jb, et * 512:(et + 1) * 512], psd[:, off:off + 512],
                t["wcol"][:, jb:jb + 1], None, ALU.mult, ALU.bypass),
                waits=[badd[jb], wc1], inc="vC"))

    sch.barrier()
    lxf = dma("gpsimd", t["big8"][:], g["x3T"].rearrange("(c p) s -> p c s", p=128),
              waits=[hg_done[-1]], chain="xld")
    out_done = []
    s_free = [None, None]
    fo_free = [None, None]
    for ec in range(FC):
        for tt in range(4):
            sel = tt % 2
            psd = (pA, pB)[sel]
            mm = None
            for jb in range(4):
                mm = o("tensor", lambda e, ec=ec, tt=tt, jb=jb, psd=psd: e.matmul(
                    psd[:, 0:512], t["o2w"][:, jb, ec * 128:(ec + 1) * 128],
                    t["P"][:, jb, tt * 512:(tt + 1) * 512],
                    start=(jb == 0), stop=(jb == 3)),
                    waits=[l_done[-1], p_done[-1], s_free[sel]], inc="tC")
            fv = o("vector", lambda e, ec=ec, tt=tt, psd=psd, sel=sel: e.tensor_tensor(
                t["fo"][:, sel, :], psd[:, 0:512],
                t["big8"].ap().rearrange("p c n -> p (c n)")[:, ec * 2048 + tt * 512:ec * 2048 + (tt + 1) * 512],
                ALU.add), waits=[mm, lxf, fo_free[sel]], inc="vC")
            s_free[sel] = fv
            dt_ = dma("sync", g["x4T"].rearrange("(c p) s -> p c s", p=128)[:, ec, tt * 512:(tt + 1) * 512],
                      t["fo"][:, sel, :], waits=[fv], chain="out")
            fo_free[sel] = dt_
            out_done.append(dt_)
    o("sync", lambda e: e.nop(), waits=[out_done[-1]])


_L1D_NC = None
_L1D_CONSTS = None


def l1d_consts():
    c = {}
    c["jiota"] = np.ascontiguousarray(
        (np.arange(4)[None, :] * 128 + np.arange(128)[:, None]).astype(np.float32))
    c["iota512"] = np.ascontiguousarray(
        np.tile(np.arange(512, dtype=np.float32)[None, :], (128, 1)))
    sue = np.zeros((128, 128), np.float32)
    for p2 in range(128):
        sue[0:p2, p2] = 1.0
    c["sue"] = sue
    c["identf"] = np.eye(128, dtype=np.float32)
    return c


def run_l1d(inputs, x3, lg):
    global _L1D_NC, _L1D_CONSTS
    hc = host_consts()
    if _L1D_CONSTS is None:
        _L1D_CONSTS = l1d_consts()
    dc = _L1D_CONSTS
    if _L1D_NC is None:
        _L1D_NC = build_l1d()
    in_maps = []
    for core in range(8):
        b, hh = core // 2, core % 2
        pm = perm_for(core)
        lgp = lg[b][pm]
        m = {
            "x3t": np.ascontiguousarray(x3[b, hh * TLOC:(hh + 1) * TLOC]).astype(BF),
            "x3T": np.ascontiguousarray(x3[b, hh * TLOC:(hh + 1) * TLOC].T).astype(BF),
            "lgb": np.ascontiguousarray(lgp.reshape(32, 128).T).astype(np.float32),
            "w1": np.asarray(inputs["L1_ffn_W1"], np.float32),
            "w2": np.asarray(inputs["L1_ffn_W2"], np.float32),
            "b1t": np.ascontiguousarray(
                np.asarray(inputs["L1_ffn_b1"], np.float32).reshape(64, 128).T),
            "b2r": np.asarray(inputs["L1_ffn_b2"], np.float32).reshape(1, D).astype(BF),
            "jiota": dc["jiota"], "iota512": dc["iota512"], "sue": dc["sue"],
            "ident": hc["ident"], "identf": dc["identf"],
            "ones_row": hc["ones_row"],
        }
        in_maps.append(m)
    res = run_bass_kernel_spmd(_L1D_NC, in_maps, core_ids=list(range(8)))
    x4 = np.zeros((B, S, D), np.float32)
    for core in range(8):
        b, hh = core // 2, core % 2
        x4[b, hh * TLOC:(hh + 1) * TLOC] = np.asarray(res.results[core]["x4T"]).astype(np.float32).T
    return x4


def kernel(**inputs):
    x2 = run_l0(inputs)
    l1a = run_l1a(inputs, x2)
    o = run_l1b(inputs, l1a)
    x3, lg = run_l1c(inputs, x2, o, l1a)
    x4 = run_l1d(inputs, x3, lg)
    return x4


# version 05
# speedup vs baseline: 1.2281x; 1.2281x over previous
import sys
sys.path.insert(0, '/opt/trn_rl_repo')
import math
from contextlib import ExitStack

import numpy as np
import ml_dtypes

import concourse.bass as bass
import concourse.mybir as mybir
from concourse.bass_utils import run_bass_kernel_spmd
from concourse.alu_op_type import AluOpType as ALU

F32 = mybir.dt.float32
BF16 = mybir.dt.bfloat16
AF = mybir.ActivationFunctionType
AX = mybir.AxisListType
BF = ml_dtypes.bfloat16

B, S, D = 4, 4096, 2048
H, KVH, DH = 16, 8, 128
TLOC = S // 2
FC = D // 128
TOPK0 = 1638
K1 = 512
DFF = 4 * D
ROPE_THETA = 1e6
ISQ = 1.0 / math.sqrt(DH)
NEG = -1.0e9
MSHIFT = -8.0

ENGINES = ("sync", "scalar", "vector", "tensor", "gpsimd")


class Sch:

    def __init__(self, nc, ctx):
        self.nc = nc
        self.ctx = ctx
        self.ops = {e: [] for e in ENGINES}
        self.cnt = {}
        self.sems = {}
        self.chain_last = {}
        self.bar = None
        for name in ("tC", "vC", "sC", "gC"):
            self.sems[name] = ctx.enter_context(nc.semaphore(name))
            self.cnt[name] = 0

    PARANOID = False

    def barrier(self):
        self.bar = [v for v in self.chain_last.values()]

    def op(self, eng, fn, waits=(), inc=None, amt=1):
        waits = list(waits)
        if eng in ("vector", "scalar", "gpsimd") and inc is not None:
            waits.append(self.chain_last.get("eng_" + eng))
        if self.bar:
            waits += self.bar
        if getattr(self, "serialize", False):
            waits += [v for k, v in self.chain_last.items()]
        if Sch.PARANOID is True or (isinstance(Sch.PARANOID, str) and
                                    ((Sch.PARANOID == "tensor" and eng == "tensor") or
                                     (Sch.PARANOID == "nontensor" and eng != "tensor"))):
            waits += [v for k, v in self.chain_last.items()]
        sem = self.sems[inc] if inc else None
        tok = None
        if sem is not None:
            self.cnt[inc] += amt
            tok = (sem, self.cnt[inc])
        self.ops[eng].append((fn, tuple(w for w in waits if w is not None),
                              sem, amt if sem else 0))
        if tok is not None and inc in ("vC", "sC", "gC", "tC"):
            self.chain_last["eng_" + eng] = tok
        return tok

    def dma(self, eng, out, in_, waits=(), chain=None, **kw):
        if chain is None:
            chain = "q_" + eng
        if chain not in self.sems:
            self.sems[chain] = self.ctx.enter_context(self.nc.semaphore(chain))
            self.cnt[chain] = 0
        prev = self.chain_last.get(chain)
        tok = self.op(eng, lambda e: e.dma_start(out=out, in_=in_, **kw),
                      waits=list(waits) + [prev], inc=chain, amt=16)
        self.chain_last[chain] = tok
        return tok

    def emit(self, ctx):
        blk = ctx.enter_context(self.nc.Block())
        for eng in ENGINES:
            ops = self.ops[eng]
            if not ops:
                continue

            def body(e, ops=ops):
                hw = {}
                for fn, waits, sem, amt in ops:
                    for (wsem, val) in waits:
                        if hw.get(id(wsem), -1) < val:
                            e.wait_ge(wsem, val)
                            hw[id(wsem)] = val
                    inst = fn(e)
                    if sem is not None:
                        inst.then_inc(sem, amt)

            getattr(blk, eng)(body)


def ceil_div(a, b):
    return -(-a // b)



_HC = None


def host_consts():
    global _HC
    if _HC is not None:
        return _HC
    c = {}
    pos = np.arange(S, dtype=np.float64)
    inv_freq = 1.0 / (ROPE_THETA ** (np.arange(0, DH, 2, dtype=np.float64) / DH))
    freqs = pos[None, :] * inv_freq[:, None]
    emb = np.concatenate([freqs, freqs], axis=0)
    c["cos_t"] = np.cos(emb).astype(np.float32)
    c["sin_t"] = np.sin(emb).astype(np.float32)
    M = np.zeros((DH, DH), dtype=np.float32)
    for j in range(64):
        M[j + 64, j] = -1.0
    for j in range(64, 128):
        M[j - 64, j] = 1.0
    c["prot"] = M.astype(BF)
    c["ident"] = np.eye(128, dtype=np.float32).astype(BF)
    c["ones_col"] = np.ones((128, 1), dtype=np.float32).astype(BF)
    c["ones_row"] = np.ones((1, 128), dtype=np.float32).astype(BF)
    bd = np.zeros((128, 16), dtype=np.float32)
    for h in range(16):
        bd[h * 8:(h + 1) * 8, h] = 1.0
    c["bdiag16"] = bd
    c["bcast16"] = np.ascontiguousarray(bd.T)
    _HC = c
    return c


def tile_vec(v):
    v = np.asarray(v, np.float32).reshape(-1)
    n = ceil_div(v.shape[0], 128)
    pad = np.zeros(128 * n, np.float32)
    pad[:v.shape[0]] = v
    return np.ascontiguousarray(pad.reshape(n, 128).T)


def perm_for(core):
    hh = core % 2
    loc = np.arange(hh * TLOC, (hh + 1) * TLOC)
    oth = np.arange((1 - hh) * TLOC, (2 - hh) * TLOC)
    return np.concatenate([loc, oth])



def emit_ln(sch, t, pA, pB, xf, hf, deps, st=None):
    o = sch.op
    if st is None:
        st = {"pa": None, "pb": None, "sq": [None, None]}
    hdone = []
    for half in range(2):
        sl = slice(half * 512, (half + 1) * 512)
        m2 = None
        for c in range(FC):
            sq = o("scalar", lambda e, c=c, sl=sl: e.activation(
                out=t["sq"][:, c % 2, :], in_=xf(c, sl), func=AF.Square),
                waits=list(deps) + [st["sq"][c % 2]], inc="sC")
            o("tensor", lambda e, c=c, sl=sl: e.matmul(
                pA[0:1, 0:512], t["onec"][:, :], xf(c, sl),
                start=(c == 0), stop=(c == FC - 1)),
                waits=list(deps) + ([st["pa"]] if c == 0 else []), inc="tC")
            m2 = o("tensor", lambda e, c=c: e.matmul(
                pA[0:1, 512:1024], t["onec"][:, :], t["sq"][:, c % 2, :],
                start=(c == 0), stop=(c == FC - 1)),
                waits=[sq], inc="tC")
            st["sq"][c % 2] = m2
        s0 = o("vector", lambda e: e.tensor_scalar(
            t["stat"][:, 0, :], pA[0:1, 0:512], 1.0 / D, None, ALU.mult, ALU.bypass),
            waits=[m2], inc="vC")
        s1 = o("vector", lambda e: e.tensor_scalar(
            t["stat"][:, 1, :], pA[0:1, 512:1024], 1.0 / D, None, ALU.mult, ALU.bypass),
            waits=[m2], inc="vC")
        s2 = o("vector", lambda e: e.scalar_tensor_tensor(
            t["stat"][:, 2, :], t["stat"][:, 0, :], -1.0, t["stat"][:, 0, :],
            ALU.mult, ALU.mult), waits=[s0], inc="vC")
        s3 = o("vector", lambda e: e.tensor_tensor(
            t["stat"][:, 1, :], t["stat"][:, 1, :], t["stat"][:, 2, :], ALU.add),
            waits=[s1, s2], inc="vC")
        s4 = o("scalar", lambda e: e.activation(
            out=t["stat"][:, 2, :], in_=t["stat"][:, 1, :], func=AF.Ln,
            bias=t["eps1"][:, :]), waits=[s3], inc="sC")
        s5 = o("scalar", lambda e: e.activation(
            out=t["stat"][:, 2, :], in_=t["stat"][:, 2, :], func=AF.Exp,
            scale=-0.5), waits=[s4], inc="sC")
        s6 = o("vector", lambda e: e.scalar_tensor_tensor(
            t["stat"][:, 3, :], t["stat"][:, 2, :], -1.0, t["stat"][:, 0, :],
            ALU.mult, ALU.mult), waits=[s5], inc="vC")
        s7 = o("vector", lambda e: e.tensor_copy(
            t["statb"][:, 0, :], t["stat"][:, 2, :]), waits=[s5], inc="vC")
        s8 = o("vector", lambda e: e.tensor_copy(
            t["statb"][:, 1, :], t["stat"][:, 3, :]), waits=[s6], inc="vC")
        b0 = o("tensor", lambda e: e.matmul(
            pB[:, 0:512], t["oner"][:, :], t["statb"][:, 0, :],
            start=True, stop=True), waits=[s7, st["pb"]], inc="tC")
        b1 = o("tensor", lambda e: e.matmul(
            pB[:, 512:1024], t["oner"][:, :], t["statb"][:, 1, :],
            start=True, stop=True), waits=[s8], inc="tC")
        st["pa"] = s1
        for c in range(FC):
            a0 = o("vector", lambda e, c=c, sl=sl: e.tensor_tensor(
                t["tmp"][:], xf(c, sl), pB[:, 0:512], ALU.mult),
                waits=[b0, hdone[-1] if hdone else None], inc="vC")
            a1 = o("vector", lambda e, c=c, sl=sl: e.tensor_tensor(
                hf(c, sl), t["tmp"][:], pB[:, 512:1024], ALU.add),
                waits=[b1, a0], inc="vC")
            hdone.append(a1)
        st["pb"] = hdone[-1]
    return hdone, st



def build_l0():
    nc = bass.Bass("TRN2", target_bir_lowering=False)
    ap = lambda *a, **k: nc.dram_tensor(*a, **k).ap()
    g = {}
    g["xT"] = ap("xT", [D, S], F32, kind="ExternalInput")
    g["wq"] = ap("wq", [D, D], F32, kind="ExternalInput")
    g["wk"] = ap("wk", [D, KVH * DH], F32, kind="ExternalInput")
    g["wv"] = ap("wv", [D, KVH * DH], F32, kind="ExternalInput")
    g["wo"] = ap("wo", [D, D], F32, kind="ExternalInput")
    g["wr"] = ap("wr", [D, H], F32, kind="ExternalInput")
    g["fwT"] = ap("fwT", [D, D], F32, kind="ExternalInput")
    g["bq"] = ap("bq", [128, FC], F32, kind="ExternalInput")
    g["bk"] = ap("bk", [128, KVH], F32, kind="ExternalInput")
    g["bv"] = ap("bv", [128, KVH], F32, kind="ExternalInput")
    g["bo"] = ap("bo", [128, FC], F32, kind="ExternalInput")
    g["br"] = ap("br", [128, 1], F32, kind="ExternalInput")
    g["fb"] = ap("fb", [128, FC], F32, kind="ExternalInput")
    g["cos_t"] = ap("cos_t", [DH, S], BF16, kind="ExternalInput")
    g["sin_t"] = ap("sin_t", [DH, S], BF16, kind="ExternalInput")
    g["prot"] = ap("prot", [DH, DH], BF16, kind="ExternalInput")
    g["ident"] = ap("ident", [128, 128], BF16, kind="ExternalInput")
    g["ones_col"] = ap("ones_col", [128, 1], BF16, kind="ExternalInput")
    g["ones_row"] = ap("ones_row", [1, 128], BF16, kind="ExternalInput")
    g["bdiag16"] = ap("bdiag16", [128, 16], F32, kind="ExternalInput")
    g["bcast16"] = ap("bcast16", [16, 128], F32, kind="ExternalInput")
    g["x2T"] = ap("x2T", [D, TLOC], BF16, kind="ExternalOutput")
    g["kdr"] = nc.dram_tensor("kdr", [KVH, DH, S], BF16).ap()
    g["vdr"] = nc.dram_tensor("vdr", [KVH, DH, S], BF16).ap()
    g["qdr"] = nc.dram_tensor("qdr", [H, DH, TLOC], BF16).ap()
    g["wqdr"] = nc.dram_tensor("wqdr", [D, D], BF16).ap()
    g["rdr"] = nc.dram_tensor("rdr", [16, S], F32).ap()
    g["mdr"] = nc.dram_tensor("mdr", [512, 128], BF16).ap()

    ctx = ExitStack()
    with ctx:
        E = ctx.enter_context
        t = {}
        t["xres"] = E(nc.sbuf_tensor("xres", [128, FC, TLOC], BF16))
        t["hbuf"] = E(nc.sbuf_tensor("hbuf", [128, 2, FC, 1024], BF16))
        t["u4"] = E(nc.sbuf_tensor("u4", [128, 16384], BF16))
        t["wbuf"] = E(nc.sbuf_tensor("wbuf", [128, 2, FC, 128], BF16))
        t["qb"] = E(nc.sbuf_tensor("qb", [128, TLOC], BF16))
        t["expb"] = E(nc.sbuf_tensor("expb", [128, 2, 1024], BF16))
        t["prot"] = E(nc.sbuf_tensor("protb", [128, 128], BF16))
        t["onec"] = E(nc.sbuf_tensor("onec", [128, 1], BF16))
        t["oner"] = E(nc.sbuf_tensor("oner", [1, 128], BF16))
        t["bd16"] = E(nc.sbuf_tensor("bd16", [128, 16], F32))
        t["bc16"] = E(nc.sbuf_tensor("bc16", [16, 128], F32))
        t["bias"] = E(nc.sbuf_tensor("biast", [128, 64], F32))
        t["br"] = E(nc.sbuf_tensor("brt", [128, 1], F32))
        t["eps1"] = E(nc.sbuf_tensor("eps1", [1, 1], F32))
        t["sq"] = E(nc.sbuf_tensor("sq", [128, 2, 512], BF16))
        t["stat"] = E(nc.sbuf_tensor("stat", [1, 4, 512], F32))
        t["statb"] = E(nc.sbuf_tensor("statb", [1, 2, 512], BF16))
        t["tmp"] = E(nc.sbuf_tensor("tmp", [128, 512], BF16))
        t["rr"] = E(nc.sbuf_tensor("rr", [16, 512], F32))
        t["rb"] = E(nc.sbuf_tensor("rb", [128, 512], F32))
        t["lo"] = E(nc.sbuf_tensor("lo", [16, 1], F32))
        t["hi"] = E(nc.sbuf_tensor("hi", [16, 1], F32))
        t["tt16"] = E(nc.sbuf_tensor("tt16", [16, 1], F32))
        t["ge"] = E(nc.sbuf_tensor("ge", [16, 1], F32))
        t["gi"] = E(nc.sbuf_tensor("gi", [16, 1], F32))
        t["d16"] = E(nc.sbuf_tensor("d16", [16, 1], F32))
        t["cmp"] = E(nc.sbuf_tensor("cmp", [128, 512], F32))
        t["cnt1"] = E(nc.sbuf_tensor("cnt1", [128, 1], F32))
        t["thr128"] = E(nc.sbuf_tensor("thr128", [128, 1], F32))
        t["maskf"] = E(nc.sbuf_tensor("maskf", [128, 512], F32))
        t["rzrep"] = E(nc.sbuf_tensor("rzrep", [128, 1024], F32))
        t["wqc"] = E(nc.sbuf_tensor("wqc", [128, 512], F32))
        t["redq"] = E(nc.sbuf_tensor("redq", [128, 64], F32))
        t["identb"] = E(nc.sbuf_tensor("identb", [128, 128], BF16))
        t["wqo"] = E(nc.sbuf_tensor("wqo", [128, 2, 512], BF16))
        t["acc"] = E(nc.sbuf_tensor("acc", [128, 1], F32))
        t["red"] = E(nc.sbuf_tensor("red", [128, 1], F32))
        t["onef"] = E(nc.sbuf_tensor("onef", [128, 1], F32))
        t["onerf"] = E(nc.sbuf_tensor("onerf", [1, 128], F32))
        t["srow"] = E(nc.sbuf_tensor("srow", [1, 2], F32))
        t["sinv"] = E(nc.sbuf_tensor("sinv", [128, 2], F32))

        t["mtmp"] = t["sq"][:, 0, :]
        t["qb2"] = t["wbuf"].ap().rearrange("p a c n -> p a (c n)")[:, 1]
        t["rz"] = t["stat"].ap().rearrange("a b c -> a (b c)")[:, 0:1024]
        t["rzb"] = t["statb"].ap().rearrange("a b c -> a (b c)")[:, 0:1024]
        t["wqa"] = t["rb"][:, :]
        t["wqx"] = t["cmp"][:, :]
        t["qa"] = t["expb"][:, 0, 0:512]
        t["qc"] = t["expb"][:, 1, 0:512]
        t["qs"] = t["expb"][:, 1, 512:1024]
        pA = E(nc.psum_tensor("pA", [128, 1024], F32))
        pB = E(nc.psum_tensor("pB", [128, 1024], F32))
        pC = E(nc.psum_tensor("pC", [128, 1024], F32))
        pD = E(nc.psum_tensor("pD", [128, 1024], F32))

        sch = Sch(nc, ctx)
        emit_l0(sch, g, t, pA, pB, pC, pD)
        sch.emit(ctx)
    return nc


def emit_l0(sch, g, t, pA, pB, pC, pD):
    o = sch.op
    dma = sch.dma
    cosv = t["u4"][:, 0:S]
    sinv = t["u4"][:, S:2 * S]

    consts = None
    for dst, src in [(cosv, g["cos_t"][:, :]), (sinv, g["sin_t"][:, :]),
                     (t["prot"][:], g["prot"][:, :]), (t["identb"][:], g["ident"][:, :]),
                     (t["onec"][:], g["ones_col"][:, :]),
                     (t["oner"][:], g["ones_row"][:, :]), (t["bd16"][:], g["bdiag16"][:, :]),
                     (t["bc16"][:], g["bcast16"][:, :]),
                     (t["bias"][:, 0:16], g["bq"][:, :]), (t["bias"][:, 16:32], g["bo"][:, :]),
                     (t["bias"][:, 32:40], g["bk"][:, :]), (t["bias"][:, 40:48], g["bv"][:, :]),
                     (t["bias"][:, 48:64], g["fb"][:, :]), (t["br"][:], g["br"][:, :])]:
        consts = dma("sync", dst, src, chain="const")
    epsr = o("vector", lambda e: e.memset(t["eps1"][:], 1e-5), inc="vC")
    onefm = o("vector", lambda e: e.memset(t["onef"][:], 1.0), inc="vC")
    onerfm = o("vector", lambda e: e.memset(t["onerf"][:], 1.0), inc="vC")

    xTr = g["xT"].rearrange("(c p) s -> p c s", p=128)

    rr_done = []
    kv_done = []
    q_done = []
    hbuf_free = [None, None]
    xstage_free = [None, None]
    ropeS = {"buf": None, "pd": None}
    lnst = [None]

    def rope_evac(psh, bcol, base, out_ap, mmtok):
        e1 = o("vector", lambda e: e.tensor_scalar(
            t["qa"], psh, bcol, None, ALU.add, ALU.bypass),
            waits=[mmtok, ropeS["buf"]], inc="vC")
        rmm = o("tensor", lambda e: e.matmul(
            pD[:, 0:512], t["prot"][:, :], t["qa"], start=True, stop=True),
            waits=[e1, ropeS["pd"]], inc="tC")
        e2 = o("vector", lambda e: e.tensor_tensor(
            t["qc"], t["qa"], cosv[:, base:base + 512], ALU.mult),
            waits=[e1], inc="vC")
        e3 = o("vector", lambda e: e.tensor_tensor(
            t["qs"], pD[:, 0:512], sinv[:, base:base + 512], ALU.mult),
            waits=[rmm], inc="vC")
        ropeS["pd"] = e3
        e4 = o("vector", lambda e: e.tensor_tensor(
            t["qa"], t["qc"], t["qs"], ALU.add),
            waits=[e2, e3, rmm], inc="vC")
        dt_ = dma("sync", out_ap, t["qa"], waits=[e4], chain="rope")
        ropeS["buf"] = dt_
        return dt_

    proj_ps = {0: None, 1: None}
    wbuf_free = [None, None]

    def proj(wdram, n_ec, hf, hdeps, evac_fn, wstart, ecw=128):
        wre = wdram.rearrange("(c p) eo -> p c eo", p=128)
        for ec in range(n_ec):
            par = ec % 2
            lw = dma("gpsimd", t["wbuf"][:, par, :, 0:ecw],
                     wre[:, :, ec * ecw:(ec + 1) * ecw],
                     waits=[wbuf_free[par], wstart], chain="wld")
            for half in range(2):
                sl = slice(half * 512, (half + 1) * 512)
                mm = None
                for c in range(FC):
                    mm = o("tensor", lambda e, c=c, sl=sl, half=half, par=par: e.matmul(
                        pC[0:ecw, half * 512:half * 512 + 512],
                        t["wbuf"][:, par, c, 0:ecw], hf(c, sl),
                        start=(c == 0), stop=(c == FC - 1)),
                        waits=list(hdeps) + [lw, proj_ps[half]], inc="tC")
                ft = evac_fn(ec, half, pC[0:ecw, half * 512:half * 512 + 512], mm)
                proj_ps[half] = ft
            wbuf_free[par] = mm

    for tt4 in range(4):
        cols = slice(tt4 * 1024, (tt4 + 1) * 1024)
        hhalf = tt4 % 2
        if tt4 < 2:
            xf = lambda c, sl, tt4=tt4: t["xres"][:, c, tt4 * 1024 + sl.start:tt4 * 1024 + sl.stop]
            lx = dma("gpsimd", t["xres"][:, :, cols], xTr[:, :, cols],
                     waits=[xstage_free[hhalf]], chain="xld")
        else:
            xf = lambda c, sl, hhalf=hhalf: t["hbuf"][:, 1 - hhalf, c, sl]
            lx = dma("gpsimd", t["hbuf"][:, 1 - hhalf], xTr[:, :, cols],
                     waits=[xstage_free[1 - hhalf], hbuf_free[1 - hhalf]],
                     chain="xld")
        hf = lambda c, sl, hhalf=hhalf: t["hbuf"][:, hhalf, c, sl]
        hdone, lnst[0] = emit_ln(
            sch, t, pA, pB, xf, hf,
            [lx, epsr, consts, hbuf_free[hhalf]], lnst[0])
        hdeps = [hdone[-1]]
        if tt4 >= 2:
            xstage_free[1 - hhalf] = hdone[-1]

        if tt4 < 2:
            def evq(ec, half, psh, mm, tt4=tt4):
                base = tt4 * 1024 + half * 512
                dt_ = rope_evac(psh, t["bias"][:, ec:ec + 1], base,
                                g["qdr"][ec, :, base:base + 512], mm)
                q_done.append(dt_)
                return dt_
            proj(g["wq"], FC, hf, hdeps, evq, hbuf_free[hhalf])

        def evk(ec, half, psh, mm, tt4=tt4):
            base = tt4 * 1024 + half * 512
            dt_ = rope_evac(psh, t["bias"][:, 32 + ec:33 + ec], base,
                            g["kdr"][ec, :, base:base + 512], mm)
            kv_done.append(dt_)
            return dt_
        proj(g["wk"], KVH, hf, hdeps, evk, hbuf_free[hhalf])

        def evv(ec, half, psh, mm, tt4=tt4):
            base = tt4 * 1024 + half * 512
            e1 = o("vector", lambda e: e.tensor_scalar(
                t["qa"], psh, t["bias"][:, 40 + ec:41 + ec], None,
                ALU.add, ALU.bypass), waits=[mm, ropeS["buf"]], inc="vC")
            dt_ = dma("sync", g["vdr"][ec, :, base:base + 512], t["qa"], waits=[e1], chain="rope")
            ropeS["buf"] = dt_
            kv_done.append(dt_)
            return e1
        proj(g["wv"], KVH, hf, hdeps, evv, hbuf_free[hhalf])

        def evr(ec, half, psh, mm, tt4=tt4):
            base = tt4 * 1024 + half * 512
            e1 = o("scalar", lambda e: e.activation(
                out=t["rr"][:, :], in_=psh,
                func=AF.Sigmoid, bias=t["br"][0:16, :]),
                waits=[mm, rr_done[-1] if rr_done else None], inc="sC")
            dr = dma("sync", g["rdr"][:, base:base + 512], t["rr"][:, :], waits=[e1], chain="rr")
            rr_done.append(dr)
            return e1
        proj(g["wr"], 1, hf, hdeps, evr, hbuf_free[hhalf], ecw=16)
        hbuf_free[hhalf] = rr_done[-1]

    d2 = dma("sync", t["rb"][:], g["rdr"].rearrange("h (bk f) -> (h bk) f", bk=8),
             waits=[rr_done[-1]], chain="rr")
    mzl = o("vector", lambda e: e.memset(t["lo"][:], 0.0), inc="vC")
    mzh = o("vector", lambda e: e.memset(t["hi"][:], 1.0), inc="vC")
    lastsel = [mzl, mzh, d2]
    pd_b_free = [ropeS["pd"]]
    for it in range(30):
        a1 = o("vector", lambda e: e.tensor_tensor(
            t["tt16"][:], t["lo"][:], t["hi"][:], ALU.add), waits=lastsel, inc="vC")
        a2 = o("vector", lambda e: e.tensor_scalar(
            t["tt16"][:], t["tt16"][:], 0.5, None, ALU.mult, ALU.bypass),
            waits=[a1], inc="vC")
        bmm = o("tensor", lambda e: e.matmul(
            pD[:, 512:513], t["bc16"][:, :], t["tt16"][:], start=True, stop=True),
            waits=[a2, pd_b_free[0], consts], inc="tC")
        cp1 = o("vector", lambda e: e.tensor_copy(t["thr128"][:], pD[:, 512:513]),
                waits=[bmm], inc="vC")
        c1 = o("vector", lambda e: e.tensor_scalar(
            t["cmp"][:], t["rb"][:], t["thr128"][:, :], None, ALU.is_ge, ALU.bypass),
            waits=[cp1, d2], inc="vC")
        c2 = o("vector", lambda e: e.tensor_reduce(
            t["cnt1"][:], t["cmp"][:], AX.X, ALU.add), waits=[c1], inc="vC")
        cmm = o("tensor", lambda e: e.matmul(
            pD[0:16, 640:641], t["bd16"][:, :], t["cnt1"][:], start=True, stop=True),
            waits=[c2], inc="tC")
        g1 = o("vector", lambda e: e.tensor_scalar(
            t["ge"][:], pD[0:16, 640:641], float(TOPK0), None, ALU.is_ge, ALU.bypass),
            waits=[cmm], inc="vC")
        g2 = o("vector", lambda e: e.tensor_scalar(
            t["gi"][:], t["ge"][:], -1.0, 1.0, ALU.mult, ALU.add),
            waits=[g1], inc="vC")
        dl = o("vector", lambda e: e.tensor_tensor(
            t["d16"][:], t["tt16"][:], t["lo"][:], ALU.subtract), waits=[a2], inc="vC")
        l1 = o("vector", lambda e: e.scalar_tensor_tensor(
            t["lo"][:], t["d16"][:], t["ge"][:, :], t["lo"][:], ALU.mult, ALU.add),
            waits=[g1, dl], inc="vC")
        dh = o("vector", lambda e: e.tensor_tensor(
            t["d16"][:], t["tt16"][:], t["hi"][:], ALU.subtract),
            waits=[l1], inc="vC")
        h1 = o("vector", lambda e: e.scalar_tensor_tensor(
            t["hi"][:], t["d16"][:], t["gi"][:, :], t["hi"][:], ALU.mult, ALU.add),
            waits=[g2, dh], inc="vC")
        lastsel = [l1, h1]
        pd_b_free[0] = g1
    bmmf = o("tensor", lambda e: e.matmul(
        pD[:, 512:513], t["bc16"][:, :], t["lo"][:], start=True, stop=True),
        waits=lastsel + [pd_b_free[0]], inc="tC")
    cpf = o("vector", lambda e: e.tensor_copy(t["thr128"][:], pD[:, 512:513]),
            waits=[bmmf], inc="vC")
    mb1 = o("vector", lambda e: e.tensor_scalar(
        t["cmp"][:], t["rb"][:], t["thr128"][:, :], None, ALU.is_ge, ALU.bypass),
        waits=[cpf], inc="vC")
    mb2 = o("vector", lambda e: e.tensor_scalar(
        t["cmp"][:], t["cmp"][:], -NEG + MSHIFT, NEG, ALU.mult, ALU.add),
        waits=[mb1], inc="vC")
    mb4 = o("vector", lambda e: e.tensor_copy(t["mtmp"], t["cmp"][:]),
            waits=[mb2], inc="vC")
    d3 = dma("sync", g["mdr"].rearrange("(p ff) c -> p (ff c)", p=128),
             t["mtmp"], waits=[mb4], chain="rr")
    fwr = g["fwT"].rearrange("(c p) o -> p c o", p=128)
    mz = o("vector", lambda e: e.memset(t["acc"][:], 0.0), inc="vC")
    acc_tok = mz
    lq_free = None
    for cc in range(FC * 4):
        c, q4 = cc // 4, cc % 4
        lq = dma("gpsimd", t["wqc"][:], fwr[:, c, q4 * 512:(q4 + 1) * 512],
                 waits=[lq_free], chain="wqld")
        r1 = o("vector", lambda e: e.tensor_reduce(
            t["red"][:], t["wqc"][:], AX.X, ALU.add, apply_absolute_value=True),
            waits=[lq, acc_tok], inc="vC")
        lq_free = r1
        acc_tok = o("vector", lambda e: e.tensor_tensor(
            t["acc"][:], t["acc"][:], t["red"][:], ALU.add), waits=[r1, acc_tok],
            inc="vC")
    tmm = o("tensor", lambda e: e.matmul(
        pD[0:1, 768:769], t["onef"][:, :], t["acc"][:], start=True, stop=True),
        waits=[acc_tok, onefm, pd_b_free[0], cpf], inc="tC")
    sv = o("vector", lambda e: e.tensor_scalar(
        t["srow"][:, 0:1], pD[0:1, 768:769], 1.0 / (D * D), None, ALU.mult, ALU.bypass),
        waits=[tmm], inc="vC")
    iv = o("vector", lambda e: e.reciprocal(t["srow"][:, 1:2], t["srow"][:, 0:1]),
           waits=[sv], inc="vC")
    bmm2 = o("tensor", lambda e: e.matmul(
        pD[:, 768:770], t["onerf"][:, :], t["srow"][:, :], start=True, stop=True),
        waits=[iv, onerfm, sv], inc="tC")
    cps = o("vector", lambda e: e.tensor_copy(t["sinv"][:], pD[:, 768:770]),
            waits=[bmm2], inc="vC")
    qdone = []
    wqo_free = [cps, cps]
    pass
    for cc in range(FC * 4):
        c, q4 = cc // 4, cc % 4
        par = cc % 2
        lq = dma("gpsimd", t["wqc"][:], fwr[:, c, q4 * 512:(q4 + 1) * 512],
                 waits=[lq_free], chain="wqld")
        q1 = o("vector", lambda e: e.tensor_scalar(
            t["wqa"], t["wqc"][:], t["sinv"][:, 1:2], 0.5, ALU.mult, ALU.is_ge),
            waits=[lq, cps], inc="vC")
        q2 = o("vector", lambda e: e.tensor_scalar(
            t["wqx"], t["wqc"][:], t["sinv"][:, 1:2], -0.5, ALU.mult, ALU.is_le),
            waits=[lq], inc="vC")
        lq_free = q2
        q3 = o("vector", lambda e: e.tensor_tensor(
            t["wqa"], t["wqa"], t["wqx"], ALU.subtract), waits=[q1, q2],
            inc="vC")
        q4t = o("vector", lambda e, par=par: e.tensor_scalar(
            t["wqo"][:, par, :], t["wqa"], t["sinv"][:, 0:1], None,
            ALU.mult, ALU.bypass), waits=[q3, wqo_free[par]], inc="vC")
        dq = dma("sync", g["wqdr"].rearrange("(c p) o -> p c o", p=128)[:, c, q4 * 512:(q4 + 1) * 512],
                 t["wqo"][:, par, :], waits=[q4t], chain="wqout")
        wqo_free[par] = dq
        qdone.append(dq)

    tde = []
    for sl4 in range(4):
        td = dma("scalar", t["expb"][:, 0, sl4 * 128:(sl4 + 1) * 128],
                 g["mdr"][sl4 * 128:(sl4 + 1) * 128, :], transpose=True,
                 waits=[d3, ropeS["buf"]], chain="vb")
        tde.append(td)
    mkf = o("vector", lambda e: e.tensor_copy(t["maskf"][:], t["expb"][:, 0, 0:512]),
            waits=tde, inc="vC")

    hbflat = t["hbuf"].ap().rearrange("p a c n -> p (a c n)")
    attn_done = []
    expb_free = [ropeS["buf"], ropeS["buf"]]
    sc_free = [lnst[0]["pa"], lnst[0]["pb"]]
    o_free = [proj_ps[0], proj_ps[1]]
    z_free = [cps]
    kb_free = [None, None]
    vb_free = [None, None]
    qb_free = [None, wbuf_free[1]]
    Kb = t["u4"]
    Vb = t["u4"]
    h_last = hbuf_free
    for kvh in range(KVH):
        par = kvh % 2
        kl = dma("sync", Kb[:, par * S:(par + 1) * S], g["kdr"][kvh, :, :],
                 waits=[kv_done[-1], kb_free[par]], chain="kb")
        vbase = 2 * S + par * S
        lvf = dma("gpsimd", Vb[:, vbase:vbase + S], g["vdr"][kvh, :, :],
                  waits=[kv_done[-1], vb_free[par]], chain="vb")
        vb_evs = []
        for bt in range(4):
            first = list(o_free) if bt == 0 else [vb_evs[-1]]
            tmms = []
            for jj in range(8):
                kc = bt * 8 + jj
                tmms.append(o("tensor", lambda e, kc=kc, jj=jj, vbase=vbase: e.matmul(
                    pC[:, jj * 128:(jj + 1) * 128],
                    Vb[:, vbase + kc * 128:vbase + (kc + 1) * 128],
                    t["identb"][:], start=True, stop=True),
                    waits=[lvf, consts] + (first if jj == 0 else []), inc="tC"))
            ve = o("vector", lambda e, bt=bt, vbase=vbase: e.tensor_copy(
                Vb[:, vbase + bt * 1024:vbase + (bt + 1) * 1024],
                pC[:, 0:1024]), waits=[tmms[-1]], inc="vC")
            vb_evs.append(ve)
        vb_data = vb_evs[-1]
        for hh in (2 * kvh, 2 * kvh + 1):
            hp = hh % 2
            qbb = (t["qb"], t["qb2"])[hp]
            lqd = dma("gpsimd", qbb[:], g["qdr"][hh, :, :],
                      waits=[q_done[-1], qb_free[hp]], chain="qbld")
            for qt in range(2):
                zs = []
                pvs = []
                mtoks = {}

                def scores(kc, qt=qt, lqd=lqd, kl=kl, par=par, qbb=qbb):
                    sel = kc % 2
                    psc = (pA, pB)[sel]
                    m1 = o("tensor", lambda e, kc=kc, psc=psc, qt=qt, par=par: e.matmul(
                        psc[:, 0:512], Kb[:, par * S + kc * 128:par * S + (kc + 1) * 128],
                        qbb[:, qt * 1024:qt * 1024 + 512],
                        start=True, stop=True),
                        waits=[kl, lqd, sc_free[sel]], inc="tC")
                    m2 = o("tensor", lambda e, kc=kc, psc=psc, qt=qt, par=par: e.matmul(
                        psc[:, 512:1024], Kb[:, par * S + kc * 128:par * S + (kc + 1) * 128],
                        qbb[:, qt * 1024 + 512:qt * 1024 + 1024],
                        start=True, stop=True), waits=[], inc="tC")
                    mtoks[kc] = (m1, m2)

                scores(0)
                scores(1)
                for kc in range(32):
                    sel = kc % 2
                    psc = (pA, pB)[sel]
                    m1, m2 = mtoks.pop(kc)
                    ex = o("scalar", lambda e, kc=kc, psc=psc, hh=hh, sel=sel: e.activation(
                        out=t["expb"][:, sel, :], in_=psc[:, :], func=AF.Exp,
                        scale=ISQ, bias=t["maskf"][:, hh * 32 + kc:hh * 32 + kc + 1]),
                        waits=[m1, m2, expb_free[sel], mkf], inc="sC")
                    sc_free[sel] = ex
                    z1 = o("tensor", lambda e, kc=kc, sel=sel: e.matmul(
                        pD[0:1, 0:512], t["onec"][:, :], t["expb"][:, sel, 0:512],
                        start=(kc == 0), stop=(kc == 31)),
                        waits=[ex] + ([z_free[0], ropeS["pd"], pd_b_free[0]]
                                      if kc == 0 else []), inc="tC")
                    z2 = o("tensor", lambda e, kc=kc, sel=sel: e.matmul(
                        pD[0:1, 512:1024], t["onec"][:, :], t["expb"][:, sel, 512:1024],
                        start=(kc == 0), stop=(kc == 31)), waits=[], inc="tC")
                    v1 = o("tensor", lambda e, kc=kc, sel=sel, par=par: e.matmul(
                        pC[:, 0:512], Vb[:, 2 * S + par * S + kc * 128:2 * S + par * S + (kc + 1) * 128],
                        t["expb"][:, sel, 0:512],
                        start=(kc == 0), stop=(kc == 31)),
                        waits=[vb_data] + (o_free if kc == 0 else []), inc="tC")
                    v2 = o("tensor", lambda e, kc=kc, sel=sel, par=par: e.matmul(
                        pC[:, 512:1024], Vb[:, 2 * S + par * S + kc * 128:2 * S + par * S + (kc + 1) * 128],
                        t["expb"][:, sel, 512:1024],
                        start=(kc == 0), stop=(kc == 31)), waits=[], inc="tC")
                    expb_free[sel] = v2
                    if kc < 30:
                        scores(kc + 2)
                    zs.append(z2)
                    pvs.append(v2)
                rc = o("vector", lambda e: e.reciprocal(t["rz"], pD[0:1, 0:1024]),
                       waits=[zs[-1]], inc="vC")
                rcb = o("vector", lambda e: e.tensor_copy(t["rzb"], t["rz"]),
                        waits=[rc], inc="vC")
                bz1 = o("tensor", lambda e: e.matmul(
                    pD[:, 0:512], t["oner"][:, :], t["rzb"][:, 0:512],
                    start=True, stop=True), waits=[rcb, rc], inc="tC")
                bz2 = o("tensor", lambda e: e.matmul(
                    pD[:, 512:1024], t["oner"][:, :], t["rzb"][:, 512:1024],
                    start=True, stop=True), waits=[], inc="tC")
                rce = o("vector", lambda e: e.tensor_copy(t["rzrep"][:], pD[:, 0:1024]),
                        waits=[bz1, bz2], inc="vC")
                z_free[0] = rce
                off = hh * 2048 + qt * 1024
                ao = o("vector", lambda e, off=off: e.tensor_tensor(
                    hbflat[:, off:off + 1024],
                    pC[:, 0:1024], t["rzrep"][:], ALU.mult),
                    waits=[rce, pvs[-1]] + ([h_last[0], h_last[1]]
                                            if hh == 0 and qt == 0 else []), inc="vC")
                o_free = [ao]
                attn_done.append(ao)
            qb_free[hp] = attn_done[-1]
        kb_free[par] = attn_done[-1]
        vb_free[par] = attn_done[-1]

    wore = g["wo"].rearrange("(hc p) eo -> p hc eo", p=128)
    wo_free = [attn_done[-1], attn_done[-1]]
    d_free = [sc_free[0], sc_free[1]]
    wo_done = []
    for ec in range(FC):
        par = ec % 2
        lw = dma("gpsimd", t["wbuf"][:, par], wore[:, :, ec * 128:(ec + 1) * 128],
                 waits=[wo_free[par], wbuf_free[par]], chain="wld")
        for th in range(4):
            sel = th % 2
            psd = (pA, pB)[sel]
            mm = None
            for hc2 in range(FC):
                mm = o("tensor", lambda e, hc2=hc2, th=th, par=par, psd=psd: e.matmul(
                    psd[:, 0:512], t["wbuf"][:, par, hc2, :],
                    hbflat[:, hc2 * 2048 + th * 512:hc2 * 2048 + (th + 1) * 512],
                    start=(hc2 == 0), stop=(hc2 == FC - 1)),
                    waits=[lw, attn_done[-1], d_free[sel]], inc="tC")
            ev = o("vector", lambda e, ec=ec, th=th, psd=psd: e.scalar_tensor_tensor(
                t["xres"][:, ec, th * 512:(th + 1) * 512], psd[:, 0:512],
                t["bias"][:, 16 + ec:17 + ec],
                t["xres"][:, ec, th * 512:(th + 1) * 512], ALU.add, ALU.add),
                waits=[mm], inc="vC")
            d_free[sel] = ev
            wo_done.append(ev)
        wo_free[par] = wo_done[-1]
        wbuf_free[par] = mm

    lnst[0]["pa"] = d_free[0]
    lnst[0]["pb"] = d_free[1]
    h2_done = []
    for tt4 in range(2):
        xf2 = lambda c, sl, tt4=tt4: t["xres"][:, c, tt4 * 1024 + sl.start:tt4 * 1024 + sl.stop]
        hf2 = lambda c, sl, tt4=tt4: t["hbuf"][:, tt4, c, sl]
        hd, lnst[0] = emit_ln(sch, t, pA, pB, xf2, hf2, [wo_done[-1]], lnst[0])
        h2_done.append(hd[-1])
    wqre = g["wqdr"].rearrange("(c p) o -> p c o", p=128)
    f_wfree = [h2_done[-1], h2_done[-1]]
    f_free = [lnst[0]["pa"], lnst[0]["pb"]]
    out_done = []
    for ec in range(FC):
        par = ec % 2
        lw = dma("sync", t["wbuf"][:, par], wqre[:, :, ec * 128:(ec + 1) * 128],
                 waits=[f_wfree[par], qdone[-1], wbuf_free[par]], chain="ewt")
        for th in range(4):
            sel = th % 2
            psd = (pA, pB)[sel]
            tt4, half = th // 2, th % 2
            mm = None
            for c in range(FC):
                mm = o("tensor", lambda e, c=c, tt4=tt4, half=half, par=par, psd=psd: e.matmul(
                    psd[:, 0:512], t["wbuf"][:, par, c, :],
                    t["hbuf"][:, tt4, c, half * 512:half * 512 + 512],
                    start=(c == 0), stop=(c == FC - 1)),
                    waits=[lw, h2_done[-1], f_free[sel]], inc="tC")
            ev = o("vector", lambda e, ec=ec, th=th, psd=psd: e.scalar_tensor_tensor(
                t["xres"][:, ec, th * 512:(th + 1) * 512], psd[:, 0:512],
                t["bias"][:, 48 + ec:49 + ec],
                t["xres"][:, ec, th * 512:(th + 1) * 512], ALU.add, ALU.add),
                waits=[mm], inc="vC")
            f_free[sel] = ev
            out_done.append(ev)
        f_wfree[par] = out_done[-1]
        wbuf_free[par] = mm

    so = dma("sync", g["x2T"].rearrange("(c p) s -> p c s", p=128), t["xres"][:],
             waits=[out_done[-1]], chain="out")
    o("sync", lambda e: e.nop(), waits=[so])



_L0_NC = None


def run_l0(inputs):
    global _L0_NC
    hc = host_consts()
    if _L0_NC is None:
        _L0_NC = build_l0()
    x0 = np.asarray(inputs["L0_attn_hidden_states"], np.float32)
    in_maps = []
    for core in range(8):
        b = core // 2
        pm = perm_for(core)
        m = {
            "xT": np.ascontiguousarray(x0[b][pm].T),
            "wq": np.asarray(inputs["L0_attn_Wq"], np.float32),
            "wk": np.asarray(inputs["L0_attn_Wk"], np.float32),
            "wv": np.asarray(inputs["L0_attn_Wv"], np.float32),
            "wo": np.asarray(inputs["L0_attn_Wo"], np.float32),
            "wr": np.asarray(inputs["L0_attn_Wr"], np.float32),
            "fwT": np.ascontiguousarray(np.asarray(inputs["L0_ffn_weight"], np.float32).T),
            "bq": tile_vec(inputs["L0_attn_bq"]),
            "bk": tile_vec(inputs["L0_attn_bk"]),
            "bv": tile_vec(inputs["L0_attn_bv"]),
            "bo": tile_vec(inputs["L0_attn_bo"]),
            "br": tile_vec(inputs["L0_attn_br"]),
            "fb": tile_vec(inputs["L0_ffn_bias"]),
            "cos_t": hc["cos_t"][:, pm].astype(BF),
            "sin_t": hc["sin_t"][:, pm].astype(BF),
            "prot": hc["prot"],
            "ident": hc["ident"],
            "ones_col": hc["ones_col"],
            "ones_row": hc["ones_row"],
            "bdiag16": hc["bdiag16"],
            "bcast16": hc["bcast16"],
        }
        in_maps.append(m)
    res = run_bass_kernel_spmd(_L0_NC, in_maps, core_ids=list(range(8)))
    x2 = np.zeros((B, S, D), np.float32)
    for core in range(8):
        b, hh = core // 2, core % 2
        xt = np.asarray(res.results[core]["x2T"]).astype(np.float32)
        x2[b, hh * TLOC:(hh + 1) * TLOC] = xt.T
    return x2


def kernel(**inputs):
    x2 = run_l0(inputs)
    raise NotImplementedError("L1 phases pending")



def build_l1a():
    nc = bass.Bass("TRN2", target_bir_lowering=False)
    ap = lambda *a, **k: nc.dram_tensor(*a, **k).ap()
    g = {}
    g["xT"] = ap("xT", [D, TLOC], BF16, kind="ExternalInput")
    for w in ("wq", "wk", "wv", "wg"):
        g[w] = ap(w, [D, D], F32, kind="ExternalInput")
    g["wa"] = ap("wa", [D, H], F32, kind="ExternalInput")
    g["wb"] = ap("wb", [D, H], F32, kind="ExternalInput")
    g["ba"] = ap("ba", [128, 1], F32, kind="ExternalInput")
    g["bb"] = ap("bb", [128, 1], F32, kind="ExternalInput")
    g["ones_col"] = ap("ones_col", [128, 1], BF16, kind="ExternalInput")
    g["ones_row"] = ap("ones_row", [1, 128], BF16, kind="ExternalInput")
    for w in ("oq", "ok", "ov", "og"):
        g[w] = ap(w, [D, TLOC], BF16, kind="ExternalOutput")
    g["oab"] = ap("oab", [16, 2, TLOC], F32, kind="ExternalOutput")

    ctx = ExitStack()
    with ctx:
        E = ctx.enter_context
        t = {}
        t["xt"] = E(nc.sbuf_tensor("xt", [128, FC, 1024], BF16))
        t["hbuf"] = E(nc.sbuf_tensor("hbuf", [128, 2, FC, 1024], BF16))
        t["wbuf"] = E(nc.sbuf_tensor("wbuf", [128, 2, FC, 128], BF16))
        t["onec"] = E(nc.sbuf_tensor("onec", [128, 1], BF16))
        t["oner"] = E(nc.sbuf_tensor("oner", [1, 128], BF16))
        t["bias"] = E(nc.sbuf_tensor("biast", [128, 2], F32))
        t["eps1"] = E(nc.sbuf_tensor("eps1", [1, 1], F32))
        t["sq"] = E(nc.sbuf_tensor("sq", [128, 2, 512], BF16))
        t["stat"] = E(nc.sbuf_tensor("stat", [1, 4, 512], F32))
        t["statb"] = E(nc.sbuf_tensor("statb", [1, 2, 512], BF16))
        t["tmp"] = E(nc.sbuf_tensor("tmp", [128, 512], BF16))
        t["ev"] = E(nc.sbuf_tensor("ev", [128, 2, 512], BF16))
        t["ab"] = E(nc.sbuf_tensor("ab", [16, 2, 512], F32))
        pA = E(nc.psum_tensor("pA", [128, 1024], F32))
        pB = E(nc.psum_tensor("pB", [128, 1024], F32))
        pC = E(nc.psum_tensor("pC", [128, 1024], F32))

        sch = Sch(nc, ctx)
        emit_l1a(sch, g, t, pA, pB, pC)
        sch.emit(ctx)
    return nc


def emit_l1a(sch, g, t, pA, pB, pC):
    o = sch.op
    dma = sch.dma
    c1 = dma("sync", t["onec"][:], g["ones_col"][:, :], chain="const")
    c2 = dma("sync", t["oner"][:], g["ones_row"][:, :], chain="const")
    c3 = dma("sync", t["bias"][:, 0:1], g["ba"][:, :], chain="const")
    c4 = dma("sync", t["bias"][:, 1:2], g["bb"][:, :], chain="const")
    consts = c4
    epsr = o("vector", lambda e: e.memset(t["eps1"][:], 1e-5), inc="vC")
    xTr = g["xT"].rearrange("(c p) s -> p c s", p=128)

    lnst = [None]
    hbuf_free = [None, None]
    proj_ps = {0: None, 1: None}
    wbuf_free = [None, None]
    ev_free = [None, None]
    ab_free = [None, None]

    def proj(wdram, n_ec, hf, hdeps, evac_fn, wstart, ecw=128):
        wre = wdram.rearrange("(c p) eo -> p c eo", p=128)
        for ec in range(n_ec):
            par = ec % 2
            lw = dma("gpsimd", t["wbuf"][:, par, :, 0:ecw],
                     wre[:, :, ec * ecw:(ec + 1) * ecw],
                     waits=[wbuf_free[par], wstart], chain="wld")
            for half in range(2):
                sl = slice(half * 512, (half + 1) * 512)
                mm = None
                for c in range(FC):
                    mm = o("tensor", lambda e, c=c, sl=sl, half=half, par=par: e.matmul(
                        pC[0:ecw, half * 512:half * 512 + 512],
                        t["wbuf"][:, par, c, 0:ecw], hf(c, sl),
                        start=(c == 0), stop=(c == FC - 1)),
                        waits=list(hdeps) + [lw, proj_ps[half]], inc="tC")
                ft = evac_fn(ec, half, pC[0:ecw, half * 512:half * 512 + 512], mm)
                proj_ps[half] = ft
            wbuf_free[par] = mm

    for tt4 in range(2):
        cols = slice(tt4 * 1024, (tt4 + 1) * 1024)
        lx = dma("gpsimd", t["xt"][:], xTr[:, :, cols], chain="xld")
        hhalf = tt4
        xf = lambda c, sl: t["xt"][:, c, sl]
        hf = lambda c, sl, hhalf=hhalf: t["hbuf"][:, hhalf, c, sl]
        hdone, lnst[0] = emit_ln(sch, t, pA, pB, xf, hf,
                                 [lx, epsr, consts, hbuf_free[hhalf]], lnst[0])
        hdeps = [hdone[-1]]

        for wname, oname in (("wq", "oq"), ("wk", "ok"), ("wv", "ov"), ("wg", "og")):
            def evp(ec, half, psh, mm, tt4=tt4, oname=oname):
                base = tt4 * 1024 + half * 512
                par2 = half
                e1 = o("vector", lambda e, par2=par2: e.tensor_copy(
                    t["ev"][:, par2, :], psh), waits=[mm, ev_free[par2]], inc="vC")
                dt_ = dma("sync", g[oname].rearrange("(c p) s -> p c s", p=128)[:, 0, 0:512]
                          if False else g[oname][0:128, 0:512],
                          t["ev"][:, par2, :], waits=[e1], chain="po")
                ev_free[par2] = dt_
                return e1
            def evp2(ec, half, psh, mm, tt4=tt4, oname=oname):
                base = tt4 * 1024 + half * 512
                par2 = half
                e1 = o("vector", lambda e, par2=par2: e.tensor_copy(
                    t["ev"][:, par2, :], psh), waits=[mm, ev_free[par2]], inc="vC")
                dt_ = dma("sync", g[oname][ec * 128:(ec + 1) * 128, base:base + 512],
                          t["ev"][:, par2, :], waits=[e1], chain="po")
                ev_free[par2] = dt_
                return e1
            proj(g[wname], FC, hf, hdeps, evp2, hbuf_free[hhalf])

        for wname, ai in (("wa", 0), ("wb", 1)):
            def eva(ec, half, psh, mm, tt4=tt4, ai=ai):
                base = tt4 * 1024 + half * 512
                par2 = half
                e1 = o("vector", lambda e, par2=par2, ai=ai: e.tensor_scalar(
                    t["ab"][:, par2, :], psh, t["bias"][0:16, ai:ai + 1], None,
                    ALU.add, ALU.bypass), waits=[mm, ab_free[par2]], inc="vC")
                dt_ = dma("sync", g["oab"][:, ai, base:base + 512],
                          t["ab"][:, par2, :], waits=[e1], chain="po")
                ab_free[par2] = dt_
                return e1
            proj(g[wname], 1, hf, hdeps, eva, hbuf_free[hhalf], ecw=16)
        hbuf_free[hhalf] = proj_ps[1]

    fin = sch.chain_last.get("po")
    o("sync", lambda e: e.nop(), waits=[fin])


_L1A_NC = None


def run_l1a(inputs, x2):
    global _L1A_NC
    hc = host_consts()
    if _L1A_NC is None:
        _L1A_NC = build_l1a()
    in_maps = []
    for core in range(8):
        b, hh = core // 2, core % 2
        m = {
            "xT": np.ascontiguousarray(x2[b, hh * TLOC:(hh + 1) * TLOC].T).astype(BF),
            "wq": np.asarray(inputs["L1_attn_Wq"], np.float32),
            "wk": np.asarray(inputs["L1_attn_Wk"], np.float32),
            "wv": np.asarray(inputs["L1_attn_Wv"], np.float32),
            "wg": np.asarray(inputs["L1_attn_Wg"], np.float32),
            "wa": np.asarray(inputs["L1_attn_Wa"], np.float32),
            "wb": np.asarray(inputs["L1_attn_Wb"], np.float32),
            "ba": tile_vec(inputs["L1_attn_ba"]),
            "bb": tile_vec(inputs["L1_attn_bb"]),
            "ones_col": hc["ones_col"],
            "ones_row": hc["ones_row"],
        }
        in_maps.append(m)
    res = run_bass_kernel_spmd(_L1A_NC, in_maps, core_ids=list(range(8)))
    out = {}
    for nm in ("oq", "ok", "ov", "og"):
        out[nm] = np.zeros((B, S, D), np.float32)
    out["ab"] = np.zeros((B, S, 2 * H), np.float32)
    for core in range(8):
        b, hh = core // 2, core % 2
        r = res.results[core]
        sl = slice(hh * TLOC, (hh + 1) * TLOC)
        for nm in ("oq", "ok", "ov", "og"):
            out[nm][b, sl] = np.asarray(r[nm]).astype(np.float32).T
        ab = np.asarray(r["oab"])
        out["ab"][b, sl, 0:H] = ab[:, 0, :].T
        out["ab"][b, sl, H:] = ab[:, 1, :].T
    return out



NCH = 32


def build_l1b(dbg=False):
    Sch.PARANOID = False
    nc = bass.Bass("TRN2", target_bir_lowering=False)
    ap = lambda *a, **k: nc.dram_tensor(*a, **k).ap()
    g = {}
    g["qraw"] = ap("qraw", [8, DH, S], BF16, kind="ExternalInput")
    g["kraw"] = ap("kraw", [8, DH, S], BF16, kind="ExternalInput")
    g["vraw"] = ap("vraw", [8, DH, S], BF16, kind="ExternalInput")
    g["convw"] = ap("convw", [3, 8, 128, 4], F32, kind="ExternalInput")
    g["convb"] = ap("convb", [3, 8, 128, 1], F32, kind="ExternalInput")
    g["ab_t"] = ap("ab_t", [128, 2, 256], F32, kind="ExternalInput")
    g["abr"] = ap("abr", [2, 8, S], F32, kind="ExternalInput")
    g["rmask"] = ap("rmask", [8, S], BF16, kind="ExternalInput")
    g["lnwrep"] = ap("lnwrep", [128, 2, S], BF16, kind="ExternalInput")
    g["smaskS8"] = ap("smaskS8", [128, 8, 128], BF16, kind="ExternalInput")
    g["smaskL8"] = ap("smaskL8", [128, 8, 128], BF16, kind="ExternalInput")
    g["smaskI8"] = ap("smaskI8", [128, 8, 128], BF16, kind="ExternalInput")
    g["triuI"] = ap("triuI", [128, 128], BF16, kind="ExternalInput")
    g["ident"] = ap("ident", [128, 128], BF16, kind="ExternalInput")
    g["ones_row"] = ap("ones_row", [1, 128], BF16, kind="ExternalInput")
    g["odr"] = ap("odr", [8, 128, NCH, 128], BF16, kind="ExternalOutput")
    for nm in ("qc", "kc", "vc"):
        if dbg:
            g[nm] = ap(nm + "dr", [8, DH, S], BF16, kind="ExternalOutput")
        else:
            g[nm] = nc.dram_tensor(nm + "dr", [8, DH, S], BF16).ap()
    g["crdr"] = nc.dram_tensor("crdr", [1, 2, 8, S], F32).ap()
    g["gcdr"] = nc.dram_tensor("gcdr", [1, 8, NCH], F32).ap()
    g["odrw"] = nc.dram_tensor("odrw", [8, 128, NCH, 128], BF16).ap()

    ctx = ExitStack()
    with ctx:
        E = ctx.enter_context
        t = {}
        t["ub"] = E(nc.sbuf_tensor("ub", [128, 4100], BF16))
        t["cvo"] = E(nc.sbuf_tensor("cvo", [128, 2, 512], BF16))
        t["diag"] = E(nc.sbuf_tensor("diag", [128, 2, 4, 128], BF16))
        t["cw"] = E(nc.sbuf_tensor("cw", [128, 2, 5], F32))
        t["ar"] = E(nc.sbuf_tensor("ar", [8, S], F32))
        t["rt1"] = E(nc.sbuf_tensor("rt1", [8, S], F32))
        t["rt2"] = E(nc.sbuf_tensor("rt2", [8, S], F32))
        t["rm"] = E(nc.sbuf_tensor("rm", [8, S], BF16))
        t["crg"] = E(nc.sbuf_tensor("crg", [8, S], F32))
        t["gC8"] = E(nc.sbuf_tensor("gC8", [8, NCH], F32))
        for nm in ("abt", "cums", "igc", "bet", "ccl", "dcl", "lat", "sc1", "lcc", "gtc"):
            t[nm] = E(nc.sbuf_tensor(nm, [128, 2, 256] if nm == "abt" else [128, 256], F32))
        t["lnab"] = E(nc.sbuf_tensor("lnab", [128, 256], BF16))
        t["gCrep"] = E(nc.sbuf_tensor("gCrep", [128, 256], F32))
        t["crowC"] = E(nc.sbuf_tensor("crowC", [1, 2, 2, 1024], F32))
        t["onerf"] = E(nc.sbuf_tensor("onerf", [1, 128], F32))
        t["ccrow"] = E(nc.sbuf_tensor("ccrow", [1, 256], F32))
        t["lam"] = E(nc.sbuf_tensor("lam", [128, 8, 128], F32))
        t["qkf"] = E(nc.sbuf_tensor("qkf", [128, 2, 2, 8, 128], BF16))
        t["ktv"] = E(nc.sbuf_tensor("ktv", [128, 2, 2, 8, 128], BF16))
        t["g1"] = E(nc.sbuf_tensor("g1", [128, 8, 128], BF16))
        t["h1"] = E(nc.sbuf_tensor("h1", [128, 8, 128], BF16))
        t["crS"] = E(nc.sbuf_tensor("crS", [128, 8, 128], BF16))
        t["igS"] = E(nc.sbuf_tensor("igS", [128, 8, 128], BF16))
        t["sqm"] = E(nc.sbuf_tensor("sqm", [128, 8, 128], BF16))
        t["sq1"] = E(nc.sbuf_tensor("sq1", [128, 8, 128], BF16))
        t["wt"] = E(nc.sbuf_tensor("wt", [128, 2, 8, 128], BF16))
        t["wu"] = E(nc.sbuf_tensor("wu", [128, 2, 8, 128], BF16))
        t["xch"] = E(nc.sbuf_tensor("xch", [128, 8, 256], BF16))
        t["dk"] = E(nc.sbuf_tensor("dk", [128, 8, 128], BF16))
        t["m2n"] = E(nc.sbuf_tensor("m2n", [128, 8, 128], BF16))
        t["b1n"] = E(nc.sbuf_tensor("b1n", [128, 8, 128], BF16))
        t["zst"] = E(nc.sbuf_tensor("zst", [128, 8, 128], F32))
        t["o8"] = E(nc.sbuf_tensor("o8", [128, 8, 128], BF16))
        t["zb"] = E(nc.sbuf_tensor("zb", [128, 8, 128], BF16))
        t["lsq"] = t["ub"][:, 0:4096].rearrange("p (nn d) -> p nn d", d=128)
        t["lnr2"] = E(nc.sbuf_tensor("lnr2", [128, 2, NCH], F32))
        t["eps128"] = E(nc.sbuf_tensor("eps128", [128, 1], F32))
        t["olane"] = E(nc.sbuf_tensor("olane", [128, NCH, 128], BF16))
        t["lnr"] = E(nc.sbuf_tensor("lnr", [128, 2, NCH], F32))
        
        t["lnw"] = E(nc.sbuf_tensor("lnw", [128, 2, S], BF16))
        t["sm8S"] = E(nc.sbuf_tensor("sm8S", [128, 8, 128], BF16))
        t["sm8L"] = E(nc.sbuf_tensor("sm8L", [128, 8, 128], BF16))
        t["sm8I"] = E(nc.sbuf_tensor("sm8I", [128, 8, 128], BF16))
        t["triuI"] = E(nc.sbuf_tensor("triuIt", [128, 128], BF16))
        t["ident"] = E(nc.sbuf_tensor("identt", [128, 128], BF16))
        t["oner"] = E(nc.sbuf_tensor("oner", [1, 128], BF16))
        t["one1"] = E(nc.sbuf_tensor("one1", [128, 1], F32))

        pA = E(nc.psum_tensor("pA", [128, 1024], F32))
        pB = E(nc.psum_tensor("pB", [128, 1024], F32))
        pC = E(nc.psum_tensor("pC", [128, 1024], F32))
        pD = E(nc.psum_tensor("pD", [128, 1024], F32))

        sch = Sch(nc, ctx)
        emit_l1b(sch, g, t, pA, pB, pC, pD)
        sch.emit(ctx)
    Sch.PARANOID = False
    return nc


def emit_l1b(sch, g, t, pA, pB, pC, pD):
    o = sch.op
    dma = sch.dma
    consts = None
    for dst, src in [(t["sm8S"][:], g["smaskS8"][:, :, :]), (t["sm8L"][:], g["smaskL8"][:, :, :]),
                     (t["sm8I"][:], g["smaskI8"][:, :, :]), (t["triuI"][:], g["triuI"][:, :]),
                     (t["ident"][:], g["ident"][:, :]), (t["oner"][:], g["ones_row"][:, :]),
                     (t["abt"][:], g["ab_t"][:, :, :]), (t["rm"][:], g["rmask"][:, :]),
                     (t["lnw"][:], g["lnwrep"][:, :, :])]:
        consts = dma("sync", dst, src, chain="const")
    one1 = o("vector", lambda e: e.memset(t["one1"][:], 1.0), inc="vC")
    eps128 = o("vector", lambda e: e.memset(t["eps128"][:], 1e-5), inc="vC")

    conv_done = []
    diag_free = [None, None]
    cwb_free = [None, None]
    cvo_free = [None, None]
    ub_free = [None]
    mz = o("vector", lambda e: e.memset(t["ub"][:, 0:4], 0.0), inc="vC")
    for tau, (rawnm, outnm) in enumerate((("qraw", "qc"), ("kraw", "kc"), ("vraw", "vc"))):
        for l in range(8):
            par = (tau * 8 + l) % 2
            lw = dma("gpsimd", t["cw"][:, par, 0:4], g["convw"][tau, l, :, :],
                     waits=[diag_free[par], cwb_free[par]], chain="cwld")
            lb = dma("gpsimd", t["cw"][:, par, 4:5], g["convb"][tau, l, :, :],
                     chain="cwld")
            dgs = []
            for j in range(4):
                dg = o("vector", lambda e, j=j, par=par: e.tensor_scalar(
                    t["diag"][:, par, j, :], t["ident"][:], t["cw"][:, par, j:j + 1],
                    None, ALU.mult, ALU.bypass), waits=[lw, consts, diag_free[par]],
                    inc="vC")
                dgs.append(dg)
            lu = dma("gpsimd", t["ub"][:, 4:4100], g[rawnm][l, :, :],
                     waits=[ub_free[0], mz], chain="uld")
            pv = None
            for sl8 in range(8):
                par2 = sl8 % 2
                mm = None
                for j in range(4):
                    mm = o("tensor", lambda e, j=j, sl8=sl8, par=par, par2=par2: e.matmul(
                        pD[:, par2 * 512:par2 * 512 + 512],
                        t["diag"][:, par, j, :],
                        t["ub"][:, sl8 * 512 + j + 1:sl8 * 512 + j + 513],
                        start=(j == 0), stop=(j == 3)),
                        waits=dgs + [lu, cvo_free[par2]], inc="tC")
                ev = o("scalar", lambda e, par=par, par2=par2: e.activation(
                    out=t["cvo"][:, par2, :], in_=pD[:, par2 * 512:par2 * 512 + 512],
                    func=AF.Silu, bias=t["cw"][:, par, 4:5]), waits=[mm, lb], inc="sC")
                dt_ = dma("sync", g[outnm][l, :, sl8 * 512:(sl8 + 1) * 512],
                          t["cvo"][:, par2, :], waits=[ev], chain="cvout")
                cvo_free[par2] = dt_
                pv = dt_
            diag_free[par] = mm
            cwb_free[par] = ev
            ub_free[0] = mm
            conv_done.append(pv)
    cvdone = conv_done[-1]

    e1 = o("scalar", lambda e: e.activation(out=t["sc1"][:], in_=t["abt"][:, 0, :],
                                            func=AF.Exp, scale=-1.0),
           waits=[consts], inc="sC")
    e2 = o("scalar", lambda e: e.activation(out=t["lat"][:], in_=t["sc1"][:],
                                            func=AF.Ln, bias=t["one1"][:, :]),
           waits=[e1, one1], inc="sC")
    e3 = o("vector", lambda e: e.tensor_scalar(t["lnab"][:], t["lat"][:], -1.0, None,
                                               ALU.mult, ALU.bypass), waits=[e2], inc="vC")
    cm = o("tensor", lambda e: e.matmul(pA[:, 0:256], t["triuI"][:], t["lnab"][:],
                                        start=True, stop=True),
           waits=[e3, consts], inc="tC")
    e4 = o("vector", lambda e: e.tensor_copy(t["cums"][:], pA[:, 0:256]),
           waits=[cm], inc="vC")
    e6 = o("scalar", lambda e: e.activation(out=t["bet"][:], in_=t["abt"][:, 1, :],
                                            func=AF.Sigmoid), waits=[consts], inc="sC")
    e7 = o("vector", lambda e: e.tensor_tensor(t["sc1"][:], t["cums"][:], t["lat"][:],
                                               ALU.add), waits=[e4, e2], inc="vC")
    e8 = o("scalar", lambda e: e.activation(out=t["sc1"][:], in_=t["sc1"][:],
                                            func=AF.Exp), waits=[e7], inc="sC")
    e9 = o("vector", lambda e: e.tensor_tensor(t["ccl"][:], t["bet"][:], t["sc1"][:],
                                               ALU.mult), waits=[e6, e8], inc="vC")
    x1 = o("scalar", lambda e: e.activation(out=t["igc"][:], in_=t["abt"][:, 1, :],
                                            func=AF.Exp, scale=-1.0), waits=[consts], inc="sC")
    x2_ = o("scalar", lambda e: e.activation(out=t["igc"][:], in_=t["igc"][:],
                                             func=AF.Ln, bias=t["one1"][:, :]),
            waits=[x1, one1], inc="sC")
    x3 = o("vector", lambda e: e.tensor_tensor(t["sc1"][:], t["cums"][:], t["lat"][:],
                                               ALU.add), waits=[e4, e2, e8], inc="vC")
    lcc = o("vector", lambda e: e.tensor_tensor(t["lcc"][:], t["sc1"][:], t["igc"][:],
                                                ALU.subtract), waits=[x3, x2_], inc="vC")
    gtc = o("scalar", lambda e: e.activation(out=t["gtc"][:], in_=t["cums"][:],
                                             func=AF.Exp), waits=[e4], inc="sC")

    la = dma("sync", t["ar"][:], g["abr"][0, :, :], chain="rr")
    r1 = o("scalar", lambda e: e.activation(out=t["rt1"][:], in_=t["ar"][:],
                                            func=AF.Exp, scale=-1.0), waits=[la], inc="sC")
    r2 = o("scalar", lambda e: e.activation(out=t["rt1"][:], in_=t["rt1"][:],
                                            func=AF.Ln, bias=t["one1"][0:8, :]),
           waits=[r1, one1], inc="sC")
    r3 = o("vector", lambda e: e.tensor_tensor_scan(t["rt2"][:], t["rm"][:], t["rt1"][:],
                                                    0.0, ALU.mult, ALU.add),
           waits=[r2, consts], inc="vC")
    r5 = o("vector", lambda e: e.tensor_scalar(
        t["gC8"][:], t["rt2"].ap().rearrange("l (n tp) -> l n tp", tp=128)[:, :, 127],
        -1.0, None, ALU.mult, ALU.bypass), waits=[r3], inc="vC")
    gout = dma("sync", g["gcdr"][0, :, :], t["gC8"][:], waits=[r5], chain="rr")
    igo = dma("sync", g["crdr"][0, 1, :, :], t["rt2"][:], waits=[r3], chain="rr")
    lb_ = dma("sync", t["ar"][:], g["abr"][1, :, :], waits=[r2], chain="rr")
    r6 = o("scalar", lambda e: e.activation(out=t["crg"][:], in_=t["ar"][:],
                                            func=AF.Exp, scale=-1.0),
           waits=[lb_], inc="sC")
    r7 = o("scalar", lambda e: e.activation(out=t["crg"][:], in_=t["crg"][:],
                                            func=AF.Ln, bias=t["one1"][0:8, :]),
           waits=[r6], inc="sC")
    ra = o("vector", lambda e: e.tensor_tensor(t["rt1"][:], t["rt1"][:], t["crg"][:],
                                               ALU.subtract), waits=[r7, r3], inc="vC")
    rb2 = o("vector", lambda e: e.tensor_tensor(t["crg"][:], t["rt1"][:], t["rt2"][:],
                                                ALU.subtract), waits=[ra], inc="vC")
    cro = dma("sync", g["crdr"][0, 0, :, :], t["crg"][:], waits=[rb2], chain="rr")
    onerf_m = o("vector", lambda e: e.memset(t["onerf"][:], 1.0), inc="vC")
    gcl = dma("sync", t["ccrow"][:, :], g["gcdr"].rearrange("x l n -> x (l n)"),
              waits=[gout], chain="rr")
    gmm = o("tensor", lambda e: e.matmul(pA[:, 256:512], t["onerf"][:],
                                         t["ccrow"][:, :], start=True, stop=True),
            waits=[gcl, cm, e4, onerf_m], inc="tC")
    dc1 = o("vector", lambda e: e.tensor_scalar(
        t["dcl"][:], pA[:, 256:512], 1.0, None, ALU.mult, ALU.bypass),
        waits=[gmm], inc="vC")
    gre = o("scalar", lambda e: e.activation(out=t["gCrep"][:], in_=t["dcl"][:],
                                             func=AF.Exp), waits=[dc1], inc="sC")
    dc2 = o("vector", lambda e: e.tensor_tensor(t["dcl"][:], t["dcl"][:], t["cums"][:],
                                                ALU.subtract), waits=[dc1, e4, gre], inc="vC")
    dstt = o("scalar", lambda e: e.activation(out=t["dcl"][:], in_=t["dcl"][:],
                                              func=AF.Exp), waits=[dc2], inc="sC")

    zinit = o("vector", lambda e: e.memset(t["zst"][:], 0.0), inc="vC")
    qk_free = [None, None]
    ktv_free = [None, None]
    crw_free = [gcl, None]
    pa_free = [e4, gre]
    pb_free = []
    pc_free = []
    pd_free = [cvo_free[0], cvo_free[1]]
    wtb_free = [None, None]
    wub_free = [None, None]
    x_free = [None]
    zb_tok = [zinit]
    z_tok = [zinit]
    o8_free = [None]
    odma = [None]

    sset = getattr(Sch, "SER_SET", None) or frozenset()
    SER = lambda name: name in sset
    bar_every = int(getattr(Sch, "L1B_BAR", 0))
    for n in range(NCH):
        if bar_every and n % bar_every == 0:
            sch.barrier()
        sch.serialize = SER("s1a")
        par = n % 2
        csl = slice(n * 128, (n + 1) * 128)
        lqk = []
        for ti, nm in ((0, "qc"), (1, "kc")):
            lqk.append(dma("gpsimd", t["qkf"][:, par, ti],
                           g[nm].rearrange("l d s -> d l s")[:, :, csl],
                           waits=[qk_free[par], cvdone], chain="qkld"))
        lvf = dma("gpsimd", t["ktv"][:, par, 1], g["vc"].rearrange("l d s -> d l s")[:, :, csl],
                  waits=[ktv_free[par], cvdone], chain="vtld")
        sch.serialize = SER("s1c")
        lcr = dma("gpsimd", t["crowC"].ap().rearrange("x pp a n -> x pp a n")[0:1, par]
                  .rearrange("x a (l s) -> x a l s", s=128),
                  g["crdr"].rearrange("x a l (nn s) -> x a l nn s", s=128)[0:1, :, :, n, :],
                  waits=[crw_free[par], cro, igo], chain="crld")
        sch.serialize = SER("s1a")
        ks1 = o("gpsimd", lambda e, par=par: e.tensor_scalar(
            t["qkf"][:, par, 1], t["qkf"][:, par, 1], ISQ, None, ALU.mult, ALU.bypass),
            waits=lqk, inc="gC")
        sch.serialize = SER("s1b")
        kT = []
        vT = []
        for l in range(8):
            kT.append(o("tensor", lambda e, l=l, par=par: e.matmul(
                pA[:, l * 128:(l + 1) * 128], t["qkf"][:, par, 1, l, :],
                t["ident"][:], start=True, stop=True),
                waits=[ks1, consts] + pa_free, inc="tC"))
            vT.append(o("tensor", lambda e, l=l, par=par: e.matmul(
                pB[:, l * 128:(l + 1) * 128], t["ktv"][:, par, 1, l, :],
                t["ident"][:], start=True, stop=True),
                waits=[lvf, consts] + pb_free, inc="tC"))
        evacK = o("scalar", lambda e, par=par: e.activation(
            out=t["ktv"][:, par, 0], in_=pA[:, 0:1024], func=AF.Copy),
            waits=[kT[-1], ktv_free[par]], inc="sC")
        evacV = o("vector", lambda e, par=par: e.tensor_copy(
            t["ktv"][:, par, 1], pB[:, 0:1024]), waits=[vT[-1]], inc="vC")
        pa_free = [evacK]
        pb_free = [evacV]

        sch.serialize = SER("s2")
        kk = []
        sq = []
        for l in range(8):
            kk.append(o("tensor", lambda e, l=l, par=par: e.matmul(
                pA[:, l * 128:(l + 1) * 128], t["qkf"][:, par, 1, l, :],
                t["qkf"][:, par, 1, l, :], start=True, stop=True),
                waits=[ks1] + pa_free, inc="tC"))
            sq.append(o("tensor", lambda e, l=l, par=par: e.matmul(
                pB[:, l * 128:(l + 1) * 128], t["qkf"][:, par, 1, l, :],
                t["qkf"][:, par, 0, l, :], start=True, stop=True),
                waits=[ks1] + pb_free, inc="tC"))
        crep = []
        for l in range(8):
            crep.append(o("tensor", lambda e, l=l, par=par: e.matmul(
                pC[:, l * 128:(l + 1) * 128], t["onerf"][:],
                t["crowC"][0:1, par, 0, l * 128:(l + 1) * 128],
                start=True, stop=True), waits=[lcr, onerf_m] + pc_free, inc="tC"))
            crep.append(o("tensor", lambda e, l=l, par=par: e.matmul(
                pD[:, l * 128:(l + 1) * 128], t["onerf"][:],
                t["crowC"][0:1, par, 1, l * 128:(l + 1) * 128],
                start=True, stop=True), waits=list(pd_free), inc="tC"))
        pa_free, pb_free, pc_free, pd_free = [], [], [], []
        sch.serialize = SER("s3")
        kkn = o("vector", lambda e: e.tensor_scalar(
            t["g1"][:], pA[:, 0:1024], -1.0, None, ALU.mult, ALU.bypass),
            waits=[kk[-1]], inc="vC")
        pa_free = [kkn]
        lg1 = o("vector", lambda e, n=n: e.tensor_scalar(
            t["lam"][:], pC[:, 0:1024], 1.0, None, ALU.mult, ALU.bypass),
            waits=crep, inc="vC")
        lg2 = []
        for l in range(8):
            lg2.append(o("vector", lambda e, l=l, n=n: e.tensor_scalar(
                t["lam"][:, l, :], t["lam"][:, l, :],
                t["cums"][:, l * 32 + n:l * 32 + n + 1], None, ALU.subtract, ALU.bypass),
                waits=[lg1], inc="vC"))
        lg3 = o("vector", lambda e: e.tensor_tensor(
            t["lam"][:], t["lam"][:], t["sm8S"][:], ALU.add),
            waits=lg2 + [consts], inc="vC")
        lg4 = o("scalar", lambda e: e.activation(
            out=t["crS"][:], in_=t["lam"][:], func=AF.Exp), waits=[lg3], inc="sC")
        wt0 = o("vector", lambda e: e.tensor_tensor(
            t["wt"][:, 0], t["crS"][:], t["g1"][:], ALU.mult),
            waits=[lg4, kkn, wtb_free[0]], inc="vC")
        pc_free = [lg1]
        lu1 = []
        for l in range(8):
            lu1.append(o("vector", lambda e, l=l, n=n: e.tensor_scalar(
                t["lam"][:, l, :], pD[:, l * 128:(l + 1) * 128],
                t["lcc"][:, l * 32 + n:l * 32 + n + 1], None, ALU.add, ALU.bypass),
                waits=[crep[-1], lcc, wt0], inc="vC"))
        lu2 = o("vector", lambda e: e.tensor_tensor(
            t["lam"][:], t["lam"][:], t["sm8L"][:], ALU.add),
            waits=lu1 + [consts], inc="vC")
        lu3 = o("scalar", lambda e: e.activation(
            out=t["igS"][:], in_=t["lam"][:], func=AF.Exp), waits=[lu2], inc="sC")
        wu0 = o("vector", lambda e: e.tensor_tensor(
            t["wu"][:, 0], t["igS"][:], t["g1"][:], ALU.mult),
            waits=[lu3, kkn, wub_free[0]], inc="vC")
        pd_free = [lu1[-1]]
        ls1 = o("vector", lambda e: e.tensor_scalar(
            t["lam"][:], pD[:, 0:1024], -1.0, None, ALU.mult, ALU.bypass),
            waits=crep + [lu1[-1], wu0], inc="vC")
        ls2 = []
        for l in range(8):
            ls2.append(o("vector", lambda e, l=l, n=n: e.tensor_scalar(
                t["lam"][:, l, :], t["lam"][:, l, :],
                t["cums"][:, l * 32 + n:l * 32 + n + 1], None, ALU.subtract, ALU.bypass),
                waits=[ls1], inc="vC"))
        ls3 = o("vector", lambda e: e.tensor_tensor(
            t["lam"][:], t["lam"][:], t["sm8I"][:], ALU.add),
            waits=ls2 + [consts], inc="vC")
        ls4 = o("scalar", lambda e: e.activation(
            out=t["sq1"][:], in_=t["lam"][:], func=AF.Exp), waits=[ls3], inc="sC")
        sqmt = o("vector", lambda e: e.tensor_tensor(
            t["sqm"][:], t["sq1"][:], pB[:, 0:1024], ALU.mult),
            waits=[ls4, sq[-1]], inc="vC")
        pb_free = [sqmt]
        pd_free = pd_free + [ls2[-1]]
        sch.serialize = SER("s4")
        xinits = []
        for l in range(8):
            xinits.append(o("gpsimd", lambda e, l=l, par=par, n=n: e.tensor_scalar(
                t["xch"][:, l, 0:128], t["ktv"][:, par, 1, l, :],
                t["bet"][:, l * 32 + n:l * 32 + n + 1], None, ALU.mult, ALU.bypass),
                waits=[evacV, e6, x_free[0]], inc="gC"))
            xinits.append(o("gpsimd", lambda e, l=l, par=par, n=n: e.tensor_scalar(
                t["xch"][:, l, 128:256], t["ktv"][:, par, 0, l, :],
                t["ccl"][:, l * 32 + n:l * 32 + n + 1], None, ALU.mult, ALU.bypass),
                waits=[evacK, e9, x_free[0]], inc="gC"))
        x_cur = xinits[-1]
        sch.serialize = SER("s5")
        wtcur, wucur = wt0, wu0
        src = 0
        x_deps = [x_cur]
        for k in range(7):
            amm = []
            for l in range(8):
                psX = pC if l < 4 else pD
                off = (l % 4) * 256
                amm.append(o("tensor", lambda e, l=l, src=src, psX=psX, off=off: e.matmul(
                    psX[:, off:off + 256], t["wt"][:, src, l, :], t["xch"][:, l, :],
                    start=True, stop=False),
                    waits=[wtcur] + x_deps + pc_free + pd_free, inc="tC"))
                amm.append(o("tensor", lambda e, l=l, psX=psX, off=off: e.matmul(
                    psX[:, off:off + 256], t["ident"][:], t["xch"][:, l, :],
                    start=False, stop=True), waits=[consts], inc="tC"))
            pc_free, pd_free = [], []
            if k % 2 == 0:
                evC = o("scalar", lambda e: e.activation(
                    out=t["xch"][:, 0:4, :], in_=pC[:, 0:1024], func=AF.Copy),
                    waits=amm, inc="sC")
                evD = o("vector", lambda e: e.tensor_copy(
                    t["xch"][:, 4:8, :], pD[:, 0:1024]), waits=amm, inc="vC")
            else:
                evC = o("vector", lambda e: e.tensor_copy(
                    t["xch"][:, 0:4, :], pC[:, 0:1024]), waits=amm, inc="vC")
                evD = o("scalar", lambda e: e.activation(
                    out=t["xch"][:, 4:8, :], in_=pD[:, 0:1024], func=AF.Copy),
                    waits=amm, inc="sC")
            pc_free = [evC]
            pd_free = [evD]
            xall = [evC, evD]
            if k < 6:
                smm = []
                for l in range(8):
                    smm.append(o("tensor", lambda e, l=l, src=src: e.matmul(
                        pA[:, l * 128:(l + 1) * 128], t["wu"][:, src, l, :],
                        t["wt"][:, src, l, :], start=True, stop=True),
                        waits=[wtcur, wucur] + pa_free, inc="tC"))
                    smm.append(o("tensor", lambda e, l=l, src=src: e.matmul(
                        pB[:, l * 128:(l + 1) * 128], t["wt"][:, src, l, :],
                        t["wu"][:, src, l, :], start=True, stop=True),
                        waits=pb_free, inc="tC"))
                pa_free, pb_free = [], []
                dst = 1 - src
                nwt = o("scalar", lambda e, dst=dst: e.activation(
                    out=t["wt"][:, dst], in_=pA[:, 0:1024], func=AF.Copy),
                    waits=smm + [wtb_free[dst]], inc="sC")
                nwu = o("vector", lambda e, dst=dst: e.tensor_copy(
                    t["wu"][:, dst], pB[:, 0:1024]), waits=smm + [wub_free[dst]],
                    inc="vC")
                pa_free = [nwt]
                pb_free = [nwu]
                wtb_free[src] = amm[-1]
                wub_free[src] = smm[-1]
                wtcur, wucur = nwt, nwu
                src = dst
            else:
                wtb_free[src] = amm[-1]
            x_cur = xall[-1]
            x_deps = xall
        uv_tok = x_deps
        sch.serialize = SER("s6")
        zb = o("vector", lambda e: e.tensor_copy(t["zb"][:], t["zst"][:]),
               waits=[z_tok[0], zb_tok[0]], inc="vC")
        dks = []
        for l in range(8):
            dks.append(o("gpsimd", lambda e, l=l, par=par, n=n: e.tensor_scalar(
                t["dk"][:, l, :], t["ktv"][:, par, 0, l, :],
                t["dcl"][:, l * 32 + n:l * 32 + n + 1], None, ALU.mult, ALU.bypass),
                waits=[evacK, dstt, xinits[-1]], inc="gC"))
        sch.serialize = SER("s7")
        mmm = []
        for l in range(8):
            mmm.append(o("tensor", lambda e, l=l: e.matmul(
                pA[:, l * 128:(l + 1) * 128], t["xch"][:, l, 128:256],
                t["dk"][:, l, :], start=True, stop=True),
                waits=[dks[l]] + uv_tok + pa_free, inc="tC"))
            mmm.append(o("tensor", lambda e, l=l: e.matmul(
                pB[:, l * 128:(l + 1) * 128], t["xch"][:, l, 128:256],
                t["sqm"][:, l, :], start=True, stop=True),
                waits=[sqmt] + pb_free, inc="tC"))
        pa_free, pb_free = [], []
        m2e = o("scalar", lambda e: e.activation(
            out=t["m2n"][:], in_=pA[:, 0:1024], func=AF.Copy, scale=-1.0),
            waits=mmm, inc="sC")
        b1e = o("vector", lambda e: e.tensor_scalar(
            t["b1n"][:], pB[:, 0:1024], -1.0, None, ALU.mult, ALU.bypass),
            waits=mmm, inc="vC")
        pa_free = [m2e]
        pb_free = [b1e]
        sch.serialize = SER("s8")
        zmm = []
        for l in range(8):
            zmm.append(o("tensor", lambda e, l=l: e.matmul(
                pC[:, l * 128:(l + 1) * 128], t["dk"][:, l, :],
                t["xch"][:, l, 0:128], start=True, stop=False),
                waits=[dks[l]] + uv_tok + pc_free, inc="tC"))
            zmm.append(o("tensor", lambda e, l=l: e.matmul(
                pC[:, l * 128:(l + 1) * 128], t["m2n"][:, l, :],
                t["zb"][:, l, :], start=False, stop=True),
                waits=[m2e, zb], inc="tC"))
        pc_free = []
        znew = []
        for l in range(8):
            znew.append(o("vector", lambda e, l=l, n=n: e.scalar_tensor_tensor(
                t["zst"][:, l, :], t["zst"][:, l, :],
                t["gCrep"][:, l * 32 + n:l * 32 + n + 1],
                pC[:, l * 128:(l + 1) * 128], ALU.mult, ALU.add),
                waits=[zmm[-1], gre], inc="vC"))
        z_tok = [znew[-1]]
        pc_free = [znew[-1]]
        sch.serialize = SER("s9")
        omm = []
        for l in range(8):
            omm.append(o("tensor", lambda e, l=l, par=par: e.matmul(
                pD[:, l * 128:(l + 1) * 128], t["qkf"][:, par, 0, l, :],
                t["zb"][:, l, :], start=True, stop=True),
                waits=[zb] + pd_free, inc="tC"))
            omm.append(o("tensor", lambda e, l=l: e.matmul(
                pA[:, l * 128:(l + 1) * 128], t["b1n"][:, l, :],
                t["zb"][:, l, :], start=True, stop=False),
                waits=[b1e] + pa_free, inc="tC"))
            omm.append(o("tensor", lambda e, l=l: e.matmul(
                pA[:, l * 128:(l + 1) * 128], t["sqm"][:, l, :],
                t["xch"][:, l, 0:128], start=False, stop=True),
                waits=[sqmt] + uv_tok, inc="tC"))
        pd_free = []
        pa_free = []
        ov2 = o("vector", lambda e: e.tensor_copy(t["sq1"][:], pA[:, 0:1024]),
                waits=omm + [ls4], inc="vC")
        pa_free = [ov2]
        oevs = []
        for l in range(8):
            oevs.append(o("vector", lambda e, l=l, n=n: e.scalar_tensor_tensor(
                t["o8"][:, l, :], pD[:, l * 128:(l + 1) * 128],
                t["gtc"][:, l * 32 + n:l * 32 + n + 1], t["sq1"][:, l, :],
                ALU.mult, ALU.add), waits=[ov2, gtc, o8_free[0]], inc="vC"))
        oev = oevs[-1]
        pd_free = [oev]
        od = dma("sync", g["odrw"].rearrange("l tp nn d -> tp l nn d")[:, :, n, :],
                 t["o8"][:], waits=[oev], chain="odma")
        o8_free = [od]
        odma[0] = od
        qk_free[par] = omm[-1]
        ktv_free[par] = dks[-1]
        crw_free[par] = crep[-1]
        x_free = [omm[-1]]
        zb_tok = [omm[-1]]

    sch.barrier()
    sch.serialize = bool(getattr(Sch, "SER_LN", False))
    ol_free = [None]
    lnfin = None
    for l in range(8):
        lo = dma("gpsimd", t["olane"][:], g["odrw"][l, :, :, :],
                 waits=[odma[0], ol_free[0]], chain="olld")
        s1 = o("vector", lambda e: e.tensor_reduce(
            t["lnr"][:, 0, :], t["olane"][:], AX.X, ALU.add), waits=[lo], inc="vC")
        s2 = o("scalar", lambda e: e.activation(
            out=t["lsq"][:], in_=t["olane"][:], func=AF.Square), waits=[lo], inc="sC")
        s3 = o("vector", lambda e: e.tensor_reduce(
            t["lnr"][:, 1, :], t["lsq"][:], AX.X, ALU.add), waits=[s2], inc="vC")
        s4 = o("vector", lambda e: e.tensor_scalar(
            t["lnr"][:, 0, :], t["lnr"][:, 0, :], 1.0 / 128, None, ALU.mult, ALU.bypass),
            waits=[s1], inc="vC")
        s5 = o("vector", lambda e: e.tensor_scalar(
            t["lnr"][:, 1, :], t["lnr"][:, 1, :], 1.0 / 128, None, ALU.mult, ALU.bypass),
            waits=[s3], inc="vC")
        s6 = o("vector", lambda e: e.scalar_tensor_tensor(
            t["lnr2"][:, 0, :], t["lnr"][:, 0, :], -1.0, t["lnr"][:, 0, :],
            ALU.mult, ALU.mult), waits=[s4], inc="vC")
        s7 = o("vector", lambda e: e.tensor_tensor(
            t["lnr"][:, 1, :], t["lnr"][:, 1, :], t["lnr2"][:, 0, :], ALU.add),
            waits=[s5, s6], inc="vC")
        s8 = o("scalar", lambda e: e.activation(
            out=t["lnr"][:, 1, :], in_=t["lnr"][:, 1, :], func=AF.Ln,
            bias=t["eps128"][:, :]), waits=[s7], inc="sC")
        s9 = o("scalar", lambda e: e.activation(
            out=t["lnr"][:, 1, :], in_=t["lnr"][:, 1, :], func=AF.Exp, scale=-0.5),
            waits=[s8], inc="sC")
        sa = o("vector", lambda e: e.tensor_tensor(
            t["lnr2"][:, 1, :], t["lnr"][:, 0, :], t["lnr"][:, 1, :], ALU.mult),
            waits=[s9], inc="vC")
        aps = []
        for nn in range(NCH):
            aps.append(o("vector", lambda e, nn=nn: e.tensor_scalar(
                t["olane"][:, nn, :], t["olane"][:, nn, :],
                t["lnr"][:, 1, nn:nn + 1], t["lnr2"][:, 1, nn:nn + 1],
                ALU.mult, ALU.subtract), waits=[s9, sa], inc="vC"))
        w1 = o("vector", lambda e: e.tensor_tensor(
            t["olane"].ap().rearrange("p a b -> p (a b)"),
            t["olane"].ap().rearrange("p a b -> p (a b)"), t["lnw"][:, 0, :],
            ALU.mult), waits=aps + [consts], inc="vC")
        w2 = o("vector", lambda e: e.tensor_tensor(
            t["olane"].ap().rearrange("p a b -> p (a b)"),
            t["olane"].ap().rearrange("p a b -> p (a b)"), t["lnw"][:, 1, :],
            ALU.add), waits=[w1], inc="vC")
        fo = dma("sync", g["odr"][l, :, :, :], t["olane"][:], waits=[w2], chain="odma")
        ol_free = [fo]
        lnfin = fo
    o("sync", lambda e: e.nop(), waits=[lnfin])


_L1B_NC = None


def l1b_consts():
    c = {}
    NEGM = -1.0e5
    su = np.full((128, 128), NEGM, np.float32)
    sl = np.full((128, 128), NEGM, np.float32)
    si = np.full((128, 128), NEGM, np.float32)
    for a_ in range(128):
        for b_ in range(128):
            if a_ < b_:
                su[a_, b_] = 0.0
                si[a_, b_] = 0.0
            if b_ < a_:
                sl[a_, b_] = 0.0
            if a_ == b_:
                si[a_, b_] = 0.0
    c["smaskS8"] = np.ascontiguousarray(np.repeat(su[:, None, :], 8, axis=1)).astype(BF)
    c["smaskL8"] = np.ascontiguousarray(np.repeat(sl[:, None, :], 8, axis=1)).astype(BF)
    c["smaskI8"] = np.ascontiguousarray(np.repeat(si[:, None, :], 8, axis=1)).astype(BF)
    ti = np.zeros((128, 128), np.float32)
    for a_ in range(128):
        ti[a_, a_:] = 1.0
    c["triuI"] = ti.astype(BF)
    rm = np.ones((8, S), np.float32)
    rm[:, 0::128] = 0.0
    c["rmask"] = rm
    return c


_L1B_CONSTS = None


def l1b_inmaps(inputs, l1a_out):
    global _L1B_CONSTS
    hc = host_consts()
    if _L1B_CONSTS is None:
        _L1B_CONSTS = l1b_consts()
    bc = _L1B_CONSTS
    cw = {k: np.asarray(inputs[f"L1_attn_conv_{k}_w"], np.float32) for k in "qkv"}
    cb = {k: np.asarray(inputs[f"L1_attn_conv_{k}_b"], np.float32) for k in "qkv"}
    lnw = np.asarray(inputs["L1_attn_ln_w"], np.float32)
    lnb = np.asarray(inputs["L1_attn_ln_b"], np.float32)
    lnwrep = np.zeros((128, 2, S), np.float32)
    lnwrep[:, 0, :] = np.tile(lnw, NCH)[None, :]
    lnwrep[:, 1, :] = np.tile(lnb, NCH)[None, :]
    lnwrep = lnwrep.astype(BF)
    ab = l1a_out["ab"]
    in_maps = []
    for core in range(8):
        b, hh = core // 2, core % 2
        heads = [hh * 8 + l for l in range(8)]
        qraw = np.stack([l1a_out["oq"][b, :, h * 128:(h + 1) * 128].T for h in heads])
        kraw = np.stack([l1a_out["ok"][b, :, h * 128:(h + 1) * 128].T for h in heads])
        vraw = np.stack([l1a_out["ov"][b, :, h * 128:(h + 1) * 128].T for h in heads])
        convw = np.stack([np.stack([cw[k][h * 128:(h + 1) * 128, 0, :] for h in heads])
                          for k in "qkv"])
        convb = np.stack([np.stack([cb[k][h * 128:(h + 1) * 128] for h in heads])
                          for k in "qkv"])[..., None]
        abt = np.zeros((128, 2, 256), np.float32)
        abr = np.zeros((2, 8, S), np.float32)
        for li, h in enumerate(heads):
            a_ = ab[b, :, h]
            b_ = ab[b, :, 16 + h]
            abr[0, li] = a_
            abr[1, li] = b_
            abt[:, 0, li * 32:(li + 1) * 32] = a_.reshape(NCH, 128).T
            abt[:, 1, li * 32:(li + 1) * 32] = b_.reshape(NCH, 128).T
        m = {
            "qraw": qraw.astype(BF), "kraw": kraw.astype(BF), "vraw": vraw.astype(BF),
            "convw": convw, "convb": convb,
            "ab_t": abt, "abr": abr, "rmask": bc["rmask"].astype(BF),
            "lnwrep": lnwrep,
            "smaskS8": bc["smaskS8"], "smaskL8": bc["smaskL8"],
            "smaskI8": bc["smaskI8"], "triuI": bc["triuI"],
            "ident": hc["ident"], "ones_row": hc["ones_row"],
        }
        in_maps.append(m)
    return in_maps


def run_l1b(inputs, l1a_out, dbg=False):
    global _L1B_NC
    if dbg:
        _L1B_NC = build_l1b(dbg=True)
    elif _L1B_NC is None:
        _L1B_NC = build_l1b()
    in_maps = l1b_inmaps(inputs, l1a_out)
    res = run_bass_kernel_spmd(_L1B_NC, in_maps, core_ids=list(range(8)))
    o = np.zeros((B, S, D), np.float32)
    dbgout = {}
    for core in range(8):
        b, hh = core // 2, core % 2
        od = np.asarray(res.results[core]["odr"]).astype(np.float32).reshape(8, 128, NCH, 128)
        for li in range(8):
            h = hh * 8 + li
            o[b, :, h * 128:(h + 1) * 128] = od[li].transpose(1, 0, 2).reshape(S, 128)
        if dbg:
            dbgout[core] = {nm: np.asarray(res.results[core][nm + "dr"]).astype(np.float32)
                            for nm in ("qc", "kc", "vc")}
    if dbg:
        return o, dbgout
    return o



def build_l1c():
    nc = bass.Bass("TRN2", target_bir_lowering=False)
    ap = lambda *a, **k: nc.dram_tensor(*a, **k).ap()
    g = {}
    g["x2T"] = ap("x2T", [D, TLOC], BF16, kind="ExternalInput")
    g["oT"] = ap("oT", [D, TLOC], BF16, kind="ExternalInput")
    g["gT"] = ap("gT", [D, TLOC], BF16, kind="ExternalInput")
    g["wo"] = ap("wo", [D, D], F32, kind="ExternalInput")
    g["wr"] = ap("wr", [128, FC], F32, kind="ExternalInput")
    g["br"] = ap("br", [1, 1], F32, kind="ExternalInput")
    g["ones_col"] = ap("ones_col", [128, 1], BF16, kind="ExternalInput")
    g["ones_row"] = ap("ones_row", [1, 128], BF16, kind="ExternalInput")
    g["x3T"] = ap("x3T", [D, TLOC], BF16, kind="ExternalOutput")
    g["lg"] = ap("lg", [1, TLOC], F32, kind="ExternalOutput")

    ctx = ExitStack()
    with ctx:
        E = ctx.enter_context
        t = {}
        t["xres"] = E(nc.sbuf_tensor("xres", [128, FC, TLOC], BF16))
        t["og"] = E(nc.sbuf_tensor("og", [128, FC, TLOC], BF16))
        t["wbuf"] = E(nc.sbuf_tensor("wbuf", [128, 2, FC, 128], BF16))
        t["gs"] = E(nc.sbuf_tensor("gs", [128, 2048], BF16))
        t["onec"] = E(nc.sbuf_tensor("onec", [128, 1], BF16))
        t["oner"] = E(nc.sbuf_tensor("oner", [1, 128], BF16))
        t["eps1"] = E(nc.sbuf_tensor("eps1", [1, 1], F32))
        t["brs"] = E(nc.sbuf_tensor("brs", [1, 1], F32))
        t["sq"] = E(nc.sbuf_tensor("sq", [128, 2, 512], BF16))
        t["stat"] = E(nc.sbuf_tensor("stat", [1, 4, 512], F32))
        t["statb"] = E(nc.sbuf_tensor("statb", [1, 2, 512], BF16))
        t["tmp"] = E(nc.sbuf_tensor("tmp", [128, 512], BF16))
        t["lgr"] = E(nc.sbuf_tensor("lgr", [1, 2, 512], F32))
        t["wrt"] = E(nc.sbuf_tensor("wrt", [128, FC], BF16))
        pA = E(nc.psum_tensor("pA", [128, 1024], F32))
        pB = E(nc.psum_tensor("pB", [128, 1024], F32))
        pC = E(nc.psum_tensor("pC", [128, 1024], F32))

        sch = Sch(nc, ctx)
        emit_l1c(sch, g, t, pA, pB, pC)
        sch.emit(ctx)
    return nc


def emit_l1c(sch, g, t, pA, pB, pC):
    o = sch.op
    dma = sch.dma
    c1 = dma("sync", t["onec"][:], g["ones_col"][:, :], chain="const")
    c2 = dma("sync", t["oner"][:], g["ones_row"][:, :], chain="const")
    c3 = dma("sync", t["brs"][:], g["br"][:, :], chain="const")
    consts = c3
    epsr = o("vector", lambda e: e.memset(t["eps1"][:], 1e-5), inc="vC")
    x2r = g["x2T"].rearrange("(c p) s -> p c s", p=128)
    oTr = g["oT"].rearrange("(c p) s -> p c s", p=128)
    gTr = g["gT"].rearrange("(c p) s -> p c s", p=128)

    lx = dma("gpsimd", t["xres"][:], x2r[:, :, :], chain="xld")
    lo_ = dma("gpsimd", t["og"][:], oTr[:, :, :], chain="xld")
    gdone = []
    gfree = None
    for c in range(FC):
        lg_ = dma("scalar", t["gs"][:], gTr[:, c, :], waits=[gfree], chain="gld")
        sg = o("scalar", lambda e: e.activation(
            out=t["gs"][:], in_=t["gs"][:], func=AF.Sigmoid), waits=[lg_], inc="sC")
        mu = o("vector", lambda e, c=c: e.tensor_tensor(
            t["og"][:, c, :], t["og"][:, c, :], t["gs"][:], ALU.mult),
            waits=[sg, lo_], inc="vC")
        gfree = mu
        gdone.append(mu)

    wore = g["wo"].rearrange("(hc p) eo -> p hc eo", p=128)
    wfree = [None, None]
    d_free = [None, None]
    wo_done = []
    for ec in range(FC):
        par = ec % 2
        lw = dma("gpsimd", t["wbuf"][:, par], wore[:, :, ec * 128:(ec + 1) * 128],
                 waits=[wfree[par]], chain="wld")
        for th in range(4):
            sel = th % 2
            psd = (pA, pB)[sel]
            mm = None
            for hc in range(FC):
                mm = o("tensor", lambda e, hc=hc, th=th, par=par, psd=psd: e.matmul(
                    psd[:, 0:512], t["wbuf"][:, par, hc, :],
                    t["og"][:, hc, th * 512:(th + 1) * 512],
                    start=(hc == 0), stop=(hc == FC - 1)),
                    waits=[lw, gdone[-1], d_free[sel]], inc="tC")
            ev = o("vector", lambda e, ec=ec, th=th, psd=psd: e.tensor_tensor(
                t["xres"][:, ec, th * 512:(th + 1) * 512], psd[:, 0:512],
                t["xres"][:, ec, th * 512:(th + 1) * 512], ALU.add),
                waits=[mm, lx], inc="vC")
            d_free[sel] = ev
            wo_done.append(ev)
        wfree[par] = mm

    lnst = [None]
    h3_done = []
    for tt4 in range(2):
        xf = lambda c, sl, tt4=tt4: t["xres"][:, c, tt4 * 1024 + sl.start:tt4 * 1024 + sl.stop]
        hf = lambda c, sl, tt4=tt4: t["og"][:, c, tt4 * 1024 + sl.start:tt4 * 1024 + sl.stop]
        hd, lnst[0] = emit_ln(sch, t, pA, pB, xf, hf,
                              [wo_done[-1], epsr, consts], lnst[0])
        h3_done.append(hd[-1])
    add_done = None
    for c in range(FC):
        add_done = o("vector", lambda e, c=c: e.tensor_tensor(
            t["xres"][:, c, :], t["xres"][:, c, :], t["og"][:, c, :], ALU.add),
            waits=[h3_done[-1]], inc="vC")
    so = dma("sync", g["x3T"].rearrange("(c p) s -> p c s", p=128), t["xres"][:],
             waits=[add_done], chain="out")
    lwr = dma("gpsimd", t["wrt"][:], g["wr"][:, :], chain="wld")
    lg_done = []
    ps_free = [lnst[0]["pa"], None]
    for th in range(4):
        sel = th % 2
        tt4, half = th // 2, th % 2
        mm = None
        for c in range(FC):
            mm = o("tensor", lambda e, c=c, tt4=tt4, half=half, sel=sel: e.matmul(
                pC[0:1, sel * 512:sel * 512 + 512], t["wrt"][:, c:c + 1],
                t["og"][:, c, tt4 * 1024 + half * 512:tt4 * 1024 + half * 512 + 512],
                start=(c == 0), stop=(c == FC - 1)),
                waits=[lwr, h3_done[-1], ps_free[sel]], inc="tC")
        ev = o("vector", lambda e, th=th, sel=sel: e.tensor_scalar(
            t["lgr"][:, sel, :], pC[0:1, sel * 512:sel * 512 + 512],
            t["brs"][:, :], None, ALU.add, ALU.bypass), waits=[mm, consts], inc="vC")
        dt_ = dma("sync", g["lg"][:, th * 512:(th + 1) * 512], t["lgr"][:, sel, :],
                  waits=[ev], chain="out")
        ps_free[sel] = dt_
        lg_done.append(dt_)
    o("sync", lambda e: e.nop(), waits=[lg_done[-1], so])


_L1C_NC = None


def run_l1c(inputs, x2, o_gate, l1a_out):
    global _L1C_NC
    hc = host_consts()
    if _L1C_NC is None:
        _L1C_NC = build_l1c()
    in_maps = []
    for core in range(8):
        b, hh = core // 2, core % 2
        sl = slice(hh * TLOC, (hh + 1) * TLOC)
        m = {
            "x2T": np.ascontiguousarray(x2[b, sl].T).astype(BF),
            "oT": np.ascontiguousarray(o_gate[b, sl].T).astype(BF),
            "gT": np.ascontiguousarray(l1a_out["og"][b, sl].T).astype(BF),
            "wo": np.asarray(inputs["L1_attn_Wo"], np.float32),
            "wr": tile_vec(inputs["L1_ffn_Wr"]),
            "br": np.asarray(inputs["L1_ffn_br"], np.float32).reshape(1, 1),
            "ones_col": hc["ones_col"],
            "ones_row": hc["ones_row"],
        }
        in_maps.append(m)
    res = run_bass_kernel_spmd(_L1C_NC, in_maps, core_ids=list(range(8)))
    x3 = np.zeros((B, S, D), np.float32)
    lg = np.zeros((B, S), np.float32)
    for core in range(8):
        b, hh = core // 2, core % 2
        sl = slice(hh * TLOC, (hh + 1) * TLOC)
        x3[b, sl] = np.asarray(res.results[core]["x3T"]).astype(np.float32).T
        lg[b, sl] = np.asarray(res.results[core]["lg"]).astype(np.float32)[0]
    return x3, lg



def build_l1d():
    Sch.PARANOID = False
    nc = bass.Bass("TRN2", target_bir_lowering=False)
    ap = lambda *a, **k: nc.dram_tensor(*a, **k).ap()
    g = {}
    g["x3t"] = ap("x3t", [TLOC, D], BF16, kind="ExternalInput")
    g["x3T"] = ap("x3T", [D, TLOC], BF16, kind="ExternalInput")
    g["lgb"] = ap("lgb", [128, 32], F32, kind="ExternalInput")
    g["w1"] = ap("w1", [D, DFF], F32, kind="ExternalInput")
    g["w2"] = ap("w2", [DFF, D], F32, kind="ExternalInput")
    g["b1t"] = ap("b1t", [128, 64], F32, kind="ExternalInput")
    g["b2r"] = ap("b2r", [1, D], BF16, kind="ExternalInput")
    g["jiota"] = ap("jiota", [128, 4], F32, kind="ExternalInput")
    g["iota512"] = ap("iota512", [128, 512], F32, kind="ExternalInput")
    g["sue"] = ap("sue", [128, 128], F32, kind="ExternalInput")
    g["ident"] = ap("ident", [128, 128], BF16, kind="ExternalInput")
    g["identf"] = ap("identf", [128, 128], F32, kind="ExternalInput")
    g["ones_row"] = ap("ones_row", [1, 128], BF16, kind="ExternalInput")
    g["x4T"] = ap("x4T", [D, TLOC], F32, kind="ExternalOutput")
    g["sdr"] = nc.dram_tensor("sdr", [1, 2, 128], F32).ap()
    g["cmdr"] = nc.dram_tensor("cmdr", [1, 32, 128], F32).ap()

    ctx = ExitStack()
    with ctx:
        E = ctx.enter_context
        t = {}
        t["big8"] = E(nc.sbuf_tensor("big8", [128, FC, 2048], BF16))
        t["h1F"] = E(nc.sbuf_tensor("h1F", [128, 64, 512], BF16))
        t["hgF"] = E(nc.sbuf_tensor("hgF", [128, FC, 512], BF16))
        t["hg"] = E(nc.sbuf_tensor("hg", [128, 4, 2048], BF16))
        t["P"] = E(nc.sbuf_tensor("P", [128, 4, 2048], BF16))
        t["o2w"] = t["hg"]
        t["wbuf"] = E(nc.sbuf_tensor("wbuf", [128, 2, FC, 128], BF16))
        t["w2b"] = E(nc.sbuf_tensor("w2b", [128, 2, 512], BF16))
        t["lgb"] = E(nc.sbuf_tensor("lgbs", [128, 32], F32))
        t["m01"] = E(nc.sbuf_tensor("m01", [128, 32], F32))
        t["exw"] = E(nc.sbuf_tensor("exw", [128, 32], F32))
        t["exwb"] = E(nc.sbuf_tensor("exwb", [128, 16], BF16))
        t["cum"] = E(nc.sbuf_tensor("cum", [128, 16], F32))
        t["cm2"] = E(nc.sbuf_tensor("cm2", [128, 32], F32))
        t["rows"] = E(nc.sbuf_tensor("rows", [1, 2, 512], F32))
        t["red"] = E(nc.sbuf_tensor("red", [128, 1], F32))
        t["r1x"] = E(nc.sbuf_tensor("r1x", [1, 128], F32))
        t["sc"] = E(nc.sbuf_tensor("sc", [1, 8], F32))
        t["thr"] = E(nc.sbuf_tensor("thr", [128, 2], F32))
        t["wcol"] = E(nc.sbuf_tensor("wcol", [128, 4], F32))
        t["onef"] = E(nc.sbuf_tensor("onef", [128, 1], F32))
        t["onerf"] = E(nc.sbuf_tensor("onerf", [1, 128], F32))
        t["oner"] = E(nc.sbuf_tensor("onerb", [1, 128], BF16))
        t["ident"] = E(nc.sbuf_tensor("identt", [128, 128], BF16))
        t["jio"] = E(nc.sbuf_tensor("jio", [128, 4], F32))
        t["io512"] = E(nc.sbuf_tensor("io512", [128, 512], F32))
        t["sue"] = E(nc.sbuf_tensor("suet", [128, 128], F32))
        t["eps1"] = E(nc.sbuf_tensor("eps1", [128, 1], F32))
        t["lnc"] = E(nc.sbuf_tensor("lnc", [128, 4], F32))
        t["lnc2"] = E(nc.sbuf_tensor("lnc2", [128, 4], F32))
        t["sq"] = t["hgF"].ap().rearrange("p c n -> p (c n)")[:, 0:2048]
        t["fo"] = E(nc.sbuf_tensor("fo", [128, 2, 512], F32))
        t["tpe"] = E(nc.sbuf_tensor("tpe", [32, 128], F32))
        t["b1t"] = E(nc.sbuf_tensor("b1ts", [128, 64], F32))
        t["b2s"] = E(nc.sbuf_tensor("b2s", [1, D], BF16))
        t["identf"] = E(nc.sbuf_tensor("identfs", [128, 128], F32))
        pA = E(nc.psum_tensor("pA", [128, 1024], F32))
        pB = E(nc.psum_tensor("pB", [128, 1024], F32))
        pC = E(nc.psum_tensor("pC", [128, 1024], F32))
        pT = E(nc.psum_tensor("pT", [128, 512], BF16))

        sch = Sch(nc, ctx)
        emit_l1d(sch, g, t, pA, pB, pC, pT)
        sch.emit(ctx)
    Sch.PARANOID = False
    return nc


def emit_l1d(sch, g, t, pA, pB, pC, pT):
    o = sch.op
    dma = sch.dma
    consts = None
    for dst, src in [(t["lgb"][:], g["lgb"][:, :]), (t["jio"][:], g["jiota"][:, :]),
                     (t["io512"][:], g["iota512"][:, :]), (t["sue"][:], g["sue"][:, :]),
                     (t["ident"][:], g["ident"][:, :]), (t["oner"][:], g["ones_row"][:, :]),
                     (t["identf"][:], g["identf"][:, :]), (t["b1t"][:], g["b1t"][:, :]),
                     (t["b2s"][:], g["b2r"][:, :])]:
        consts = dma("sync", dst, src, chain="const")
    onefm = o("vector", lambda e: e.memset(t["onef"][:], 1.0), inc="vC")
    onerfm = o("vector", lambda e: e.memset(t["onerf"][:], 1.0), inc="vC")
    epsm = o("vector", lambda e: e.memset(t["eps1"][:], 1e-5), inc="vC")
    lx3 = dma("gpsimd", t["big8"][:],
              g["x3t"].rearrange("(tc p) dd -> p tc dd", p=128), chain="xld")

    sch.barrier()
    lo1 = o("vector", lambda e: e.memset(t["sc"][:, 0:1], -1000.0), inc="vC")
    hi1 = o("vector", lambda e: e.memset(t["sc"][:, 1:2], 1000.0), inc="vC")
    lastb = [lo1, hi1]
    pd_free = [None]
    for it in range(34):
        a1 = o("vector", lambda e: e.tensor_tensor(t["sc"][:, 2:3], t["sc"][:, 0:1],
                                                   t["sc"][:, 1:2], ALU.add),
               waits=lastb, inc="vC")
        a2 = o("vector", lambda e: e.tensor_scalar(t["sc"][:, 2:3], t["sc"][:, 2:3], 0.5,
                                                   None, ALU.mult, ALU.bypass),
               waits=[a1], inc="vC")
        bmm = o("tensor", lambda e: e.matmul(pC[:, 0:1], t["onerf"][:], t["sc"][:, 2:3],
                                             start=True, stop=True),
                waits=[a2, onerfm] + pd_free, inc="tC")
        c1 = o("vector", lambda e: e.tensor_scalar(t["m01"][:], t["lgb"][:],
                                                   pC[:, 0:1], None, ALU.is_ge, ALU.bypass),
               waits=[bmm, consts], inc="vC")
        c2 = o("vector", lambda e: e.tensor_reduce(t["red"][:], t["m01"][:], AX.X, ALU.add),
               waits=[c1], inc="vC")
        cmm = o("tensor", lambda e: e.matmul(pC[0:1, 4:5], t["onef"][:], t["red"][:],
                                             start=True, stop=True),
                waits=[c2, onefm], inc="tC")
        g1 = o("vector", lambda e: e.tensor_scalar(t["sc"][:, 3:4], pC[0:1, 4:5],
                                                   float(K1), None, ALU.is_ge, ALU.bypass),
               waits=[cmm], inc="vC")
        g2 = o("vector", lambda e: e.tensor_scalar(t["sc"][:, 4:5], t["sc"][:, 3:4], -1.0,
                                                   1.0, ALU.mult, ALU.add),
               waits=[g1], inc="vC")
        dl = o("vector", lambda e: e.tensor_tensor(t["sc"][:, 5:6], t["sc"][:, 2:3],
                                                   t["sc"][:, 0:1], ALU.subtract),
               waits=[a2], inc="vC")
        l1 = o("vector", lambda e: e.scalar_tensor_tensor(
            t["sc"][:, 0:1], t["sc"][:, 5:6], t["sc"][:, 3:4], t["sc"][:, 0:1],
            ALU.mult, ALU.add), waits=[g1, dl], inc="vC")
        dh = o("vector", lambda e: e.tensor_tensor(t["sc"][:, 5:6], t["sc"][:, 2:3],
                                                   t["sc"][:, 1:2], ALU.subtract),
               waits=[l1], inc="vC")
        h1 = o("vector", lambda e: e.scalar_tensor_tensor(
            t["sc"][:, 1:2], t["sc"][:, 5:6], t["sc"][:, 4:5], t["sc"][:, 1:2],
            ALU.mult, ALU.add), waits=[g2, dh], inc="vC")
        lastb = [l1, h1]
        pd_free = [g1, c1]

    sch.barrier()
    bmf = o("tensor", lambda e: e.matmul(pC[:, 0:1], t["onerf"][:], t["sc"][:, 0:1],
                                         start=True, stop=True),
            waits=lastb + pd_free, inc="tC")
    th1 = o("vector", lambda e: e.tensor_copy(t["thr"][:, 0:1], pC[:, 0:1]),
            waits=[bmf], inc="vC")
    th2 = o("vector", lambda e: e.tensor_scalar(t["thr"][:, 1:2], t["thr"][:, 0:1], -1.0,
                                                None, ALU.mult, ALU.bypass),
            waits=[th1], inc="vC")
    m1 = o("vector", lambda e: e.tensor_scalar(t["m01"][:], t["lgb"][:], t["thr"][:, 0:1],
                                               None, ALU.is_ge, ALU.bypass),
          waits=[th1], inc="vC")
    e1 = o("scalar", lambda e: e.activation(out=t["exw"][:], in_=t["lgb"][:], func=AF.Exp,
                                            bias=t["thr"][:, 1:2]),
          waits=[th2, consts], inc="sC")
    e2 = o("vector", lambda e: e.tensor_tensor(t["exw"][:], t["exw"][:], t["m01"][:],
                                               ALU.mult), waits=[e1, m1], inc="vC")
    e3 = o("vector", lambda e: e.tensor_reduce(t["red"][:], t["exw"][:], AX.X, ALU.add),
          waits=[e2], inc="vC")
    smm = o("tensor", lambda e: e.matmul(pC[0:1, 4:5], t["onef"][:], t["red"][:],
                                         start=True, stop=True), waits=[e3], inc="tC")
    rs1 = o("vector", lambda e: e.reciprocal(t["sc"][:, 6:7], pC[0:1, 4:5]),
            waits=[smm], inc="vC")
    eb = o("vector", lambda e: e.tensor_copy(t["exwb"][:], t["exw"][:, 0:16]),
          waits=[e2], inc="vC")

    sch.barrier()
    im1 = o("tensor", lambda e: e.matmul(pA[:, 0:16], t["sue"][:], t["m01"][:, 0:16],
                                         start=True, stop=True),
            waits=[m1, consts], inc="tC")
    cs1 = o("tensor", lambda e: e.matmul(pA[0:1, 16:32], t["onef"][:], t["m01"][:, 0:16],
                                         start=True, stop=True),
            waits=[m1, onefm], inc="tC")
    sc0 = o("vector", lambda e: e.tensor_copy(t["rows"][0:1, 0, 0:16], pA[0:1, 16:32]),
            waits=[cs1], inc="vC")
    sc1 = o("vector", lambda e: e.tensor_tensor_scan(
        t["cm2"][0:1, 0:16], t["rows"][0:1, 0, 0:16], t["rows"][0:1, 0, 0:16],
        0.0, ALU.add, ALU.bypass), waits=[sc0], inc="vC")
    sc2 = o("vector", lambda e: e.tensor_tensor(t["cm2"][0:1, 16:32], t["cm2"][0:1, 0:16],
                                                pA[0:1, 16:32], ALU.subtract),
            waits=[sc1], inc="vC")
    com = o("tensor", lambda e: e.matmul(pB[:, 0:16], t["onerf"][:], t["cm2"][0:1, 16:32],
                                         start=True, stop=True),
            waits=[sc2, onerfm], inc="tC")
    cu1 = o("vector", lambda e: e.tensor_copy(t["cum"][:], pA[:, 0:16]),
            waits=[im1, sc2], inc="vC")
    cu2 = o("vector", lambda e: e.tensor_tensor(t["cum"][:], t["cum"][:], pB[:, 0:16],
                                                ALU.add), waits=[cu1, com], inc="vC")

    sch.barrier()
    PT = t["h1F"]
    pt_done = []
    for tc in range(FC):
        p1 = o("vector", lambda e, tc=tc: e.tensor_scalar(
            PT[:, tc, :], t["io512"][:], t["cum"][:, tc:tc + 1], None,
            ALU.is_equal, ALU.bypass), waits=[cu2, consts], inc="vC")
        p2 = o("vector", lambda e, tc=tc: e.tensor_scalar(
            PT[:, tc, :], PT[:, tc, :], t["m01"][:, tc:tc + 1], None,
            ALU.mult, ALU.bypass), waits=[p1], inc="vC")
        pt_done.append(p2)

    sch.barrier()
    cm1 = o("vector", lambda e: e.tensor_copy(t["cm2"][:, 0:16], t["cum"][:]),
            waits=[cu2], inc="vC")
    cm2b = o("vector", lambda e: e.tensor_copy(t["cm2"][:, 16:32], t["m01"][:, 0:16]),
             waits=[m1, sc2], inc="vC")
    tp = o("tensor", lambda e: e.transpose(pA[0:32, 0:128], t["cm2"][:, :], t["identf"][:]),
           waits=[cm1, cm2b, cu1, im1, cs1, consts], inc="tC")
    te = o("vector", lambda e: e.tensor_copy(t["tpe"][:], pA[0:32, 0:128]),
           waits=[tp], inc="vC")
    td = dma("sync", g["cmdr"][0, :, :], t["tpe"][:], waits=[te], chain="rr")
    sch.barrier()
    p_done = []
    rep_free = [None]
    cmr = g["cmdr"].rearrange("x (a tc) pp -> x a (tc pp)", a=2)
    for tt in range(4):
        tl = dma("sync", t["rows"][0:1, :, :],
                 cmr[0:1, :, tt * 512:(tt + 1) * 512],
                 waits=[td, rep_free[0]], chain="rr")
        rm1 = o("tensor", lambda e, tt=tt: e.matmul(
            pB[:, 0:512], t["onerf"][:], t["rows"][0:1, 0, :],
            start=True, stop=True), waits=[tl, rep_free[0], onerfm], inc="tC")
        rm2 = o("tensor", lambda e, tt=tt: e.matmul(
            pB[:, 512:1024], t["onerf"][:], t["rows"][0:1, 1, :],
            start=True, stop=True), waits=[tl], inc="tC")
        mre = o("vector", lambda e: e.tensor_copy(t["fo"][:, 0, :], pB[:, 512:1024]),
                waits=[rm2], inc="vC")
        for jc in range(4):
            q1 = o("vector", lambda e, jc=jc, tt=tt: e.tensor_scalar(
                t["P"][:, jc, tt * 512:(tt + 1) * 512], pB[:, 0:512],
                t["jio"][:, jc:jc + 1], None, ALU.is_equal, ALU.bypass),
                waits=[rm1, consts], inc="vC")
            q2 = o("vector", lambda e, jc=jc, tt=tt: e.tensor_tensor(
                t["P"][:, jc, tt * 512:(tt + 1) * 512],
                t["P"][:, jc, tt * 512:(tt + 1) * 512], t["fo"][:, 0, :], ALU.mult),
                waits=[q1, mre], inc="vC")
            p_done.append(q2)
        rep_free[0] = p_done[-1]

    sch.barrier()
    hg_done = []
    gp_free = [None, None]
    for jb in range(4):
        for ft in range(4):
            sel = ft % 2
            psd = (pA, pB)[sel]
            mm = None
            for tc in range(FC):
                mm = o("tensor", lambda e, jb=jb, ft=ft, tc=tc, psd=psd: e.matmul(
                    psd[:, 0:512], PT[:, tc, jb * 128:(jb + 1) * 128],
                    t["big8"][:, tc, ft * 512:(ft + 1) * 512],
                    start=(tc == 0), stop=(tc == FC - 1)),
                    waits=[pt_done[-1], lx3, gp_free[sel], tp], inc="tC")
            ev = o("vector", lambda e, jb=jb, ft=ft, psd=psd: e.tensor_copy(
                t["hg"][:, jb, ft * 512:(ft + 1) * 512], psd[:, 0:512]),
                waits=[mm], inc="vC")
            gp_free[sel] = ev
            hg_done.append(ev)
    wg = []
    for jb in range(4):
        for tc in range(FC):
            wg.append(o("tensor", lambda e, jb=jb, tc=tc: e.matmul(
                pC[:, 5 + jb:6 + jb], PT[:, tc, jb * 128:(jb + 1) * 128],
                t["exwb"][:, tc:tc + 1], start=(tc == 0), stop=(tc == FC - 1)),
                waits=[pt_done[-1], eb, rs1], inc="tC"))
    rsb = o("tensor", lambda e: e.matmul(pC[:, 9:10], t["onerf"][:], t["sc"][:, 6:7],
                                         start=True, stop=True),
            waits=[rs1, onerfm], inc="tC")
    rse = o("vector", lambda e: e.tensor_copy(t["thr"][:, 1:2], pC[:, 9:10]),
            waits=[rsb], inc="vC")
    wc1 = o("vector", lambda e: e.tensor_scalar(
        t["wcol"][:], pC[:, 5:9], t["thr"][:, 1:2], None, ALU.mult, ALU.bypass),
        waits=[wg[-1], rse], inc="vC")

    sch.barrier()
    ln_done = []
    for jb in range(4):
        s1 = o("vector", lambda e, jb=jb: e.tensor_reduce(
            t["lnc"][:, jb:jb + 1], t["hg"][:, jb, :], AX.X, ALU.add),
            waits=[hg_done[4 * jb + 3]], inc="vC")
        s2 = o("scalar", lambda e, jb=jb: e.activation(
            out=t["sq"], in_=t["hg"][:, jb, :], func=AF.Square),
            waits=[hg_done[4 * jb + 3], ln_done[-1] if ln_done else None], inc="sC")
        s3 = o("vector", lambda e, jb=jb: e.tensor_reduce(
            t["lnc2"][:, jb:jb + 1], t["sq"], AX.X, ALU.add), waits=[s2], inc="vC")
        s4 = o("vector", lambda e, jb=jb: e.tensor_scalar(
            t["lnc"][:, jb:jb + 1], t["lnc"][:, jb:jb + 1], 1.0 / D, None,
            ALU.mult, ALU.bypass), waits=[s1], inc="vC")
        s5 = o("vector", lambda e, jb=jb: e.scalar_tensor_tensor(
            t["red"][:], t["lnc"][:, jb:jb + 1], -1.0, t["lnc"][:, jb:jb + 1],
            ALU.mult, ALU.mult), waits=[s4], inc="vC")
        s6 = o("vector", lambda e, jb=jb: e.tensor_scalar(
            t["lnc2"][:, jb:jb + 1], t["lnc2"][:, jb:jb + 1], 1.0 / D, None,
            ALU.mult, ALU.bypass), waits=[s3], inc="vC")
        s7 = o("vector", lambda e, jb=jb: e.tensor_tensor(
            t["lnc2"][:, jb:jb + 1], t["lnc2"][:, jb:jb + 1], t["red"][:], ALU.add),
            waits=[s5, s6], inc="vC")
        s8 = o("scalar", lambda e, jb=jb: e.activation(
            out=t["lnc2"][:, jb:jb + 1], in_=t["lnc2"][:, jb:jb + 1], func=AF.Ln,
            bias=t["eps1"][:, :]), waits=[s7, epsm], inc="sC")
        s9 = o("scalar", lambda e, jb=jb: e.activation(
            out=t["lnc2"][:, jb:jb + 1], in_=t["lnc2"][:, jb:jb + 1], func=AF.Exp,
            scale=-0.5), waits=[s8], inc="sC")
        sa = o("vector", lambda e, jb=jb: e.tensor_tensor(
            t["red"][:], t["lnc"][:, jb:jb + 1], t["lnc2"][:, jb:jb + 1], ALU.mult),
            waits=[s9], inc="vC")
        sb = o("vector", lambda e, jb=jb: e.tensor_scalar(
            t["hg"][:, jb, :], t["hg"][:, jb, :], t["lnc2"][:, jb:jb + 1],
            t["red"][:, :], ALU.mult, ALU.subtract), waits=[sa], inc="vC")
        ln_done.append(sb)

    sch.barrier()
    tr_done = []
    pt_free = [None]
    for fc in range(FC):
        tms = []
        for jb in range(4):
            tms.append(o("tensor", lambda e, fc=fc, jb=jb: e.transpose(
                pT[:, jb * 128:(jb + 1) * 128],
                t["hg"][:, jb, fc * 128:(fc + 1) * 128], t["ident"][:]),
                waits=[ln_done[jb], pt_free[0], consts], inc="tC"))
        ev = o("scalar", lambda e, fc=fc: e.activation(
            out=t["hgF"][:, fc, :], in_=pT[:, 0:512], func=AF.Copy),
            waits=tms, inc="sC")
        pt_free[0] = ev
        tr_done.append(ev)

    sch.barrier()
    w1re = g["w1"].rearrange("(c p) ff -> p c ff", p=128)
    h1_done = []
    w1_free = [None, None]
    k_free = [None, None]
    for dffc in range(64):
        par = dffc % 2
        lw = dma("gpsimd", t["wbuf"][:, par], w1re[:, :, dffc * 128:(dffc + 1) * 128],
                 waits=[w1_free[par]], chain="wld")
        mm = None
        psd = (pA, pB)[par]
        for c in range(FC):
            mm = o("tensor", lambda e, c=c, par=par, psd=psd: e.matmul(
                psd[:, 0:512], t["wbuf"][:, par, c, :], t["hgF"][:, c, :],
                start=(c == 0), stop=(c == FC - 1)),
                waits=[lw, tr_done[-1], k_free[par], wc1, wg[-1]], inc="tC")
        ev = o("scalar", lambda e, dffc=dffc, psd=psd: e.activation(
            out=t["h1F"][:, dffc, :], in_=psd[:, 0:512], func=AF.Gelu,
            bias=t["b1t"][:, dffc:dffc + 1]), waits=[mm, pt_done[-1]], inc="sC")
        k_free[par] = ev
        w1_free[par] = mm
        h1_done.append(ev)
    sch.barrier()
    w2re = g["w2"].rearrange("(dc p) ee -> p dc ee", p=128)
    l_done = []
    w2_free = [None, None]
    for et in range(4):
        mm2 = {}
        for dffc in range(64):
            par = dffc % 2
            lw2 = dma("gpsimd", t["w2b"][:, par, :],
                      w2re[:, dffc, et * 512:(et + 1) * 512],
                      waits=[w2_free[par]], chain="w2ld")
            for jb in range(4):
                psd = (pA, pB)[jb // 2]
                off = (jb % 2) * 512
                mm2[jb] = o("tensor", lambda e, dffc=dffc, jb=jb, par=par, psd=psd, off=off: e.matmul(
                    psd[:, off:off + 512], t["h1F"][:, dffc, jb * 128:(jb + 1) * 128],
                    t["w2b"][:, par, :], start=(dffc == 0), stop=(dffc == 63)),
                    waits=[lw2, h1_done[-1]] + ([l_done[-1]] if dffc == 0 and l_done else []),
                    inc="tC")
            w2_free[par] = mm2[3]
        badd = {}
        for jb in range(4):
            psd = (pA, pB)[jb // 2]
            off = (jb % 2) * 512
            badd[jb] = o("tensor", lambda e, jb=jb, et=et, psd=psd, off=off: e.matmul(
                psd[:, off:off + 512], t["oner"][:],
                t["b2s"][0:1, et * 512:(et + 1) * 512],
                start=False, stop=True), waits=[mm2[jb], consts], inc="tC")
        for jb in range(4):
            psd = (pA, pB)[jb // 2]
            off = (jb % 2) * 512
            l_done.append(o("vector", lambda e, jb=jb, et=et, psd=psd, off=off: e.tensor_scalar(
                t["o2w"][:, jb, et * 512:(et + 1) * 512], psd[:, off:off + 512],
                t["wcol"][:, jb:jb + 1], None, ALU.mult, ALU.bypass),
                waits=[badd[jb], wc1], inc="vC"))

    sch.barrier()
    lxf = dma("gpsimd", t["big8"][:], g["x3T"].rearrange("(c p) s -> p c s", p=128),
              waits=[hg_done[-1]], chain="xld")
    out_done = []
    s_free = [None, None]
    fo_free = [None, None]
    for ec in range(FC):
        for tt in range(4):
            sel = tt % 2
            psd = (pA, pB)[sel]
            mm = None
            for jb in range(4):
                mm = o("tensor", lambda e, ec=ec, tt=tt, jb=jb, psd=psd: e.matmul(
                    psd[:, 0:512], t["o2w"][:, jb, ec * 128:(ec + 1) * 128],
                    t["P"][:, jb, tt * 512:(tt + 1) * 512],
                    start=(jb == 0), stop=(jb == 3)),
                    waits=[l_done[-1], p_done[-1], s_free[sel]], inc="tC")
            fv = o("vector", lambda e, ec=ec, tt=tt, psd=psd, sel=sel: e.tensor_tensor(
                t["fo"][:, sel, :], psd[:, 0:512],
                t["big8"].ap().rearrange("p c n -> p (c n)")[:, ec * 2048 + tt * 512:ec * 2048 + (tt + 1) * 512],
                ALU.add), waits=[mm, lxf, fo_free[sel]], inc="vC")
            s_free[sel] = fv
            dt_ = dma("sync", g["x4T"].rearrange("(c p) s -> p c s", p=128)[:, ec, tt * 512:(tt + 1) * 512],
                      t["fo"][:, sel, :], waits=[fv], chain="out")
            fo_free[sel] = dt_
            out_done.append(dt_)
    o("sync", lambda e: e.nop(), waits=[out_done[-1]])


_L1D_NC = None
_L1D_CONSTS = None


def l1d_consts():
    c = {}
    c["jiota"] = np.ascontiguousarray(
        (np.arange(4)[None, :] * 128 + np.arange(128)[:, None]).astype(np.float32))
    c["iota512"] = np.ascontiguousarray(
        np.tile(np.arange(512, dtype=np.float32)[None, :], (128, 1)))
    sue = np.zeros((128, 128), np.float32)
    for p2 in range(128):
        sue[0:p2, p2] = 1.0
    c["sue"] = sue
    c["identf"] = np.eye(128, dtype=np.float32)
    return c


def run_l1d(inputs, x3, lg):
    global _L1D_NC, _L1D_CONSTS
    hc = host_consts()
    if _L1D_CONSTS is None:
        _L1D_CONSTS = l1d_consts()
    dc = _L1D_CONSTS
    if _L1D_NC is None:
        _L1D_NC = build_l1d()
    in_maps = []
    for core in range(8):
        b, hh = core // 2, core % 2
        pm = perm_for(core)
        lgp = lg[b][pm]
        m = {
            "x3t": np.ascontiguousarray(x3[b, hh * TLOC:(hh + 1) * TLOC]).astype(BF),
            "x3T": np.ascontiguousarray(x3[b, hh * TLOC:(hh + 1) * TLOC].T).astype(BF),
            "lgb": np.ascontiguousarray(lgp.reshape(32, 128).T).astype(np.float32),
            "w1": np.asarray(inputs["L1_ffn_W1"], np.float32),
            "w2": np.asarray(inputs["L1_ffn_W2"], np.float32),
            "b1t": np.ascontiguousarray(
                np.asarray(inputs["L1_ffn_b1"], np.float32).reshape(64, 128).T),
            "b2r": np.asarray(inputs["L1_ffn_b2"], np.float32).reshape(1, D).astype(BF),
            "jiota": dc["jiota"], "iota512": dc["iota512"], "sue": dc["sue"],
            "ident": hc["ident"], "identf": dc["identf"],
            "ones_row": hc["ones_row"],
        }
        in_maps.append(m)
    res = run_bass_kernel_spmd(_L1D_NC, in_maps, core_ids=list(range(8)))
    x4 = np.zeros((B, S, D), np.float32)
    for core in range(8):
        b, hh = core // 2, core % 2
        x4[b, hh * TLOC:(hh + 1) * TLOC] = np.asarray(res.results[core]["x4T"]).astype(np.float32).T
    return x4


def kernel(**inputs):
    x2 = run_l0(inputs)
    l1a = run_l1a(inputs, x2)
    o = run_l1b(inputs, l1a)
    x3, lg = run_l1c(inputs, x2, o, l1a)
    x4 = run_l1d(inputs, x3, lg)
    return x4


# version 06
# speedup vs baseline: 1.2725x; 1.0362x over previous
import sys
sys.path.insert(0, '/opt/trn_rl_repo')
import math
from contextlib import ExitStack

import numpy as np
import ml_dtypes

import concourse.bass as bass
import concourse.mybir as mybir
from concourse.bass_utils import run_bass_kernel_spmd
from concourse.alu_op_type import AluOpType as ALU

F32 = mybir.dt.float32
BF16 = mybir.dt.bfloat16
AF = mybir.ActivationFunctionType
AX = mybir.AxisListType
BF = ml_dtypes.bfloat16

B, S, D = 4, 4096, 2048
H, KVH, DH = 16, 8, 128
TLOC = S // 2
FC = D // 128
TOPK0 = 1638
K1 = 512
DFF = 4 * D
ROPE_THETA = 1e6
ISQ = 1.0 / math.sqrt(DH)
NEG = -1.0e9
MSHIFT = -8.0

ENGINES = ("sync", "scalar", "vector", "tensor", "gpsimd")


class Sch:

    def __init__(self, nc, ctx):
        self.nc = nc
        self.ctx = ctx
        self.ops = {e: [] for e in ENGINES}
        self.cnt = {}
        self.sems = {}
        self.chain_last = {}
        self.bar = None
        for name in ("tC", "vC", "sC", "gC"):
            self.sems[name] = ctx.enter_context(nc.semaphore(name))
            self.cnt[name] = 0

    PARANOID = False

    def barrier(self):
        self.bar = [v for v in self.chain_last.values()]

    def op(self, eng, fn, waits=(), inc=None, amt=1):
        waits = list(waits)
        if eng in ("vector", "scalar", "gpsimd") and inc is not None:
            waits.append(self.chain_last.get("eng_" + eng))
        if self.bar:
            waits += self.bar
        if getattr(self, "serialize", False):
            waits += [v for k, v in self.chain_last.items()]
        if Sch.PARANOID is True or (isinstance(Sch.PARANOID, str) and
                                    ((Sch.PARANOID == "tensor" and eng == "tensor") or
                                     (Sch.PARANOID == "nontensor" and eng != "tensor"))):
            waits += [v for k, v in self.chain_last.items()]
        sem = self.sems[inc] if inc else None
        tok = None
        if sem is not None:
            self.cnt[inc] += amt
            tok = (sem, self.cnt[inc])
        self.ops[eng].append((fn, tuple(w for w in waits if w is not None),
                              sem, amt if sem else 0))
        if tok is not None and inc in ("vC", "sC", "gC", "tC"):
            self.chain_last["eng_" + eng] = tok
        return tok

    def dma(self, eng, out, in_, waits=(), chain=None, **kw):
        if chain is None:
            chain = "q_" + eng
        if chain not in self.sems:
            self.sems[chain] = self.ctx.enter_context(self.nc.semaphore(chain))
            self.cnt[chain] = 0
        prev = self.chain_last.get(chain)
        tok = self.op(eng, lambda e: e.dma_start(out=out, in_=in_, **kw),
                      waits=list(waits) + [prev], inc=chain, amt=16)
        self.chain_last[chain] = tok
        return tok

    def emit(self, ctx):
        blk = ctx.enter_context(self.nc.Block())
        for eng in ENGINES:
            ops = self.ops[eng]
            if not ops:
                continue

            def body(e, ops=ops):
                hw = {}
                for fn, waits, sem, amt in ops:
                    for (wsem, val) in waits:
                        if hw.get(id(wsem), -1) < val:
                            e.wait_ge(wsem, val)
                            hw[id(wsem)] = val
                    inst = fn(e)
                    if sem is not None:
                        inst.then_inc(sem, amt)

            getattr(blk, eng)(body)


def ceil_div(a, b):
    return -(-a // b)



_HC = None


def host_consts():
    global _HC
    if _HC is not None:
        return _HC
    c = {}
    pos = np.arange(S, dtype=np.float64)
    inv_freq = 1.0 / (ROPE_THETA ** (np.arange(0, DH, 2, dtype=np.float64) / DH))
    freqs = pos[None, :] * inv_freq[:, None]
    emb = np.concatenate([freqs, freqs], axis=0)
    c["cos_t"] = np.cos(emb).astype(np.float32)
    c["sin_t"] = np.sin(emb).astype(np.float32)
    M = np.zeros((DH, DH), dtype=np.float32)
    for j in range(64):
        M[j + 64, j] = -1.0
    for j in range(64, 128):
        M[j - 64, j] = 1.0
    c["prot"] = M.astype(BF)
    c["ident"] = np.eye(128, dtype=np.float32).astype(BF)
    c["ones_col"] = np.ones((128, 1), dtype=np.float32).astype(BF)
    c["ones_row"] = np.ones((1, 128), dtype=np.float32).astype(BF)
    bd = np.zeros((128, 16), dtype=np.float32)
    for h in range(16):
        bd[h * 8:(h + 1) * 8, h] = 1.0
    c["bdiag16"] = bd
    c["bcast16"] = np.ascontiguousarray(bd.T)
    _HC = c
    return c


def tile_vec(v):
    v = np.asarray(v, np.float32).reshape(-1)
    n = ceil_div(v.shape[0], 128)
    pad = np.zeros(128 * n, np.float32)
    pad[:v.shape[0]] = v
    return np.ascontiguousarray(pad.reshape(n, 128).T)


def perm_for(core):
    hh = core % 2
    loc = np.arange(hh * TLOC, (hh + 1) * TLOC)
    oth = np.arange((1 - hh) * TLOC, (2 - hh) * TLOC)
    return np.concatenate([loc, oth])



def emit_ln(sch, t, pA, pB, xf, hf, deps, st=None):
    o = sch.op
    if st is None:
        st = {"pa": None, "pb": None, "sq": [None, None]}
    hdone = []
    for half in range(2):
        sl = slice(half * 512, (half + 1) * 512)
        m2 = None
        for c in range(FC):
            sq = o("scalar", lambda e, c=c, sl=sl: e.activation(
                out=t["sq"][:, c % 2, :], in_=xf(c, sl), func=AF.Square),
                waits=list(deps) + [st["sq"][c % 2]], inc="sC")
            o("tensor", lambda e, c=c, sl=sl: e.matmul(
                pA[0:1, 0:512], t["onec"][:, :], xf(c, sl),
                start=(c == 0), stop=(c == FC - 1)),
                waits=list(deps) + ([st["pa"]] if c == 0 else []), inc="tC")
            m2 = o("tensor", lambda e, c=c: e.matmul(
                pA[0:1, 512:1024], t["onec"][:, :], t["sq"][:, c % 2, :],
                start=(c == 0), stop=(c == FC - 1)),
                waits=[sq], inc="tC")
            st["sq"][c % 2] = m2
        s0 = o("vector", lambda e: e.tensor_scalar(
            t["stat"][:, 0, :], pA[0:1, 0:512], 1.0 / D, None, ALU.mult, ALU.bypass),
            waits=[m2], inc="vC")
        s1 = o("vector", lambda e: e.tensor_scalar(
            t["stat"][:, 1, :], pA[0:1, 512:1024], 1.0 / D, None, ALU.mult, ALU.bypass),
            waits=[m2], inc="vC")
        s2 = o("vector", lambda e: e.scalar_tensor_tensor(
            t["stat"][:, 2, :], t["stat"][:, 0, :], -1.0, t["stat"][:, 0, :],
            ALU.mult, ALU.mult), waits=[s0], inc="vC")
        s3 = o("vector", lambda e: e.tensor_tensor(
            t["stat"][:, 1, :], t["stat"][:, 1, :], t["stat"][:, 2, :], ALU.add),
            waits=[s1, s2], inc="vC")
        s4 = o("scalar", lambda e: e.activation(
            out=t["stat"][:, 2, :], in_=t["stat"][:, 1, :], func=AF.Ln,
            bias=t["eps1"][:, :]), waits=[s3], inc="sC")
        s5 = o("scalar", lambda e: e.activation(
            out=t["stat"][:, 2, :], in_=t["stat"][:, 2, :], func=AF.Exp,
            scale=-0.5), waits=[s4], inc="sC")
        s6 = o("vector", lambda e: e.scalar_tensor_tensor(
            t["stat"][:, 3, :], t["stat"][:, 2, :], -1.0, t["stat"][:, 0, :],
            ALU.mult, ALU.mult), waits=[s5], inc="vC")
        s7 = o("vector", lambda e: e.tensor_copy(
            t["statb"][:, 0, :], t["stat"][:, 2, :]), waits=[s5], inc="vC")
        s8 = o("vector", lambda e: e.tensor_copy(
            t["statb"][:, 1, :], t["stat"][:, 3, :]), waits=[s6], inc="vC")
        b0 = o("tensor", lambda e: e.matmul(
            pB[:, 0:512], t["oner"][:, :], t["statb"][:, 0, :],
            start=True, stop=True), waits=[s7, st["pb"]], inc="tC")
        b1 = o("tensor", lambda e: e.matmul(
            pB[:, 512:1024], t["oner"][:, :], t["statb"][:, 1, :],
            start=True, stop=True), waits=[s8], inc="tC")
        st["pa"] = s1
        for c in range(FC):
            a0 = o("vector", lambda e, c=c, sl=sl: e.tensor_tensor(
                t["tmp"][:], xf(c, sl), pB[:, 0:512], ALU.mult),
                waits=[b0, hdone[-1] if hdone else None], inc="vC")
            a1 = o("vector", lambda e, c=c, sl=sl: e.tensor_tensor(
                hf(c, sl), t["tmp"][:], pB[:, 512:1024], ALU.add),
                waits=[b1, a0], inc="vC")
            hdone.append(a1)
        st["pb"] = hdone[-1]
    return hdone, st



def build_l0():
    nc = bass.Bass("TRN2", target_bir_lowering=False)
    ap = lambda *a, **k: nc.dram_tensor(*a, **k).ap()
    g = {}
    g["xT"] = ap("xT", [D, S], F32, kind="ExternalInput")
    g["wq"] = ap("wq", [D, D], F32, kind="ExternalInput")
    g["wk"] = ap("wk", [D, KVH * DH], F32, kind="ExternalInput")
    g["wv"] = ap("wv", [D, KVH * DH], F32, kind="ExternalInput")
    g["wo"] = ap("wo", [D, D], F32, kind="ExternalInput")
    g["wr"] = ap("wr", [D, H], F32, kind="ExternalInput")
    g["fwT"] = ap("fwT", [D, D], F32, kind="ExternalInput")
    g["bq"] = ap("bq", [128, FC], F32, kind="ExternalInput")
    g["bk"] = ap("bk", [128, KVH], F32, kind="ExternalInput")
    g["bv"] = ap("bv", [128, KVH], F32, kind="ExternalInput")
    g["bo"] = ap("bo", [128, FC], F32, kind="ExternalInput")
    g["br"] = ap("br", [128, 1], F32, kind="ExternalInput")
    g["fb"] = ap("fb", [128, FC], F32, kind="ExternalInput")
    g["cos_t"] = ap("cos_t", [DH, S], BF16, kind="ExternalInput")
    g["sin_t"] = ap("sin_t", [DH, S], BF16, kind="ExternalInput")
    g["prot"] = ap("prot", [DH, DH], BF16, kind="ExternalInput")
    g["ident"] = ap("ident", [128, 128], BF16, kind="ExternalInput")
    g["ones_col"] = ap("ones_col", [128, 1], BF16, kind="ExternalInput")
    g["ones_row"] = ap("ones_row", [1, 128], BF16, kind="ExternalInput")
    g["bdiag16"] = ap("bdiag16", [128, 16], F32, kind="ExternalInput")
    g["bcast16"] = ap("bcast16", [16, 128], F32, kind="ExternalInput")
    g["x2T"] = ap("x2T", [D, TLOC], BF16, kind="ExternalOutput")
    g["kdr"] = nc.dram_tensor("kdr", [KVH, DH, S], BF16).ap()
    g["vdr"] = nc.dram_tensor("vdr", [KVH, DH, S], BF16).ap()
    g["qdr"] = nc.dram_tensor("qdr", [H, DH, TLOC], BF16).ap()
    g["wqdr"] = nc.dram_tensor("wqdr", [D, D], BF16).ap()
    g["rdr"] = nc.dram_tensor("rdr", [16, S], F32).ap()
    g["mdr"] = nc.dram_tensor("mdr", [512, 128], BF16).ap()

    ctx = ExitStack()
    with ctx:
        E = ctx.enter_context
        t = {}
        t["xres"] = E(nc.sbuf_tensor("xres", [128, FC, TLOC], BF16))
        t["hbuf"] = E(nc.sbuf_tensor("hbuf", [128, 2, FC, 1024], BF16))
        t["u4"] = E(nc.sbuf_tensor("u4", [128, 16384], BF16))
        t["wbuf"] = E(nc.sbuf_tensor("wbuf", [128, 2, FC, 128], BF16))
        t["qb"] = E(nc.sbuf_tensor("qb", [128, TLOC], BF16))
        t["expb"] = E(nc.sbuf_tensor("expb", [128, 2, 1024], BF16))
        t["prot"] = E(nc.sbuf_tensor("protb", [128, 128], BF16))
        t["onec"] = E(nc.sbuf_tensor("onec", [128, 1], BF16))
        t["oner"] = E(nc.sbuf_tensor("oner", [1, 128], BF16))
        t["bd16"] = E(nc.sbuf_tensor("bd16", [128, 16], F32))
        t["bc16"] = E(nc.sbuf_tensor("bc16", [16, 128], F32))
        t["bias"] = E(nc.sbuf_tensor("biast", [128, 64], F32))
        t["br"] = E(nc.sbuf_tensor("brt", [128, 1], F32))
        t["eps1"] = E(nc.sbuf_tensor("eps1", [1, 1], F32))
        t["sq"] = E(nc.sbuf_tensor("sq", [128, 2, 512], BF16))
        t["stat"] = E(nc.sbuf_tensor("stat", [1, 4, 512], F32))
        t["statb"] = E(nc.sbuf_tensor("statb", [1, 2, 512], BF16))
        t["tmp"] = E(nc.sbuf_tensor("tmp", [128, 512], BF16))
        t["rr"] = E(nc.sbuf_tensor("rr", [16, 512], F32))
        t["rb"] = E(nc.sbuf_tensor("rb", [128, 512], F32))
        t["lo"] = E(nc.sbuf_tensor("lo", [16, 1], F32))
        t["hi"] = E(nc.sbuf_tensor("hi", [16, 1], F32))
        t["tt16"] = E(nc.sbuf_tensor("tt16", [16, 1], F32))
        t["ge"] = E(nc.sbuf_tensor("ge", [16, 1], F32))
        t["gi"] = E(nc.sbuf_tensor("gi", [16, 1], F32))
        t["d16"] = E(nc.sbuf_tensor("d16", [16, 1], F32))
        t["cmp"] = E(nc.sbuf_tensor("cmp", [128, 512], F32))
        t["cnt1"] = E(nc.sbuf_tensor("cnt1", [128, 1], F32))
        t["thr128"] = E(nc.sbuf_tensor("thr128", [128, 1], F32))
        t["maskf"] = E(nc.sbuf_tensor("maskf", [128, 512], F32))
        t["rzrep"] = E(nc.sbuf_tensor("rzrep", [128, 1024], F32))
        t["wqc"] = E(nc.sbuf_tensor("wqc", [128, 512], F32))
        t["redq"] = E(nc.sbuf_tensor("redq", [128, 64], F32))
        t["identb"] = E(nc.sbuf_tensor("identb", [128, 128], BF16))
        t["wqo"] = E(nc.sbuf_tensor("wqo", [128, 2, 512], BF16))
        t["acc"] = E(nc.sbuf_tensor("acc", [128, 1], F32))
        t["red"] = E(nc.sbuf_tensor("red", [128, 1], F32))
        t["onef"] = E(nc.sbuf_tensor("onef", [128, 1], F32))
        t["onerf"] = E(nc.sbuf_tensor("onerf", [1, 128], F32))
        t["srow"] = E(nc.sbuf_tensor("srow", [1, 2], F32))
        t["sinv"] = E(nc.sbuf_tensor("sinv", [128, 2], F32))

        t["mtmp"] = t["sq"][:, 0, :]
        t["qb2"] = t["wbuf"].ap().rearrange("p a c n -> p a (c n)")[:, 1]
        t["rz"] = t["stat"].ap().rearrange("a b c -> a (b c)")[:, 0:1024]
        t["rzb"] = t["statb"].ap().rearrange("a b c -> a (b c)")[:, 0:1024]
        t["wqa"] = t["rb"][:, :]
        t["wqx"] = t["cmp"][:, :]
        t["qa"] = t["expb"][:, 0, 0:512]
        t["qc"] = t["expb"][:, 1, 0:512]
        t["qs"] = t["expb"][:, 1, 512:1024]
        pA = E(nc.psum_tensor("pA", [128, 1024], F32))
        pB = E(nc.psum_tensor("pB", [128, 1024], F32))
        pC = E(nc.psum_tensor("pC", [128, 1024], F32))
        pD = E(nc.psum_tensor("pD", [128, 1024], F32))

        sch = Sch(nc, ctx)
        emit_l0(sch, g, t, pA, pB, pC, pD)
        sch.emit(ctx)
    return nc


def emit_l0(sch, g, t, pA, pB, pC, pD):
    o = sch.op
    dma = sch.dma
    cosv = t["u4"][:, 0:S]
    sinv = t["u4"][:, S:2 * S]

    consts = None
    for dst, src in [(cosv, g["cos_t"][:, :]), (sinv, g["sin_t"][:, :]),
                     (t["prot"][:], g["prot"][:, :]), (t["identb"][:], g["ident"][:, :]),
                     (t["onec"][:], g["ones_col"][:, :]),
                     (t["oner"][:], g["ones_row"][:, :]), (t["bd16"][:], g["bdiag16"][:, :]),
                     (t["bc16"][:], g["bcast16"][:, :]),
                     (t["bias"][:, 0:16], g["bq"][:, :]), (t["bias"][:, 16:32], g["bo"][:, :]),
                     (t["bias"][:, 32:40], g["bk"][:, :]), (t["bias"][:, 40:48], g["bv"][:, :]),
                     (t["bias"][:, 48:64], g["fb"][:, :]), (t["br"][:], g["br"][:, :])]:
        consts = dma("sync", dst, src, chain="const")
    epsr = o("vector", lambda e: e.memset(t["eps1"][:], 1e-5), inc="vC")
    onefm = o("vector", lambda e: e.memset(t["onef"][:], 1.0), inc="vC")
    onerfm = o("vector", lambda e: e.memset(t["onerf"][:], 1.0), inc="vC")

    xTr = g["xT"].rearrange("(c p) s -> p c s", p=128)

    rr_done = []
    kv_done = []
    q_done = []
    hbuf_free = [None, None]
    xstage_free = [None, None]
    ropeS = {"buf": None, "pd": None}
    lnst = [None]

    def rope_evac(psh, bcol, base, out_ap, mmtok):
        e1 = o("vector", lambda e: e.tensor_scalar(
            t["qa"], psh, bcol, None, ALU.add, ALU.bypass),
            waits=[mmtok, ropeS["buf"]], inc="vC")
        rmm = o("tensor", lambda e: e.matmul(
            pD[:, 0:512], t["prot"][:, :], t["qa"], start=True, stop=True),
            waits=[e1, ropeS["pd"]], inc="tC")
        e2 = o("vector", lambda e: e.tensor_tensor(
            t["qc"], t["qa"], cosv[:, base:base + 512], ALU.mult),
            waits=[e1], inc="vC")
        e3 = o("vector", lambda e: e.tensor_tensor(
            t["qs"], pD[:, 0:512], sinv[:, base:base + 512], ALU.mult),
            waits=[rmm], inc="vC")
        ropeS["pd"] = e3
        e4 = o("vector", lambda e: e.tensor_tensor(
            t["qa"], t["qc"], t["qs"], ALU.add),
            waits=[e2, e3, rmm], inc="vC")
        dt_ = dma("sync", out_ap, t["qa"], waits=[e4], chain="rope")
        ropeS["buf"] = dt_
        return dt_

    proj_ps = {0: None, 1: None}
    wbuf_free = [None, None]

    def proj(wdram, n_ec, hf, hdeps, evac_fn, wstart, ecw=128):
        wre = wdram.rearrange("(c p) eo -> p c eo", p=128)
        for ec in range(n_ec):
            par = ec % 2
            lw = dma("gpsimd", t["wbuf"][:, par, :, 0:ecw],
                     wre[:, :, ec * ecw:(ec + 1) * ecw],
                     waits=[wbuf_free[par], wstart], chain="wld")
            for half in range(2):
                sl = slice(half * 512, (half + 1) * 512)
                mm = None
                for c in range(FC):
                    mm = o("tensor", lambda e, c=c, sl=sl, half=half, par=par: e.matmul(
                        pC[0:ecw, half * 512:half * 512 + 512],
                        t["wbuf"][:, par, c, 0:ecw], hf(c, sl),
                        start=(c == 0), stop=(c == FC - 1)),
                        waits=list(hdeps) + [lw, proj_ps[half]], inc="tC")
                ft = evac_fn(ec, half, pC[0:ecw, half * 512:half * 512 + 512], mm)
                proj_ps[half] = ft
            wbuf_free[par] = mm

    for tt4 in range(4):
        cols = slice(tt4 * 1024, (tt4 + 1) * 1024)
        hhalf = tt4 % 2
        if tt4 < 2:
            xf = lambda c, sl, tt4=tt4: t["xres"][:, c, tt4 * 1024 + sl.start:tt4 * 1024 + sl.stop]
            lx = dma("gpsimd", t["xres"][:, :, cols], xTr[:, :, cols],
                     waits=[xstage_free[hhalf]], chain="xld")
        else:
            xf = lambda c, sl, hhalf=hhalf: t["hbuf"][:, 1 - hhalf, c, sl]
            lx = dma("gpsimd", t["hbuf"][:, 1 - hhalf], xTr[:, :, cols],
                     waits=[xstage_free[1 - hhalf], hbuf_free[1 - hhalf]],
                     chain="xld")
        hf = lambda c, sl, hhalf=hhalf: t["hbuf"][:, hhalf, c, sl]
        hdone, lnst[0] = emit_ln(
            sch, t, pA, pB, xf, hf,
            [lx, epsr, consts, hbuf_free[hhalf]], lnst[0])
        hdeps = [hdone[-1]]
        if tt4 >= 2:
            xstage_free[1 - hhalf] = hdone[-1]

        if tt4 < 2:
            def evq(ec, half, psh, mm, tt4=tt4):
                base = tt4 * 1024 + half * 512
                dt_ = rope_evac(psh, t["bias"][:, ec:ec + 1], base,
                                g["qdr"][ec, :, base:base + 512], mm)
                q_done.append(dt_)
                return dt_
            proj(g["wq"], FC, hf, hdeps, evq, hbuf_free[hhalf])

        def evk(ec, half, psh, mm, tt4=tt4):
            base = tt4 * 1024 + half * 512
            dt_ = rope_evac(psh, t["bias"][:, 32 + ec:33 + ec], base,
                            g["kdr"][ec, :, base:base + 512], mm)
            kv_done.append(dt_)
            return dt_
        proj(g["wk"], KVH, hf, hdeps, evk, hbuf_free[hhalf])

        def evv(ec, half, psh, mm, tt4=tt4):
            base = tt4 * 1024 + half * 512
            e1 = o("vector", lambda e: e.tensor_scalar(
                t["qa"], psh, t["bias"][:, 40 + ec:41 + ec], None,
                ALU.add, ALU.bypass), waits=[mm, ropeS["buf"]], inc="vC")
            dt_ = dma("sync", g["vdr"][ec, :, base:base + 512], t["qa"], waits=[e1], chain="rope")
            ropeS["buf"] = dt_
            kv_done.append(dt_)
            return e1
        proj(g["wv"], KVH, hf, hdeps, evv, hbuf_free[hhalf])

        def evr(ec, half, psh, mm, tt4=tt4):
            base = tt4 * 1024 + half * 512
            e1 = o("scalar", lambda e: e.activation(
                out=t["rr"][:, :], in_=psh,
                func=AF.Sigmoid, bias=t["br"][0:16, :]),
                waits=[mm, rr_done[-1] if rr_done else None], inc="sC")
            dr = dma("sync", g["rdr"][:, base:base + 512], t["rr"][:, :], waits=[e1], chain="rr")
            rr_done.append(dr)
            return e1
        proj(g["wr"], 1, hf, hdeps, evr, hbuf_free[hhalf], ecw=16)
        hbuf_free[hhalf] = rr_done[-1]

    d2 = dma("sync", t["rb"][:], g["rdr"].rearrange("h (bk f) -> (h bk) f", bk=8),
             waits=[rr_done[-1]], chain="rr")
    mzl = o("vector", lambda e: e.memset(t["lo"][:], 0.0), inc="vC")
    mzh = o("vector", lambda e: e.memset(t["hi"][:], 1.0), inc="vC")
    lastsel = [mzl, mzh, d2]
    pd_b_free = [ropeS["pd"]]
    for it in range(30):
        a1 = o("vector", lambda e: e.tensor_tensor(
            t["tt16"][:], t["lo"][:], t["hi"][:], ALU.add), waits=lastsel, inc="vC")
        a2 = o("vector", lambda e: e.tensor_scalar(
            t["tt16"][:], t["tt16"][:], 0.5, None, ALU.mult, ALU.bypass),
            waits=[a1], inc="vC")
        bmm = o("tensor", lambda e: e.matmul(
            pD[:, 512:513], t["bc16"][:, :], t["tt16"][:], start=True, stop=True),
            waits=[a2, pd_b_free[0], consts], inc="tC")
        cp1 = o("vector", lambda e: e.tensor_copy(t["thr128"][:], pD[:, 512:513]),
                waits=[bmm], inc="vC")
        c1 = o("vector", lambda e: e.tensor_scalar(
            t["cmp"][:], t["rb"][:], t["thr128"][:, :], None, ALU.is_ge, ALU.bypass),
            waits=[cp1, d2], inc="vC")
        c2 = o("vector", lambda e: e.tensor_reduce(
            t["cnt1"][:], t["cmp"][:], AX.X, ALU.add), waits=[c1], inc="vC")
        cmm = o("tensor", lambda e: e.matmul(
            pD[0:16, 640:641], t["bd16"][:, :], t["cnt1"][:], start=True, stop=True),
            waits=[c2], inc="tC")
        g1 = o("vector", lambda e: e.tensor_scalar(
            t["ge"][:], pD[0:16, 640:641], float(TOPK0), None, ALU.is_ge, ALU.bypass),
            waits=[cmm], inc="vC")
        g2 = o("vector", lambda e: e.tensor_scalar(
            t["gi"][:], t["ge"][:], -1.0, 1.0, ALU.mult, ALU.add),
            waits=[g1], inc="vC")
        dl = o("vector", lambda e: e.tensor_tensor(
            t["d16"][:], t["tt16"][:], t["lo"][:], ALU.subtract), waits=[a2], inc="vC")
        l1 = o("vector", lambda e: e.scalar_tensor_tensor(
            t["lo"][:], t["d16"][:], t["ge"][:, :], t["lo"][:], ALU.mult, ALU.add),
            waits=[g1, dl], inc="vC")
        dh = o("vector", lambda e: e.tensor_tensor(
            t["d16"][:], t["tt16"][:], t["hi"][:], ALU.subtract),
            waits=[l1], inc="vC")
        h1 = o("vector", lambda e: e.scalar_tensor_tensor(
            t["hi"][:], t["d16"][:], t["gi"][:, :], t["hi"][:], ALU.mult, ALU.add),
            waits=[g2, dh], inc="vC")
        lastsel = [l1, h1]
        pd_b_free[0] = g1
    bmmf = o("tensor", lambda e: e.matmul(
        pD[:, 512:513], t["bc16"][:, :], t["lo"][:], start=True, stop=True),
        waits=lastsel + [pd_b_free[0]], inc="tC")
    cpf = o("vector", lambda e: e.tensor_copy(t["thr128"][:], pD[:, 512:513]),
            waits=[bmmf], inc="vC")
    mb1 = o("vector", lambda e: e.tensor_scalar(
        t["cmp"][:], t["rb"][:], t["thr128"][:, :], None, ALU.is_ge, ALU.bypass),
        waits=[cpf], inc="vC")
    mb2 = o("vector", lambda e: e.tensor_scalar(
        t["cmp"][:], t["cmp"][:], -NEG + MSHIFT, NEG, ALU.mult, ALU.add),
        waits=[mb1], inc="vC")
    mb4 = o("vector", lambda e: e.tensor_copy(t["mtmp"], t["cmp"][:]),
            waits=[mb2], inc="vC")
    d3 = dma("sync", g["mdr"].rearrange("(p ff) c -> p (ff c)", p=128),
             t["mtmp"], waits=[mb4], chain="rr")
    fwr = g["fwT"].rearrange("(c p) o -> p c o", p=128)
    wq_bufs = [t["wqc"], t["maskf"]]
    lq_free = [None, None]
    red_last = None
    for cc in range(FC * 4):
        par = cc % 2
        lq = dma("scalar", wq_bufs[par][:],
                 fwr[:, cc // 4, (cc % 4) * 512:(cc % 4 + 1) * 512],
                 waits=[lq_free[par]], chain=("wqld", "wqld2")[par])
        r1 = o("vector", lambda e, cc=cc, par=par: e.tensor_reduce(
            t["redq"][:, cc:cc + 1], wq_bufs[par][:], AX.X, ALU.add,
            apply_absolute_value=True), waits=[lq], inc="vC")
        lq_free[par] = r1
        red_last = r1
    racc = o("vector", lambda e: e.tensor_reduce(
        t["acc"][:], t["redq"][:], AX.X, ALU.add), waits=[red_last], inc="vC")
    tmm = o("tensor", lambda e: e.matmul(
        pD[0:1, 768:769], t["onef"][:, :], t["acc"][:], start=True, stop=True),
        waits=[racc, onefm, pd_b_free[0], cpf], inc="tC")
    sv = o("vector", lambda e: e.tensor_scalar(
        t["srow"][:, 0:1], pD[0:1, 768:769], 1.0 / (D * D), None, ALU.mult, ALU.bypass),
        waits=[tmm], inc="vC")
    iv = o("vector", lambda e: e.reciprocal(t["srow"][:, 1:2], t["srow"][:, 0:1]),
           waits=[sv], inc="vC")
    bmm2 = o("tensor", lambda e: e.matmul(
        pD[:, 768:770], t["onerf"][:, :], t["srow"][:, :], start=True, stop=True),
        waits=[iv, onerfm, sv], inc="tC")
    cps = o("vector", lambda e: e.tensor_copy(t["sinv"][:], pD[:, 768:770]),
            waits=[bmm2], inc="vC")
    qdone = []
    q_state = {"wqo_free": [cps, cps], "cc": 0, "lqf": lq_free[0]}

    def emit_qchunk():
        cc = q_state["cc"]
        if cc >= FC * 4:
            return
        q_state["cc"] = cc + 1
        par = cc % 2
        lq = dma("scalar", t["wqc"][:],
                 fwr[:, cc // 4, (cc % 4) * 512:(cc % 4 + 1) * 512],
                 waits=[q_state["lqf"]], chain="wqld")
        q1 = o("vector", lambda e: e.tensor_scalar(
            t["wqa"], t["wqc"][:], t["sinv"][:, 1:2], 0.5, ALU.mult, ALU.is_ge),
            waits=[lq, cps, mkf], inc="vC")
        q2 = o("vector", lambda e: e.tensor_scalar(
            t["wqx"], t["wqc"][:], t["sinv"][:, 1:2], -0.5, ALU.mult, ALU.is_le),
            waits=[lq], inc="vC")
        q_state["lqf"] = q2
        q3 = o("vector", lambda e: e.tensor_tensor(
            t["wqa"], t["wqa"], t["wqx"], ALU.subtract), waits=[q1, q2],
            inc="vC")
        q4t = o("vector", lambda e, par=par: e.tensor_scalar(
            t["wqo"][:, par, :], t["wqa"], t["sinv"][:, 0:1], None,
            ALU.mult, ALU.bypass), waits=[q3, q_state["wqo_free"][par]], inc="vC")
        dq = dma("sync", g["wqdr"].rearrange("(c p) o -> p c o", p=128)
                 [:, cc // 4, (cc % 4) * 512:(cc % 4 + 1) * 512],
                 t["wqo"][:, par, :], waits=[q4t], chain="wqout")
        q_state["wqo_free"][par] = dq
        qdone.append(dq)

    tde = []
    for sl4 in range(4):
        td = dma("scalar", t["expb"][:, 0, sl4 * 128:(sl4 + 1) * 128],
                 g["mdr"][sl4 * 128:(sl4 + 1) * 128, :], transpose=True,
                 waits=[d3, ropeS["buf"]], chain="vb")
        tde.append(td)
    mkf = o("vector", lambda e: e.tensor_copy(t["maskf"][:], t["expb"][:, 0, 0:512]),
            waits=tde, inc="vC")

    hbflat = t["hbuf"].ap().rearrange("p a c n -> p (a c n)")
    attn_done = []
    expb_free = [ropeS["buf"], ropeS["buf"]]
    sc_free = [lnst[0]["pa"], lnst[0]["pb"]]
    o_free = [proj_ps[0], proj_ps[1]]
    z_free = [cps]
    kb_free = [None, None]
    vb_free = [None, None]
    qb_free = [None, wbuf_free[1]]
    Kb = t["u4"]
    Vb = t["u4"]
    h_last = hbuf_free
    for kvh in range(KVH):
        par = kvh % 2
        kl = dma("sync", Kb[:, par * S:(par + 1) * S], g["kdr"][kvh, :, :],
                 waits=[kv_done[-1], kb_free[par]], chain="kb")
        vbase = 2 * S + par * S
        lvf = dma("gpsimd", Vb[:, vbase:vbase + S], g["vdr"][kvh, :, :],
                  waits=[kv_done[-1], vb_free[par]], chain="vbg")
        vb_evs = []
        for bt in range(4):
            first = list(o_free) if bt == 0 else [vb_evs[-1]]
            tmms = []
            for jj in range(8):
                kc = bt * 8 + jj
                tmms.append(o("tensor", lambda e, kc=kc, jj=jj, vbase=vbase: e.matmul(
                    pC[:, jj * 128:(jj + 1) * 128],
                    Vb[:, vbase + kc * 128:vbase + (kc + 1) * 128],
                    t["identb"][:], start=True, stop=True),
                    waits=[lvf, consts] + (first if jj == 0 else []), inc="tC"))
            ve = o("vector", lambda e, bt=bt, vbase=vbase: e.tensor_copy(
                Vb[:, vbase + bt * 1024:vbase + (bt + 1) * 1024],
                pC[:, 0:1024]), waits=[tmms[-1]], inc="vC")
            vb_evs.append(ve)
        vb_data = vb_evs[-1]
        for hh in (2 * kvh, 2 * kvh + 1):
            hp = hh % 2
            qbb = (t["qb"], t["qb2"])[hp]
            lqd = dma("gpsimd", qbb[:], g["qdr"][hh, :, :],
                      waits=[q_done[-1], qb_free[hp]], chain="qbld")
            for qt in range(2):
                zs = []
                pvs = []
                mtoks = {}

                def scores(kc, qt=qt, lqd=lqd, kl=kl, par=par, qbb=qbb):
                    sel = kc % 2
                    psc = (pA, pB)[sel]
                    m1 = o("tensor", lambda e, kc=kc, psc=psc, qt=qt, par=par: e.matmul(
                        psc[:, 0:512], Kb[:, par * S + kc * 128:par * S + (kc + 1) * 128],
                        qbb[:, qt * 1024:qt * 1024 + 512],
                        start=True, stop=True),
                        waits=[kl, lqd, sc_free[sel]], inc="tC")
                    m2 = o("tensor", lambda e, kc=kc, psc=psc, qt=qt, par=par: e.matmul(
                        psc[:, 512:1024], Kb[:, par * S + kc * 128:par * S + (kc + 1) * 128],
                        qbb[:, qt * 1024 + 512:qt * 1024 + 1024],
                        start=True, stop=True), waits=[], inc="tC")
                    mtoks[kc] = (m1, m2)

                scores(0)
                scores(1)
                for kc in range(32):
                    sel = kc % 2
                    psc = (pA, pB)[sel]
                    m1, m2 = mtoks.pop(kc)
                    ex = o("scalar", lambda e, kc=kc, psc=psc, hh=hh, sel=sel: e.activation(
                        out=t["expb"][:, sel, :], in_=psc[:, :], func=AF.Exp,
                        scale=ISQ, bias=t["maskf"][:, hh * 32 + kc:hh * 32 + kc + 1]),
                        waits=[m1, m2, expb_free[sel], mkf], inc="sC")
                    sc_free[sel] = ex
                    z1 = o("tensor", lambda e, kc=kc, sel=sel: e.matmul(
                        pD[0:1, 0:512], t["onec"][:, :], t["expb"][:, sel, 0:512],
                        start=(kc == 0), stop=(kc == 31)),
                        waits=[ex] + ([z_free[0], ropeS["pd"], pd_b_free[0]]
                                      if kc == 0 else []), inc="tC")
                    z2 = o("tensor", lambda e, kc=kc, sel=sel: e.matmul(
                        pD[0:1, 512:1024], t["onec"][:, :], t["expb"][:, sel, 512:1024],
                        start=(kc == 0), stop=(kc == 31)), waits=[], inc="tC")
                    v1 = o("tensor", lambda e, kc=kc, sel=sel, par=par: e.matmul(
                        pC[:, 0:512], Vb[:, 2 * S + par * S + kc * 128:2 * S + par * S + (kc + 1) * 128],
                        t["expb"][:, sel, 0:512],
                        start=(kc == 0), stop=(kc == 31)),
                        waits=[vb_data] + (o_free if kc == 0 else []), inc="tC")
                    v2 = o("tensor", lambda e, kc=kc, sel=sel, par=par: e.matmul(
                        pC[:, 512:1024], Vb[:, 2 * S + par * S + kc * 128:2 * S + par * S + (kc + 1) * 128],
                        t["expb"][:, sel, 512:1024],
                        start=(kc == 0), stop=(kc == 31)), waits=[], inc="tC")
                    expb_free[sel] = v2
                    if kc < 30:
                        scores(kc + 2)
                    zs.append(z2)
                    pvs.append(v2)
                rc = o("vector", lambda e: e.reciprocal(t["rz"], pD[0:1, 0:1024]),
                       waits=[zs[-1]], inc="vC")
                rcb = o("vector", lambda e: e.tensor_copy(t["rzb"], t["rz"]),
                        waits=[rc], inc="vC")
                bz1 = o("tensor", lambda e: e.matmul(
                    pD[:, 0:512], t["oner"][:, :], t["rzb"][:, 0:512],
                    start=True, stop=True), waits=[rcb, rc], inc="tC")
                bz2 = o("tensor", lambda e: e.matmul(
                    pD[:, 512:1024], t["oner"][:, :], t["rzb"][:, 512:1024],
                    start=True, stop=True), waits=[], inc="tC")
                rce = o("vector", lambda e: e.tensor_copy(t["rzrep"][:], pD[:, 0:1024]),
                        waits=[bz1, bz2], inc="vC")
                z_free[0] = rce
                off = hh * 2048 + qt * 1024
                ao = o("vector", lambda e, off=off: e.tensor_tensor(
                    hbflat[:, off:off + 1024],
                    pC[:, 0:1024], t["rzrep"][:], ALU.mult),
                    waits=[rce, pvs[-1]] + ([h_last[0], h_last[1]]
                                            if hh == 0 and qt == 0 else []), inc="vC")
                o_free = [ao]
                attn_done.append(ao)
                emit_qchunk()
                emit_qchunk()
            qb_free[hp] = attn_done[-1]
        kb_free[par] = attn_done[-1]
        vb_free[par] = attn_done[-1]

    wore = g["wo"].rearrange("(hc p) eo -> p hc eo", p=128)
    wo_free = [attn_done[-1], attn_done[-1]]
    d_free = [sc_free[0], sc_free[1]]
    wo_done = []
    for ec in range(FC):
        par = ec % 2
        lw = dma("gpsimd", t["wbuf"][:, par], wore[:, :, ec * 128:(ec + 1) * 128],
                 waits=[wo_free[par], wbuf_free[par]], chain="wld")
        for th in range(4):
            sel = th % 2
            psd = (pA, pB)[sel]
            mm = None
            for hc2 in range(FC):
                mm = o("tensor", lambda e, hc2=hc2, th=th, par=par, psd=psd: e.matmul(
                    psd[:, 0:512], t["wbuf"][:, par, hc2, :],
                    hbflat[:, hc2 * 2048 + th * 512:hc2 * 2048 + (th + 1) * 512],
                    start=(hc2 == 0), stop=(hc2 == FC - 1)),
                    waits=[lw, attn_done[-1], d_free[sel]], inc="tC")
            ev = o("vector", lambda e, ec=ec, th=th, psd=psd: e.scalar_tensor_tensor(
                t["xres"][:, ec, th * 512:(th + 1) * 512], psd[:, 0:512],
                t["bias"][:, 16 + ec:17 + ec],
                t["xres"][:, ec, th * 512:(th + 1) * 512], ALU.add, ALU.add),
                waits=[mm], inc="vC")
            d_free[sel] = ev
            wo_done.append(ev)
        wo_free[par] = wo_done[-1]
        wbuf_free[par] = mm

    lnst[0]["pa"] = d_free[0]
    lnst[0]["pb"] = d_free[1]
    h2_done = []
    for tt4 in range(2):
        xf2 = lambda c, sl, tt4=tt4: t["xres"][:, c, tt4 * 1024 + sl.start:tt4 * 1024 + sl.stop]
        hf2 = lambda c, sl, tt4=tt4: t["hbuf"][:, tt4, c, sl]
        hd, lnst[0] = emit_ln(sch, t, pA, pB, xf2, hf2, [wo_done[-1]], lnst[0])
        h2_done.append(hd[-1])
    wqre = g["wqdr"].rearrange("(c p) o -> p c o", p=128)
    f_wfree = [h2_done[-1], h2_done[-1]]
    f_free = [lnst[0]["pa"], lnst[0]["pb"]]
    out_done = []
    for ec in range(FC):
        par = ec % 2
        lw = dma("sync", t["wbuf"][:, par], wqre[:, :, ec * 128:(ec + 1) * 128],
                 waits=[f_wfree[par], qdone[-1], wbuf_free[par]], chain="ewt")
        for th in range(4):
            sel = th % 2
            psd = (pA, pB)[sel]
            tt4, half = th // 2, th % 2
            mm = None
            for c in range(FC):
                mm = o("tensor", lambda e, c=c, tt4=tt4, half=half, par=par, psd=psd: e.matmul(
                    psd[:, 0:512], t["wbuf"][:, par, c, :],
                    t["hbuf"][:, tt4, c, half * 512:half * 512 + 512],
                    start=(c == 0), stop=(c == FC - 1)),
                    waits=[lw, h2_done[-1], f_free[sel]], inc="tC")
            ev = o("vector", lambda e, ec=ec, th=th, psd=psd: e.scalar_tensor_tensor(
                t["xres"][:, ec, th * 512:(th + 1) * 512], psd[:, 0:512],
                t["bias"][:, 48 + ec:49 + ec],
                t["xres"][:, ec, th * 512:(th + 1) * 512], ALU.add, ALU.add),
                waits=[mm], inc="vC")
            f_free[sel] = ev
            out_done.append(ev)
        f_wfree[par] = out_done[-1]
        wbuf_free[par] = mm

    so = dma("sync", g["x2T"].rearrange("(c p) s -> p c s", p=128), t["xres"][:],
             waits=[out_done[-1]], chain="out")
    o("sync", lambda e: e.nop(), waits=[so])



_L0_NC = None


def run_l0(inputs):
    global _L0_NC
    hc = host_consts()
    if _L0_NC is None:
        _L0_NC = build_l0()
    x0 = np.asarray(inputs["L0_attn_hidden_states"], np.float32)
    in_maps = []
    for core in range(8):
        b = core // 2
        pm = perm_for(core)
        m = {
            "xT": np.ascontiguousarray(x0[b][pm].T),
            "wq": np.asarray(inputs["L0_attn_Wq"], np.float32),
            "wk": np.asarray(inputs["L0_attn_Wk"], np.float32),
            "wv": np.asarray(inputs["L0_attn_Wv"], np.float32),
            "wo": np.asarray(inputs["L0_attn_Wo"], np.float32),
            "wr": np.asarray(inputs["L0_attn_Wr"], np.float32),
            "fwT": np.ascontiguousarray(np.asarray(inputs["L0_ffn_weight"], np.float32).T),
            "bq": tile_vec(inputs["L0_attn_bq"]),
            "bk": tile_vec(inputs["L0_attn_bk"]),
            "bv": tile_vec(inputs["L0_attn_bv"]),
            "bo": tile_vec(inputs["L0_attn_bo"]),
            "br": tile_vec(inputs["L0_attn_br"]),
            "fb": tile_vec(inputs["L0_ffn_bias"]),
            "cos_t": hc["cos_t"][:, pm].astype(BF),
            "sin_t": hc["sin_t"][:, pm].astype(BF),
            "prot": hc["prot"],
            "ident": hc["ident"],
            "ones_col": hc["ones_col"],
            "ones_row": hc["ones_row"],
            "bdiag16": hc["bdiag16"],
            "bcast16": hc["bcast16"],
        }
        in_maps.append(m)
    res = run_bass_kernel_spmd(_L0_NC, in_maps, core_ids=list(range(8)))
    x2 = np.zeros((B, S, D), np.float32)
    for core in range(8):
        b, hh = core // 2, core % 2
        xt = np.asarray(res.results[core]["x2T"]).astype(np.float32)
        x2[b, hh * TLOC:(hh + 1) * TLOC] = xt.T
    return x2


def kernel(**inputs):
    x2 = run_l0(inputs)
    raise NotImplementedError("L1 phases pending")



def build_l1a():
    nc = bass.Bass("TRN2", target_bir_lowering=False)
    ap = lambda *a, **k: nc.dram_tensor(*a, **k).ap()
    g = {}
    g["xT"] = ap("xT", [D, TLOC], BF16, kind="ExternalInput")
    for w in ("wq", "wk", "wv", "wg"):
        g[w] = ap(w, [D, D], F32, kind="ExternalInput")
    g["wa"] = ap("wa", [D, H], F32, kind="ExternalInput")
    g["wb"] = ap("wb", [D, H], F32, kind="ExternalInput")
    g["ba"] = ap("ba", [128, 1], F32, kind="ExternalInput")
    g["bb"] = ap("bb", [128, 1], F32, kind="ExternalInput")
    g["ones_col"] = ap("ones_col", [128, 1], BF16, kind="ExternalInput")
    g["ones_row"] = ap("ones_row", [1, 128], BF16, kind="ExternalInput")
    for w in ("oq", "ok", "ov", "og"):
        g[w] = ap(w, [D, TLOC], BF16, kind="ExternalOutput")
    g["oab"] = ap("oab", [16, 2, TLOC], F32, kind="ExternalOutput")

    ctx = ExitStack()
    with ctx:
        E = ctx.enter_context
        t = {}
        t["xt"] = E(nc.sbuf_tensor("xt", [128, FC, 1024], BF16))
        t["hbuf"] = E(nc.sbuf_tensor("hbuf", [128, 2, FC, 1024], BF16))
        t["wbuf"] = E(nc.sbuf_tensor("wbuf", [128, 2, FC, 128], BF16))
        t["onec"] = E(nc.sbuf_tensor("onec", [128, 1], BF16))
        t["oner"] = E(nc.sbuf_tensor("oner", [1, 128], BF16))
        t["bias"] = E(nc.sbuf_tensor("biast", [128, 2], F32))
        t["eps1"] = E(nc.sbuf_tensor("eps1", [1, 1], F32))
        t["sq"] = E(nc.sbuf_tensor("sq", [128, 2, 512], BF16))
        t["stat"] = E(nc.sbuf_tensor("stat", [1, 4, 512], F32))
        t["statb"] = E(nc.sbuf_tensor("statb", [1, 2, 512], BF16))
        t["tmp"] = E(nc.sbuf_tensor("tmp", [128, 512], BF16))
        t["ev"] = E(nc.sbuf_tensor("ev", [128, 2, 512], BF16))
        t["ab"] = E(nc.sbuf_tensor("ab", [16, 2, 512], F32))
        pA = E(nc.psum_tensor("pA", [128, 1024], F32))
        pB = E(nc.psum_tensor("pB", [128, 1024], F32))
        pC = E(nc.psum_tensor("pC", [128, 1024], F32))

        sch = Sch(nc, ctx)
        emit_l1a(sch, g, t, pA, pB, pC)
        sch.emit(ctx)
    return nc


def emit_l1a(sch, g, t, pA, pB, pC):
    o = sch.op
    dma = sch.dma
    c1 = dma("sync", t["onec"][:], g["ones_col"][:, :], chain="const")
    c2 = dma("sync", t["oner"][:], g["ones_row"][:, :], chain="const")
    c3 = dma("sync", t["bias"][:, 0:1], g["ba"][:, :], chain="const")
    c4 = dma("sync", t["bias"][:, 1:2], g["bb"][:, :], chain="const")
    consts = c4
    epsr = o("vector", lambda e: e.memset(t["eps1"][:], 1e-5), inc="vC")
    xTr = g["xT"].rearrange("(c p) s -> p c s", p=128)

    lnst = [None]
    hbuf_free = [None, None]
    proj_ps = {0: None, 1: None}
    wbuf_free = [None, None]
    ev_free = [None, None]
    ab_free = [None, None]

    def proj(wdram, n_ec, hf, hdeps, evac_fn, wstart, ecw=128):
        wre = wdram.rearrange("(c p) eo -> p c eo", p=128)
        for ec in range(n_ec):
            par = ec % 2
            lw = dma("gpsimd", t["wbuf"][:, par, :, 0:ecw],
                     wre[:, :, ec * ecw:(ec + 1) * ecw],
                     waits=[wbuf_free[par], wstart], chain="wld")
            for half in range(2):
                sl = slice(half * 512, (half + 1) * 512)
                mm = None
                for c in range(FC):
                    mm = o("tensor", lambda e, c=c, sl=sl, half=half, par=par: e.matmul(
                        pC[0:ecw, half * 512:half * 512 + 512],
                        t["wbuf"][:, par, c, 0:ecw], hf(c, sl),
                        start=(c == 0), stop=(c == FC - 1)),
                        waits=list(hdeps) + [lw, proj_ps[half]], inc="tC")
                ft = evac_fn(ec, half, pC[0:ecw, half * 512:half * 512 + 512], mm)
                proj_ps[half] = ft
            wbuf_free[par] = mm

    for tt4 in range(2):
        cols = slice(tt4 * 1024, (tt4 + 1) * 1024)
        lx = dma("gpsimd", t["xt"][:], xTr[:, :, cols], chain="xld")
        hhalf = tt4
        xf = lambda c, sl: t["xt"][:, c, sl]
        hf = lambda c, sl, hhalf=hhalf: t["hbuf"][:, hhalf, c, sl]
        hdone, lnst[0] = emit_ln(sch, t, pA, pB, xf, hf,
                                 [lx, epsr, consts, hbuf_free[hhalf]], lnst[0])
        hdeps = [hdone[-1]]

        for wname, oname in (("wq", "oq"), ("wk", "ok"), ("wv", "ov"), ("wg", "og")):
            def evp(ec, half, psh, mm, tt4=tt4, oname=oname):
                base = tt4 * 1024 + half * 512
                par2 = half
                e1 = o("vector", lambda e, par2=par2: e.tensor_copy(
                    t["ev"][:, par2, :], psh), waits=[mm, ev_free[par2]], inc="vC")
                dt_ = dma("sync", g[oname].rearrange("(c p) s -> p c s", p=128)[:, 0, 0:512]
                          if False else g[oname][0:128, 0:512],
                          t["ev"][:, par2, :], waits=[e1], chain="po")
                ev_free[par2] = dt_
                return e1
            def evp2(ec, half, psh, mm, tt4=tt4, oname=oname):
                base = tt4 * 1024 + half * 512
                par2 = half
                e1 = o("vector", lambda e, par2=par2: e.tensor_copy(
                    t["ev"][:, par2, :], psh), waits=[mm, ev_free[par2]], inc="vC")
                dt_ = dma("sync", g[oname][ec * 128:(ec + 1) * 128, base:base + 512],
                          t["ev"][:, par2, :], waits=[e1], chain="po")
                ev_free[par2] = dt_
                return e1
            proj(g[wname], FC, hf, hdeps, evp2, hbuf_free[hhalf])

        for wname, ai in (("wa", 0), ("wb", 1)):
            def eva(ec, half, psh, mm, tt4=tt4, ai=ai):
                base = tt4 * 1024 + half * 512
                par2 = half
                e1 = o("vector", lambda e, par2=par2, ai=ai: e.tensor_scalar(
                    t["ab"][:, par2, :], psh, t["bias"][0:16, ai:ai + 1], None,
                    ALU.add, ALU.bypass), waits=[mm, ab_free[par2]], inc="vC")
                dt_ = dma("sync", g["oab"][:, ai, base:base + 512],
                          t["ab"][:, par2, :], waits=[e1], chain="po")
                ab_free[par2] = dt_
                return e1
            proj(g[wname], 1, hf, hdeps, eva, hbuf_free[hhalf], ecw=16)
        hbuf_free[hhalf] = proj_ps[1]

    fin = sch.chain_last.get("po")
    o("sync", lambda e: e.nop(), waits=[fin])


_L1A_NC = None


def run_l1a(inputs, x2):
    global _L1A_NC
    hc = host_consts()
    if _L1A_NC is None:
        _L1A_NC = build_l1a()
    in_maps = []
    for core in range(8):
        b, hh = core // 2, core % 2
        m = {
            "xT": np.ascontiguousarray(x2[b, hh * TLOC:(hh + 1) * TLOC].T).astype(BF),
            "wq": np.asarray(inputs["L1_attn_Wq"], np.float32),
            "wk": np.asarray(inputs["L1_attn_Wk"], np.float32),
            "wv": np.asarray(inputs["L1_attn_Wv"], np.float32),
            "wg": np.asarray(inputs["L1_attn_Wg"], np.float32),
            "wa": np.asarray(inputs["L1_attn_Wa"], np.float32),
            "wb": np.asarray(inputs["L1_attn_Wb"], np.float32),
            "ba": tile_vec(inputs["L1_attn_ba"]),
            "bb": tile_vec(inputs["L1_attn_bb"]),
            "ones_col": hc["ones_col"],
            "ones_row": hc["ones_row"],
        }
        in_maps.append(m)
    res = run_bass_kernel_spmd(_L1A_NC, in_maps, core_ids=list(range(8)))
    out = {}
    for nm in ("oq", "ok", "ov", "og"):
        out[nm] = np.zeros((B, S, D), np.float32)
    out["ab"] = np.zeros((B, S, 2 * H), np.float32)
    for core in range(8):
        b, hh = core // 2, core % 2
        r = res.results[core]
        sl = slice(hh * TLOC, (hh + 1) * TLOC)
        for nm in ("oq", "ok", "ov", "og"):
            out[nm][b, sl] = np.asarray(r[nm]).astype(np.float32).T
        ab = np.asarray(r["oab"])
        out["ab"][b, sl, 0:H] = ab[:, 0, :].T
        out["ab"][b, sl, H:] = ab[:, 1, :].T
    return out



NCH = 32


def build_l1b(dbg=False):
    Sch.PARANOID = False
    nc = bass.Bass("TRN2", target_bir_lowering=False)
    ap = lambda *a, **k: nc.dram_tensor(*a, **k).ap()
    g = {}
    g["qraw"] = ap("qraw", [8, DH, S], BF16, kind="ExternalInput")
    g["kraw"] = ap("kraw", [8, DH, S], BF16, kind="ExternalInput")
    g["vraw"] = ap("vraw", [8, DH, S], BF16, kind="ExternalInput")
    g["convw"] = ap("convw", [3, 8, 128, 4], F32, kind="ExternalInput")
    g["convb"] = ap("convb", [3, 8, 128, 1], F32, kind="ExternalInput")
    g["ab_t"] = ap("ab_t", [128, 2, 256], F32, kind="ExternalInput")
    g["abr"] = ap("abr", [2, 8, S], F32, kind="ExternalInput")
    g["rmask"] = ap("rmask", [8, S], BF16, kind="ExternalInput")
    g["lnwrep"] = ap("lnwrep", [128, 2, S], BF16, kind="ExternalInput")
    g["smaskS8"] = ap("smaskS8", [128, 8, 128], BF16, kind="ExternalInput")
    g["smaskL8"] = ap("smaskL8", [128, 8, 128], BF16, kind="ExternalInput")
    g["smaskI8"] = ap("smaskI8", [128, 8, 128], BF16, kind="ExternalInput")
    g["triuI"] = ap("triuI", [128, 128], BF16, kind="ExternalInput")
    g["ident"] = ap("ident", [128, 128], BF16, kind="ExternalInput")
    g["ones_row"] = ap("ones_row", [1, 128], BF16, kind="ExternalInput")
    g["odr"] = ap("odr", [8, 128, NCH, 128], BF16, kind="ExternalOutput")
    for nm in ("qc", "kc", "vc"):
        if dbg:
            g[nm] = ap(nm + "dr", [8, DH, S], BF16, kind="ExternalOutput")
        else:
            g[nm] = nc.dram_tensor(nm + "dr", [8, DH, S], BF16).ap()
    g["crdr"] = nc.dram_tensor("crdr", [1, 2, 8, S], F32).ap()
    g["gcdr"] = nc.dram_tensor("gcdr", [1, 8, NCH], F32).ap()
    g["odrw"] = nc.dram_tensor("odrw", [8, 128, NCH, 128], BF16).ap()

    ctx = ExitStack()
    with ctx:
        E = ctx.enter_context
        t = {}
        t["ub"] = E(nc.sbuf_tensor("ub", [128, 4100], BF16))
        t["cvo"] = E(nc.sbuf_tensor("cvo", [128, 2, 512], BF16))
        t["diag"] = E(nc.sbuf_tensor("diag", [128, 2, 4, 128], BF16))
        t["cw"] = E(nc.sbuf_tensor("cw", [128, 2, 5], F32))
        t["ar"] = E(nc.sbuf_tensor("ar", [8, S], F32))
        t["rt1"] = E(nc.sbuf_tensor("rt1", [8, S], F32))
        t["rt2"] = E(nc.sbuf_tensor("rt2", [8, S], F32))
        t["rm"] = E(nc.sbuf_tensor("rm", [8, S], BF16))
        t["crg"] = E(nc.sbuf_tensor("crg", [8, S], F32))
        t["gC8"] = E(nc.sbuf_tensor("gC8", [8, NCH], F32))
        for nm in ("abt", "cums", "igc", "bet", "ccl", "dcl", "lat", "sc1", "lcc", "gtc"):
            t[nm] = E(nc.sbuf_tensor(nm, [128, 2, 256] if nm == "abt" else [128, 256], F32))
        t["lnab"] = E(nc.sbuf_tensor("lnab", [128, 256], BF16))
        t["gCrep"] = E(nc.sbuf_tensor("gCrep", [128, 256], F32))
        t["crowC"] = E(nc.sbuf_tensor("crowC", [1, 2, 2, 1024], F32))
        t["onerf"] = E(nc.sbuf_tensor("onerf", [1, 128], F32))
        t["ccrow"] = E(nc.sbuf_tensor("ccrow", [1, 256], F32))
        t["lam"] = E(nc.sbuf_tensor("lam", [128, 8, 128], F32))
        t["qkf"] = E(nc.sbuf_tensor("qkf", [128, 2, 2, 8, 128], BF16))
        t["ktv"] = E(nc.sbuf_tensor("ktv", [128, 2, 2, 8, 128], BF16))
        t["g1"] = E(nc.sbuf_tensor("g1", [128, 8, 128], BF16))
        t["h1"] = E(nc.sbuf_tensor("h1", [128, 8, 128], BF16))
        t["crS"] = E(nc.sbuf_tensor("crS", [128, 8, 128], BF16))
        t["igS"] = E(nc.sbuf_tensor("igS", [128, 8, 128], BF16))
        t["sqm"] = E(nc.sbuf_tensor("sqm", [128, 8, 128], BF16))
        t["sq1"] = E(nc.sbuf_tensor("sq1", [128, 8, 128], BF16))
        t["wt"] = E(nc.sbuf_tensor("wt", [128, 2, 8, 128], BF16))
        t["wu"] = E(nc.sbuf_tensor("wu", [128, 2, 8, 128], BF16))
        t["xch"] = E(nc.sbuf_tensor("xch", [128, 8, 256], BF16))
        t["dk"] = E(nc.sbuf_tensor("dk", [128, 8, 128], BF16))
        t["m2n"] = E(nc.sbuf_tensor("m2n", [128, 8, 128], BF16))
        t["b1n"] = E(nc.sbuf_tensor("b1n", [128, 8, 128], BF16))
        t["zst"] = E(nc.sbuf_tensor("zst", [128, 8, 128], F32))
        t["o8"] = E(nc.sbuf_tensor("o8", [128, 8, 128], BF16))
        t["zb"] = E(nc.sbuf_tensor("zb", [128, 8, 128], BF16))
        t["lsq"] = t["ub"][:, 0:4096].rearrange("p (nn d) -> p nn d", d=128)
        t["lnr2"] = E(nc.sbuf_tensor("lnr2", [128, 2, NCH], F32))
        t["eps128"] = E(nc.sbuf_tensor("eps128", [128, 1], F32))
        t["olane"] = E(nc.sbuf_tensor("olane", [128, NCH, 128], BF16))
        t["lnr"] = E(nc.sbuf_tensor("lnr", [128, 2, NCH], F32))
        
        t["lnw"] = E(nc.sbuf_tensor("lnw", [128, 2, S], BF16))
        t["sm8S"] = E(nc.sbuf_tensor("sm8S", [128, 8, 128], BF16))
        t["sm8L"] = E(nc.sbuf_tensor("sm8L", [128, 8, 128], BF16))
        t["sm8I"] = E(nc.sbuf_tensor("sm8I", [128, 8, 128], BF16))
        t["triuI"] = E(nc.sbuf_tensor("triuIt", [128, 128], BF16))
        t["ident"] = E(nc.sbuf_tensor("identt", [128, 128], BF16))
        t["oner"] = E(nc.sbuf_tensor("oner", [1, 128], BF16))
        t["one1"] = E(nc.sbuf_tensor("one1", [128, 1], F32))

        pA = E(nc.psum_tensor("pA", [128, 1024], F32))
        pB = E(nc.psum_tensor("pB", [128, 1024], F32))
        pC = E(nc.psum_tensor("pC", [128, 1024], F32))
        pD = E(nc.psum_tensor("pD", [128, 1024], F32))

        sch = Sch(nc, ctx)
        emit_l1b(sch, g, t, pA, pB, pC, pD)
        sch.emit(ctx)
    Sch.PARANOID = False
    return nc


def emit_l1b(sch, g, t, pA, pB, pC, pD):
    o = sch.op
    dma = sch.dma
    consts = None
    for dst, src in [(t["sm8S"][:], g["smaskS8"][:, :, :]), (t["sm8L"][:], g["smaskL8"][:, :, :]),
                     (t["sm8I"][:], g["smaskI8"][:, :, :]), (t["triuI"][:], g["triuI"][:, :]),
                     (t["ident"][:], g["ident"][:, :]), (t["oner"][:], g["ones_row"][:, :]),
                     (t["abt"][:], g["ab_t"][:, :, :]), (t["rm"][:], g["rmask"][:, :]),
                     (t["lnw"][:], g["lnwrep"][:, :, :])]:
        consts = dma("sync", dst, src, chain="const")
    one1 = o("vector", lambda e: e.memset(t["one1"][:], 1.0), inc="vC")
    eps128 = o("vector", lambda e: e.memset(t["eps128"][:], 1e-5), inc="vC")

    conv_done = []
    diag_free = [None, None]
    cwb_free = [None, None]
    cvo_free = [None, None]
    ub_free = [None]
    mz = o("vector", lambda e: e.memset(t["ub"][:, 0:4], 0.0), inc="vC")
    for tau, (rawnm, outnm) in enumerate((("qraw", "qc"), ("kraw", "kc"), ("vraw", "vc"))):
        for l in range(8):
            par = (tau * 8 + l) % 2
            lw = dma("gpsimd", t["cw"][:, par, 0:4], g["convw"][tau, l, :, :],
                     waits=[diag_free[par], cwb_free[par]], chain="cwld")
            lb = dma("gpsimd", t["cw"][:, par, 4:5], g["convb"][tau, l, :, :],
                     chain="cwld")
            dgs = []
            for j in range(4):
                dg = o("vector", lambda e, j=j, par=par: e.tensor_scalar(
                    t["diag"][:, par, j, :], t["ident"][:], t["cw"][:, par, j:j + 1],
                    None, ALU.mult, ALU.bypass), waits=[lw, consts, diag_free[par]],
                    inc="vC")
                dgs.append(dg)
            lu = dma("gpsimd", t["ub"][:, 4:4100], g[rawnm][l, :, :],
                     waits=[ub_free[0], mz], chain="uld")
            pv = None
            for sl8 in range(8):
                par2 = sl8 % 2
                mm = None
                for j in range(4):
                    mm = o("tensor", lambda e, j=j, sl8=sl8, par=par, par2=par2: e.matmul(
                        pD[:, par2 * 512:par2 * 512 + 512],
                        t["diag"][:, par, j, :],
                        t["ub"][:, sl8 * 512 + j + 1:sl8 * 512 + j + 513],
                        start=(j == 0), stop=(j == 3)),
                        waits=dgs + [lu, cvo_free[par2]], inc="tC")
                ev = o("scalar", lambda e, par=par, par2=par2: e.activation(
                    out=t["cvo"][:, par2, :], in_=pD[:, par2 * 512:par2 * 512 + 512],
                    func=AF.Silu, bias=t["cw"][:, par, 4:5]), waits=[mm, lb], inc="sC")
                dt_ = dma("sync", g[outnm][l, :, sl8 * 512:(sl8 + 1) * 512],
                          t["cvo"][:, par2, :], waits=[ev], chain="cvout")
                cvo_free[par2] = dt_
                pv = dt_
            diag_free[par] = mm
            cwb_free[par] = ev
            ub_free[0] = mm
            conv_done.append(pv)
    cvdone = conv_done[-1]

    e1 = o("scalar", lambda e: e.activation(out=t["sc1"][:], in_=t["abt"][:, 0, :],
                                            func=AF.Exp, scale=-1.0),
           waits=[consts], inc="sC")
    e2 = o("scalar", lambda e: e.activation(out=t["lat"][:], in_=t["sc1"][:],
                                            func=AF.Ln, bias=t["one1"][:, :]),
           waits=[e1, one1], inc="sC")
    e3 = o("vector", lambda e: e.tensor_scalar(t["lnab"][:], t["lat"][:], -1.0, None,
                                               ALU.mult, ALU.bypass), waits=[e2], inc="vC")
    cm = o("tensor", lambda e: e.matmul(pA[:, 0:256], t["triuI"][:], t["lnab"][:],
                                        start=True, stop=True),
           waits=[e3, consts], inc="tC")
    e4 = o("vector", lambda e: e.tensor_copy(t["cums"][:], pA[:, 0:256]),
           waits=[cm], inc="vC")
    e6 = o("scalar", lambda e: e.activation(out=t["bet"][:], in_=t["abt"][:, 1, :],
                                            func=AF.Sigmoid), waits=[consts], inc="sC")
    e7 = o("vector", lambda e: e.tensor_tensor(t["sc1"][:], t["cums"][:], t["lat"][:],
                                               ALU.add), waits=[e4, e2], inc="vC")
    e8 = o("scalar", lambda e: e.activation(out=t["sc1"][:], in_=t["sc1"][:],
                                            func=AF.Exp), waits=[e7], inc="sC")
    e9 = o("vector", lambda e: e.tensor_tensor(t["ccl"][:], t["bet"][:], t["sc1"][:],
                                               ALU.mult), waits=[e6, e8], inc="vC")
    x1 = o("scalar", lambda e: e.activation(out=t["igc"][:], in_=t["abt"][:, 1, :],
                                            func=AF.Exp, scale=-1.0), waits=[consts], inc="sC")
    x2_ = o("scalar", lambda e: e.activation(out=t["igc"][:], in_=t["igc"][:],
                                             func=AF.Ln, bias=t["one1"][:, :]),
            waits=[x1, one1], inc="sC")
    x3 = o("vector", lambda e: e.tensor_tensor(t["sc1"][:], t["cums"][:], t["lat"][:],
                                               ALU.add), waits=[e4, e2, e8], inc="vC")
    lcc = o("vector", lambda e: e.tensor_tensor(t["lcc"][:], t["sc1"][:], t["igc"][:],
                                                ALU.subtract), waits=[x3, x2_], inc="vC")
    gtc = o("scalar", lambda e: e.activation(out=t["gtc"][:], in_=t["cums"][:],
                                             func=AF.Exp), waits=[e4], inc="sC")

    la = dma("sync", t["ar"][:], g["abr"][0, :, :], chain="rr")
    r1 = o("scalar", lambda e: e.activation(out=t["rt1"][:], in_=t["ar"][:],
                                            func=AF.Exp, scale=-1.0), waits=[la], inc="sC")
    r2 = o("scalar", lambda e: e.activation(out=t["rt1"][:], in_=t["rt1"][:],
                                            func=AF.Ln, bias=t["one1"][0:8, :]),
           waits=[r1, one1], inc="sC")
    r3 = o("vector", lambda e: e.tensor_tensor_scan(t["rt2"][:], t["rm"][:], t["rt1"][:],
                                                    0.0, ALU.mult, ALU.add),
           waits=[r2, consts], inc="vC")
    r5 = o("vector", lambda e: e.tensor_scalar(
        t["gC8"][:], t["rt2"].ap().rearrange("l (n tp) -> l n tp", tp=128)[:, :, 127],
        -1.0, None, ALU.mult, ALU.bypass), waits=[r3], inc="vC")
    gout = dma("sync", g["gcdr"][0, :, :], t["gC8"][:], waits=[r5], chain="rr")
    igo = dma("sync", g["crdr"][0, 1, :, :], t["rt2"][:], waits=[r3], chain="rr")
    lb_ = dma("sync", t["ar"][:], g["abr"][1, :, :], waits=[r2], chain="rr")
    r6 = o("scalar", lambda e: e.activation(out=t["crg"][:], in_=t["ar"][:],
                                            func=AF.Exp, scale=-1.0),
           waits=[lb_], inc="sC")
    r7 = o("scalar", lambda e: e.activation(out=t["crg"][:], in_=t["crg"][:],
                                            func=AF.Ln, bias=t["one1"][0:8, :]),
           waits=[r6], inc="sC")
    ra = o("vector", lambda e: e.tensor_tensor(t["rt1"][:], t["rt1"][:], t["crg"][:],
                                               ALU.subtract), waits=[r7, r3], inc="vC")
    rb2 = o("vector", lambda e: e.tensor_tensor(t["crg"][:], t["rt1"][:], t["rt2"][:],
                                                ALU.subtract), waits=[ra], inc="vC")
    cro = dma("sync", g["crdr"][0, 0, :, :], t["crg"][:], waits=[rb2], chain="rr")
    onerf_m = o("vector", lambda e: e.memset(t["onerf"][:], 1.0), inc="vC")
    gcl = dma("sync", t["ccrow"][:, :], g["gcdr"].rearrange("x l n -> x (l n)"),
              waits=[gout], chain="rr")
    gmm = o("tensor", lambda e: e.matmul(pA[:, 256:512], t["onerf"][:],
                                         t["ccrow"][:, :], start=True, stop=True),
            waits=[gcl, cm, e4, onerf_m], inc="tC")
    dc1 = o("vector", lambda e: e.tensor_scalar(
        t["dcl"][:], pA[:, 256:512], 1.0, None, ALU.mult, ALU.bypass),
        waits=[gmm], inc="vC")
    gre = o("scalar", lambda e: e.activation(out=t["gCrep"][:], in_=t["dcl"][:],
                                             func=AF.Exp), waits=[dc1], inc="sC")
    dc2 = o("vector", lambda e: e.tensor_tensor(t["dcl"][:], t["dcl"][:], t["cums"][:],
                                                ALU.subtract), waits=[dc1, e4, gre], inc="vC")
    dstt = o("scalar", lambda e: e.activation(out=t["dcl"][:], in_=t["dcl"][:],
                                              func=AF.Exp), waits=[dc2], inc="sC")

    zinit = o("vector", lambda e: e.memset(t["zst"][:], 0.0), inc="vC")
    qk_free = [None, None]
    ktv_free = [None, None]
    crw_free = [gcl, None]
    pa_free = [e4, gre]
    pb_free = []
    pc_free = []
    pd_free = [cvo_free[0], cvo_free[1]]
    wtb_free = [None, None]
    wub_free = [None, None]
    x_free = [None]
    zb_tok = [zinit]
    z_tok = [zinit]
    o8_free = [None]
    odma = [None]

    sset = getattr(Sch, "SER_SET", None) or frozenset()
    SER = lambda name: name in sset
    bar_every = int(getattr(Sch, "L1B_BAR", 0))
    for n in range(NCH):
        if bar_every and n % bar_every == 0:
            sch.barrier()
        sch.serialize = SER("s1a")
        par = n % 2
        csl = slice(n * 128, (n + 1) * 128)
        lqk = []
        for ti, nm in ((0, "qc"), (1, "kc")):
            lqk.append(dma("gpsimd", t["qkf"][:, par, ti],
                           g[nm].rearrange("l d s -> d l s")[:, :, csl],
                           waits=[qk_free[par], cvdone], chain="qkld"))
        lvf = dma("gpsimd", t["ktv"][:, par, 1], g["vc"].rearrange("l d s -> d l s")[:, :, csl],
                  waits=[ktv_free[par], cvdone], chain="vtld")
        sch.serialize = SER("s1c")
        lcr = dma("gpsimd", t["crowC"].ap().rearrange("x pp a n -> x pp a n")[0:1, par]
                  .rearrange("x a (l s) -> x a l s", s=128),
                  g["crdr"].rearrange("x a l (nn s) -> x a l nn s", s=128)[0:1, :, :, n, :],
                  waits=[crw_free[par], cro, igo], chain="crld")
        sch.serialize = SER("s1a")
        ks1 = o("gpsimd", lambda e, par=par: e.tensor_scalar(
            t["qkf"][:, par, 1], t["qkf"][:, par, 1], ISQ, None, ALU.mult, ALU.bypass),
            waits=lqk, inc="gC")
        sch.serialize = SER("s1b")
        kT = []
        vT = []
        for l in range(8):
            kT.append(o("tensor", lambda e, l=l, par=par: e.matmul(
                pA[:, l * 128:(l + 1) * 128], t["qkf"][:, par, 1, l, :],
                t["ident"][:], start=True, stop=True),
                waits=[ks1, consts] + pa_free, inc="tC"))
            vT.append(o("tensor", lambda e, l=l, par=par: e.matmul(
                pB[:, l * 128:(l + 1) * 128], t["ktv"][:, par, 1, l, :],
                t["ident"][:], start=True, stop=True),
                waits=[lvf, consts] + pb_free, inc="tC"))
        evacK = o("scalar", lambda e, par=par: e.activation(
            out=t["ktv"][:, par, 0], in_=pA[:, 0:1024], func=AF.Copy),
            waits=[kT[-1], ktv_free[par]], inc="sC")
        evacV = o("vector", lambda e, par=par: e.tensor_copy(
            t["ktv"][:, par, 1], pB[:, 0:1024]), waits=[vT[-1]], inc="vC")
        pa_free = [evacK]
        pb_free = [evacV]

        sch.serialize = SER("s2")
        kk = []
        sq = []
        for l in range(8):
            kk.append(o("tensor", lambda e, l=l, par=par: e.matmul(
                pA[:, l * 128:(l + 1) * 128], t["qkf"][:, par, 1, l, :],
                t["qkf"][:, par, 1, l, :], start=True, stop=True),
                waits=[ks1] + pa_free, inc="tC"))
            sq.append(o("tensor", lambda e, l=l, par=par: e.matmul(
                pB[:, l * 128:(l + 1) * 128], t["qkf"][:, par, 1, l, :],
                t["qkf"][:, par, 0, l, :], start=True, stop=True),
                waits=[ks1] + pb_free, inc="tC"))
        crep = []
        for l in range(8):
            crep.append(o("tensor", lambda e, l=l, par=par: e.matmul(
                pC[:, l * 128:(l + 1) * 128], t["onerf"][:],
                t["crowC"][0:1, par, 0, l * 128:(l + 1) * 128],
                start=True, stop=True), waits=[lcr, onerf_m] + pc_free, inc="tC"))
            crep.append(o("tensor", lambda e, l=l, par=par: e.matmul(
                pD[:, l * 128:(l + 1) * 128], t["onerf"][:],
                t["crowC"][0:1, par, 1, l * 128:(l + 1) * 128],
                start=True, stop=True), waits=list(pd_free), inc="tC"))
        pa_free, pb_free, pc_free, pd_free = [], [], [], []
        sch.serialize = SER("s3")
        kkn = o("vector", lambda e: e.tensor_scalar(
            t["g1"][:], pA[:, 0:1024], -1.0, None, ALU.mult, ALU.bypass),
            waits=[kk[-1]], inc="vC")
        pa_free = [kkn]
        lg1 = o("vector", lambda e, n=n: e.tensor_scalar(
            t["lam"][:], pC[:, 0:1024], 1.0, None, ALU.mult, ALU.bypass),
            waits=crep, inc="vC")
        lg2 = []
        for l in range(8):
            lg2.append(o("vector", lambda e, l=l, n=n: e.tensor_scalar(
                t["lam"][:, l, :], t["lam"][:, l, :],
                t["cums"][:, l * 32 + n:l * 32 + n + 1], None, ALU.subtract, ALU.bypass),
                waits=[lg1], inc="vC"))
        lg3 = o("vector", lambda e: e.tensor_tensor(
            t["lam"][:], t["lam"][:], t["sm8S"][:], ALU.add),
            waits=lg2 + [consts], inc="vC")
        lg4 = o("scalar", lambda e: e.activation(
            out=t["crS"][:], in_=t["lam"][:], func=AF.Exp), waits=[lg3], inc="sC")
        wt0 = o("vector", lambda e: e.tensor_tensor(
            t["wt"][:, 0], t["crS"][:], t["g1"][:], ALU.mult),
            waits=[lg4, kkn, wtb_free[0]], inc="vC")
        pc_free = [lg1]
        lu1 = []
        for l in range(8):
            lu1.append(o("vector", lambda e, l=l, n=n: e.tensor_scalar(
                t["lam"][:, l, :], pD[:, l * 128:(l + 1) * 128],
                t["lcc"][:, l * 32 + n:l * 32 + n + 1], None, ALU.add, ALU.bypass),
                waits=[crep[-1], lcc, wt0], inc="vC"))
        lu2 = o("vector", lambda e: e.tensor_tensor(
            t["lam"][:], t["lam"][:], t["sm8L"][:], ALU.add),
            waits=lu1 + [consts], inc="vC")
        lu3 = o("scalar", lambda e: e.activation(
            out=t["igS"][:], in_=t["lam"][:], func=AF.Exp), waits=[lu2], inc="sC")
        wu0 = o("vector", lambda e: e.tensor_tensor(
            t["wu"][:, 0], t["igS"][:], t["g1"][:], ALU.mult),
            waits=[lu3, kkn, wub_free[0]], inc="vC")
        pd_free = [lu1[-1]]
        ls1 = o("vector", lambda e: e.tensor_scalar(
            t["lam"][:], pD[:, 0:1024], -1.0, None, ALU.mult, ALU.bypass),
            waits=crep + [lu1[-1], wu0], inc="vC")
        ls2 = []
        for l in range(8):
            ls2.append(o("vector", lambda e, l=l, n=n: e.tensor_scalar(
                t["lam"][:, l, :], t["lam"][:, l, :],
                t["cums"][:, l * 32 + n:l * 32 + n + 1], None, ALU.subtract, ALU.bypass),
                waits=[ls1], inc="vC"))
        ls3 = o("vector", lambda e: e.tensor_tensor(
            t["lam"][:], t["lam"][:], t["sm8I"][:], ALU.add),
            waits=ls2 + [consts], inc="vC")
        ls4 = o("scalar", lambda e: e.activation(
            out=t["sq1"][:], in_=t["lam"][:], func=AF.Exp), waits=[ls3], inc="sC")
        sqmt = o("vector", lambda e: e.tensor_tensor(
            t["sqm"][:], t["sq1"][:], pB[:, 0:1024], ALU.mult),
            waits=[ls4, sq[-1]], inc="vC")
        pb_free = [sqmt]
        pd_free = pd_free + [ls2[-1]]
        sch.serialize = SER("s4")
        xinits = []
        for l in range(8):
            xinits.append(o("gpsimd", lambda e, l=l, par=par, n=n: e.tensor_scalar(
                t["xch"][:, l, 0:128], t["ktv"][:, par, 1, l, :],
                t["bet"][:, l * 32 + n:l * 32 + n + 1], None, ALU.mult, ALU.bypass),
                waits=[evacV, e6, x_free[0]], inc="gC"))
            xinits.append(o("gpsimd", lambda e, l=l, par=par, n=n: e.tensor_scalar(
                t["xch"][:, l, 128:256], t["ktv"][:, par, 0, l, :],
                t["ccl"][:, l * 32 + n:l * 32 + n + 1], None, ALU.mult, ALU.bypass),
                waits=[evacK, e9, x_free[0]], inc="gC"))
        x_cur = xinits[-1]
        sch.serialize = SER("s5")
        wtcur, wucur = wt0, wu0
        src = 0
        x_deps = [x_cur]
        for k in range(7):
            amm = []
            for l in range(8):
                psX = pC if l < 4 else pD
                off = (l % 4) * 256
                amm.append(o("tensor", lambda e, l=l, src=src, psX=psX, off=off: e.matmul(
                    psX[:, off:off + 256], t["wt"][:, src, l, :], t["xch"][:, l, :],
                    start=True, stop=False),
                    waits=[wtcur] + x_deps + pc_free + pd_free, inc="tC"))
                amm.append(o("tensor", lambda e, l=l, psX=psX, off=off: e.matmul(
                    psX[:, off:off + 256], t["ident"][:], t["xch"][:, l, :],
                    start=False, stop=True), waits=[consts], inc="tC"))
            pc_free, pd_free = [], []
            if k % 2 == 0:
                evC = o("scalar", lambda e: e.activation(
                    out=t["xch"][:, 0:4, :], in_=pC[:, 0:1024], func=AF.Copy),
                    waits=amm, inc="sC")
                evD = o("vector", lambda e: e.tensor_copy(
                    t["xch"][:, 4:8, :], pD[:, 0:1024]), waits=amm, inc="vC")
            else:
                evC = o("vector", lambda e: e.tensor_copy(
                    t["xch"][:, 0:4, :], pC[:, 0:1024]), waits=amm, inc="vC")
                evD = o("scalar", lambda e: e.activation(
                    out=t["xch"][:, 4:8, :], in_=pD[:, 0:1024], func=AF.Copy),
                    waits=amm, inc="sC")
            pc_free = [evC]
            pd_free = [evD]
            xall = [evC, evD]
            if k < 6:
                smm = []
                for l in range(8):
                    smm.append(o("tensor", lambda e, l=l, src=src: e.matmul(
                        pA[:, l * 128:(l + 1) * 128], t["wu"][:, src, l, :],
                        t["wt"][:, src, l, :], start=True, stop=True),
                        waits=[wtcur, wucur] + pa_free, inc="tC"))
                    smm.append(o("tensor", lambda e, l=l, src=src: e.matmul(
                        pB[:, l * 128:(l + 1) * 128], t["wt"][:, src, l, :],
                        t["wu"][:, src, l, :], start=True, stop=True),
                        waits=pb_free, inc="tC"))
                pa_free, pb_free = [], []
                dst = 1 - src
                nwt = o("scalar", lambda e, dst=dst: e.activation(
                    out=t["wt"][:, dst], in_=pA[:, 0:1024], func=AF.Copy),
                    waits=smm + [wtb_free[dst]], inc="sC")
                nwu = o("vector", lambda e, dst=dst: e.tensor_copy(
                    t["wu"][:, dst], pB[:, 0:1024]), waits=smm + [wub_free[dst]],
                    inc="vC")
                pa_free = [nwt]
                pb_free = [nwu]
                wtb_free[src] = amm[-1]
                wub_free[src] = smm[-1]
                wtcur, wucur = nwt, nwu
                src = dst
            else:
                wtb_free[src] = amm[-1]
            x_cur = xall[-1]
            x_deps = xall
        uv_tok = x_deps
        sch.serialize = SER("s6")
        zb = o("vector", lambda e: e.tensor_copy(t["zb"][:], t["zst"][:]),
               waits=[z_tok[0], zb_tok[0]], inc="vC")
        dks = []
        for l in range(8):
            dks.append(o("gpsimd", lambda e, l=l, par=par, n=n: e.tensor_scalar(
                t["dk"][:, l, :], t["ktv"][:, par, 0, l, :],
                t["dcl"][:, l * 32 + n:l * 32 + n + 1], None, ALU.mult, ALU.bypass),
                waits=[evacK, dstt, xinits[-1]], inc="gC"))
        sch.serialize = SER("s7")
        mmm = []
        for l in range(8):
            mmm.append(o("tensor", lambda e, l=l: e.matmul(
                pA[:, l * 128:(l + 1) * 128], t["xch"][:, l, 128:256],
                t["dk"][:, l, :], start=True, stop=True),
                waits=[dks[l]] + uv_tok + pa_free, inc="tC"))
            mmm.append(o("tensor", lambda e, l=l: e.matmul(
                pB[:, l * 128:(l + 1) * 128], t["xch"][:, l, 128:256],
                t["sqm"][:, l, :], start=True, stop=True),
                waits=[sqmt] + pb_free, inc="tC"))
        pa_free, pb_free = [], []
        m2e = o("scalar", lambda e: e.activation(
            out=t["m2n"][:], in_=pA[:, 0:1024], func=AF.Copy, scale=-1.0),
            waits=mmm, inc="sC")
        b1e = o("vector", lambda e: e.tensor_scalar(
            t["b1n"][:], pB[:, 0:1024], -1.0, None, ALU.mult, ALU.bypass),
            waits=mmm, inc="vC")
        pa_free = [m2e]
        pb_free = [b1e]
        sch.serialize = SER("s8")
        zmm = []
        for l in range(8):
            zmm.append(o("tensor", lambda e, l=l: e.matmul(
                pC[:, l * 128:(l + 1) * 128], t["dk"][:, l, :],
                t["xch"][:, l, 0:128], start=True, stop=False),
                waits=[dks[l]] + uv_tok + pc_free, inc="tC"))
            zmm.append(o("tensor", lambda e, l=l: e.matmul(
                pC[:, l * 128:(l + 1) * 128], t["m2n"][:, l, :],
                t["zb"][:, l, :], start=False, stop=True),
                waits=[m2e, zb], inc="tC"))
        pc_free = []
        znew = []
        for l in range(8):
            znew.append(o("vector", lambda e, l=l, n=n: e.scalar_tensor_tensor(
                t["zst"][:, l, :], t["zst"][:, l, :],
                t["gCrep"][:, l * 32 + n:l * 32 + n + 1],
                pC[:, l * 128:(l + 1) * 128], ALU.mult, ALU.add),
                waits=[zmm[-1], gre], inc="vC"))
        z_tok = [znew[-1]]
        pc_free = [znew[-1]]
        sch.serialize = SER("s9")
        omm = []
        for l in range(8):
            omm.append(o("tensor", lambda e, l=l, par=par: e.matmul(
                pD[:, l * 128:(l + 1) * 128], t["qkf"][:, par, 0, l, :],
                t["zb"][:, l, :], start=True, stop=True),
                waits=[zb] + pd_free, inc="tC"))
            omm.append(o("tensor", lambda e, l=l: e.matmul(
                pA[:, l * 128:(l + 1) * 128], t["b1n"][:, l, :],
                t["zb"][:, l, :], start=True, stop=False),
                waits=[b1e] + pa_free, inc="tC"))
            omm.append(o("tensor", lambda e, l=l: e.matmul(
                pA[:, l * 128:(l + 1) * 128], t["sqm"][:, l, :],
                t["xch"][:, l, 0:128], start=False, stop=True),
                waits=[sqmt] + uv_tok, inc="tC"))
        pd_free = []
        pa_free = []
        ov2 = o("vector", lambda e: e.tensor_copy(t["sq1"][:], pA[:, 0:1024]),
                waits=omm + [ls4], inc="vC")
        pa_free = [ov2]
        oevs = []
        for l in range(8):
            oevs.append(o("vector", lambda e, l=l, n=n: e.scalar_tensor_tensor(
                t["o8"][:, l, :], pD[:, l * 128:(l + 1) * 128],
                t["gtc"][:, l * 32 + n:l * 32 + n + 1], t["sq1"][:, l, :],
                ALU.mult, ALU.add), waits=[ov2, gtc, o8_free[0]], inc="vC"))
        oev = oevs[-1]
        pd_free = [oev]
        od = dma("sync", g["odrw"].rearrange("l tp nn d -> tp l nn d")[:, :, n, :],
                 t["o8"][:], waits=[oev], chain="odma")
        o8_free = [od]
        odma[0] = od
        qk_free[par] = omm[-1]
        ktv_free[par] = dks[-1]
        crw_free[par] = crep[-1]
        x_free = [omm[-1]]
        zb_tok = [omm[-1]]

    sch.barrier()
    sch.serialize = bool(getattr(Sch, "SER_LN", False))
    ol_free = [None]
    lnfin = None
    for l in range(8):
        lo = dma("gpsimd", t["olane"][:], g["odrw"][l, :, :, :],
                 waits=[odma[0], ol_free[0]], chain="olld")
        s1 = o("vector", lambda e: e.tensor_reduce(
            t["lnr"][:, 0, :], t["olane"][:], AX.X, ALU.add), waits=[lo], inc="vC")
        s2 = o("scalar", lambda e: e.activation(
            out=t["lsq"][:], in_=t["olane"][:], func=AF.Square), waits=[lo], inc="sC")
        s3 = o("vector", lambda e: e.tensor_reduce(
            t["lnr"][:, 1, :], t["lsq"][:], AX.X, ALU.add), waits=[s2], inc="vC")
        s4 = o("vector", lambda e: e.tensor_scalar(
            t["lnr"][:, 0, :], t["lnr"][:, 0, :], 1.0 / 128, None, ALU.mult, ALU.bypass),
            waits=[s1], inc="vC")
        s5 = o("vector", lambda e: e.tensor_scalar(
            t["lnr"][:, 1, :], t["lnr"][:, 1, :], 1.0 / 128, None, ALU.mult, ALU.bypass),
            waits=[s3], inc="vC")
        s6 = o("vector", lambda e: e.scalar_tensor_tensor(
            t["lnr2"][:, 0, :], t["lnr"][:, 0, :], -1.0, t["lnr"][:, 0, :],
            ALU.mult, ALU.mult), waits=[s4], inc="vC")
        s7 = o("vector", lambda e: e.tensor_tensor(
            t["lnr"][:, 1, :], t["lnr"][:, 1, :], t["lnr2"][:, 0, :], ALU.add),
            waits=[s5, s6], inc="vC")
        s8 = o("scalar", lambda e: e.activation(
            out=t["lnr"][:, 1, :], in_=t["lnr"][:, 1, :], func=AF.Ln,
            bias=t["eps128"][:, :]), waits=[s7], inc="sC")
        s9 = o("scalar", lambda e: e.activation(
            out=t["lnr"][:, 1, :], in_=t["lnr"][:, 1, :], func=AF.Exp, scale=-0.5),
            waits=[s8], inc="sC")
        sa = o("vector", lambda e: e.tensor_tensor(
            t["lnr2"][:, 1, :], t["lnr"][:, 0, :], t["lnr"][:, 1, :], ALU.mult),
            waits=[s9], inc="vC")
        aps = []
        for nn in range(NCH):
            aps.append(o("vector", lambda e, nn=nn: e.tensor_scalar(
                t["olane"][:, nn, :], t["olane"][:, nn, :],
                t["lnr"][:, 1, nn:nn + 1], t["lnr2"][:, 1, nn:nn + 1],
                ALU.mult, ALU.subtract), waits=[s9, sa], inc="vC"))
        w1 = o("vector", lambda e: e.tensor_tensor(
            t["olane"].ap().rearrange("p a b -> p (a b)"),
            t["olane"].ap().rearrange("p a b -> p (a b)"), t["lnw"][:, 0, :],
            ALU.mult), waits=aps + [consts], inc="vC")
        w2 = o("vector", lambda e: e.tensor_tensor(
            t["olane"].ap().rearrange("p a b -> p (a b)"),
            t["olane"].ap().rearrange("p a b -> p (a b)"), t["lnw"][:, 1, :],
            ALU.add), waits=[w1], inc="vC")
        fo = dma("sync", g["odr"][l, :, :, :], t["olane"][:], waits=[w2], chain="odma")
        ol_free = [fo]
        lnfin = fo
    o("sync", lambda e: e.nop(), waits=[lnfin])


_L1B_NC = None


def l1b_consts():
    c = {}
    NEGM = -1.0e5
    su = np.full((128, 128), NEGM, np.float32)
    sl = np.full((128, 128), NEGM, np.float32)
    si = np.full((128, 128), NEGM, np.float32)
    for a_ in range(128):
        for b_ in range(128):
            if a_ < b_:
                su[a_, b_] = 0.0
                si[a_, b_] = 0.0
            if b_ < a_:
                sl[a_, b_] = 0.0
            if a_ == b_:
                si[a_, b_] = 0.0
    c["smaskS8"] = np.ascontiguousarray(np.repeat(su[:, None, :], 8, axis=1)).astype(BF)
    c["smaskL8"] = np.ascontiguousarray(np.repeat(sl[:, None, :], 8, axis=1)).astype(BF)
    c["smaskI8"] = np.ascontiguousarray(np.repeat(si[:, None, :], 8, axis=1)).astype(BF)
    ti = np.zeros((128, 128), np.float32)
    for a_ in range(128):
        ti[a_, a_:] = 1.0
    c["triuI"] = ti.astype(BF)
    rm = np.ones((8, S), np.float32)
    rm[:, 0::128] = 0.0
    c["rmask"] = rm
    return c


_L1B_CONSTS = None


def l1b_inmaps(inputs, l1a_out):
    global _L1B_CONSTS
    hc = host_consts()
    if _L1B_CONSTS is None:
        _L1B_CONSTS = l1b_consts()
    bc = _L1B_CONSTS
    cw = {k: np.asarray(inputs[f"L1_attn_conv_{k}_w"], np.float32) for k in "qkv"}
    cb = {k: np.asarray(inputs[f"L1_attn_conv_{k}_b"], np.float32) for k in "qkv"}
    lnw = np.asarray(inputs["L1_attn_ln_w"], np.float32)
    lnb = np.asarray(inputs["L1_attn_ln_b"], np.float32)
    lnwrep = np.zeros((128, 2, S), np.float32)
    lnwrep[:, 0, :] = np.tile(lnw, NCH)[None, :]
    lnwrep[:, 1, :] = np.tile(lnb, NCH)[None, :]
    lnwrep = lnwrep.astype(BF)
    ab = l1a_out["ab"]
    in_maps = []
    for core in range(8):
        b, hh = core // 2, core % 2
        heads = [hh * 8 + l for l in range(8)]
        qraw = np.stack([l1a_out["oq"][b, :, h * 128:(h + 1) * 128].T for h in heads])
        kraw = np.stack([l1a_out["ok"][b, :, h * 128:(h + 1) * 128].T for h in heads])
        vraw = np.stack([l1a_out["ov"][b, :, h * 128:(h + 1) * 128].T for h in heads])
        convw = np.stack([np.stack([cw[k][h * 128:(h + 1) * 128, 0, :] for h in heads])
                          for k in "qkv"])
        convb = np.stack([np.stack([cb[k][h * 128:(h + 1) * 128] for h in heads])
                          for k in "qkv"])[..., None]
        abt = np.zeros((128, 2, 256), np.float32)
        abr = np.zeros((2, 8, S), np.float32)
        for li, h in enumerate(heads):
            a_ = ab[b, :, h]
            b_ = ab[b, :, 16 + h]
            abr[0, li] = a_
            abr[1, li] = b_
            abt[:, 0, li * 32:(li + 1) * 32] = a_.reshape(NCH, 128).T
            abt[:, 1, li * 32:(li + 1) * 32] = b_.reshape(NCH, 128).T
        m = {
            "qraw": qraw.astype(BF), "kraw": kraw.astype(BF), "vraw": vraw.astype(BF),
            "convw": convw, "convb": convb,
            "ab_t": abt, "abr": abr, "rmask": bc["rmask"].astype(BF),
            "lnwrep": lnwrep,
            "smaskS8": bc["smaskS8"], "smaskL8": bc["smaskL8"],
            "smaskI8": bc["smaskI8"], "triuI": bc["triuI"],
            "ident": hc["ident"], "ones_row": hc["ones_row"],
        }
        in_maps.append(m)
    return in_maps


def run_l1b(inputs, l1a_out, dbg=False):
    global _L1B_NC
    if dbg:
        _L1B_NC = build_l1b(dbg=True)
    elif _L1B_NC is None:
        _L1B_NC = build_l1b()
    in_maps = l1b_inmaps(inputs, l1a_out)
    res = run_bass_kernel_spmd(_L1B_NC, in_maps, core_ids=list(range(8)))
    o = np.zeros((B, S, D), np.float32)
    dbgout = {}
    for core in range(8):
        b, hh = core // 2, core % 2
        od = np.asarray(res.results[core]["odr"]).astype(np.float32).reshape(8, 128, NCH, 128)
        for li in range(8):
            h = hh * 8 + li
            o[b, :, h * 128:(h + 1) * 128] = od[li].transpose(1, 0, 2).reshape(S, 128)
        if dbg:
            dbgout[core] = {nm: np.asarray(res.results[core][nm + "dr"]).astype(np.float32)
                            for nm in ("qc", "kc", "vc")}
    if dbg:
        return o, dbgout
    return o



def build_l1c():
    nc = bass.Bass("TRN2", target_bir_lowering=False)
    ap = lambda *a, **k: nc.dram_tensor(*a, **k).ap()
    g = {}
    g["x2T"] = ap("x2T", [D, TLOC], BF16, kind="ExternalInput")
    g["oT"] = ap("oT", [D, TLOC], BF16, kind="ExternalInput")
    g["gT"] = ap("gT", [D, TLOC], BF16, kind="ExternalInput")
    g["wo"] = ap("wo", [D, D], F32, kind="ExternalInput")
    g["wr"] = ap("wr", [128, FC], F32, kind="ExternalInput")
    g["br"] = ap("br", [1, 1], F32, kind="ExternalInput")
    g["ones_col"] = ap("ones_col", [128, 1], BF16, kind="ExternalInput")
    g["ones_row"] = ap("ones_row", [1, 128], BF16, kind="ExternalInput")
    g["x3T"] = ap("x3T", [D, TLOC], BF16, kind="ExternalOutput")
    g["lg"] = ap("lg", [1, TLOC], F32, kind="ExternalOutput")

    ctx = ExitStack()
    with ctx:
        E = ctx.enter_context
        t = {}
        t["xres"] = E(nc.sbuf_tensor("xres", [128, FC, TLOC], BF16))
        t["og"] = E(nc.sbuf_tensor("og", [128, FC, TLOC], BF16))
        t["wbuf"] = E(nc.sbuf_tensor("wbuf", [128, 2, FC, 128], BF16))
        t["gs"] = E(nc.sbuf_tensor("gs", [128, 2048], BF16))
        t["onec"] = E(nc.sbuf_tensor("onec", [128, 1], BF16))
        t["oner"] = E(nc.sbuf_tensor("oner", [1, 128], BF16))
        t["eps1"] = E(nc.sbuf_tensor("eps1", [1, 1], F32))
        t["brs"] = E(nc.sbuf_tensor("brs", [1, 1], F32))
        t["sq"] = E(nc.sbuf_tensor("sq", [128, 2, 512], BF16))
        t["stat"] = E(nc.sbuf_tensor("stat", [1, 4, 512], F32))
        t["statb"] = E(nc.sbuf_tensor("statb", [1, 2, 512], BF16))
        t["tmp"] = E(nc.sbuf_tensor("tmp", [128, 512], BF16))
        t["lgr"] = E(nc.sbuf_tensor("lgr", [1, 2, 512], F32))
        t["wrt"] = E(nc.sbuf_tensor("wrt", [128, FC], BF16))
        pA = E(nc.psum_tensor("pA", [128, 1024], F32))
        pB = E(nc.psum_tensor("pB", [128, 1024], F32))
        pC = E(nc.psum_tensor("pC", [128, 1024], F32))

        sch = Sch(nc, ctx)
        emit_l1c(sch, g, t, pA, pB, pC)
        sch.emit(ctx)
    return nc


def emit_l1c(sch, g, t, pA, pB, pC):
    o = sch.op
    dma = sch.dma
    c1 = dma("sync", t["onec"][:], g["ones_col"][:, :], chain="const")
    c2 = dma("sync", t["oner"][:], g["ones_row"][:, :], chain="const")
    c3 = dma("sync", t["brs"][:], g["br"][:, :], chain="const")
    consts = c3
    epsr = o("vector", lambda e: e.memset(t["eps1"][:], 1e-5), inc="vC")
    x2r = g["x2T"].rearrange("(c p) s -> p c s", p=128)
    oTr = g["oT"].rearrange("(c p) s -> p c s", p=128)
    gTr = g["gT"].rearrange("(c p) s -> p c s", p=128)

    lx = dma("gpsimd", t["xres"][:], x2r[:, :, :], chain="xld")
    lo_ = dma("gpsimd", t["og"][:], oTr[:, :, :], chain="xld")
    gdone = []
    gfree = None
    for c in range(FC):
        lg_ = dma("scalar", t["gs"][:], gTr[:, c, :], waits=[gfree], chain="gld")
        sg = o("scalar", lambda e: e.activation(
            out=t["gs"][:], in_=t["gs"][:], func=AF.Sigmoid), waits=[lg_], inc="sC")
        mu = o("vector", lambda e, c=c: e.tensor_tensor(
            t["og"][:, c, :], t["og"][:, c, :], t["gs"][:], ALU.mult),
            waits=[sg, lo_], inc="vC")
        gfree = mu
        gdone.append(mu)

    wore = g["wo"].rearrange("(hc p) eo -> p hc eo", p=128)
    wfree = [None, None]
    d_free = [None, None]
    wo_done = []
    for ec in range(FC):
        par = ec % 2
        lw = dma("gpsimd", t["wbuf"][:, par], wore[:, :, ec * 128:(ec + 1) * 128],
                 waits=[wfree[par]], chain="wld")
        for th in range(4):
            sel = th % 2
            psd = (pA, pB)[sel]
            mm = None
            for hc in range(FC):
                mm = o("tensor", lambda e, hc=hc, th=th, par=par, psd=psd: e.matmul(
                    psd[:, 0:512], t["wbuf"][:, par, hc, :],
                    t["og"][:, hc, th * 512:(th + 1) * 512],
                    start=(hc == 0), stop=(hc == FC - 1)),
                    waits=[lw, gdone[-1], d_free[sel]], inc="tC")
            ev = o("vector", lambda e, ec=ec, th=th, psd=psd: e.tensor_tensor(
                t["xres"][:, ec, th * 512:(th + 1) * 512], psd[:, 0:512],
                t["xres"][:, ec, th * 512:(th + 1) * 512], ALU.add),
                waits=[mm, lx], inc="vC")
            d_free[sel] = ev
            wo_done.append(ev)
        wfree[par] = mm

    lnst = [None]
    h3_done = []
    for tt4 in range(2):
        xf = lambda c, sl, tt4=tt4: t["xres"][:, c, tt4 * 1024 + sl.start:tt4 * 1024 + sl.stop]
        hf = lambda c, sl, tt4=tt4: t["og"][:, c, tt4 * 1024 + sl.start:tt4 * 1024 + sl.stop]
        hd, lnst[0] = emit_ln(sch, t, pA, pB, xf, hf,
                              [wo_done[-1], epsr, consts], lnst[0])
        h3_done.append(hd[-1])
    add_done = None
    for c in range(FC):
        add_done = o("vector", lambda e, c=c: e.tensor_tensor(
            t["xres"][:, c, :], t["xres"][:, c, :], t["og"][:, c, :], ALU.add),
            waits=[h3_done[-1]], inc="vC")
    so = dma("sync", g["x3T"].rearrange("(c p) s -> p c s", p=128), t["xres"][:],
             waits=[add_done], chain="out")
    lwr = dma("gpsimd", t["wrt"][:], g["wr"][:, :], chain="wld")
    lg_done = []
    ps_free = [lnst[0]["pa"], None]
    for th in range(4):
        sel = th % 2
        tt4, half = th // 2, th % 2
        mm = None
        for c in range(FC):
            mm = o("tensor", lambda e, c=c, tt4=tt4, half=half, sel=sel: e.matmul(
                pC[0:1, sel * 512:sel * 512 + 512], t["wrt"][:, c:c + 1],
                t["og"][:, c, tt4 * 1024 + half * 512:tt4 * 1024 + half * 512 + 512],
                start=(c == 0), stop=(c == FC - 1)),
                waits=[lwr, h3_done[-1], ps_free[sel]], inc="tC")
        ev = o("vector", lambda e, th=th, sel=sel: e.tensor_scalar(
            t["lgr"][:, sel, :], pC[0:1, sel * 512:sel * 512 + 512],
            t["brs"][:, :], None, ALU.add, ALU.bypass), waits=[mm, consts], inc="vC")
        dt_ = dma("sync", g["lg"][:, th * 512:(th + 1) * 512], t["lgr"][:, sel, :],
                  waits=[ev], chain="out")
        ps_free[sel] = dt_
        lg_done.append(dt_)
    o("sync", lambda e: e.nop(), waits=[lg_done[-1], so])


_L1C_NC = None


def run_l1c(inputs, x2, o_gate, l1a_out):
    global _L1C_NC
    hc = host_consts()
    if _L1C_NC is None:
        _L1C_NC = build_l1c()
    in_maps = []
    for core in range(8):
        b, hh = core // 2, core % 2
        sl = slice(hh * TLOC, (hh + 1) * TLOC)
        m = {
            "x2T": np.ascontiguousarray(x2[b, sl].T).astype(BF),
            "oT": np.ascontiguousarray(o_gate[b, sl].T).astype(BF),
            "gT": np.ascontiguousarray(l1a_out["og"][b, sl].T).astype(BF),
            "wo": np.asarray(inputs["L1_attn_Wo"], np.float32),
            "wr": tile_vec(inputs["L1_ffn_Wr"]),
            "br": np.asarray(inputs["L1_ffn_br"], np.float32).reshape(1, 1),
            "ones_col": hc["ones_col"],
            "ones_row": hc["ones_row"],
        }
        in_maps.append(m)
    res = run_bass_kernel_spmd(_L1C_NC, in_maps, core_ids=list(range(8)))
    x3 = np.zeros((B, S, D), np.float32)
    lg = np.zeros((B, S), np.float32)
    for core in range(8):
        b, hh = core // 2, core % 2
        sl = slice(hh * TLOC, (hh + 1) * TLOC)
        x3[b, sl] = np.asarray(res.results[core]["x3T"]).astype(np.float32).T
        lg[b, sl] = np.asarray(res.results[core]["lg"]).astype(np.float32)[0]
    return x3, lg



def build_l1d():
    Sch.PARANOID = False
    nc = bass.Bass("TRN2", target_bir_lowering=False)
    ap = lambda *a, **k: nc.dram_tensor(*a, **k).ap()
    g = {}
    g["x3t"] = ap("x3t", [TLOC, D], BF16, kind="ExternalInput")
    g["x3T"] = ap("x3T", [D, TLOC], BF16, kind="ExternalInput")
    g["lgb"] = ap("lgb", [128, 32], F32, kind="ExternalInput")
    g["w1"] = ap("w1", [D, DFF], F32, kind="ExternalInput")
    g["w2"] = ap("w2", [DFF, D], F32, kind="ExternalInput")
    g["b1t"] = ap("b1t", [128, 64], F32, kind="ExternalInput")
    g["b2r"] = ap("b2r", [1, D], BF16, kind="ExternalInput")
    g["jiota"] = ap("jiota", [128, 4], F32, kind="ExternalInput")
    g["iota512"] = ap("iota512", [128, 512], F32, kind="ExternalInput")
    g["sue"] = ap("sue", [128, 128], F32, kind="ExternalInput")
    g["ident"] = ap("ident", [128, 128], BF16, kind="ExternalInput")
    g["identf"] = ap("identf", [128, 128], F32, kind="ExternalInput")
    g["ones_row"] = ap("ones_row", [1, 128], BF16, kind="ExternalInput")
    g["x4T"] = ap("x4T", [D, TLOC], F32, kind="ExternalOutput")
    g["sdr"] = nc.dram_tensor("sdr", [1, 2, 128], F32).ap()
    g["cmdr"] = nc.dram_tensor("cmdr", [1, 32, 128], F32).ap()

    ctx = ExitStack()
    with ctx:
        E = ctx.enter_context
        t = {}
        t["big8"] = E(nc.sbuf_tensor("big8", [128, FC, 2048], BF16))
        t["h1F"] = E(nc.sbuf_tensor("h1F", [128, 64, 512], BF16))
        t["hgF"] = E(nc.sbuf_tensor("hgF", [128, FC, 512], BF16))
        t["hg"] = E(nc.sbuf_tensor("hg", [128, 4, 2048], BF16))
        t["P"] = E(nc.sbuf_tensor("P", [128, 4, 2048], BF16))
        t["o2w"] = t["hg"]
        t["wbuf"] = E(nc.sbuf_tensor("wbuf", [128, 2, FC, 128], BF16))
        t["w2b"] = E(nc.sbuf_tensor("w2b", [128, 2, 512], BF16))
        t["lgb"] = E(nc.sbuf_tensor("lgbs", [128, 32], F32))
        t["m01"] = E(nc.sbuf_tensor("m01", [128, 32], F32))
        t["exw"] = E(nc.sbuf_tensor("exw", [128, 32], F32))
        t["exwb"] = E(nc.sbuf_tensor("exwb", [128, 16], BF16))
        t["cum"] = E(nc.sbuf_tensor("cum", [128, 16], F32))
        t["cm2"] = E(nc.sbuf_tensor("cm2", [128, 32], F32))
        t["rows"] = E(nc.sbuf_tensor("rows", [1, 2, 512], F32))
        t["red"] = E(nc.sbuf_tensor("red", [128, 1], F32))
        t["r1x"] = E(nc.sbuf_tensor("r1x", [1, 128], F32))
        t["sc"] = E(nc.sbuf_tensor("sc", [1, 8], F32))
        t["thr"] = E(nc.sbuf_tensor("thr", [128, 2], F32))
        t["wcol"] = E(nc.sbuf_tensor("wcol", [128, 4], F32))
        t["onef"] = E(nc.sbuf_tensor("onef", [128, 1], F32))
        t["onerf"] = E(nc.sbuf_tensor("onerf", [1, 128], F32))
        t["oner"] = E(nc.sbuf_tensor("onerb", [1, 128], BF16))
        t["ident"] = E(nc.sbuf_tensor("identt", [128, 128], BF16))
        t["jio"] = E(nc.sbuf_tensor("jio", [128, 4], F32))
        t["io512"] = E(nc.sbuf_tensor("io512", [128, 512], F32))
        t["sue"] = E(nc.sbuf_tensor("suet", [128, 128], F32))
        t["eps1"] = E(nc.sbuf_tensor("eps1", [128, 1], F32))
        t["lnc"] = E(nc.sbuf_tensor("lnc", [128, 4], F32))
        t["lnc2"] = E(nc.sbuf_tensor("lnc2", [128, 4], F32))
        t["sq"] = t["hgF"].ap().rearrange("p c n -> p (c n)")[:, 0:2048]
        t["fo"] = E(nc.sbuf_tensor("fo", [128, 2, 512], F32))
        t["tpe"] = E(nc.sbuf_tensor("tpe", [32, 128], F32))
        t["b1t"] = E(nc.sbuf_tensor("b1ts", [128, 64], F32))
        t["b2s"] = E(nc.sbuf_tensor("b2s", [1, D], BF16))
        t["identf"] = E(nc.sbuf_tensor("identfs", [128, 128], F32))
        pA = E(nc.psum_tensor("pA", [128, 1024], F32))
        pB = E(nc.psum_tensor("pB", [128, 1024], F32))
        pC = E(nc.psum_tensor("pC", [128, 1024], F32))
        pT = E(nc.psum_tensor("pT", [128, 512], BF16))

        sch = Sch(nc, ctx)
        emit_l1d(sch, g, t, pA, pB, pC, pT)
        sch.emit(ctx)
    Sch.PARANOID = False
    return nc


def emit_l1d(sch, g, t, pA, pB, pC, pT):
    o = sch.op
    dma = sch.dma
    consts = None
    for dst, src in [(t["lgb"][:], g["lgb"][:, :]), (t["jio"][:], g["jiota"][:, :]),
                     (t["io512"][:], g["iota512"][:, :]), (t["sue"][:], g["sue"][:, :]),
                     (t["ident"][:], g["ident"][:, :]), (t["oner"][:], g["ones_row"][:, :]),
                     (t["identf"][:], g["identf"][:, :]), (t["b1t"][:], g["b1t"][:, :]),
                     (t["b2s"][:], g["b2r"][:, :])]:
        consts = dma("sync", dst, src, chain="const")
    onefm = o("vector", lambda e: e.memset(t["onef"][:], 1.0), inc="vC")
    onerfm = o("vector", lambda e: e.memset(t["onerf"][:], 1.0), inc="vC")
    epsm = o("vector", lambda e: e.memset(t["eps1"][:], 1e-5), inc="vC")
    lx3 = dma("gpsimd", t["big8"][:],
              g["x3t"].rearrange("(tc p) dd -> p tc dd", p=128), chain="xld")

    if getattr(Sch, "L1D_BAR", True):
        sch.barrier()
    lo1 = o("vector", lambda e: e.memset(t["sc"][:, 0:1], -1000.0), inc="vC")
    hi1 = o("vector", lambda e: e.memset(t["sc"][:, 1:2], 1000.0), inc="vC")
    lastb = [lo1, hi1]
    pd_free = [None]
    for it in range(34):
        a1 = o("vector", lambda e: e.tensor_tensor(t["sc"][:, 2:3], t["sc"][:, 0:1],
                                                   t["sc"][:, 1:2], ALU.add),
               waits=lastb, inc="vC")
        a2 = o("vector", lambda e: e.tensor_scalar(t["sc"][:, 2:3], t["sc"][:, 2:3], 0.5,
                                                   None, ALU.mult, ALU.bypass),
               waits=[a1], inc="vC")
        bmm = o("tensor", lambda e: e.matmul(pC[:, 0:1], t["onerf"][:], t["sc"][:, 2:3],
                                             start=True, stop=True),
                waits=[a2, onerfm] + pd_free, inc="tC")
        c1 = o("vector", lambda e: e.tensor_scalar(t["m01"][:], t["lgb"][:],
                                                   pC[:, 0:1], None, ALU.is_ge, ALU.bypass),
               waits=[bmm, consts], inc="vC")
        c2 = o("vector", lambda e: e.tensor_reduce(t["red"][:], t["m01"][:], AX.X, ALU.add),
               waits=[c1], inc="vC")
        cmm = o("tensor", lambda e: e.matmul(pC[0:1, 4:5], t["onef"][:], t["red"][:],
                                             start=True, stop=True),
                waits=[c2, onefm], inc="tC")
        g1 = o("vector", lambda e: e.tensor_scalar(t["sc"][:, 3:4], pC[0:1, 4:5],
                                                   float(K1), None, ALU.is_ge, ALU.bypass),
               waits=[cmm], inc="vC")
        g2 = o("vector", lambda e: e.tensor_scalar(t["sc"][:, 4:5], t["sc"][:, 3:4], -1.0,
                                                   1.0, ALU.mult, ALU.add),
               waits=[g1], inc="vC")
        dl = o("vector", lambda e: e.tensor_tensor(t["sc"][:, 5:6], t["sc"][:, 2:3],
                                                   t["sc"][:, 0:1], ALU.subtract),
               waits=[a2], inc="vC")
        l1 = o("vector", lambda e: e.scalar_tensor_tensor(
            t["sc"][:, 0:1], t["sc"][:, 5:6], t["sc"][:, 3:4], t["sc"][:, 0:1],
            ALU.mult, ALU.add), waits=[g1, dl], inc="vC")
        dh = o("vector", lambda e: e.tensor_tensor(t["sc"][:, 5:6], t["sc"][:, 2:3],
                                                   t["sc"][:, 1:2], ALU.subtract),
               waits=[l1], inc="vC")
        h1 = o("vector", lambda e: e.scalar_tensor_tensor(
            t["sc"][:, 1:2], t["sc"][:, 5:6], t["sc"][:, 4:5], t["sc"][:, 1:2],
            ALU.mult, ALU.add), waits=[g2, dh], inc="vC")
        lastb = [l1, h1]
        pd_free = [g1, c1]

    if getattr(Sch, "L1D_BAR", True):
        sch.barrier()
    bmf = o("tensor", lambda e: e.matmul(pC[:, 0:1], t["onerf"][:], t["sc"][:, 0:1],
                                         start=True, stop=True),
            waits=lastb + pd_free, inc="tC")
    th1 = o("vector", lambda e: e.tensor_copy(t["thr"][:, 0:1], pC[:, 0:1]),
            waits=[bmf], inc="vC")
    th2 = o("vector", lambda e: e.tensor_scalar(t["thr"][:, 1:2], t["thr"][:, 0:1], -1.0,
                                                None, ALU.mult, ALU.bypass),
            waits=[th1], inc="vC")
    m1 = o("vector", lambda e: e.tensor_scalar(t["m01"][:], t["lgb"][:], t["thr"][:, 0:1],
                                               None, ALU.is_ge, ALU.bypass),
          waits=[th1], inc="vC")
    e1 = o("scalar", lambda e: e.activation(out=t["exw"][:], in_=t["lgb"][:], func=AF.Exp,
                                            bias=t["thr"][:, 1:2]),
          waits=[th2, consts], inc="sC")
    e2 = o("vector", lambda e: e.tensor_tensor(t["exw"][:], t["exw"][:], t["m01"][:],
                                               ALU.mult), waits=[e1, m1], inc="vC")
    e3 = o("vector", lambda e: e.tensor_reduce(t["red"][:], t["exw"][:], AX.X, ALU.add),
          waits=[e2], inc="vC")
    smm = o("tensor", lambda e: e.matmul(pC[0:1, 4:5], t["onef"][:], t["red"][:],
                                         start=True, stop=True), waits=[e3], inc="tC")
    rs1 = o("vector", lambda e: e.reciprocal(t["sc"][:, 6:7], pC[0:1, 4:5]),
            waits=[smm], inc="vC")
    eb = o("vector", lambda e: e.tensor_copy(t["exwb"][:], t["exw"][:, 0:16]),
          waits=[e2], inc="vC")

    if getattr(Sch, "L1D_BAR", True):
        sch.barrier()
    im1 = o("tensor", lambda e: e.matmul(pA[:, 0:16], t["sue"][:], t["m01"][:, 0:16],
                                         start=True, stop=True),
            waits=[m1, consts], inc="tC")
    cs1 = o("tensor", lambda e: e.matmul(pA[0:1, 16:32], t["onef"][:], t["m01"][:, 0:16],
                                         start=True, stop=True),
            waits=[m1, onefm], inc="tC")
    sc0 = o("vector", lambda e: e.tensor_copy(t["rows"][0:1, 0, 0:16], pA[0:1, 16:32]),
            waits=[cs1], inc="vC")
    sc1 = o("vector", lambda e: e.tensor_tensor_scan(
        t["cm2"][0:1, 0:16], t["rows"][0:1, 0, 0:16], t["rows"][0:1, 0, 0:16],
        0.0, ALU.add, ALU.bypass), waits=[sc0], inc="vC")
    sc2 = o("vector", lambda e: e.tensor_tensor(t["cm2"][0:1, 16:32], t["cm2"][0:1, 0:16],
                                                pA[0:1, 16:32], ALU.subtract),
            waits=[sc1], inc="vC")
    com = o("tensor", lambda e: e.matmul(pB[:, 0:16], t["onerf"][:], t["cm2"][0:1, 16:32],
                                         start=True, stop=True),
            waits=[sc2, onerfm], inc="tC")
    cu1 = o("vector", lambda e: e.tensor_copy(t["cum"][:], pA[:, 0:16]),
            waits=[im1, sc2], inc="vC")
    cu2 = o("vector", lambda e: e.tensor_tensor(t["cum"][:], t["cum"][:], pB[:, 0:16],
                                                ALU.add), waits=[cu1, com], inc="vC")

    if getattr(Sch, "L1D_BAR", True):
        sch.barrier()
    PT = t["h1F"]
    pt_done = []
    for tc in range(FC):
        p1 = o("vector", lambda e, tc=tc: e.tensor_scalar(
            PT[:, tc, :], t["io512"][:], t["cum"][:, tc:tc + 1], None,
            ALU.is_equal, ALU.bypass), waits=[cu2, consts], inc="vC")
        p2 = o("vector", lambda e, tc=tc: e.tensor_scalar(
            PT[:, tc, :], PT[:, tc, :], t["m01"][:, tc:tc + 1], None,
            ALU.mult, ALU.bypass), waits=[p1], inc="vC")
        pt_done.append(p2)

    if getattr(Sch, "L1D_BAR", True):
        sch.barrier()
    cm1 = o("vector", lambda e: e.tensor_copy(t["cm2"][:, 0:16], t["cum"][:]),
            waits=[cu2], inc="vC")
    cm2b = o("vector", lambda e: e.tensor_copy(t["cm2"][:, 16:32], t["m01"][:, 0:16]),
             waits=[m1, sc2], inc="vC")
    tp = o("tensor", lambda e: e.transpose(pA[0:32, 0:128], t["cm2"][:, :], t["identf"][:]),
           waits=[cm1, cm2b, cu1, im1, cs1, consts], inc="tC")
    te = o("vector", lambda e: e.tensor_copy(t["tpe"][:], pA[0:32, 0:128]),
           waits=[tp], inc="vC")
    td = dma("sync", g["cmdr"][0, :, :], t["tpe"][:], waits=[te], chain="rr")
    if getattr(Sch, "L1D_BAR", True):
        sch.barrier()
    p_done = []
    rep_free = [None]
    cmr = g["cmdr"].rearrange("x (a tc) pp -> x a (tc pp)", a=2)
    for tt in range(4):
        tl = dma("sync", t["rows"][0:1, :, :],
                 cmr[0:1, :, tt * 512:(tt + 1) * 512],
                 waits=[td, rep_free[0]], chain="rr")
        rm1 = o("tensor", lambda e, tt=tt: e.matmul(
            pB[:, 0:512], t["onerf"][:], t["rows"][0:1, 0, :],
            start=True, stop=True), waits=[tl, rep_free[0], onerfm], inc="tC")
        rm2 = o("tensor", lambda e, tt=tt: e.matmul(
            pB[:, 512:1024], t["onerf"][:], t["rows"][0:1, 1, :],
            start=True, stop=True), waits=[tl], inc="tC")
        mre = o("vector", lambda e: e.tensor_copy(t["fo"][:, 0, :], pB[:, 512:1024]),
                waits=[rm2], inc="vC")
        for jc in range(4):
            q1 = o("vector", lambda e, jc=jc, tt=tt: e.tensor_scalar(
                t["P"][:, jc, tt * 512:(tt + 1) * 512], pB[:, 0:512],
                t["jio"][:, jc:jc + 1], None, ALU.is_equal, ALU.bypass),
                waits=[rm1, consts], inc="vC")
            q2 = o("vector", lambda e, jc=jc, tt=tt: e.tensor_tensor(
                t["P"][:, jc, tt * 512:(tt + 1) * 512],
                t["P"][:, jc, tt * 512:(tt + 1) * 512], t["fo"][:, 0, :], ALU.mult),
                waits=[q1, mre], inc="vC")
            p_done.append(q2)
        rep_free[0] = p_done[-1]

    if getattr(Sch, "L1D_BAR", True):
        sch.barrier()
    hg_done = []
    gp_free = [None, None]
    for jb in range(4):
        for ft in range(4):
            sel = ft % 2
            psd = (pA, pB)[sel]
            mm = None
            for tc in range(FC):
                mm = o("tensor", lambda e, jb=jb, ft=ft, tc=tc, psd=psd: e.matmul(
                    psd[:, 0:512], PT[:, tc, jb * 128:(jb + 1) * 128],
                    t["big8"][:, tc, ft * 512:(ft + 1) * 512],
                    start=(tc == 0), stop=(tc == FC - 1)),
                    waits=[pt_done[-1], lx3, gp_free[sel], tp], inc="tC")
            ev = o("vector", lambda e, jb=jb, ft=ft, psd=psd: e.tensor_copy(
                t["hg"][:, jb, ft * 512:(ft + 1) * 512], psd[:, 0:512]),
                waits=[mm], inc="vC")
            gp_free[sel] = ev
            hg_done.append(ev)
    wg = []
    for jb in range(4):
        for tc in range(FC):
            wg.append(o("tensor", lambda e, jb=jb, tc=tc: e.matmul(
                pC[:, 5 + jb:6 + jb], PT[:, tc, jb * 128:(jb + 1) * 128],
                t["exwb"][:, tc:tc + 1], start=(tc == 0), stop=(tc == FC - 1)),
                waits=[pt_done[-1], eb, rs1], inc="tC"))
    rsb = o("tensor", lambda e: e.matmul(pC[:, 9:10], t["onerf"][:], t["sc"][:, 6:7],
                                         start=True, stop=True),
            waits=[rs1, onerfm], inc="tC")
    rse = o("vector", lambda e: e.tensor_copy(t["thr"][:, 1:2], pC[:, 9:10]),
            waits=[rsb], inc="vC")
    wc1 = o("vector", lambda e: e.tensor_scalar(
        t["wcol"][:], pC[:, 5:9], t["thr"][:, 1:2], None, ALU.mult, ALU.bypass),
        waits=[wg[-1], rse], inc="vC")

    if getattr(Sch, "L1D_BAR", True):
        sch.barrier()
    ln_done = []
    for jb in range(4):
        s1 = o("vector", lambda e, jb=jb: e.tensor_reduce(
            t["lnc"][:, jb:jb + 1], t["hg"][:, jb, :], AX.X, ALU.add),
            waits=[hg_done[4 * jb + 3]], inc="vC")
        s2 = o("scalar", lambda e, jb=jb: e.activation(
            out=t["sq"], in_=t["hg"][:, jb, :], func=AF.Square),
            waits=[hg_done[4 * jb + 3], ln_done[-1] if ln_done else None], inc="sC")
        s3 = o("vector", lambda e, jb=jb: e.tensor_reduce(
            t["lnc2"][:, jb:jb + 1], t["sq"], AX.X, ALU.add), waits=[s2], inc="vC")
        s4 = o("vector", lambda e, jb=jb: e.tensor_scalar(
            t["lnc"][:, jb:jb + 1], t["lnc"][:, jb:jb + 1], 1.0 / D, None,
            ALU.mult, ALU.bypass), waits=[s1], inc="vC")
        s5 = o("vector", lambda e, jb=jb: e.scalar_tensor_tensor(
            t["red"][:], t["lnc"][:, jb:jb + 1], -1.0, t["lnc"][:, jb:jb + 1],
            ALU.mult, ALU.mult), waits=[s4], inc="vC")
        s6 = o("vector", lambda e, jb=jb: e.tensor_scalar(
            t["lnc2"][:, jb:jb + 1], t["lnc2"][:, jb:jb + 1], 1.0 / D, None,
            ALU.mult, ALU.bypass), waits=[s3], inc="vC")
        s7 = o("vector", lambda e, jb=jb: e.tensor_tensor(
            t["lnc2"][:, jb:jb + 1], t["lnc2"][:, jb:jb + 1], t["red"][:], ALU.add),
            waits=[s5, s6], inc="vC")
        s8 = o("scalar", lambda e, jb=jb: e.activation(
            out=t["lnc2"][:, jb:jb + 1], in_=t["lnc2"][:, jb:jb + 1], func=AF.Ln,
            bias=t["eps1"][:, :]), waits=[s7, epsm], inc="sC")
        s9 = o("scalar", lambda e, jb=jb: e.activation(
            out=t["lnc2"][:, jb:jb + 1], in_=t["lnc2"][:, jb:jb + 1], func=AF.Exp,
            scale=-0.5), waits=[s8], inc="sC")
        sa = o("vector", lambda e, jb=jb: e.tensor_tensor(
            t["red"][:], t["lnc"][:, jb:jb + 1], t["lnc2"][:, jb:jb + 1], ALU.mult),
            waits=[s9], inc="vC")
        sb = o("vector", lambda e, jb=jb: e.tensor_scalar(
            t["hg"][:, jb, :], t["hg"][:, jb, :], t["lnc2"][:, jb:jb + 1],
            t["red"][:, :], ALU.mult, ALU.subtract), waits=[sa], inc="vC")
        ln_done.append(sb)

    if getattr(Sch, "L1D_BAR", True):
        sch.barrier()
    tr_done = []
    pt_free = [None]
    for fc in range(FC):
        tms = []
        for jb in range(4):
            tms.append(o("tensor", lambda e, fc=fc, jb=jb: e.transpose(
                pT[:, jb * 128:(jb + 1) * 128],
                t["hg"][:, jb, fc * 128:(fc + 1) * 128], t["ident"][:]),
                waits=[ln_done[jb], pt_free[0], consts], inc="tC"))
        ev = o("scalar", lambda e, fc=fc: e.activation(
            out=t["hgF"][:, fc, :], in_=pT[:, 0:512], func=AF.Copy),
            waits=tms, inc="sC")
        pt_free[0] = ev
        tr_done.append(ev)

    if getattr(Sch, "L1D_BAR", True):
        sch.barrier()
    w1re = g["w1"].rearrange("(c p) ff -> p c ff", p=128)
    h1_done = []
    w1_free = [None, None]
    k_free = [None, None]
    for dffc in range(64):
        par = dffc % 2
        lw = dma("gpsimd", t["wbuf"][:, par], w1re[:, :, dffc * 128:(dffc + 1) * 128],
                 waits=[w1_free[par]], chain="wld")
        mm = None
        psd = (pA, pB)[par]
        for c in range(FC):
            mm = o("tensor", lambda e, c=c, par=par, psd=psd: e.matmul(
                psd[:, 0:512], t["wbuf"][:, par, c, :], t["hgF"][:, c, :],
                start=(c == 0), stop=(c == FC - 1)),
                waits=[lw, tr_done[-1], k_free[par], wc1, wg[-1]], inc="tC")
        ev = o("scalar", lambda e, dffc=dffc, psd=psd: e.activation(
            out=t["h1F"][:, dffc, :], in_=psd[:, 0:512], func=AF.Gelu,
            bias=t["b1t"][:, dffc:dffc + 1]), waits=[mm, pt_done[-1]], inc="sC")
        k_free[par] = ev
        w1_free[par] = mm
        h1_done.append(ev)
    if getattr(Sch, "L1D_BAR", True):
        sch.barrier()
    w2re = g["w2"].rearrange("(dc p) ee -> p dc ee", p=128)
    l_done = []
    w2_free = [None, None]
    for et in range(4):
        mm2 = {}
        for dffc in range(64):
            par = dffc % 2
            lw2 = dma("gpsimd", t["w2b"][:, par, :],
                      w2re[:, dffc, et * 512:(et + 1) * 512],
                      waits=[w2_free[par]], chain="w2ld")
            for jb in range(4):
                psd = (pA, pB)[jb // 2]
                off = (jb % 2) * 512
                mm2[jb] = o("tensor", lambda e, dffc=dffc, jb=jb, par=par, psd=psd, off=off: e.matmul(
                    psd[:, off:off + 512], t["h1F"][:, dffc, jb * 128:(jb + 1) * 128],
                    t["w2b"][:, par, :], start=(dffc == 0), stop=(dffc == 63)),
                    waits=[lw2, h1_done[-1]] + ([l_done[-1]] if dffc == 0 and l_done else []),
                    inc="tC")
            w2_free[par] = mm2[3]
        badd = {}
        for jb in range(4):
            psd = (pA, pB)[jb // 2]
            off = (jb % 2) * 512
            badd[jb] = o("tensor", lambda e, jb=jb, et=et, psd=psd, off=off: e.matmul(
                psd[:, off:off + 512], t["oner"][:],
                t["b2s"][0:1, et * 512:(et + 1) * 512],
                start=False, stop=True), waits=[mm2[jb], consts], inc="tC")
        for jb in range(4):
            psd = (pA, pB)[jb // 2]
            off = (jb % 2) * 512
            l_done.append(o("vector", lambda e, jb=jb, et=et, psd=psd, off=off: e.tensor_scalar(
                t["o2w"][:, jb, et * 512:(et + 1) * 512], psd[:, off:off + 512],
                t["wcol"][:, jb:jb + 1], None, ALU.mult, ALU.bypass),
                waits=[badd[jb], wc1], inc="vC"))

    if getattr(Sch, "L1D_BAR", True):
        sch.barrier()
    lxf = dma("gpsimd", t["big8"][:], g["x3T"].rearrange("(c p) s -> p c s", p=128),
              waits=[hg_done[-1]], chain="xld")
    out_done = []
    s_free = [None, None]
    fo_free = [None, None]
    for ec in range(FC):
        for tt in range(4):
            sel = tt % 2
            psd = (pA, pB)[sel]
            mm = None
            for jb in range(4):
                mm = o("tensor", lambda e, ec=ec, tt=tt, jb=jb, psd=psd: e.matmul(
                    psd[:, 0:512], t["o2w"][:, jb, ec * 128:(ec + 1) * 128],
                    t["P"][:, jb, tt * 512:(tt + 1) * 512],
                    start=(jb == 0), stop=(jb == 3)),
                    waits=[l_done[-1], p_done[-1], s_free[sel]], inc="tC")
            fv = o("vector", lambda e, ec=ec, tt=tt, psd=psd, sel=sel: e.tensor_tensor(
                t["fo"][:, sel, :], psd[:, 0:512],
                t["big8"].ap().rearrange("p c n -> p (c n)")[:, ec * 2048 + tt * 512:ec * 2048 + (tt + 1) * 512],
                ALU.add), waits=[mm, lxf, fo_free[sel]], inc="vC")
            s_free[sel] = fv
            dt_ = dma("sync", g["x4T"].rearrange("(c p) s -> p c s", p=128)[:, ec, tt * 512:(tt + 1) * 512],
                      t["fo"][:, sel, :], waits=[fv], chain="out")
            fo_free[sel] = dt_
            out_done.append(dt_)
    o("sync", lambda e: e.nop(), waits=[out_done[-1]])


_L1D_NC = None
_L1D_CONSTS = None


def l1d_consts():
    c = {}
    c["jiota"] = np.ascontiguousarray(
        (np.arange(4)[None, :] * 128 + np.arange(128)[:, None]).astype(np.float32))
    c["iota512"] = np.ascontiguousarray(
        np.tile(np.arange(512, dtype=np.float32)[None, :], (128, 1)))
    sue = np.zeros((128, 128), np.float32)
    for p2 in range(128):
        sue[0:p2, p2] = 1.0
    c["sue"] = sue
    c["identf"] = np.eye(128, dtype=np.float32)
    return c


def run_l1d(inputs, x3, lg):
    global _L1D_NC, _L1D_CONSTS
    hc = host_consts()
    if _L1D_CONSTS is None:
        _L1D_CONSTS = l1d_consts()
    dc = _L1D_CONSTS
    if _L1D_NC is None:
        _L1D_NC = build_l1d()
    in_maps = []
    for core in range(8):
        b, hh = core // 2, core % 2
        pm = perm_for(core)
        lgp = lg[b][pm]
        m = {
            "x3t": np.ascontiguousarray(x3[b, hh * TLOC:(hh + 1) * TLOC]).astype(BF),
            "x3T": np.ascontiguousarray(x3[b, hh * TLOC:(hh + 1) * TLOC].T).astype(BF),
            "lgb": np.ascontiguousarray(lgp.reshape(32, 128).T).astype(np.float32),
            "w1": np.asarray(inputs["L1_ffn_W1"], np.float32),
            "w2": np.asarray(inputs["L1_ffn_W2"], np.float32),
            "b1t": np.ascontiguousarray(
                np.asarray(inputs["L1_ffn_b1"], np.float32).reshape(64, 128).T),
            "b2r": np.asarray(inputs["L1_ffn_b2"], np.float32).reshape(1, D).astype(BF),
            "jiota": dc["jiota"], "iota512": dc["iota512"], "sue": dc["sue"],
            "ident": hc["ident"], "identf": dc["identf"],
            "ones_row": hc["ones_row"],
        }
        in_maps.append(m)
    res = run_bass_kernel_spmd(_L1D_NC, in_maps, core_ids=list(range(8)))
    x4 = np.zeros((B, S, D), np.float32)
    for core in range(8):
        b, hh = core // 2, core % 2
        x4[b, hh * TLOC:(hh + 1) * TLOC] = np.asarray(res.results[core]["x4T"]).astype(np.float32).T
    return x4


def kernel(**inputs):
    x2 = run_l0(inputs)
    l1a = run_l1a(inputs, x2)
    o = run_l1b(inputs, l1a)
    x3, lg = run_l1c(inputs, x2, o, l1a)
    x4 = run_l1d(inputs, x3, lg)
    return x4
